# version 06
# baseline (speedup 1.0000x reference)
from contextlib import ExitStack
import numpy as np
import ml_dtypes
import concourse.bass as bass
import concourse.mybir as mybir
from concourse.bass_utils import run_bass_kernel_spmd

F32 = mybir.dt.float32
BF16 = mybir.dt.bfloat16
AF = mybir.ActivationFunctionType
ALU = mybir.AluOpType
AX = mybir.AxisListType.X
bfdt = ml_dtypes.bfloat16

ENGINES = ("sync", "tensor", "vector", "scalar", "gpsimd")
DMA_ENG = ("sync", "scalar", "gpsimd")


class Prog:

    NLANES = 8

    def __init__(self, nc):
        self.nc = nc
        self.c_sem = {e: nc.alloc_semaphore(f"c_{e}") for e in ENGINES}
        self.d_sem = {(e, l): nc.alloc_semaphore(f"d_{e}{l}")
                      for e in DMA_ENG for l in range(self.NLANES)}
        self.ccnt = {e: 0 for e in ENGINES}
        self.dcnt = {(e, l): 0 for e in DMA_ENG for l in range(self.NLANES)}
        self.dtot = {e: 0 for e in DMA_ENG}
        self.steps = {e: [] for e in ENGINES}
        self.lw = {}
        self.readers = {}
        self.wm = {e: {} for e in ENGINES}
        self.n_inst = 0

    def _sem_of(self, kind, oe):
        return self.c_sem[oe] if kind == "c" else self.d_sem[oe]

    def _wait(self, e, waits, dep):
        kind, oe, cnt = dep
        if oe == e and kind == "c":
            if e == "tensor":
                return
        got = self.wm[e].get((kind, oe), 0)
        if cnt <= got:
            return
        waits[(kind, oe)] = max(waits.get((kind, oe), 0), cnt)

    def emit(self, e, fn, reads=(), writes=(), dma=False):
        assert e in ENGINES
        if dma:
            assert e in DMA_ENG
        waits = {}
        for k in reads:
            d = self.lw.get(k)
            if d is not None:
                self._wait(e, waits, d)
        for k in writes:
            d = self.lw.get(k)
            if d is not None:
                self._wait(e, waits, d)
            for r in self.readers.get(k, ()):
                self._wait(e, waits, r)
        if dma:
            lane = self.dtot[e] % self.NLANES
            self.dtot[e] += 1
            if self.dcnt[(e, lane)] > 0:
                self._wait(e, waits, ("d", (e, lane), self.dcnt[(e, lane)]))
        wl = []
        for (kind, oe), cnt in waits.items():
            wl.append((self._sem_of(kind, oe), cnt))
            self.wm[e][(kind, oe)] = cnt
        if dma:
            self.dcnt[(e, lane)] += 16
            inc = (self.d_sem[(e, lane)], 16)
            me = ("d", (e, lane), self.dcnt[(e, lane)])
        else:
            self.ccnt[e] += 1
            inc = (self.c_sem[e], 1)
            me = ("c", e, self.ccnt[e])
        self.steps[e].append((fn, wl, inc))
        self.n_inst += 1
        for k in writes:
            self.lw[k] = me
            self.readers[k] = []
        for k in reads:
            self.readers.setdefault(k, []).append(me)

    def barrier(self):
        tgt = [("c", oe, self.ccnt[oe]) for oe in ENGINES if self.ccnt[oe]] + \
              [("d", (oe, l), self.dcnt[(oe, l)]) for oe in DMA_ENG
               for l in range(self.NLANES) if self.dcnt[(oe, l)]]
        for e in ENGINES:
            wl = []
            for kind, oe, cnt in tgt:
                if self.wm[e].get((kind, oe), 0) >= cnt:
                    continue
                wl.append((self._sem_of(kind, oe), cnt))
                self.wm[e][(kind, oe)] = cnt
            if wl:
                self.steps[e].append((None, wl, None))

    def dma(self, dst, src, e="sync", reads=(), writes=(), transpose=False, slow=False):
        if transpose:
            fn = lambda eng, d=dst, s=src: eng.dma_start_transpose(d, s)
        elif slow:
            fn = lambda eng, d=dst, s=src: eng.dma_start(d, s, allow_slow_non_contiguous=True)
        else:
            fn = lambda eng, d=dst, s=src: eng.dma_start(d, s)
        self.emit(e, fn, reads, writes, dma=True)

    def mm(self, out, lhsT, rhs, start, stop, reads=(), writes=()):
        fn = lambda eng, o=out, l=lhsT, r=rhs, st=start, sp=stop: eng.matmul(o, l, r, start=st, stop=sp)
        self.emit("tensor", fn, reads, writes)

    def trans(self, out, in_, ident, reads=(), writes=()):
        fn = lambda eng, o=out, i=in_, d=ident: eng.transpose(o, i, d)
        self.emit("tensor", fn, reads, writes)

    def act(self, out, in_, func, bias=0.0, scale=1.0, accum_out=None, reads=(), writes=()):
        fn = lambda eng, o=out, i=in_, f=func, b=bias, s=scale, a=accum_out: \
            eng.activation(o, i, f, bias=b, scale=s, accum_out=a)
        self.emit("scalar", fn, reads, writes)

    def tt(self, out, a, b, op, e="vector", reads=(), writes=()):
        fn = lambda eng, o=out, x=a, y=b, p=op: eng.tensor_tensor(o, x, y, p)
        self.emit(e, fn, reads, writes)

    def ts(self, out, in_, s1, s2, op0, op1=ALU.bypass, e="vector", reads=(), writes=()):
        fn = lambda eng, o=out, i=in_, a=s1, b=s2, p0=op0, p1=op1: \
            eng.tensor_scalar(o, i, a, b, p0, p1)
        self.emit(e, fn, reads, writes)

    def stt(self, out, in0, scalar, in1, op0, op1, e="vector", reads=(), writes=()):
        fn = lambda eng, o=out, i0=in0, s=scalar, i1=in1, p0=op0, p1=op1: \
            eng.scalar_tensor_tensor(o, i0, s, i1, p0, p1)
        self.emit(e, fn, reads, writes)

    def copy(self, out, in_, e="vector", reads=(), writes=()):
        fn = lambda eng, o=out, i=in_: eng.tensor_copy(o, i)
        self.emit(e, fn, reads, writes)

    def red(self, out, in_, op, negate=False, absval=False, e="vector", reads=(), writes=()):
        fn = lambda eng, o=out, i=in_, p=op, n=negate, a=absval: \
            eng.tensor_reduce(o, i, AX, p, apply_absolute_value=a, negate=n)
        self.emit(e, fn, reads, writes)

    def memset(self, ap, val, e="vector", reads=(), writes=()):
        fn = lambda eng, a=ap, v=val: eng.memset(a, v)
        self.emit(e, fn, reads, writes)

    def recip(self, out, in_, reads=(), writes=()):
        fn = lambda eng, o=out, i=in_: eng.reciprocal(o, i)
        self.emit("vector", fn, reads, writes)

    def scan(self, out, d0, d1, init, op0, op1, reads=(), writes=()):
        fn = lambda eng, o=out, a=d0, b=d1, i=init, p0=op0, p1=op1: \
            eng.tensor_tensor_scan(o, a, b, i, p0, p1)
        self.emit("vector", fn, reads, writes)

    def finalize(self):
        nc = self.nc
        self.barrier()
        with nc.Block() as block:
            for e in ENGINES:
                mine = self.steps[e]
                if not mine and e != "sync":
                    continue

                def body(eng, mine=mine):
                    for fn, wl, inc in mine:
                        for sem, cnt in wl:
                            eng.wait_ge(sem, cnt)
                        if fn is not None:
                            inst = fn(eng)
                            inst.then_inc(inc[0], inc[1])
                getattr(block, e)(body)


class Ctx:
    def __init__(self, nc, p):
        self.nc, self.p = nc, p
        self.id128 = nc.alloc_sbuf_tensor("c_id128", [128, 128], BF16)
        self.ones_bf_row = nc.alloc_sbuf_tensor("c_ones_bf_row", [1, 128], BF16)
        self.ones_f_row = nc.alloc_sbuf_tensor("c_ones_f_row", [1, 128], F32)
        self.ones_bf_col = nc.alloc_sbuf_tensor("c_ones_bf_col", [128, 1], BF16)
        self.ones_f_col = nc.alloc_sbuf_tensor("c_ones_f_col", [128, 1], F32)
        self.mask128 = nc.alloc_sbuf_tensor("c_mask128", [128, 128], BF16)
        for t, arr in ((self.id128, np.eye(128, dtype=bfdt)),
                       (self.mask128, np.triu(np.ones((128, 128), bfdt))),
                       (self.ones_bf_row, np.ones((1, 128), bfdt)),
                       (self.ones_f_row, np.ones((1, 128), np.float32)),
                       (self.ones_bf_col, np.ones((128, 1), bfdt)),
                       (self.ones_f_col, np.ones((128, 1), np.float32))):
            d = nc.inline_tensor(arr, t.name + "_d").ap()
            p.dma(t[:], d, writes=(t.name,))
        self.pbA = nc.alloc_psum_tensor("pbA", [128, 1024], F32)
        self.pbB = nc.alloc_psum_tensor("pbB", [128, 1024], F32)
        self.pbC = nc.alloc_psum_tensor("pbC", [128, 512], F32)
        self.pbD = nc.alloc_psum_tensor("pbD", [128, 512], F32)
        self.pbT = nc.alloc_psum_tensor("pbT", [128, 1024], BF16)
        self.pbU = nc.alloc_psum_tensor("pbU", [128, 1024], BF16)

    def rsqrt(self, p, out, in_, key, eps=0.0, rkeys=()):
        p.act(out, in_, AF.Ln, bias=eps, reads=rkeys, writes=(key,))
        p.act(out, out, AF.Exp, scale=-0.5, reads=(key,), writes=(key,))



def cast_load(nc, p, stage, dst_slices_srcs, key, stage_key):
    for dst, srcap, cols in dst_slices_srcs:
        p.dma(stage[:, 0:cols], srcap, writes=(stage_key,))
        p.copy(dst, stage[:, 0:cols], reads=(stage_key,), writes=(key,))


def build_l0(nc, p, cx, x_ap, w_aps, x1_ap, dbg=None, dev=False):
    NT = 32
    SEQ = 2048
    Wq, Wk, Wv, Mem, Mn, Mw, Wup, Wdown = (
        w_aps["L0_attn_Wq"], w_aps["L0_attn_Wk"], w_aps["L0_attn_Wv"],
        w_aps["L0_attn_memory"], w_aps["L0_attn_memory_norm"],
        w_aps["L0_attn_memory_weight"], w_aps["L0_ffn_w_up"], w_aps["L0_ffn_w_down"])

    l0ws = ExitStack()
    wq_sb = l0ws.enter_context(nc.sbuf_tensor("wq_sb", [128, 8, 1024], BF16))
    wkd_sb = l0ws.enter_context(nc.sbuf_tensor("wkd_sb", [128, 8, 8, 128], BF16))
    wv_sb = l0ws.enter_context(nc.sbuf_tensor("wv_sb", [128, 8, 512], BF16))
    mem_sb = l0ws.enter_context(nc.sbuf_tensor("mem_sb", [128, 4, 512], BF16))
    mn_sb = l0ws.enter_context(nc.sbuf_tensor("mn_sb", [128, 4], BF16))
    with nc.sbuf_tensor("l0stage", [128, 1024], F32) as stg:
        Wq_r = Wq.rearrange("(c p) n -> c p n", p=128)
        cast_load(nc, p, stg, [(wq_sb[:, c, :], Wq_r[c], 1024) for c in range(8)],
                  "wq", "l0stg")
        wk_r = Wk.rearrange("(c p) (h i) -> c p h i", p=128, i=64)
        wk_v = wk_r.rearrange("c p h i -> c p (h i)")
        for c in range(8):
            p.dma(stg[:, 0:512], wk_v[c], writes=("l0stg",))
            sview = stg[:, 0:512].rearrange("p (h i) -> p h i", i=64)
            p.copy(wkd_sb[:, c, :, 0:64], sview, reads=("l0stg",), writes=("wkd0",))
            p.copy(wkd_sb[:, c, :, 64:128], sview, reads=("l0stg",), writes=("wkd1",))
        Wv_r = Wv.rearrange("(c p) n -> c p n", p=128)
        cast_load(nc, p, stg, [(wv_sb[:, c, :], Wv_r[c], 512) for c in range(8)],
                  "wv", "l0stg")
        Mem_r = Mem.rearrange("(c p) n -> c p n", p=128)
        cast_load(nc, p, stg, [(mem_sb[:, c, :], Mem_r[c], 512) for c in range(4)],
                  "mem", "l0stg")
        Mn_r = Mn.rearrange("(c p) -> c p", p=128)
        for c in range(4):
            p.dma(stg[:, 0:1], Mn_r[c].rearrange("(p o) -> p o", o=1), writes=("l0stg",))
            p.copy(mn_sb[:, c:c + 1], stg[:, 0:1], reads=("l0stg",), writes=("mn",))
    p.barrier()

    mw_sb = l0ws.enter_context(nc.sbuf_tensor("mw_sb", [1, 1], F32))
    wrow = l0ws.enter_context(nc.sbuf_tensor("wrow", [1, 2], F32))
    wcols = l0ws.enter_context(nc.sbuf_tensor("wcols", [128, 2], F32))
    p.dma(mw_sb[:], Mw.rearrange("(a b) -> a b", a=1), writes=("mw",))
    p.act(wrow[:, 0:1], mw_sb[:], AF.Exp, scale=-1.0, reads=("mw",), writes=("wr_w",))
    p.ts(wrow[:, 0:1], wrow[:, 0:1], 1.0, None, ALU.add, reads=("wr_w",), writes=("wr_w",))
    p.recip(wrow[:, 0:1], wrow[:, 0:1], reads=("wr_w",), writes=("wr_w",))
    p.ts(wrow[:, 1:2], wrow[:, 0:1], 1.0, -1.0, ALU.subtract, ALU.mult,
         reads=("wr_w",), writes=("wr_w1",))
    p.mm(cx.pbC[0:128, 0:2], cx.ones_f_row[:], wrow[:], True, True,
         reads=("wr_w", "wr_w1", "c_ones_f_row"), writes=("pbC",))
    p.copy(wcols[:], cx.pbC[0:128, 0:2], reads=("pbC",), writes=("wcols",))
    wcol, w1col = wcols[:, 0:1], wcols[:, 1:2]

    KIND = "ExternalOutput" if dev else "Internal"
    wtup = nc.dram_tensor("wtup", [4096, 1024], BF16, kind=KIND).ap()
    wtdown = nc.dram_tensor("wtdown", [1024, 4096], BF16, kind=KIND).ap()

    with (nc.sbuf_tensor("t_w0", [128, 4096], F32) as tw0,
          nc.sbuf_tensor("t_w1", [128, 4096], F32) as tw1,
          nc.sbuf_tensor("t_sg", [128, 4096], F32) as tsg,
          nc.sbuf_tensor("t_ab", [128, 4096], F32) as tab,
          nc.sbuf_tensor("t_wt", [128, 4096], BF16) as twt,
          nc.sbuf_tensor("t_sm", [128, 4], F32) as tsm):

        def ternarize(W, n_rows, n_cols, out_dram, tag):
            nt = n_rows // 128
            wld = (tw0[:, 0:n_cols], tw1[:, 0:n_cols])
            acc, r = tsm[:, 0:1], tsm[:, 1:2]
            thc = tsm[:, 2:3]
            th = tsm[0:1, 3:4]
            p.memset(acc, 0.0, writes=("t_acc",))
            Wr = W.rearrange("(t p) n -> t p n", p=128)
            for i in range(nt):
                b, bk = wld[i % 2], f"twl{i % 2}"
                p.dma(b, Wr[i], writes=(bk,))
                p.red(r, b, ALU.add, absval=True, reads=(bk,), writes=("t_r",))
                p.tt(acc, acc, r, ALU.add, reads=("t_r", "t_acc"), writes=("t_acc",))
            p.mm(cx.pbC[0:1, 0:1], acc, cx.ones_f_col[:], True, True,
                 reads=("t_acc", "c_ones_f_col"), writes=("pbC",))
            p.ts(th, cx.pbC[0:1, 0:1], 0.5 / (n_rows * n_cols), None, ALU.mult,
                 reads=("pbC",), writes=("t_th",))
            p.mm(cx.pbC[0:128, 0:1], cx.ones_f_row[:], th, True, True,
                 reads=("t_th", "c_ones_f_row"), writes=("pbC",))
            p.copy(thc, cx.pbC[0:128, 0:1], reads=("pbC",), writes=("t_thc",))
            for i in range(nt):
                b, bk = wld[i % 2], f"twl{i % 2}"
                p.dma(b, Wr[i], writes=(bk,))
                p.act(tsg[:, 0:n_cols], b, AF.Sign, reads=(bk,), writes=("t_sg",))
                p.act(tab[:, 0:n_cols], b, AF.Abs, reads=(bk,), writes=("t_ab",))
                p.ts(tab[:, 0:n_cols], tab[:, 0:n_cols], thc[:, 0:1], None, ALU.is_gt,
                     reads=("t_ab", "t_thc"), writes=("t_ab",))
                p.tt(twt[:, 0:n_cols], tsg[:, 0:n_cols], tab[:, 0:n_cols], ALU.mult,
                     reads=("t_sg", "t_ab"), writes=("t_wt",))
                p.dma(out_dram[i * 128:(i + 1) * 128, :], twt[:, 0:n_cols],
                      reads=("t_wt",), writes=(f"twd_{tag}{i}",))

        ternarize(Wup, 4096, 1024, wtup, "u")
        ternarize(Wdown, 1024, 4096, wtdown, "d")
        p.barrier()

    xnbf_d = nc.dram_tensor("xnbf", [4096, 1024], BF16, kind=KIND).ap()
    attn_d = nc.dram_tensor("attn0", [4096, 1024], F32, kind=KIND).ap()
    rstd_d = nc.dram_tensor("rstd0", [4096], F32, kind=KIND).ap()
    if dbg is not None:
        dbg["xnbf"] = xnbf_d
        dbg["attn0"] = attn_d

    x_r = x_ap.rearrange("(t p) n -> t p n", p=128)
    x1_r = x1_ap.rearrange("(t p) n -> t p n", p=128)
    xn_r = xnbf_d.rearrange("(t p) n -> t p n", p=128)
    at_r = attn_d.rearrange("(t p) n -> t p n", p=128)

    with (nc.sbuf_tensor("xt", [128, 2, 1024], F32) as xt,
          nc.sbuf_tensor("lnt", [128, 6], F32) as lnt,
          nc.sbuf_tensor("xnt", [128, 2, 1024], BF16) as xnt,
          nc.sbuf_tensor("sqt", [128, 1024], F32) as sqt):
        m = lnt[:, 0:1]; s2 = lnt[:, 1:2]; var = lnt[:, 2:3]
        rstd = lnt[:, 3:4]; rsf = lnt[:, 4:5]; m2 = lnt[:, 5:6]
        for i in range(NT):
            b = i % 2
            xk, nk = f"xt{b}", f"xnt{b}"
            p.dma(xt[:, b], x_r[i], writes=(xk,))
            p.red(m, xt[:, b], ALU.add, reads=(xk,), writes=("ln_m",))
            p.act(sqt[:], xt[:, b], AF.Square, reads=(xk,), writes=("ln_sq",))
            p.red(s2, sqt[:], ALU.add, reads=("ln_sq",), writes=("ln_s2",))
            p.ts(m, m, 1.0 / 1024, None, ALU.mult, reads=("ln_m",), writes=("ln_m",))
            p.ts(s2, s2, 1.0 / 1024, 1e-5, ALU.mult, ALU.add, reads=("ln_s2",), writes=("ln_s2",))
            p.tt(m2, m, m, ALU.mult, reads=("ln_m",), writes=("ln_m2",))
            p.tt(var, s2, m2, ALU.subtract, reads=("ln_s2", "ln_m2"), writes=("ln_v",))
            cx.rsqrt(p, rstd, var, "ln_rstd", rkeys=("ln_v",))
            p.ts(xnt[:, b], xt[:, b], m[:, 0:1], rstd[:, 0:1], ALU.subtract, ALU.mult,
                 reads=(xk, "ln_m", "ln_rstd"), writes=(nk,))
            p.dma(xn_r[i], xnt[:, b], reads=(nk,), writes=(f"xn_d{i}",))
            p.tt(sqt[:], xnt[:, b], xnt[:, b], ALU.mult, reads=(nk,), writes=("ln_sq",))
            p.red(s2, sqt[:], ALU.add, reads=("ln_sq",), writes=("ln_s2",))
            p.ts(s2, s2, 1.0 / 1024, None, ALU.mult, reads=("ln_s2",), writes=("ln_s2",))
            p.act(rsf, s2, AF.Ln, reads=("ln_s2",), writes=("ln_rsf",))
            p.act(rsf, rsf, AF.Exp, scale=0.5, reads=("ln_rsf",), writes=("ln_rsf",))
            p.ts(rsf, rsf, 1e-8, None, ALU.add, reads=("ln_rsf",), writes=("ln_rsf",))
            p.recip(rsf, rsf, reads=("ln_rsf",), writes=("ln_rsf",))
            p.dma(rstd_d[i * 128:(i + 1) * 128].rearrange("(p o) -> p o", o=1), rsf,
                  reads=("ln_rsf",), writes=(f"rs_d{i}",))
    p.barrier()

    for s in range(2):
        T0 = s * 16
        with nc.sbuf_tensor(f"xnfm{s}", [128, 8, 2048], BF16) as xnfm:
            for c in range(8):
                p.dma(xnfm[:, c, :], xnbf_d[s * SEQ:(s + 1) * SEQ, c * 128:(c + 1) * 128],
                      transpose=True, reads=tuple(f"xn_d{T0 + i}" for i in range(16)),
                      writes=(f"xnfm{c}",))
            with (nc.sbuf_tensor(f"kdup{s}", [128, 8, 2048], BF16) as kdup,
                  nc.sbuf_tensor(f"vaug{s}", [128, 16, 8, 68], BF16) as vaug):
                for j in range(8):
                    for qb in range(4):
                        sl = slice(qb * 512, (qb + 1) * 512)
                        for c in range(8):
                            p.mm(cx.pbA[0:128, 0:512], wkd_sb[:, c, j, :], xnfm[:, c, sl],
                                 c == 0, c == 7, reads=("wkd0", "wkd1", f"xnfm{c}"),
                                 writes=("pbA",))
                        p.copy(kdup[:, j, sl], cx.pbA[0:128, 0:512], reads=("pbA",),
                               writes=(f"kd{j}",))
                p.memset(vaug[:, :, :, 64:65], 1.0, writes=("va_ones",))
                for i in range(16):
                    for c in range(8):
                        p.mm(cx.pbB[0:128, 0:512], xnfm[:, c, i * 128:(i + 1) * 128],
                             wv_sb[:, c, :], c == 0, c == 7,
                             reads=(f"xnfm{c}", "wv"), writes=("pbB",))
                    for h in range(8):
                        p.copy(vaug[:, i, h, 0:64], cx.pbB[0:128, h * 64:(h + 1) * 64],
                               reads=("pbB",), writes=(f"va{i}",))

                with (nc.sbuf_tensor(f"qfm{s}", [128, 8, 512], BF16) as qfm,
                      nc.sbuf_tensor(f"sq_bf{s}", [128, 8, 512], BF16) as sq_bf,
                      nc.sbuf_tensor(f"elut{s}", [128, 8, 512], F32) as elut,
                      nc.sbuf_tensor(f"ebuf{s}", [128, 2, 1024], BF16) as ebuf,
                      nc.sbuf_tensor(f"pvs{s}", [65, 2, 512], BF16) as pvs,
                      nc.sbuf_tensor(f"acc{s}", [128, 4, 1024], F32) as acc,
                      nc.sbuf_tensor(f"eps_t{s}", [128, 8], F32) as eps_t):
                    for qb in range(4):
                        qsl = slice(qb * 512, (qb + 1) * 512)
                        for c in range(8):
                            for d in range(8):
                                p.mm(cx.pbA[0:128, 0:512], wq_sb[:, d, c * 128:(c + 1) * 128],
                                     xnfm[:, d, qsl], d == 0, d == 7,
                                     reads=("wq", f"xnfm{d}"), writes=("pbA",))
                            p.copy(qfm[:, c, :], cx.pbA[0:128, 0:512], reads=("pbA",),
                                   writes=(f"qfm{c}",))
                        qk = tuple(f"qfm{c}" for c in range(8))
                        p.ts(elut[:], qfm[:], 0.0, None, ALU.min, reads=qk, writes=("elut",))
                        p.act(elut[:], elut[:], AF.Exp, reads=("elut",), writes=("elut",))
                        p.ts(sq_bf[:], qfm[:], 0.0, -1.0, ALU.max, ALU.add, reads=qk,
                             writes=("sqbf",))
                        p.tt(sq_bf[:], sq_bf[:], elut[:], ALU.add, reads=("sqbf", "elut"),
                             writes=("sqbf",))
                        for g in range(2):
                            for j in range(4):
                                for c in range(4):
                                    cc = 4 * g + c
                                    p.mm(cx.pbA[0:128, 0:512],
                                         sq_bf[:, cc, j * 128:(j + 1) * 128],
                                         mem_sb[:, c, :], c == 0, c == 3,
                                         reads=("sqbf", "mem"), writes=("pbA",))
                                    p.mm(cx.pbC[0:128, 0:1],
                                         sq_bf[:, cc, j * 128:(j + 1) * 128],
                                         mn_sb[:, c:c + 1], c == 0, c == 3,
                                         reads=("sqbf", "mn"), writes=("pbC",))
                                p.recip(eps_t[:, 0:1], cx.pbC[0:128, 0:1], reads=("pbC",),
                                        writes=("rdw",))
                                p.tt(eps_t[:, 0:1], eps_t[:, 0:1], wcol, ALU.mult,
                                     reads=("rdw", "wcols"), writes=("rdw",))
                                p.ts(acc[:, j, g * 512:(g + 1) * 512], cx.pbA[0:128, 0:512],
                                     eps_t[:, 0:1], None, ALU.mult,
                                     reads=("pbA", "rdw"), writes=(f"acc{j}",))
                        for hp in range(8):
                            for kt in range(16):
                                pb, pk = (cx.pbA, "pbA") if kt % 2 == 0 else (cx.pbB, "pbB")
                                eb, ek = (kt % 2), f"eb{kt % 2}"
                                ksl = slice(kt * 128, (kt + 1) * 128)
                                p.mm(pb[0:128, 0:512], kdup[0:64, hp, ksl], qfm[0:64, hp, :],
                                     True, True, reads=(f"kd{hp}", f"qfm{hp}"), writes=(pk,))
                                p.mm(pb[0:128, 512:1024], kdup[64:128, hp, ksl],
                                     qfm[64:128, hp, :],
                                     True, True, reads=(f"kd{hp}", f"qfm{hp}"), writes=(pk,))
                                p.act(ebuf[:, eb], pb[:], AF.Exp, scale=0.125, reads=(pk,),
                                      writes=(ek,))
                                p.mm(cx.pbC[0:65, 0:512], vaug[:, kt, hp, 0:65],
                                     ebuf[:, eb, 0:512], kt == 0, kt == 15,
                                     reads=(f"va{kt}", "va_ones", ek), writes=("pbC",))
                                p.mm(cx.pbD[0:65, 0:512], vaug[:, kt, hp, 0:65],
                                     ebuf[:, eb, 512:1024], kt == 0, kt == 15,
                                     reads=(f"va{kt}", "va_ones", ek), writes=("pbD",))
                            for hh in range(2):
                                pvp, pvk = (cx.pbC, "pbC") if hh == 0 else (cx.pbD, "pbD")
                                pT, pTk = (cx.pbT, "pbT") if hh == 0 else (cx.pbU, "pbU")
                                n = 2 * hp + hh
                                p.copy(pvs[:, hh], pvp[0:65, 0:512], reads=(pvk,),
                                       writes=(f"pvs{hh}",))
                                for j in range(4):
                                    p.trans(pT[0:128, j * 256:j * 256 + 65],
                                            pvs[0:65, hh, j * 128:(j + 1) * 128],
                                            cx.id128[0:65, 0:65],
                                            reads=(f"pvs{hh}", "c_id128"), writes=(pTk,))
                                for j in range(4):
                                    p.recip(eps_t[:, 1:2], pT[0:128, j * 256 + 64:j * 256 + 65],
                                            reads=(pTk,), writes=("rin1",))
                                    p.tt(eps_t[:, 1:2], eps_t[:, 1:2], w1col, ALU.mult,
                                         reads=("rin1", "wcols"), writes=("rin1",))
                                    p.stt(acc[:, j, n * 64:(n + 1) * 64],
                                          pT[0:128, j * 256:j * 256 + 64], eps_t[:, 1:2],
                                          acc[:, j, n * 64:(n + 1) * 64], ALU.mult, ALU.add,
                                          reads=(pTk, "rin1", f"acc{j}"), writes=(f"acc{j}",))
                        for j in range(4):
                            p.dma(at_r[T0 + qb * 4 + j], acc[:, j, :], reads=(f"acc{j}",),
                                  writes=(f"at_d{T0 + qb * 4 + j}",))
                p.barrier()

            with (nc.sbuf_tensor(f"hstore{s}", [128, 32, 512], BF16) as hst,
                  nc.sbuf_tensor(f"xffn{s}", [128, 8, 512], BF16) as xffn,
                  nc.sbuf_tensor(f"wupT{s}", [128, 2, 8, 512], BF16) as wupT,
                  nc.sbuf_tensor(f"wdT{s}", [128, 2, 1024], BF16) as wdT,
                  nc.sbuf_tensor(f"r1row{s}", [1, 512], F32) as r1row,
                  nc.sbuf_tensor(f"r1b{s}", [128, 512], F32) as r1b,
                  nc.sbuf_tensor(f"h2{s}", [128, 512], BF16) as h2t,
                  nc.sbuf_tensor(f"ff{s}", [128, 8], F32) as ff,
                  nc.sbuf_tensor(f"xres{s}", [128, 1024], F32) as xres,
                  nc.sbuf_tensor(f"ares{s}", [128, 1024], F32) as ares,
                  nc.sbuf_tensor(f"x1t{s}", [128, 1024], F32) as x1t):
                for blk in range(4):
                    t0 = s * SEQ + blk * 512
                    bt0 = blk * 512
                    p.dma(r1row[:], rstd_d[t0:t0 + 512].rearrange("(o n) -> o n", o=1),
                          reads=tuple(f"rs_d{t0 // 128 + i}" for i in range(4)),
                          writes=("r1row",))
                    p.mm(cx.pbD[0:128, 0:512], cx.ones_f_row[:], r1row[:], True, True,
                         reads=("r1row", "c_ones_f_row"), writes=("pbD",))
                    p.copy(r1b[:], cx.pbD[0:128, 0:512], reads=("pbD",), writes=("r1b",))
                    for c in range(8):
                        p.tt(xffn[:, c, :], xnfm[:, c, bt0:bt0 + 512], r1b[:], ALU.mult,
                             reads=(f"xnfm{c}", "r1b"), writes=(f"xf{c}",))
                    for fg in range(8):
                        wb, wk_ = fg % 2, f"wupT{fg % 2}"
                        for d in range(8):
                            p.dma(wupT[:, wb, d, :],
                                  wtup[fg * 512:(fg + 1) * 512, d * 128:(d + 1) * 128],
                                  transpose=True, writes=(wk_,))
                        for fs in range(4):
                            fi = fg * 4 + fs
                            pb, pk = (cx.pbA, "pbA") if fi % 2 == 0 else (cx.pbB, "pbB")
                            for d in range(8):
                                p.mm(pb[0:128, 0:512], wupT[:, wb, d, fs * 128:(fs + 1) * 128],
                                     xffn[:, d, :], d == 0, d == 7,
                                     reads=(wk_,) + tuple(f"xf{dd}" for dd in range(8)),
                                     writes=(pk,))
                            p.act(hst[:, fi, :], pb[0:128, 0:512], AF.Gelu,
                                  reads=(pk,), writes=(f"h{fi}",))
                            p.tt(h2t[:], hst[:, fi, :], hst[:, fi, :], ALU.mult,
                                 reads=(f"h{fi}",), writes=("h2",))
                            for j in range(4):
                                p.mm(cx.pbC[0:128, j * 32 + fi:j * 32 + fi + 1],
                                     h2t[:, j * 128:(j + 1) * 128],
                                     cx.ones_bf_col[:], True, True,
                                     reads=("h2", "c_ones_bf_col"), writes=("pbC",))
                    hkeys = tuple(f"h{fi}" for fi in range(32))
                    for j in range(4):
                        p.red(ff[:, 4 + j:5 + j], cx.pbC[0:128, j * 32:j * 32 + 32],
                              ALU.add, reads=("pbC",), writes=("ff_ss",))
                    p.ts(ff[:, 0:4], ff[:, 4:8], 1.0 / 4096, None, ALU.mult,
                         reads=("ff_ss",), writes=("ff_r2",))
                    p.act(ff[:, 0:4], ff[:, 0:4], AF.Ln, reads=("ff_r2",), writes=("ff_r2",))
                    p.act(ff[:, 0:4], ff[:, 0:4], AF.Exp, scale=0.5, reads=("ff_r2",),
                          writes=("ff_r2",))
                    p.ts(ff[:, 0:4], ff[:, 0:4], 1e-8, None, ALU.add, reads=("ff_r2",),
                         writes=("ff_r2",))
                    p.recip(ff[:, 0:4], ff[:, 0:4], reads=("ff_r2",), writes=("ff_r2",))
                    for jp in range(2):
                        for fc in range(32):
                            wb2, wdk = fc % 2, f"wdT{fc % 2}"
                            p.dma(wdT[:, wb2, :], wtdown[0:1024, fc * 128:(fc + 1) * 128],
                                  transpose=True, writes=(wdk,))
                            for jj in range(2):
                                j = 2 * jp + jj
                                pb, pk = (cx.pbA, "pbA") if jj == 0 else (cx.pbB, "pbB")
                                p.mm(pb[0:128, 0:512], hst[:, fc, j * 128:(j + 1) * 128],
                                     wdT[:, wb2, 0:512], fc == 0, fc == 31,
                                     reads=(wdk,) + hkeys, writes=(pk,))
                                p.mm(pb[0:128, 512:1024], hst[:, fc, j * 128:(j + 1) * 128],
                                     wdT[:, wb2, 512:1024], fc == 0, fc == 31,
                                     reads=(wdk,) + hkeys, writes=(pk,))
                        for jj in range(2):
                            j = 2 * jp + jj
                            ti = t0 // 128 + j
                            pb, pk = (cx.pbA, "pbA") if jj == 0 else (cx.pbB, "pbB")
                            p.dma(xres[:], x_r[ti], writes=("xres",))
                            p.dma(ares[:], at_r[ti], reads=(f"at_d{ti}",), writes=("ares",))
                            p.tt(x1t[:], xres[:], ares[:], ALU.add, reads=("xres", "ares"),
                                 writes=("x1t",))
                            p.stt(x1t[:], pb[0:128, 0:1024], ff[:, j:j + 1], x1t[:],
                                  ALU.mult, ALU.add, reads=(pk, "ff_r2", "x1t"),
                                  writes=("x1t",))
                            p.dma(x1_r[ti], x1t[:], reads=("x1t",), writes=(f"x1_d{ti}",))
                p.barrier()
    l0ws.close()
    p.barrier()
    return {"x1_r": x1_r}


def build_ln(nc, p, cx, src_ap, dst_ap, tag, src_keys=None):
    src_r = src_ap.rearrange("(t p) n -> t p n", p=128)
    dst_r = dst_ap.rearrange("(t p) n -> t p n", p=128)
    NT = src_r.shape[0]
    with (nc.sbuf_tensor(f"lnx{tag}", [128, 2, 1024], F32) as xt,
          nc.sbuf_tensor(f"lns{tag}", [128, 6], F32) as lnt,
          nc.sbuf_tensor(f"lno{tag}", [128, 2, 1024], BF16) as xnt,
          nc.sbuf_tensor(f"lnq{tag}", [128, 1024], F32) as sqt):
        m = lnt[:, 0:1]; s2 = lnt[:, 1:2]; var = lnt[:, 2:3]
        rstd = lnt[:, 3:4]; m2 = lnt[:, 5:6]
        for i in range(NT):
            b = i % 2
            xk, nk = f"{tag}xt{b}", f"{tag}xnt{b}"
            rk = (src_keys[i],) if src_keys else ()
            p.dma(xt[:, b], src_r[i], reads=rk, writes=(xk,))
            p.red(m, xt[:, b], ALU.add, reads=(xk,), writes=(f"{tag}m",))
            p.act(sqt[:], xt[:, b], AF.Square, reads=(xk,), writes=(f"{tag}sq",))
            p.red(s2, sqt[:], ALU.add, reads=(f"{tag}sq",), writes=(f"{tag}s2",))
            p.ts(m, m, 1.0 / 1024, None, ALU.mult, reads=(f"{tag}m",), writes=(f"{tag}m",))
            p.ts(s2, s2, 1.0 / 1024, 1e-5, ALU.mult, ALU.add, reads=(f"{tag}s2",),
                 writes=(f"{tag}s2",))
            p.tt(m2, m, m, ALU.mult, reads=(f"{tag}m",), writes=(f"{tag}m2",))
            p.tt(var, s2, m2, ALU.subtract, reads=(f"{tag}s2", f"{tag}m2"),
                 writes=(f"{tag}v",))
            cx.rsqrt(p, rstd, var, f"{tag}rstd", rkeys=(f"{tag}v",))
            p.ts(xnt[:, b], xt[:, b], m[:, 0:1], rstd[:, 0:1], ALU.subtract, ALU.mult,
                 reads=(xk, f"{tag}m", f"{tag}rstd"), writes=(nk,))
            p.dma(dst_r[i], xnt[:, b], reads=(nk,), writes=(f"{tag}d{i}",))
    p.barrier()


def build_gla(nc, p, cx, xn1bf_d, w_aps, gla_d, dev=False):
    SEQ = 2048
    HB = 256
    NHB = SEQ // HB
    LWq, LWk, LWv, LWg = (w_aps["L1_attn_Wq"], w_aps["L1_attn_Wk"],
                          w_aps["L1_attn_Wv"], w_aps["L1_attn_Wg"])
    GK1, GK2, GKB = w_aps["L1_attn_gk_w1"], w_aps["L1_attn_gk_w2"], w_aps["L1_attn_gk_b"]
    LWo, RMSW = w_aps["L1_attn_Wo"], w_aps["L1_attn_rms_w"]

    glws = ExitStack()
    lwq = glws.enter_context(nc.sbuf_tensor("lwq", [128, 8, 1024], BF16))
    lwk = glws.enter_context(nc.sbuf_tensor("lwk", [128, 8, 1024], BF16))
    lwv = glws.enter_context(nc.sbuf_tensor("lwv", [128, 8, 2048], BF16))
    lwg = glws.enter_context(nc.sbuf_tensor("lwg", [128, 8, 2048], BF16))
    gk1 = glws.enter_context(nc.sbuf_tensor("gk1", [128, 8, 16], BF16))
    gk2 = glws.enter_context(nc.sbuf_tensor("gk2", [16, 1024], BF16))
    gkb = glws.enter_context(nc.sbuf_tensor("gkb", [128, 8], F32))
    rw = glws.enter_context(nc.sbuf_tensor("rw", [128, 4], F32))
    with nc.sbuf_tensor("glstage", [128, 2048], F32) as stg:
        for nm, dst, W, nch, cols in (("lwq", lwq, LWq, 8, 1024), ("lwk", lwk, LWk, 8, 1024),
                                      ("lwv", lwv, LWv, 8, 2048), ("lwg", lwg, LWg, 8, 2048),
                                      ("gk1", gk1, GK1, 8, 16)):
            Wr = W.rearrange("(c p) n -> c p n", p=128)
            cast_load(nc, p, stg, [(dst[:, c, :], Wr[c], cols) for c in range(nch)],
                      nm, "glstg")
        p.dma(stg[0:16, 0:1024], GK2, writes=("glstg",))
        p.copy(gk2[:], stg[0:16, 0:1024], reads=("glstg",), writes=("gk2",))
        GKB_r = GKB.rearrange("(c p) -> c p", p=128)
        for c in range(8):
            p.dma(gkb[:, c:c + 1], GKB_r[c].rearrange("(p o) -> p o", o=1), writes=("gkb",))
        RW_r = RMSW.rearrange("(c p) -> c p", p=128)
        for c in range(4):
            p.dma(rw[:, c:c + 1], RW_r[c].rearrange("(p o) -> p o", o=1), writes=("rw",))
    p.barrier()

    og_d = nc.dram_tensor("og_d", [2048, 2048], BF16,
                          kind="ExternalOutput" if dev else "Internal").ap()
    gla_r = gla_d.rearrange("(t p) n -> t p n", p=128)

    for s in range(2):
        with (nc.sbuf_tensor(f"gxnfm{s}", [128, 8, 2048], BF16) as xnfm,
              nc.sbuf_tensor(f"gS{s}", [128, 2, 4, 512], F32) as Sst,
              nc.sbuf_tensor(f"gS0b{s}", [128, 2, 512], BF16) as S0b):
            for c in range(8):
                p.dma(xnfm[:, c, :], xn1bf_d[s * SEQ:(s + 1) * SEQ, c * 128:(c + 1) * 128],
                      transpose=True, reads=tuple(f"N1d{s * 16 + i}" for i in range(16)),
                      writes=(f"gxn{c}",))
            p.memset(Sst[:], 0.0, writes=("S",))
            for hb in range(NHB):
                t0 = hb * HB
                hbk = f"{s}_{hb}"
                with ExitStack() as _st:
                    glc = _st.enter_context(nc.sbuf_tensor(f"g_gl{hbk}", [128, 256], F32))
                    Gb = _st.enter_context(nc.sbuf_tensor(f"g_G{hbk}", [128, 8, 256], F32))
                    tmp = _st.enter_context(nc.sbuf_tensor(f"g_tmp{hbk}", [128, 2, 256], F32))
                    Eb = _st.enter_context(nc.sbuf_tensor(f"g_E{hbk}", [128, 256], F32))
                    zb = _st.enter_context(nc.sbuf_tensor(f"g_z{hbk}", [128, 128], F32))
                    qt = _st.enter_context(nc.sbuf_tensor(f"g_qt{hbk}", [128, 8, 256], BF16))
                    ktl = _st.enter_context(nc.sbuf_tensor(f"g_kt{hbk}", [128, 8, 256], BF16))
                    khat = _st.enter_context(nc.sbuf_tensor(f"g_kh{hbk}", [128, 8, 256], BF16))
                    vb = _st.enter_context(nc.sbuf_tensor(f"g_v{hbk}", [128, 2, 2048], BF16))
                    gateb = _st.enter_context(nc.sbuf_tensor(f"g_gate{hbk}", [128, 16, 256], BF16))
                    ogb = _st.enter_context(nc.sbuf_tensor(f"g_og{hbk}", [128, 16, 256], BF16))
                    Asb = _st.enter_context(nc.sbuf_tensor(f"g_A{hbk}", [128, 128], BF16))
                    kTb = _st.enter_context(nc.sbuf_tensor(f"g_kT{hbk}", [128, 2, 128], BF16))
                    sqb = _st.enter_context(nc.sbuf_tensor(f"g_sq{hbk}", [128, 512], BF16))
                    rrow = _st.enter_context(nc.sbuf_tensor(f"g_rr{hbk}", [1, 128], F32))
                    rsb = _st.enter_context(nc.sbuf_tensor(f"g_rs{hbk}", [128, 4], F32))
                    rstdb = _st.enter_context(nc.sbuf_tensor(f"g_rb{hbk}", [128, 128], F32))
                    rlow = _st.enter_context(nc.sbuf_tensor(f"g_r{hbk}", [16, 256], BF16))
                    p.memset(zb[:], 0.0, writes=("z",))
                    tsl = slice(t0, t0 + 256)
                    for d in range(8):
                        p.mm(cx.pbC[0:16, 0:256], gk1[:, d, :], xnfm[:, d, tsl],
                             d == 0, d == 7, reads=("gk1", f"gxn{d}"), writes=("pbC",))
                    p.copy(rlow[:], cx.pbC[0:16, 0:256], reads=("pbC",), writes=("rlow",))
                    for c in range(8):
                        p.mm(cx.pbD[0:128, 0:256], gk2[:, c * 128:(c + 1) * 128], rlow[:],
                             True, True, reads=("gk2", "rlow"), writes=("pbD",))
                        fn = (lambda eng, o=glc[:], i=cx.pbD[0:128, 0:256],
                              b=gkb[:, c:c + 1]: eng.activation(o, i, AF.Identity, bias=b))
                        p.emit("scalar", fn, reads=("pbD", "gkb"), writes=("glc",))
                        p.act(tmp[:, 0, :], glc[:], AF.Abs, reads=("glc",),
                              writes=("tmp0",))
                        p.act(tmp[:, 0, :], tmp[:, 0, :], AF.Exp, scale=-1.0,
                              reads=("tmp0",), writes=("tmp0",))
                        p.act(tmp[:, 0, :], tmp[:, 0, :], AF.Ln, bias=1.0,
                              reads=("tmp0",), writes=("tmp0",))
                        p.ts(tmp[:, 1, :], glc[:], 0.0, None, ALU.min,
                             reads=("glc",), writes=("tmp1",))
                        p.tt(glc[:], tmp[:, 1, :], tmp[:, 0, :], ALU.subtract,
                             reads=("tmp1", "tmp0"), writes=("glc",))
                        for ci in range(2):
                            p.scan(Gb[:, c, ci * 128:(ci + 1) * 128],
                                   glc[:, ci * 128:(ci + 1) * 128], zb[:], 0.0,
                                   ALU.add, ALU.add, reads=("glc", "z"),
                                   writes=(f"G{c}",))
                    for c in range(8):
                        p.act(Eb[:], Gb[:, c, :], AF.Exp, scale=1.0 / 16,
                              reads=(f"G{c}",), writes=("Eb",))
                        for d in range(8):
                            p.mm(cx.pbA[0:128, 0:256], lwq[:, d, c * 128:(c + 1) * 128],
                                 xnfm[:, d, tsl], d == 0, d == 7,
                                 reads=("lwq", f"gxn{d}"), writes=("pbA",))
                        p.stt(qt[:, c, :], cx.pbA[0:128, 0:256], 0.0625, Eb[:],
                              ALU.mult, ALU.mult, reads=("pbA", "Eb"), writes=(f"qt{c}",))
                        p.act(Eb[:], Gb[:, c, :], AF.Exp, scale=-1.0 / 16,
                              reads=(f"G{c}",), writes=("Eb",))
                        for d in range(8):
                            p.mm(cx.pbB[0:128, 0:256], lwk[:, d, c * 128:(c + 1) * 128],
                                 xnfm[:, d, tsl], d == 0, d == 7,
                                 reads=("lwk", f"gxn{d}"), writes=("pbB",))
                        p.tt(ktl[:, c, :], cx.pbB[0:128, 0:256], Eb[:], ALU.mult,
                             reads=("pbB", "Eb"), writes=(f"kt{c}",))
                        for ci in range(2):
                            p.act(rsb[:, 0:1], Gb[:, c, ci * 128 + 127:ci * 128 + 128],
                                  AF.Exp, scale=1.0 / 16, reads=(f"G{c}",), writes=("E4",))
                            p.ts(khat[:, c, ci * 128:(ci + 1) * 128],
                                 ktl[:, c, ci * 128:(ci + 1) * 128], rsb[:, 0:1], None,
                                 ALU.mult, reads=(f"kt{c}", "E4"), writes=(f"kh{c}",))
                    for tt in range(2):
                        for vc in range(4):
                            for d in range(8):
                                p.mm(cx.pbA[0:128, 0:512],
                                     xnfm[:, d, t0 + tt * 128:t0 + (tt + 1) * 128],
                                     lwv[:, d, vc * 512:(vc + 1) * 512], d == 0, d == 7,
                                     reads=("lwv", f"gxn{d}"), writes=("pbA",))
                            p.copy(vb[:, tt, vc * 512:(vc + 1) * 512],
                                   cx.pbA[0:128, 0:512], reads=("pbA",), writes=(f"v{tt}",))
                    for vc in range(16):
                        for d in range(8):
                            p.mm(cx.pbB[0:128, 0:256], lwg[:, d, vc * 128:(vc + 1) * 128],
                                 xnfm[:, d, tsl], d == 0, d == 7,
                                 reads=("lwg", f"gxn{d}"), writes=("pbB",))
                        p.act(Eb[:], cx.pbB[0:128, 0:256], AF.Exp, scale=-1.0,
                              reads=("pbB",), writes=("Eb",))
                        p.ts(Eb[:], Eb[:], 1.0, None, ALU.add, reads=("Eb",),
                             writes=("Eb",))
                        p.recip(Eb[:], Eb[:], reads=("Eb",), writes=("Eb",))
                        p.tt(gateb[:, vc, :], cx.pbB[0:128, 0:256], Eb[:], ALU.mult,
                             reads=("pbB", "Eb"), writes=(f"gate{vc}",))
                    for ci in range(2):
                        csl = slice(ci * 128, (ci + 1) * 128)
                        for h in range(4):
                            cc0, cc1 = 2 * h, 2 * h + 1
                            for i, cc in enumerate((cc0, cc1)):
                                p.copy(S0b[:, i, :], Sst[:, i, h, :], reads=("S",),
                                       writes=(f"S0b{i}",))
                            p.mm(cx.pbC[0:128, 0:128], ktl[:, cc0, csl], qt[:, cc0, csl],
                                 True, False, reads=(f"kt{cc0}", f"qt{cc0}"),
                                 writes=("pbC",))
                            p.mm(cx.pbC[0:128, 0:128], ktl[:, cc1, csl], qt[:, cc1, csl],
                                 False, True, reads=(f"kt{cc1}", f"qt{cc1}"),
                                 writes=("pbC",))
                            p.tt(Asb[:], cx.pbC[0:128, 0:128], cx.mask128[:], ALU.mult,
                                 reads=("pbC", "c_mask128"), writes=("Asb",))
                            for j in range(4):
                                jsl = slice(512 * h + 128 * j, 512 * h + 128 * (j + 1))
                                p.mm(cx.pbA[0:128, 128 * j:128 * (j + 1)],
                                     vb[:, ci, jsl], Asb[:], True, False,
                                     reads=(f"v{ci}", "Asb"), writes=("pbA",))
                                p.mm(cx.pbA[0:128, 128 * j:128 * (j + 1)],
                                     S0b[:, 0, 128 * j:128 * (j + 1)], qt[:, cc0, csl],
                                     False, False, reads=("S0b0", f"qt{cc0}"),
                                     writes=("pbA",))
                                p.mm(cx.pbA[0:128, 128 * j:128 * (j + 1)],
                                     S0b[:, 1, 128 * j:128 * (j + 1)], qt[:, cc1, csl],
                                     False, True, reads=("S0b1", f"qt{cc1}"),
                                     writes=("pbA",))
                            for i, cc in enumerate((cc0, cc1)):
                                p.trans(cx.pbT[0:128, 0:128], khat[:, cc, csl],
                                        cx.id128[:], reads=(f"kh{cc}", "c_id128"),
                                        writes=("pbT",))
                                p.copy(kTb[:, i, :], cx.pbT[0:128, 0:128], reads=("pbT",),
                                       writes=(f"kT{i}",))
                            for i, cc in enumerate((cc0, cc1)):
                                p.mm(cx.pbB[0:128, 512 * i:512 * (i + 1)], kTb[:, i, :],
                                     vb[:, ci, 512 * h:512 * (h + 1)], True, True,
                                     reads=(f"kT{i}", f"v{ci}"), writes=("pbB",))
                            for i, cc in enumerate((cc0, cc1)):
                                p.act(rsb[:, 1:2], Gb[:, cc, ci * 128 + 127:ci * 128 + 128],
                                      AF.Exp, scale=1.0 / 16, reads=(f"G{cc}",),
                                      writes=("E4b",))
                                p.stt(Sst[:, i, h, :], Sst[:, i, h, :], rsb[:, 1:2],
                                      cx.pbB[0:128, 512 * i:512 * (i + 1)], ALU.mult,
                                      ALU.add, reads=("S", "E4b", "pbB"), writes=("S",))
                            p.act(sqb[:], cx.pbA[0:128, 0:512], AF.Square, reads=("pbA",),
                                  writes=("osq",))
                            for j in range(4):
                                p.mm(cx.pbD[0:1, 0:128], cx.ones_bf_col[:],
                                     sqb[:, 128 * j:128 * (j + 1)], j == 0, j == 3,
                                     reads=("osq", "c_ones_bf_col"), writes=("pbD",))
                            p.copy(rrow[:], cx.pbD[0:1, 0:128], reads=("pbD",),
                                   writes=("rrow",))
                            p.ts(rrow[:], rrow[:], 1.0 / 512, 1e-5, ALU.mult, ALU.add,
                                 reads=("rrow",), writes=("rrow",))
                            p.act(rrow[:], rrow[:], AF.Ln, reads=("rrow",), writes=("rrow",))
                            p.act(rrow[:], rrow[:], AF.Exp, scale=-0.5, reads=("rrow",),
                                  writes=("rrow",))
                            p.mm(cx.pbD[0:128, 128:256], cx.ones_f_row[:], rrow[:],
                                 True, True, reads=("rrow", "c_ones_f_row"), writes=("pbD",))
                            p.copy(rstdb[:], cx.pbD[0:128, 128:256], reads=("pbD",),
                                   writes=("orb",))
                            for j in range(4):
                                p.stt(ogb[:, 4 * h + j, csl],
                                      cx.pbA[0:128, 128 * j:128 * (j + 1)], rw[:, j:j + 1],
                                      rstdb[:], ALU.mult, ALU.mult,
                                      reads=("pbA", "rw", "orb"), writes=(f"og{4 * h + j}",))
                                p.tt(ogb[:, 4 * h + j, csl], ogb[:, 4 * h + j, csl],
                                     gateb[:, 4 * h + j, csl], ALU.mult,
                                     reads=(f"og{4 * h + j}", f"gate{4 * h + j}"),
                                     writes=(f"og{4 * h + j}",))
                    for vc in range(16):
                        p.dma(og_d[vc * 128:(vc + 1) * 128, t0:t0 + 256], ogb[:, vc, :],
                              reads=(f"og{vc}",), writes=(f"ogd{s}_{hb}_{vc}",))
            p.barrier()
            with (nc.sbuf_tensor(f"lwo{s}", [128, 16, 1024], BF16) as lwo,
                  nc.sbuf_tensor(f"ogs{s}", [128, 2, 16, 128], BF16) as ogs,
                  nc.sbuf_tensor(f"glo{s}", [128, 2, 1024], F32) as glo):
                with nc.sbuf_tensor(f"lwostg{s}", [128, 1024], F32) as stg2:
                    LWo_r = LWo.rearrange("(c p) n -> c p n", p=128)
                    cast_load(nc, p, stg2, [(lwo[:, c, :], LWo_r[c], 1024)
                                            for c in range(16)], "lwo", "lwostg")
                p.barrier()
                ogr = og_d.rearrange("(vc p) t -> p vc t", p=128)
                for ti in range(16):
                    b = ti % 2
                    p.dma(ogs[:, b], ogr[:, :, ti * 128:(ti + 1) * 128],
                          reads=tuple(f"ogd{s}_{ti // 2}_{vc}" for vc in range(16)),
                          writes=(f"ogs{b}",))
                    for vc in range(16):
                        p.mm(cx.pbA[0:128, 0:512], ogs[:, b, vc, :],
                             lwo[:, vc, 0:512], vc == 0, vc == 15,
                             reads=(f"ogs{b}", "lwo"), writes=("pbA",))
                        p.mm(cx.pbA[0:128, 512:1024], ogs[:, b, vc, :],
                             lwo[:, vc, 512:1024], vc == 0, vc == 15,
                             reads=(f"ogs{b}", "lwo"), writes=("pbA",))
                    p.copy(glo[:, b], cx.pbA[0:128, 0:1024], reads=("pbA",),
                           writes=(f"glo{b}",))
                    p.dma(gla_r[s * 16 + ti], glo[:, b], reads=(f"glo{b}",),
                          writes=(f"glad{s * 16 + ti}",))
            p.barrier()
    glws.close()
    p.barrier()
    return og_d


def build_ffn1(nc, p, cx, x1_ap, xn1bf_d, gla_d, w_aps, out_ap, dev=False):
    SEQ = 2048
    Wqkv, Bqkv = w_aps["L1_ffn_W_qkv"], w_aps["L1_ffn_b_qkv"]
    Wg, Bg = w_aps["L1_ffn_W_gate"], w_aps["L1_ffn_b_gate"]
    We, Be = w_aps["L1_ffn_W_experts"], w_aps["L1_ffn_b_experts"]

    f1ws = ExitStack()
    wgate = f1ws.enter_context(nc.sbuf_tensor("wgate", [128, 8, 8], BF16))
    bgr = f1ws.enter_context(nc.sbuf_tensor("bgr", [1, 8], F32))
    bgb = f1ws.enter_context(nc.sbuf_tensor("bgb", [128, 8], F32))
    bqc = f1ws.enter_context(nc.sbuf_tensor("bqc", [128, 16], F32))
    bvr = f1ws.enter_context(nc.sbuf_tensor("bvr", [1, 1024], F32))
    bvb = f1ws.enter_context(nc.sbuf_tensor("bvb", [128, 1024], F32))
    ber = f1ws.enter_context(nc.sbuf_tensor("ber", [1, 8, 1024], BF16))
    with nc.sbuf_tensor("f1stage", [128, 1024], F32) as stg:
        Wg_r = Wg.rearrange("(c p) n -> c p n", p=128)
        cast_load(nc, p, stg, [(wgate[:, c, :], Wg_r[c], 8) for c in range(8)],
                  "wgate", "f1stg")
        p.dma(bgr[:], Bg.rearrange("(o n) -> o n", o=1), writes=("bgr",))
        Bq_r = Bqkv[0:2048].rearrange("(c p) -> c p", p=128)
        for c in range(16):
            p.dma(bqc[:, c:c + 1], Bq_r[c].rearrange("(p o) -> p o", o=1), writes=("bqc",))
        p.dma(bvr[:], Bqkv[2048:3072].rearrange("(o n) -> o n", o=1), writes=("bvr",))
        with nc.sbuf_tensor("berf", [1, 8, 1024], F32) as berf:
            p.dma(berf[:], Be.rearrange("(o e) n -> o e n", o=1), writes=("berf",))
            p.copy(ber[:], berf[:], reads=("berf",), writes=("ber",))
    p.barrier()
    p.mm(cx.pbC[0:128, 0:8], cx.ones_f_row[:], bgr[:], True, True,
         reads=("bgr", "c_ones_f_row"), writes=("pbC",))
    p.copy(bgb[:], cx.pbC[0:128, 0:8], reads=("pbC",), writes=("bgb",))
    p.mm(cx.pbA[0:128, 0:512], cx.ones_f_row[:], bvr[:, 0:512], True, True,
         reads=("bvr", "c_ones_f_row"), writes=("pbA",))
    p.copy(bvb[:, 0:512], cx.pbA[0:128, 0:512], reads=("pbA",), writes=("bvb",))
    p.mm(cx.pbA[0:128, 0:512], cx.ones_f_row[:], bvr[:, 512:1024], True, True,
         reads=("bvr", "c_ones_f_row"), writes=("pbA",))
    p.copy(bvb[:, 512:1024], cx.pbA[0:128, 0:512], reads=("pbA",), writes=("bvb",))

    attn1_d = nc.dram_tensor("attn1_d", [4096, 1024], F32,
                             kind="ExternalOutput" if dev else "Internal").ap()
    if dev:
        moe_dbg = nc.dram_tensor("moe_dbg", [4096, 1024], F32, kind="ExternalOutput").ap()
        cmb_dbg = nc.dram_tensor("cmb_dbg", [4096, 8], F32, kind="ExternalOutput").ap()
    at_r = attn1_d.rearrange("(t p) n -> t p n", p=128)
    x1_r = x1_ap.rearrange("(t p) n -> t p n", p=128)
    gla_r2 = gla_d.rearrange("(t p) n -> t p n", p=128)
    out_r = out_ap.rearrange("(t p) n -> t p n", p=128)

    for s in range(2):
        with ExitStack() as sst:
            xnfm = sst.enter_context(nc.sbuf_tensor(f"fxn{s}", [128, 8, 2048], BF16))
            for c in range(8):
                p.dma(xnfm[:, c, :], xn1bf_d[s * SEQ:(s + 1) * SEQ, c * 128:(c + 1) * 128],
                      transpose=True, reads=tuple(f"N1d{s * 16 + i}" for i in range(16)),
                      writes=(f"fxn{c}",))
            qkv_st = ExitStack()
            qfm = qkv_st.enter_context(nc.sbuf_tensor(f"fq{s}", [128, 8, 2048], BF16))
            kfm = qkv_st.enter_context(nc.sbuf_tensor(f"fk{s}", [128, 8, 2048], BF16))
            vaug = qkv_st.enter_context(nc.sbuf_tensor(f"fv{s}", [128, 16, 16, 68], BF16))
            with nc.sbuf_tensor(f"fwqkv{s}", [128, 8, 3072], BF16) as wqkv:
                with nc.sbuf_tensor(f"fwqs{s}", [128, 1024], F32) as stg3:
                    Wqkv_r = Wqkv.rearrange("(c p) n -> c p n", p=128)
                    cast_load(nc, p, stg3,
                              [(wqkv[:, c, j * 1024:(j + 1) * 1024],
                                Wqkv_r[c][:, j * 1024:(j + 1) * 1024], 1024)
                               for c in range(8) for j in range(3)], "wqkv", "fwqs")
                p.barrier()
                for dstt, base, kk in ((qfm, 0, "fqc"), (kfm, 1024, "fkc")):
                    for c in range(8):
                        for qb in range(4):
                            sl = slice(qb * 512, (qb + 1) * 512)
                            for d in range(8):
                                p.mm(cx.pbA[0:128, 0:512],
                                     wqkv[:, d, base + c * 128:base + (c + 1) * 128],
                                     xnfm[:, d, sl], d == 0, d == 7,
                                     reads=("wqkv", f"fxn{d}"), writes=("pbA",))
                            bc = (base + c * 128) // 128
                            fn = (lambda eng, o=dstt[:, c, sl], i=cx.pbA[0:128, 0:512],
                                  b=bqc[:, bc:bc + 1]: eng.activation(o, i, AF.Identity,
                                                                      bias=b))
                            p.emit("scalar", fn, reads=("pbA", "bqc"), writes=(kk + str(c),))
                p.memset(vaug[:, :, :, 64:65], 1.0, writes=("fva_ones",))
                for i in range(16):
                    for d in range(8):
                        p.mm(cx.pbB[0:128, 0:512], xnfm[:, d, i * 128:(i + 1) * 128],
                             wqkv[:, d, 2048:2560], d == 0, d == 7,
                             reads=("wqkv", f"fxn{d}"), writes=("pbB",))
                        p.mm(cx.pbB[0:128, 512:1024], xnfm[:, d, i * 128:(i + 1) * 128],
                             wqkv[:, d, 2560:3072], d == 0, d == 7,
                             reads=("wqkv", f"fxn{d}"), writes=("pbB",))
                    for h in range(16):
                        p.stt(vaug[:, i, h, 0:64], cx.pbB[0:128, h * 64:(h + 1) * 64], 1.0,
                              bvb[:, h * 64:(h + 1) * 64], ALU.mult, ALU.add,
                              reads=("pbB", "bvb"), writes=(f"fva{i}",))
            p.barrier()
            with (nc.sbuf_tensor(f"feb{s}", [128, 2, 1024], BF16) as ebuf,
                  nc.sbuf_tensor(f"fpv{s}", [65, 2, 512], BF16) as pvs,
                  nc.sbuf_tensor(f"fac{s}", [128, 4, 1024], F32) as acc,
                  nc.sbuf_tensor(f"fep{s}", [128, 8], F32) as eps_t):
                for qb in range(4):
                    for hp in range(8):
                        for kt in range(16):
                            pb, pk = (cx.pbA, "pbA") if kt % 2 == 0 else (cx.pbB, "pbB")
                            eb, ek = (kt % 2), f"feb{kt % 2}"
                            ksl = slice(kt * 128, (kt + 1) * 128)
                            qsl = slice(qb * 512, (qb + 1) * 512)
                            p.mm(pb[0:128, 0:512], kfm[0:64, hp, ksl], qfm[0:64, hp, qsl],
                                 True, True, reads=(f"fkc{hp}", f"fqc{hp}"), writes=(pk,))
                            p.mm(pb[0:128, 512:1024], kfm[64:128, hp, ksl],
                                 qfm[64:128, hp, qsl],
                                 True, True, reads=(f"fkc{hp}", f"fqc{hp}"), writes=(pk,))
                            p.act(ebuf[:, eb], pb[:], AF.Exp, scale=0.125, reads=(pk,),
                                  writes=(ek,))
                            p.mm(cx.pbC[0:65, 0:512], vaug[:, kt, 2 * hp, 0:65],
                                 ebuf[:, eb, 0:512], kt == 0, kt == 15,
                                 reads=(f"fva{kt}", "fva_ones", ek), writes=("pbC",))
                            p.mm(cx.pbD[0:65, 0:512], vaug[:, kt, 2 * hp + 1, 0:65],
                                 ebuf[:, eb, 512:1024], kt == 0, kt == 15,
                                 reads=(f"fva{kt}", "fva_ones", ek), writes=("pbD",))
                        for hh in range(2):
                            pvp, pvk = (cx.pbC, "pbC") if hh == 0 else (cx.pbD, "pbD")
                            pT, pTk = (cx.pbT, "pbT") if hh == 0 else (cx.pbU, "pbU")
                            n = 2 * hp + hh
                            p.copy(pvs[:, hh], pvp[0:65, 0:512], reads=(pvk,),
                                   writes=(f"fpv{hh}",))
                            for j in range(4):
                                p.trans(pT[0:128, j * 256:j * 256 + 65],
                                        pvs[0:65, hh, j * 128:(j + 1) * 128],
                                        cx.id128[0:65, 0:65],
                                        reads=(f"fpv{hh}", "c_id128"), writes=(pTk,))
                            for j in range(4):
                                p.recip(eps_t[:, 1:2],
                                        pT[0:128, j * 256 + 64:j * 256 + 65],
                                        reads=(pTk,), writes=("frin",))
                                p.ts(acc[:, j, n * 64:(n + 1) * 64],
                                     pT[0:128, j * 256:j * 256 + 64], eps_t[:, 1:2], None,
                                     ALU.mult, reads=(pTk, "frin"), writes=(f"fatt{j}",))
                    for j in range(4):
                        p.dma(at_r[s * 16 + qb * 4 + j], acc[:, j, :], reads=(f"fatt{j}",),
                              writes=(f"at1_d{s * 16 + qb * 4 + j}",))
            p.barrier()
            qkv_st.close()
            comb = sst.enter_context(nc.sbuf_tensor(f"fcb{s}", [128, 16, 8], F32))
            with nc.sbuf_tensor(f"fgt{s}", [128, 6, 8], F32) as gt:
                for i in range(16):
                    for d in range(8):
                        p.mm(cx.pbC[0:128, 0:8], xnfm[:, d, i * 128:(i + 1) * 128],
                             wgate[:, d, :], d == 0, d == 7,
                             reads=(f"fxn{d}", "wgate"), writes=("pbC",))
                    p.tt(gt[:, 0, :], cx.pbC[0:128, 0:8], bgb[:], ALU.add,
                         reads=("pbC", "bgb"), writes=("fg0",))
                    p.red(gt[:, 1, 0:1], gt[:, 0, :], ALU.max, negate=True,
                          reads=("fg0",), writes=("fg1",))
                    fn = (lambda eng, o=gt[:, 2, :], i_=gt[:, 0, :], b=gt[:, 1, 0:1]:
                          eng.activation(o, i_, AF.Exp, bias=b))
                    p.emit("scalar", fn, reads=("fg0", "fg1"), writes=("fg2",))
                    p.red(gt[:, 1, 1:2], gt[:, 2, :], ALU.add, reads=("fg2",),
                          writes=("fg3",))
                    p.recip(gt[:, 1, 1:2], gt[:, 1, 1:2], reads=("fg3",), writes=("fg3",))
                    p.ts(gt[:, 2, :], gt[:, 2, :], gt[:, 1, 1:2], None, ALU.mult,
                         reads=("fg2", "fg3"), writes=("fg2",))
                    p.red(gt[:, 1, 2:3], gt[:, 2, :], ALU.max, reads=("fg2",),
                          writes=("fg4",))
                    p.ts(gt[:, 3, :], gt[:, 2, :], gt[:, 1, 2:3], None, ALU.is_ge,
                         reads=("fg2", "fg4"), writes=("fg5",))
                    p.stt(gt[:, 3, :], gt[:, 3, :], -1e30, gt[:, 2, :], ALU.mult, ALU.add,
                          reads=("fg5", "fg2"), writes=("fg5",))
                    p.red(gt[:, 1, 3:4], gt[:, 3, :], ALU.max, reads=("fg5",),
                          writes=("fg6",))
                    p.ts(gt[:, 4, :], gt[:, 2, :], gt[:, 1, 3:4], None, ALU.is_ge,
                         reads=("fg2", "fg6"), writes=("fg7",))
                    p.tt(comb[:, i, :], gt[:, 2, :], gt[:, 4, :], ALU.mult,
                         reads=("fg2", "fg7"), writes=(f"fcb{i}",))
            p.barrier()
            moe = sst.enter_context(nc.sbuf_tensor(f"fmo{s}", [128, 16, 1024], BF16))
            p.memset(moe[:], 0.0, writes=tuple(f"fmo{i}" for i in range(16)))
            with (nc.sbuf_tensor(f"fwe{s}", [128, 2, 8, 1024], BF16) as wex,
                  nc.sbuf_tensor(f"fwf{s}", [128, 2, 4, 1024], F32) as wf32):
                Wer = We.rearrange("e (c p) n -> e c p n", p=128)
                for e in range(8):
                    wb, wk_ = e % 2, f"fwe{e % 2}"
                    for half in range(2):
                        fk_ = f"fwf{wb}"
                        for c in range(4):
                            p.dma(wf32[:, wb, c, :], Wer[e, half * 4 + c], writes=(fk_,))
                        p.copy(wex[:, wb, half * 4:half * 4 + 4, :], wf32[:, wb],
                               reads=(fk_,), writes=(wk_,))
                    for i in range(16):
                        for d in range(8):
                            p.mm(cx.pbA[0:128, 0:512], xnfm[:, d, i * 128:(i + 1) * 128],
                                 wex[:, wb, d, 0:512], d == 0, False,
                                 reads=(wk_, f"fxn{d}"), writes=("pbA",))
                            p.mm(cx.pbA[0:128, 512:1024], xnfm[:, d, i * 128:(i + 1) * 128],
                                 wex[:, wb, d, 512:1024], d == 0, False,
                                 reads=(wk_, f"fxn{d}"), writes=("pbA",))
                        p.mm(cx.pbA[0:128, 0:512], cx.ones_bf_row[:], ber[0:1, e, 0:512],
                             False, True, reads=("ber", "c_ones_bf_row"), writes=("pbA",))
                        p.mm(cx.pbA[0:128, 512:1024], cx.ones_bf_row[:],
                             ber[0:1, e, 512:1024],
                             False, True, reads=("ber", "c_ones_bf_row"), writes=("pbA",))
                        p.stt(moe[:, i, :], cx.pbA[0:128, 0:1024], comb[:, i, e:e + 1],
                              moe[:, i, :], ALU.mult, ALU.add,
                              reads=("pbA", f"fcb{i}", f"fmo{i}"), writes=(f"fmo{i}",))
            if dev:
                for i in range(16):
                    ti = s * 16 + i
                    p.dma(moe_dbg[ti * 128:(ti + 1) * 128, :], moe[:, i, :], e="gpsimd",
                          reads=(f"fmo{i}",), writes=(f"md{ti}",))
                    p.dma(cmb_dbg[ti * 128:(ti + 1) * 128, :], comb[:, i, :],
                          reads=(f"fcb{i}",), writes=(f"cd{ti}",))
            p.barrier()
            with (nc.sbuf_tensor(f"fo1{s}", [128, 1024], F32) as t1,
                  nc.sbuf_tensor(f"fo2{s}", [128, 1024], F32) as t2,
                  nc.sbuf_tensor(f"fo3{s}", [128, 1024], F32) as t3):
                for i in range(16):
                    ti = s * 16 + i
                    p.dma(t1[:], x1_r[ti], writes=("fo1",))
                    p.dma(t2[:], gla_r2[ti], reads=(f"glad{ti}",), writes=("fo2",))
                    p.dma(t3[:], at_r[ti], reads=(f"at1_d{ti}",), writes=("fo3",))
                    p.tt(t1[:], t1[:], t2[:], ALU.add, reads=("fo1", "fo2"), writes=("fo1",))
                    p.tt(t1[:], t1[:], t3[:], ALU.add, reads=("fo1", "fo3"), writes=("fo1",))
                    p.tt(t1[:], t1[:], moe[:, i, :], ALU.add, reads=("fo1", f"fmo{i}"),
                         writes=("fo1",))
                    p.dma(out_r[ti], t1[:], reads=("fo1",), writes=(f"out_d{ti}",))
            p.barrier()
    f1ws.close()
    p.barrier()


L0_WSHAPES = {
    "L0_attn_Wq": (1024, 1024), "L0_attn_Wk": (1024, 512), "L0_attn_Wv": (1024, 512),
    "L0_attn_memory": (512, 512), "L0_attn_memory_norm": (512,),
    "L0_attn_memory_weight": (1,), "L0_ffn_w_up": (4096, 1024),
    "L0_ffn_w_down": (1024, 4096),
}


def build_nc_l0(dev=False):
    nc = bass.Bass("TRN2", target_bir_lowering=False, debug=False)
    p = Prog(nc)
    cx = Ctx(nc, p)
    x = nc.dram_tensor("x", [4096, 1024], F32, kind="ExternalInput").ap()
    w_aps = {k: nc.dram_tensor(k, list(s), F32, kind="ExternalInput").ap()
             for k, s in L0_WSHAPES.items()}
    x1 = nc.dram_tensor("x1", [4096, 1024], F32, kind="ExternalOutput").ap()
    build_l0(nc, p, cx, x, w_aps, x1, dev=dev)
    p.finalize()
    return nc


def kernel_unfused(**inputs):
    cores = list(range(8))
    x = np.ascontiguousarray(np.asarray(inputs["L0_attn_x"], np.float32))
    shards = [np.ascontiguousarray(x[2 * i:2 * i + 2].reshape(4096, 1024))
              for i in range(8)]

    def wm(shapes):
        return {k: np.ascontiguousarray(np.asarray(inputs[k], np.float32))
                for k in shapes}

    w0, w1a, w1f = wm(L0_WSHAPES), wm(L1A_WSHAPES), wm(L1F_WSHAPES)
    r0 = run_bass_kernel_spmd(build_nc_l0(), [dict(x=s, **w0) for s in shards],
                              core_ids=cores)
    x1s = [r0.results[i]["x1"] for i in range(8)]
    r1 = run_bass_kernel_spmd(build_nc_gla(), [dict(x1=x1s[i], **w1a) for i in range(8)],
                              core_ids=cores)
    r2 = run_bass_kernel_spmd(
        build_nc_ffn1(),
        [dict(x1=x1s[i], gla=r1.results[i]["gla"], **w1f)
         for i in range(8)], core_ids=cores)
    out = np.stack([np.asarray(r2.results[i]["out"]).reshape(2, 2048, 1024)
                    for i in range(8)])
    return out.reshape(16, 2048, 1024).astype(np.float32)


L1A_WSHAPES = {
    "L1_attn_Wq": (1024, 1024), "L1_attn_Wk": (1024, 1024),
    "L1_attn_Wv": (1024, 2048), "L1_attn_Wg": (1024, 2048),
    "L1_attn_gk_w1": (1024, 16), "L1_attn_gk_w2": (16, 1024),
    "L1_attn_gk_b": (1024,), "L1_attn_Wo": (2048, 1024), "L1_attn_rms_w": (512,),
}


def build_nc_gla(dev=False):
    nc = bass.Bass("TRN2", target_bir_lowering=False, debug=False)
    p = Prog(nc)
    cx = Ctx(nc, p)
    x1 = nc.dram_tensor("x1", [4096, 1024], F32, kind="ExternalInput").ap()
    w_aps = {k: nc.dram_tensor(k, list(s), F32, kind="ExternalInput").ap()
             for k, s in L1A_WSHAPES.items()}
    gla = nc.dram_tensor("gla", [4096, 1024], F32, kind="ExternalOutput").ap()
    xn1bf = nc.dram_tensor("xn1bf", [4096, 1024], BF16, kind="Internal").ap()
    build_ln(nc, p, cx, x1, xn1bf, "N1")
    build_gla(nc, p, cx, xn1bf, w_aps, gla, dev=dev)
    p.finalize()
    return nc


L1F_WSHAPES = {
    "L1_ffn_W_qkv": (1024, 3072), "L1_ffn_b_qkv": (3072,),
    "L1_ffn_W_gate": (1024, 8), "L1_ffn_b_gate": (8,),
    "L1_ffn_W_experts": (8, 1024, 1024), "L1_ffn_b_experts": (8, 1024),
}


def build_nc_ffn1(dev=False):
    nc = bass.Bass("TRN2", target_bir_lowering=False, debug=False)
    p = Prog(nc)
    cx = Ctx(nc, p)
    x1 = nc.dram_tensor("x1", [4096, 1024], F32, kind="ExternalInput").ap()
    xn1bf = nc.dram_tensor("xn1bf", [4096, 1024], BF16, kind="Internal").ap()
    gla = nc.dram_tensor("gla", [4096, 1024], F32, kind="ExternalInput").ap()
    w_aps = {k: nc.dram_tensor(k, list(s), F32, kind="ExternalInput").ap()
             for k, s in L1F_WSHAPES.items()}
    out = nc.dram_tensor("out", [4096, 1024], F32, kind="ExternalOutput").ap()
    build_ln(nc, p, cx, x1, xn1bf, "N1")
    build_ffn1(nc, p, cx, x1, xn1bf, gla, w_aps, out, dev=dev)
    p.finalize()
    return nc


ALL_WSHAPES = {}
ALL_WSHAPES.update(L0_WSHAPES)
ALL_WSHAPES.update(L1A_WSHAPES)
ALL_WSHAPES.update(L1F_WSHAPES)


def build_nc_fused(dev=False):
    nc = bass.Bass("TRN2", target_bir_lowering=False, debug=False)
    p = Prog(nc)
    cx = Ctx(nc, p)
    x = nc.dram_tensor("x", [4096, 1024], F32, kind="ExternalInput").ap()
    w_aps = {k: nc.dram_tensor(k, list(s), F32, kind="ExternalInput").ap()
             for k, s in ALL_WSHAPES.items()}
    out = nc.dram_tensor("out", [4096, 1024], F32, kind="ExternalOutput").ap()
    x1 = nc.dram_tensor("x1", [4096, 1024], F32, kind="Internal").ap()
    gla = nc.dram_tensor("gla", [4096, 1024], F32, kind="Internal").ap()
    xn1bf = nc.dram_tensor("xn1bf", [4096, 1024], BF16, kind="Internal").ap()
    build_l0(nc, p, cx, x, w_aps, x1, dev=dev)
    p.barrier()
    build_ln(nc, p, cx, x1, xn1bf, "N1")
    build_gla(nc, p, cx, xn1bf, w_aps, gla, dev=dev)
    p.barrier()
    build_ffn1(nc, p, cx, x1, xn1bf, gla, w_aps, out, dev=dev)
    p.finalize()
    return nc


def kernel(**inputs):
    cores = list(range(8))
    x = np.ascontiguousarray(np.asarray(inputs["L0_attn_x"], np.float32))
    shards = [np.ascontiguousarray(x[2 * i:2 * i + 2].reshape(4096, 1024))
              for i in range(8)]
    wmap = {k: np.ascontiguousarray(np.asarray(inputs[k], np.float32))
            for k in ALL_WSHAPES}
    nc = build_nc_fused()
    r = run_bass_kernel_spmd(nc, [dict(x=s, **wmap) for s in shards], core_ids=cores)
    out = np.stack([np.asarray(r.results[i]["out"]).reshape(2, 2048, 1024)
                    for i in range(8)])
    return out.reshape(16, 2048, 1024)


# version 08
# speedup vs baseline: 1.0717x; 1.0717x over previous
from contextlib import ExitStack
import numpy as np
import ml_dtypes
import concourse.bass as bass
import concourse.mybir as mybir
from concourse.bass_utils import run_bass_kernel_spmd

F32 = mybir.dt.float32
BF16 = mybir.dt.bfloat16
AF = mybir.ActivationFunctionType
ALU = mybir.AluOpType
AX = mybir.AxisListType.X
bfdt = ml_dtypes.bfloat16

ENGINES = ("sync", "tensor", "vector", "scalar", "gpsimd")
DMA_ENG = ("sync", "scalar", "gpsimd")


class Prog:

    NLANES = 8

    def __init__(self, nc):
        self.nc = nc
        self.c_sem = {e: nc.alloc_semaphore(f"c_{e}") for e in ENGINES}
        self.d_sem = {(e, l): nc.alloc_semaphore(f"d_{e}{l}")
                      for e in DMA_ENG for l in range(self.NLANES)}
        self.ccnt = {e: 0 for e in ENGINES}
        self.dcnt = {(e, l): 0 for e in DMA_ENG for l in range(self.NLANES)}
        self.dtot = {e: 0 for e in DMA_ENG}
        self.steps = {e: [] for e in ENGINES}
        self.lw = {}
        self.readers = {}
        self.wm = {e: {} for e in ENGINES}
        self.n_inst = 0

    def _sem_of(self, kind, oe):
        return self.c_sem[oe] if kind == "c" else self.d_sem[oe]

    def _wait(self, e, waits, dep):
        kind, oe, cnt = dep
        if oe == e and kind == "c":
            if e == "tensor":
                return
        got = self.wm[e].get((kind, oe), 0)
        if cnt <= got:
            return
        waits[(kind, oe)] = max(waits.get((kind, oe), 0), cnt)

    def emit(self, e, fn, reads=(), writes=(), dma=False):
        assert e in ENGINES
        if dma:
            assert e in DMA_ENG
        waits = {}
        for k in reads:
            d = self.lw.get(k)
            if d is not None:
                self._wait(e, waits, d)
        for k in writes:
            d = self.lw.get(k)
            if d is not None:
                self._wait(e, waits, d)
            for r in self.readers.get(k, ()):
                self._wait(e, waits, r)
        if dma:
            lane = self.dtot[e] % self.NLANES
            self.dtot[e] += 1
            if self.dcnt[(e, lane)] > 0:
                self._wait(e, waits, ("d", (e, lane), self.dcnt[(e, lane)]))
        wl = []
        for (kind, oe), cnt in waits.items():
            wl.append((self._sem_of(kind, oe), cnt))
            self.wm[e][(kind, oe)] = cnt
        if dma:
            self.dcnt[(e, lane)] += 16
            inc = (self.d_sem[(e, lane)], 16)
            me = ("d", (e, lane), self.dcnt[(e, lane)])
        else:
            self.ccnt[e] += 1
            inc = (self.c_sem[e], 1)
            me = ("c", e, self.ccnt[e])
        self.steps[e].append((fn, wl, inc))
        self.n_inst += 1
        for k in writes:
            self.lw[k] = me
            self.readers[k] = []
        for k in reads:
            self.readers.setdefault(k, []).append(me)

    def barrier(self):
        tgt = [("c", oe, self.ccnt[oe]) for oe in ENGINES if self.ccnt[oe]] + \
              [("d", (oe, l), self.dcnt[(oe, l)]) for oe in DMA_ENG
               for l in range(self.NLANES) if self.dcnt[(oe, l)]]
        for e in ENGINES:
            wl = []
            for kind, oe, cnt in tgt:
                if self.wm[e].get((kind, oe), 0) >= cnt:
                    continue
                wl.append((self._sem_of(kind, oe), cnt))
                self.wm[e][(kind, oe)] = cnt
            if wl:
                self.steps[e].append((None, wl, None))

    def dma(self, dst, src, e="sync", reads=(), writes=(), transpose=False, slow=False):
        if transpose:
            fn = lambda eng, d=dst, s=src: eng.dma_start_transpose(d, s)
        elif slow:
            fn = lambda eng, d=dst, s=src: eng.dma_start(d, s, allow_slow_non_contiguous=True)
        else:
            fn = lambda eng, d=dst, s=src: eng.dma_start(d, s)
        self.emit(e, fn, reads, writes, dma=True)

    def mm(self, out, lhsT, rhs, start, stop, reads=(), writes=()):
        fn = lambda eng, o=out, l=lhsT, r=rhs, st=start, sp=stop: eng.matmul(o, l, r, start=st, stop=sp)
        self.emit("tensor", fn, reads, writes)

    def trans(self, out, in_, ident, reads=(), writes=()):
        fn = lambda eng, o=out, i=in_, d=ident: eng.transpose(o, i, d)
        self.emit("tensor", fn, reads, writes)

    def act(self, out, in_, func, bias=0.0, scale=1.0, accum_out=None, reads=(), writes=()):
        fn = lambda eng, o=out, i=in_, f=func, b=bias, s=scale, a=accum_out: \
            eng.activation(o, i, f, bias=b, scale=s, accum_out=a)
        self.emit("scalar", fn, reads, writes)

    def tt(self, out, a, b, op, e="vector", reads=(), writes=()):
        fn = lambda eng, o=out, x=a, y=b, p=op: eng.tensor_tensor(o, x, y, p)
        self.emit(e, fn, reads, writes)

    def ts(self, out, in_, s1, s2, op0, op1=ALU.bypass, e="vector", reads=(), writes=()):
        fn = lambda eng, o=out, i=in_, a=s1, b=s2, p0=op0, p1=op1: \
            eng.tensor_scalar(o, i, a, b, p0, p1)
        self.emit(e, fn, reads, writes)

    def stt(self, out, in0, scalar, in1, op0, op1, e="vector", reads=(), writes=()):
        fn = lambda eng, o=out, i0=in0, s=scalar, i1=in1, p0=op0, p1=op1: \
            eng.scalar_tensor_tensor(o, i0, s, i1, p0, p1)
        self.emit(e, fn, reads, writes)

    def copy(self, out, in_, e="vector", reads=(), writes=()):
        fn = lambda eng, o=out, i=in_: eng.tensor_copy(o, i)
        self.emit(e, fn, reads, writes)

    def red(self, out, in_, op, negate=False, absval=False, e="vector", reads=(), writes=()):
        fn = lambda eng, o=out, i=in_, p=op, n=negate, a=absval: \
            eng.tensor_reduce(o, i, AX, p, apply_absolute_value=a, negate=n)
        self.emit(e, fn, reads, writes)

    def memset(self, ap, val, e="vector", reads=(), writes=()):
        fn = lambda eng, a=ap, v=val: eng.memset(a, v)
        self.emit(e, fn, reads, writes)

    def recip(self, out, in_, reads=(), writes=()):
        fn = lambda eng, o=out, i=in_: eng.reciprocal(o, i)
        self.emit("vector", fn, reads, writes)

    def scan(self, out, d0, d1, init, op0, op1, reads=(), writes=()):
        fn = lambda eng, o=out, a=d0, b=d1, i=init, p0=op0, p1=op1: \
            eng.tensor_tensor_scan(o, a, b, i, p0, p1)
        self.emit("vector", fn, reads, writes)

    def finalize(self):
        nc = self.nc
        self.barrier()
        with nc.Block() as block:
            for e in ENGINES:
                mine = self.steps[e]
                if not mine and e != "sync":
                    continue

                def body(eng, mine=mine):
                    for fn, wl, inc in mine:
                        for sem, cnt in wl:
                            eng.wait_ge(sem, cnt)
                        if fn is not None:
                            inst = fn(eng)
                            inst.then_inc(inc[0], inc[1])
                getattr(block, e)(body)


class Ctx:
    def __init__(self, nc, p):
        self.nc, self.p = nc, p
        self.id128 = nc.alloc_sbuf_tensor("c_id128", [128, 128], BF16)
        self.ones_bf_row = nc.alloc_sbuf_tensor("c_ones_bf_row", [1, 128], BF16)
        self.ones_f_row = nc.alloc_sbuf_tensor("c_ones_f_row", [1, 128], F32)
        self.ones_bf_col = nc.alloc_sbuf_tensor("c_ones_bf_col", [128, 1], BF16)
        self.ones_f_col = nc.alloc_sbuf_tensor("c_ones_f_col", [128, 1], F32)
        self.mask128 = nc.alloc_sbuf_tensor("c_mask128", [128, 128], BF16)
        for t, arr in ((self.id128, np.eye(128, dtype=bfdt)),
                       (self.mask128, np.triu(np.ones((128, 128), bfdt))),
                       (self.ones_bf_row, np.ones((1, 128), bfdt)),
                       (self.ones_f_row, np.ones((1, 128), np.float32)),
                       (self.ones_bf_col, np.ones((128, 1), bfdt)),
                       (self.ones_f_col, np.ones((128, 1), np.float32))):
            d = nc.inline_tensor(arr, t.name + "_d").ap()
            p.dma(t[:], d, writes=(t.name,))
        self.pbA = nc.alloc_psum_tensor("pbA", [128, 1024], F32)
        self.pbB = nc.alloc_psum_tensor("pbB", [128, 1024], F32)
        self.pbC = nc.alloc_psum_tensor("pbC", [128, 512], F32)
        self.pbD = nc.alloc_psum_tensor("pbD", [128, 512], F32)
        self.pbT = nc.alloc_psum_tensor("pbT", [128, 1024], BF16)
        self.pbU = nc.alloc_psum_tensor("pbU", [128, 1024], BF16)

    def rsqrt(self, p, out, in_, key, eps=0.0, rkeys=()):
        p.act(out, in_, AF.Ln, bias=eps, reads=rkeys, writes=(key,))
        p.act(out, out, AF.Exp, scale=-0.5, reads=(key,), writes=(key,))



def cast_load(nc, p, stage, dst_slices_srcs, key, stage_key):
    for dst, srcap, cols in dst_slices_srcs:
        p.dma(stage[:, 0:cols], srcap, writes=(stage_key,))
        p.copy(dst, stage[:, 0:cols], reads=(stage_key,), writes=(key,))


def build_l0(nc, p, cx, x_ap, w_aps, x1_ap, dbg=None, dev=False):
    NT = 32
    SEQ = 2048
    Wq, Wk, Wv, Mem, Mn, Mw, Wup, Wdown = (
        w_aps["L0_attn_Wq"], w_aps["L0_attn_Wk"], w_aps["L0_attn_Wv"],
        w_aps["L0_attn_memory"], w_aps["L0_attn_memory_norm"],
        w_aps["L0_attn_memory_weight"], w_aps["L0_ffn_w_up"], w_aps["L0_ffn_w_down"])

    l0ws = ExitStack()
    wq_sb = l0ws.enter_context(nc.sbuf_tensor("wq_sb", [128, 8, 1024], BF16))
    wkd_sb = l0ws.enter_context(nc.sbuf_tensor("wkd_sb", [128, 8, 8, 128], BF16))
    wv_sb = l0ws.enter_context(nc.sbuf_tensor("wv_sb", [128, 8, 512], BF16))
    mem_sb = l0ws.enter_context(nc.sbuf_tensor("mem_sb", [128, 4, 512], BF16))
    mn_sb = l0ws.enter_context(nc.sbuf_tensor("mn_sb", [128, 4], BF16))
    with nc.sbuf_tensor("l0stage", [128, 1024], F32) as stg:
        Wq_r = Wq.rearrange("(c p) n -> c p n", p=128)
        cast_load(nc, p, stg, [(wq_sb[:, c, :], Wq_r[c], 1024) for c in range(8)],
                  "wq", "l0stg")
        wk_r = Wk.rearrange("(c p) (h i) -> c p h i", p=128, i=64)
        wk_v = wk_r.rearrange("c p h i -> c p (h i)")
        for c in range(8):
            p.dma(stg[:, 0:512], wk_v[c], writes=("l0stg",))
            sview = stg[:, 0:512].rearrange("p (h i) -> p h i", i=64)
            p.copy(wkd_sb[:, c, :, 0:64], sview, reads=("l0stg",), writes=("wkd0",))
            p.copy(wkd_sb[:, c, :, 64:128], sview, reads=("l0stg",), writes=("wkd1",))
        Wv_r = Wv.rearrange("(c p) n -> c p n", p=128)
        cast_load(nc, p, stg, [(wv_sb[:, c, :], Wv_r[c], 512) for c in range(8)],
                  "wv", "l0stg")
        Mem_r = Mem.rearrange("(c p) n -> c p n", p=128)
        cast_load(nc, p, stg, [(mem_sb[:, c, :], Mem_r[c], 512) for c in range(4)],
                  "mem", "l0stg")
        Mn_r = Mn.rearrange("(c p) -> c p", p=128)
        for c in range(4):
            p.dma(stg[:, 0:1], Mn_r[c].rearrange("(p o) -> p o", o=1), writes=("l0stg",))
            p.copy(mn_sb[:, c:c + 1], stg[:, 0:1], reads=("l0stg",), writes=("mn",))
    p.barrier()

    mw_sb = l0ws.enter_context(nc.sbuf_tensor("mw_sb", [1, 1], F32))
    wrow = l0ws.enter_context(nc.sbuf_tensor("wrow", [1, 2], F32))
    wcols = l0ws.enter_context(nc.sbuf_tensor("wcols", [128, 2], F32))
    p.dma(mw_sb[:], Mw.rearrange("(a b) -> a b", a=1), writes=("mw",))
    p.act(wrow[:, 0:1], mw_sb[:], AF.Exp, scale=-1.0, reads=("mw",), writes=("wr_w",))
    p.ts(wrow[:, 0:1], wrow[:, 0:1], 1.0, None, ALU.add, reads=("wr_w",), writes=("wr_w",))
    p.recip(wrow[:, 0:1], wrow[:, 0:1], reads=("wr_w",), writes=("wr_w",))
    p.ts(wrow[:, 1:2], wrow[:, 0:1], 1.0, -1.0, ALU.subtract, ALU.mult,
         reads=("wr_w",), writes=("wr_w1",))
    p.mm(cx.pbC[0:128, 0:2], cx.ones_f_row[:], wrow[:], True, True,
         reads=("wr_w", "wr_w1", "c_ones_f_row"), writes=("pbC",))
    p.copy(wcols[:], cx.pbC[0:128, 0:2], reads=("pbC",), writes=("wcols",))
    wcol, w1col = wcols[:, 0:1], wcols[:, 1:2]

    KIND = "ExternalOutput" if dev else "Internal"
    wtup = nc.dram_tensor("wtup", [4096, 1024], BF16, kind=KIND).ap()
    wtdown = nc.dram_tensor("wtdown", [1024, 4096], BF16, kind=KIND).ap()

    with (nc.sbuf_tensor("t_w0", [128, 4096], F32) as tw0,
          nc.sbuf_tensor("t_w1", [128, 4096], F32) as tw1,
          nc.sbuf_tensor("t_sg", [128, 4096], F32) as tsg,
          nc.sbuf_tensor("t_ab", [128, 4096], F32) as tab,
          nc.sbuf_tensor("t_wt", [128, 4096], BF16) as twt,
          nc.sbuf_tensor("t_sm", [128, 4], F32) as tsm):

        def ternarize(W, n_rows, n_cols, out_dram, tag):
            nt = n_rows // 128
            wld = (tw0[:, 0:n_cols], tw1[:, 0:n_cols])
            acc, r = tsm[:, 0:1], tsm[:, 1:2]
            thc = tsm[:, 2:3]
            th = tsm[0:1, 3:4]
            p.memset(acc, 0.0, writes=("t_acc",))
            Wr = W.rearrange("(t p) n -> t p n", p=128)
            for i in range(nt):
                b, bk = wld[i % 2], f"twl{i % 2}"
                p.dma(b, Wr[i], writes=(bk,))
                p.red(r, b, ALU.add, absval=True, reads=(bk,), writes=("t_r",))
                p.tt(acc, acc, r, ALU.add, reads=("t_r", "t_acc"), writes=("t_acc",))
            p.mm(cx.pbC[0:1, 0:1], acc, cx.ones_f_col[:], True, True,
                 reads=("t_acc", "c_ones_f_col"), writes=("pbC",))
            p.ts(th, cx.pbC[0:1, 0:1], 0.5 / (n_rows * n_cols), None, ALU.mult,
                 reads=("pbC",), writes=("t_th",))
            p.mm(cx.pbC[0:128, 0:1], cx.ones_f_row[:], th, True, True,
                 reads=("t_th", "c_ones_f_row"), writes=("pbC",))
            p.copy(thc, cx.pbC[0:128, 0:1], reads=("pbC",), writes=("t_thc",))
            for i in range(nt):
                b, bk = wld[i % 2], f"twl{i % 2}"
                p.dma(b, Wr[i], writes=(bk,))
                p.act(tsg[:, 0:n_cols], b, AF.Sign, reads=(bk,), writes=("t_sg",))
                p.act(tab[:, 0:n_cols], b, AF.Abs, reads=(bk,), writes=("t_ab",))
                p.ts(tab[:, 0:n_cols], tab[:, 0:n_cols], thc[:, 0:1], None, ALU.is_gt,
                     reads=("t_ab", "t_thc"), writes=("t_ab",))
                p.tt(twt[:, 0:n_cols], tsg[:, 0:n_cols], tab[:, 0:n_cols], ALU.mult,
                     reads=("t_sg", "t_ab"), writes=("t_wt",))
                p.dma(out_dram[i * 128:(i + 1) * 128, :], twt[:, 0:n_cols],
                      reads=("t_wt",), writes=(f"twd_{tag}{i}",))

        ternarize(Wup, 4096, 1024, wtup, "u")
        ternarize(Wdown, 1024, 4096, wtdown, "d")
        p.barrier()

    xnbf_d = nc.dram_tensor("xnbf", [4096, 1024], BF16, kind=KIND).ap()
    attn_d = nc.dram_tensor("attn0", [4096, 1024], F32, kind=KIND).ap()
    rstd_d = nc.dram_tensor("rstd0", [4096], F32, kind=KIND).ap()
    if dbg is not None:
        dbg["xnbf"] = xnbf_d
        dbg["attn0"] = attn_d

    x_r = x_ap.rearrange("(t p) n -> t p n", p=128)
    x1_r = x1_ap.rearrange("(t p) n -> t p n", p=128)
    xn_r = xnbf_d.rearrange("(t p) n -> t p n", p=128)
    at_r = attn_d.rearrange("(t p) n -> t p n", p=128)

    with (nc.sbuf_tensor("xt", [128, 2, 1024], F32) as xt,
          nc.sbuf_tensor("lnt", [128, 6], F32) as lnt,
          nc.sbuf_tensor("xnt", [128, 2, 1024], BF16) as xnt,
          nc.sbuf_tensor("sqt", [128, 1024], F32) as sqt):
        m = lnt[:, 0:1]; s2 = lnt[:, 1:2]; var = lnt[:, 2:3]
        rstd = lnt[:, 3:4]; rsf = lnt[:, 4:5]; m2 = lnt[:, 5:6]
        for i in range(NT):
            b = i % 2
            xk, nk = f"xt{b}", f"xnt{b}"
            p.dma(xt[:, b], x_r[i], writes=(xk,))
            p.red(m, xt[:, b], ALU.add, reads=(xk,), writes=("ln_m",))
            p.act(sqt[:], xt[:, b], AF.Square, reads=(xk,), writes=("ln_sq",))
            p.red(s2, sqt[:], ALU.add, reads=("ln_sq",), writes=("ln_s2",))
            p.ts(m, m, 1.0 / 1024, None, ALU.mult, reads=("ln_m",), writes=("ln_m",))
            p.ts(s2, s2, 1.0 / 1024, 1e-5, ALU.mult, ALU.add, reads=("ln_s2",), writes=("ln_s2",))
            p.tt(m2, m, m, ALU.mult, reads=("ln_m",), writes=("ln_m2",))
            p.tt(var, s2, m2, ALU.subtract, reads=("ln_s2", "ln_m2"), writes=("ln_v",))
            cx.rsqrt(p, rstd, var, "ln_rstd", rkeys=("ln_v",))
            p.ts(xnt[:, b], xt[:, b], m[:, 0:1], rstd[:, 0:1], ALU.subtract, ALU.mult,
                 reads=(xk, "ln_m", "ln_rstd"), writes=(nk,))
            p.dma(xn_r[i], xnt[:, b], reads=(nk,), writes=(f"xn_d{i}",))
            p.tt(sqt[:], xnt[:, b], xnt[:, b], ALU.mult, reads=(nk,), writes=("ln_sq",))
            p.red(s2, sqt[:], ALU.add, reads=("ln_sq",), writes=("ln_s2",))
            p.ts(s2, s2, 1.0 / 1024, None, ALU.mult, reads=("ln_s2",), writes=("ln_s2",))
            p.act(rsf, s2, AF.Ln, reads=("ln_s2",), writes=("ln_rsf",))
            p.act(rsf, rsf, AF.Exp, scale=0.5, reads=("ln_rsf",), writes=("ln_rsf",))
            p.ts(rsf, rsf, 1e-8, None, ALU.add, reads=("ln_rsf",), writes=("ln_rsf",))
            p.recip(rsf, rsf, reads=("ln_rsf",), writes=("ln_rsf",))
            p.dma(rstd_d[i * 128:(i + 1) * 128].rearrange("(p o) -> p o", o=1), rsf,
                  reads=("ln_rsf",), writes=(f"rs_d{i}",))
    p.barrier()

    for s in range(2):
        T0 = s * 16
        with nc.sbuf_tensor(f"xnfm{s}", [128, 8, 2048], BF16) as xnfm:
            for c in range(8):
                p.dma(xnfm[:, c, :], xnbf_d[s * SEQ:(s + 1) * SEQ, c * 128:(c + 1) * 128],
                      transpose=True, reads=tuple(f"xn_d{T0 + i}" for i in range(16)),
                      writes=(f"xnfm{c}",))
            with (nc.sbuf_tensor(f"kdup{s}", [128, 8, 2048], BF16) as kdup,
                  nc.sbuf_tensor(f"vaug{s}", [128, 16, 8, 68], BF16) as vaug):
                for j in range(8):
                    for qb in range(4):
                        sl = slice(qb * 512, (qb + 1) * 512)
                        for c in range(8):
                            p.mm(cx.pbA[0:128, 0:512], wkd_sb[:, c, j, :], xnfm[:, c, sl],
                                 c == 0, c == 7, reads=("wkd0", "wkd1", f"xnfm{c}"),
                                 writes=("pbA",))
                        p.copy(kdup[:, j, sl], cx.pbA[0:128, 0:512], reads=("pbA",),
                               writes=(f"kd{j}",))
                p.memset(vaug[:, :, :, 64:65], 1.0, writes=("va_ones",))
                for i in range(16):
                    for c in range(8):
                        p.mm(cx.pbB[0:128, 0:512], xnfm[:, c, i * 128:(i + 1) * 128],
                             wv_sb[:, c, :], c == 0, c == 7,
                             reads=(f"xnfm{c}", "wv"), writes=("pbB",))
                    for h in range(8):
                        p.copy(vaug[:, i, h, 0:64], cx.pbB[0:128, h * 64:(h + 1) * 64],
                               reads=("pbB",), writes=(f"va{i}",))

                with (nc.sbuf_tensor(f"qfm{s}", [128, 8, 512], BF16) as qfm,
                      nc.sbuf_tensor(f"sq_bf{s}", [128, 8, 512], BF16) as sq_bf,
                      nc.sbuf_tensor(f"elut{s}", [128, 8, 512], F32) as elut,
                      nc.sbuf_tensor(f"ebuf{s}", [128, 2, 1024], BF16) as ebuf,
                      nc.sbuf_tensor(f"pvs{s}", [65, 2, 512], BF16) as pvs,
                      nc.sbuf_tensor(f"acc{s}", [128, 4, 1024], F32) as acc,
                      nc.sbuf_tensor(f"eps_t{s}", [128, 8], F32) as eps_t):
                    for qb in range(4):
                        qsl = slice(qb * 512, (qb + 1) * 512)
                        for c in range(8):
                            for d in range(8):
                                p.mm(cx.pbA[0:128, 0:512], wq_sb[:, d, c * 128:(c + 1) * 128],
                                     xnfm[:, d, qsl], d == 0, d == 7,
                                     reads=("wq", f"xnfm{d}"), writes=("pbA",))
                            p.copy(qfm[:, c, :], cx.pbA[0:128, 0:512], reads=("pbA",),
                                   writes=(f"qfm{c}",))
                        qk = tuple(f"qfm{c}" for c in range(8))
                        p.ts(elut[:], qfm[:], 0.0, None, ALU.min, reads=qk, writes=("elut",))
                        p.act(elut[:], elut[:], AF.Exp, reads=("elut",), writes=("elut",))
                        p.ts(sq_bf[:], qfm[:], 0.0, -1.0, ALU.max, ALU.add, reads=qk,
                             writes=("sqbf",))
                        p.tt(sq_bf[:], sq_bf[:], elut[:], ALU.add, reads=("sqbf", "elut"),
                             writes=("sqbf",))
                        for g in range(2):
                            for j in range(4):
                                for c in range(4):
                                    cc = 4 * g + c
                                    p.mm(cx.pbA[0:128, 0:512],
                                         sq_bf[:, cc, j * 128:(j + 1) * 128],
                                         mem_sb[:, c, :], c == 0, c == 3,
                                         reads=("sqbf", "mem"), writes=("pbA",))
                                    p.mm(cx.pbC[0:128, 0:1],
                                         sq_bf[:, cc, j * 128:(j + 1) * 128],
                                         mn_sb[:, c:c + 1], c == 0, c == 3,
                                         reads=("sqbf", "mn"), writes=("pbC",))
                                p.recip(eps_t[:, 0:1], cx.pbC[0:128, 0:1], reads=("pbC",),
                                        writes=("rdw",))
                                p.tt(eps_t[:, 0:1], eps_t[:, 0:1], wcol, ALU.mult,
                                     reads=("rdw", "wcols"), writes=("rdw",))
                                p.ts(acc[:, j, g * 512:(g + 1) * 512], cx.pbA[0:128, 0:512],
                                     eps_t[:, 0:1], None, ALU.mult,
                                     reads=("pbA", "rdw"), writes=(f"acc{j}",))
                        for hp in range(8):
                            for kt in range(16):
                                pb, pk = (cx.pbA, "pbA") if kt % 2 == 0 else (cx.pbB, "pbB")
                                eb, ek = (kt % 2), f"eb{kt % 2}"
                                ksl = slice(kt * 128, (kt + 1) * 128)
                                p.mm(pb[0:128, 0:512], kdup[0:64, hp, ksl], qfm[0:64, hp, :],
                                     True, True, reads=(f"kd{hp}", f"qfm{hp}"), writes=(pk,))
                                p.mm(pb[0:128, 512:1024], kdup[64:128, hp, ksl],
                                     qfm[64:128, hp, :],
                                     True, True, reads=(f"kd{hp}", f"qfm{hp}"), writes=(pk,))
                                p.act(ebuf[:, eb], pb[:], AF.Exp, scale=0.125, reads=(pk,),
                                      writes=(ek,))
                                p.mm(cx.pbC[0:65, 0:512], vaug[:, kt, hp, 0:65],
                                     ebuf[:, eb, 0:512], kt == 0, kt == 15,
                                     reads=(f"va{kt}", "va_ones", ek), writes=("pbC",))
                                p.mm(cx.pbD[0:65, 0:512], vaug[:, kt, hp, 0:65],
                                     ebuf[:, eb, 512:1024], kt == 0, kt == 15,
                                     reads=(f"va{kt}", "va_ones", ek), writes=("pbD",))
                            for hh in range(2):
                                pvp, pvk = (cx.pbC, "pbC") if hh == 0 else (cx.pbD, "pbD")
                                pT, pTk = (cx.pbT, "pbT") if hh == 0 else (cx.pbU, "pbU")
                                n = 2 * hp + hh
                                p.copy(pvs[:, hh], pvp[0:65, 0:512], reads=(pvk,),
                                       writes=(f"pvs{hh}",))
                                for j in range(4):
                                    p.trans(pT[0:128, j * 256:j * 256 + 65],
                                            pvs[0:65, hh, j * 128:(j + 1) * 128],
                                            cx.id128[0:65, 0:65],
                                            reads=(f"pvs{hh}", "c_id128"), writes=(pTk,))
                                dcols = pT[0:128, 256 * 0 + 64:256 * 3 + 65:256]
                                p.recip(eps_t[:, 4:8], dcols, reads=(pTk,), writes=("rin1",))
                                p.ts(eps_t[:, 4:8], eps_t[:, 4:8], w1col[:, 0:1], None,
                                     ALU.mult, reads=("rin1", "wcols"), writes=("rin1",))
                                for j in range(4):
                                    p.stt(acc[:, j, n * 64:(n + 1) * 64],
                                          pT[0:128, j * 256:j * 256 + 64], eps_t[:, 4 + j:5 + j],
                                          acc[:, j, n * 64:(n + 1) * 64], ALU.mult, ALU.add,
                                          reads=(pTk, "rin1", f"acc{j}"), writes=(f"acc{j}",))
                        for j in range(4):
                            p.dma(at_r[T0 + qb * 4 + j], acc[:, j, :], reads=(f"acc{j}",),
                                  writes=(f"at_d{T0 + qb * 4 + j}",))
                p.barrier()

            with (nc.sbuf_tensor(f"hstore{s}", [128, 32, 512], BF16) as hst,
                  nc.sbuf_tensor(f"xffn{s}", [128, 8, 512], BF16) as xffn,
                  nc.sbuf_tensor(f"wupT{s}", [128, 2, 8, 512], BF16) as wupT,
                  nc.sbuf_tensor(f"wdT{s}", [128, 2, 1024], BF16) as wdT,
                  nc.sbuf_tensor(f"r1row{s}", [1, 512], F32) as r1row,
                  nc.sbuf_tensor(f"r1b{s}", [128, 512], F32) as r1b,
                  nc.sbuf_tensor(f"h2{s}", [128, 512], BF16) as h2t,
                  nc.sbuf_tensor(f"ff{s}", [128, 8], F32) as ff,
                  nc.sbuf_tensor(f"xres{s}", [128, 1024], F32) as xres,
                  nc.sbuf_tensor(f"ares{s}", [128, 1024], F32) as ares,
                  nc.sbuf_tensor(f"x1t{s}", [128, 1024], F32) as x1t):
                for blk in range(4):
                    t0 = s * SEQ + blk * 512
                    bt0 = blk * 512
                    p.dma(r1row[:], rstd_d[t0:t0 + 512].rearrange("(o n) -> o n", o=1),
                          reads=tuple(f"rs_d{t0 // 128 + i}" for i in range(4)),
                          writes=("r1row",))
                    p.mm(cx.pbD[0:128, 0:512], cx.ones_f_row[:], r1row[:], True, True,
                         reads=("r1row", "c_ones_f_row"), writes=("pbD",))
                    p.copy(r1b[:], cx.pbD[0:128, 0:512], reads=("pbD",), writes=("r1b",))
                    for c in range(8):
                        p.tt(xffn[:, c, :], xnfm[:, c, bt0:bt0 + 512], r1b[:], ALU.mult,
                             reads=(f"xnfm{c}", "r1b"), writes=(f"xf{c}",))
                    for fg in range(8):
                        wb, wk_ = fg % 2, f"wupT{fg % 2}"
                        for d in range(8):
                            p.dma(wupT[:, wb, d, :],
                                  wtup[fg * 512:(fg + 1) * 512, d * 128:(d + 1) * 128],
                                  transpose=True, writes=(wk_,))
                        for fs in range(4):
                            fi = fg * 4 + fs
                            pb, pk = (cx.pbA, "pbA") if fi % 2 == 0 else (cx.pbB, "pbB")
                            for d in range(8):
                                p.mm(pb[0:128, 0:512], wupT[:, wb, d, fs * 128:(fs + 1) * 128],
                                     xffn[:, d, :], d == 0, d == 7,
                                     reads=(wk_,) + tuple(f"xf{dd}" for dd in range(8)),
                                     writes=(pk,))
                            p.act(hst[:, fi, :], pb[0:128, 0:512], AF.Gelu,
                                  reads=(pk,), writes=(f"h{fi}",))
                            p.tt(h2t[:], hst[:, fi, :], hst[:, fi, :], ALU.mult,
                                 reads=(f"h{fi}",), writes=("h2",))
                            for j in range(4):
                                p.mm(cx.pbC[0:128, j * 32 + fi:j * 32 + fi + 1],
                                     h2t[:, j * 128:(j + 1) * 128],
                                     cx.ones_bf_col[:], True, True,
                                     reads=("h2", "c_ones_bf_col"), writes=("pbC",))
                    hkeys = tuple(f"h{fi}" for fi in range(32))
                    for j in range(4):
                        p.red(ff[:, 4 + j:5 + j], cx.pbC[0:128, j * 32:j * 32 + 32],
                              ALU.add, reads=("pbC",), writes=("ff_ss",))
                    p.ts(ff[:, 0:4], ff[:, 4:8], 1.0 / 4096, None, ALU.mult,
                         reads=("ff_ss",), writes=("ff_r2",))
                    p.act(ff[:, 0:4], ff[:, 0:4], AF.Ln, reads=("ff_r2",), writes=("ff_r2",))
                    p.act(ff[:, 0:4], ff[:, 0:4], AF.Exp, scale=0.5, reads=("ff_r2",),
                          writes=("ff_r2",))
                    p.ts(ff[:, 0:4], ff[:, 0:4], 1e-8, None, ALU.add, reads=("ff_r2",),
                         writes=("ff_r2",))
                    p.recip(ff[:, 0:4], ff[:, 0:4], reads=("ff_r2",), writes=("ff_r2",))
                    for jp in range(2):
                        for fc in range(32):
                            wb2, wdk = fc % 2, f"wdT{fc % 2}"
                            p.dma(wdT[:, wb2, :], wtdown[0:1024, fc * 128:(fc + 1) * 128],
                                  transpose=True, writes=(wdk,))
                            for jj in range(2):
                                j = 2 * jp + jj
                                pb, pk = (cx.pbA, "pbA") if jj == 0 else (cx.pbB, "pbB")
                                p.mm(pb[0:128, 0:512], hst[:, fc, j * 128:(j + 1) * 128],
                                     wdT[:, wb2, 0:512], fc == 0, fc == 31,
                                     reads=(wdk,) + hkeys, writes=(pk,))
                                p.mm(pb[0:128, 512:1024], hst[:, fc, j * 128:(j + 1) * 128],
                                     wdT[:, wb2, 512:1024], fc == 0, fc == 31,
                                     reads=(wdk,) + hkeys, writes=(pk,))
                        for jj in range(2):
                            j = 2 * jp + jj
                            ti = t0 // 128 + j
                            pb, pk = (cx.pbA, "pbA") if jj == 0 else (cx.pbB, "pbB")
                            p.dma(xres[:], x_r[ti], writes=("xres",))
                            p.dma(ares[:], at_r[ti], reads=(f"at_d{ti}",), writes=("ares",))
                            p.tt(x1t[:], xres[:], ares[:], ALU.add, reads=("xres", "ares"),
                                 writes=("x1t",))
                            p.stt(x1t[:], pb[0:128, 0:1024], ff[:, j:j + 1], x1t[:],
                                  ALU.mult, ALU.add, reads=(pk, "ff_r2", "x1t"),
                                  writes=("x1t",))
                            p.dma(x1_r[ti], x1t[:], reads=("x1t",), writes=(f"x1_d{ti}",))
                p.barrier()
    l0ws.close()
    p.barrier()
    return {"x1_r": x1_r}


def build_ln(nc, p, cx, src_ap, dst_ap, tag, src_keys=None):
    src_r = src_ap.rearrange("(t p) n -> t p n", p=128)
    dst_r = dst_ap.rearrange("(t p) n -> t p n", p=128)
    NT = src_r.shape[0]
    with (nc.sbuf_tensor(f"lnx{tag}", [128, 2, 1024], F32) as xt,
          nc.sbuf_tensor(f"lns{tag}", [128, 6], F32) as lnt,
          nc.sbuf_tensor(f"lno{tag}", [128, 2, 1024], BF16) as xnt,
          nc.sbuf_tensor(f"lnq{tag}", [128, 1024], F32) as sqt):
        m = lnt[:, 0:1]; s2 = lnt[:, 1:2]; var = lnt[:, 2:3]
        rstd = lnt[:, 3:4]; m2 = lnt[:, 5:6]
        for i in range(NT):
            b = i % 2
            xk, nk = f"{tag}xt{b}", f"{tag}xnt{b}"
            rk = (src_keys[i],) if src_keys else ()
            p.dma(xt[:, b], src_r[i], reads=rk, writes=(xk,))
            p.red(m, xt[:, b], ALU.add, reads=(xk,), writes=(f"{tag}m",))
            p.act(sqt[:], xt[:, b], AF.Square, reads=(xk,), writes=(f"{tag}sq",))
            p.red(s2, sqt[:], ALU.add, reads=(f"{tag}sq",), writes=(f"{tag}s2",))
            p.ts(m, m, 1.0 / 1024, None, ALU.mult, reads=(f"{tag}m",), writes=(f"{tag}m",))
            p.ts(s2, s2, 1.0 / 1024, 1e-5, ALU.mult, ALU.add, reads=(f"{tag}s2",),
                 writes=(f"{tag}s2",))
            p.tt(m2, m, m, ALU.mult, reads=(f"{tag}m",), writes=(f"{tag}m2",))
            p.tt(var, s2, m2, ALU.subtract, reads=(f"{tag}s2", f"{tag}m2"),
                 writes=(f"{tag}v",))
            cx.rsqrt(p, rstd, var, f"{tag}rstd", rkeys=(f"{tag}v",))
            p.ts(xnt[:, b], xt[:, b], m[:, 0:1], rstd[:, 0:1], ALU.subtract, ALU.mult,
                 reads=(xk, f"{tag}m", f"{tag}rstd"), writes=(nk,))
            p.dma(dst_r[i], xnt[:, b], reads=(nk,), writes=(f"{tag}d{i}",))
    p.barrier()


def build_gla(nc, p, cx, xn1bf_d, w_aps, gla_d, dev=False):
    SEQ = 2048
    HB = 256
    NHB = SEQ // HB
    LWq, LWk, LWv, LWg = (w_aps["L1_attn_Wq"], w_aps["L1_attn_Wk"],
                          w_aps["L1_attn_Wv"], w_aps["L1_attn_Wg"])
    GK1, GK2, GKB = w_aps["L1_attn_gk_w1"], w_aps["L1_attn_gk_w2"], w_aps["L1_attn_gk_b"]
    LWo, RMSW = w_aps["L1_attn_Wo"], w_aps["L1_attn_rms_w"]

    glws = ExitStack()
    lwq = glws.enter_context(nc.sbuf_tensor("lwq", [128, 8, 1024], BF16))
    lwk = glws.enter_context(nc.sbuf_tensor("lwk", [128, 8, 1024], BF16))
    lwv = glws.enter_context(nc.sbuf_tensor("lwv", [128, 8, 2048], BF16))
    lwg = glws.enter_context(nc.sbuf_tensor("lwg", [128, 8, 2048], BF16))
    gk1 = glws.enter_context(nc.sbuf_tensor("gk1", [128, 8, 16], BF16))
    gk2 = glws.enter_context(nc.sbuf_tensor("gk2", [16, 1024], BF16))
    gkb = glws.enter_context(nc.sbuf_tensor("gkb", [128, 8], F32))
    rw = glws.enter_context(nc.sbuf_tensor("rw", [128, 4], F32))
    with nc.sbuf_tensor("glstage", [128, 2048], F32) as stg:
        for nm, dst, W, nch, cols in (("lwq", lwq, LWq, 8, 1024), ("lwk", lwk, LWk, 8, 1024),
                                      ("lwv", lwv, LWv, 8, 2048), ("lwg", lwg, LWg, 8, 2048),
                                      ("gk1", gk1, GK1, 8, 16)):
            Wr = W.rearrange("(c p) n -> c p n", p=128)
            cast_load(nc, p, stg, [(dst[:, c, :], Wr[c], cols) for c in range(nch)],
                      nm, "glstg")
        p.dma(stg[0:16, 0:1024], GK2, writes=("glstg",))
        p.copy(gk2[:], stg[0:16, 0:1024], reads=("glstg",), writes=("gk2",))
        GKB_r = GKB.rearrange("(c p) -> c p", p=128)
        for c in range(8):
            p.dma(gkb[:, c:c + 1], GKB_r[c].rearrange("(p o) -> p o", o=1), writes=("gkb",))
        RW_r = RMSW.rearrange("(c p) -> c p", p=128)
        for c in range(4):
            p.dma(rw[:, c:c + 1], RW_r[c].rearrange("(p o) -> p o", o=1), writes=("rw",))
    p.barrier()

    og_d = nc.dram_tensor("og_d", [2048, 2048], BF16,
                          kind="ExternalOutput" if dev else "Internal").ap()
    gla_r = gla_d.rearrange("(t p) n -> t p n", p=128)

    for s in range(2):
        with (nc.sbuf_tensor(f"gxnfm{s}", [128, 8, 2048], BF16) as xnfm,
              nc.sbuf_tensor(f"gS{s}", [128, 2, 4, 512], F32) as Sst,
              nc.sbuf_tensor(f"gS0b{s}", [128, 2, 2, 512], BF16) as S0b):
            for c in range(8):
                p.dma(xnfm[:, c, :], xn1bf_d[s * SEQ:(s + 1) * SEQ, c * 128:(c + 1) * 128],
                      transpose=True, reads=tuple(f"N1d{s * 16 + i}" for i in range(16)),
                      writes=(f"gxn{c}",))
            p.memset(Sst[:], 0.0, writes=("S",))
            for hb in range(NHB):
                t0 = hb * HB
                hbk = f"{s}_{hb}"
                with ExitStack() as _st:
                    glc = _st.enter_context(nc.sbuf_tensor(f"g_gl{hbk}", [128, 256], F32))
                    Gb = _st.enter_context(nc.sbuf_tensor(f"g_G{hbk}", [128, 8, 256], F32))
                    tmp = _st.enter_context(nc.sbuf_tensor(f"g_tmp{hbk}", [128, 2, 256], F32))

                    zb = _st.enter_context(nc.sbuf_tensor(f"g_z{hbk}", [128, 128], F32))
                    qt = _st.enter_context(nc.sbuf_tensor(f"g_qt{hbk}", [128, 8, 256], BF16))
                    ktl = _st.enter_context(nc.sbuf_tensor(f"g_kt{hbk}", [128, 8, 256], BF16))
                    khat = _st.enter_context(nc.sbuf_tensor(f"g_kh{hbk}", [128, 8, 256], BF16))
                    vb = _st.enter_context(nc.sbuf_tensor(f"g_v{hbk}", [128, 2, 2048], BF16))
                    gateb = _st.enter_context(nc.sbuf_tensor(f"g_gate{hbk}", [128, 16, 256], BF16))
                    ogb = _st.enter_context(nc.sbuf_tensor(f"g_og{hbk}", [128, 16, 256], BF16))
                    Asb = _st.enter_context(nc.sbuf_tensor(f"g_A{hbk}", [128, 4, 128], BF16))
                    kTb = _st.enter_context(nc.sbuf_tensor(f"g_kT{hbk}", [128, 2, 128], BF16))
                    sqb = _st.enter_context(nc.sbuf_tensor(f"g_sq{hbk}", [128, 1024], BF16))
                    rrow = _st.enter_context(nc.sbuf_tensor(f"g_rr{hbk}", [1, 512], F32))
                    rsb = _st.enter_context(nc.sbuf_tensor(f"g_rs{hbk}", [128, 4], F32))
                    rstdb = _st.enter_context(nc.sbuf_tensor(f"g_rb{hbk}", [128, 512], F32))
                    rlow = _st.enter_context(nc.sbuf_tensor(f"g_r{hbk}", [16, 256], BF16))
                    p.memset(zb[:], 0.0, writes=("z",))
                    tsl = slice(t0, t0 + 256)
                    for d in range(8):
                        p.mm(cx.pbC[0:16, 0:256], gk1[:, d, :], xnfm[:, d, tsl],
                             d == 0, d == 7, reads=("gk1", f"gxn{d}"), writes=("pbC",))
                    p.copy(rlow[:], cx.pbC[0:16, 0:256], reads=("pbC",), writes=("rlow",))
                    for c in range(8):
                        p.mm(cx.pbD[0:128, 0:256], gk2[:, c * 128:(c + 1) * 128], rlow[:],
                             True, True, reads=("gk2", "rlow"), writes=("pbD",))
                        fn = (lambda eng, o=glc[:], i=cx.pbD[0:128, 0:256],
                              b=gkb[:, c:c + 1]: eng.activation(o, i, AF.Identity, bias=b))
                        p.emit("scalar", fn, reads=("pbD", "gkb"), writes=("glc",))
                        p.act(tmp[:, 0, :], glc[:], AF.Abs, reads=("glc",),
                              writes=("tmp0",))
                        p.act(tmp[:, 0, :], tmp[:, 0, :], AF.Exp, scale=-1.0,
                              reads=("tmp0",), writes=("tmp0",))
                        p.act(tmp[:, 0, :], tmp[:, 0, :], AF.Ln, bias=1.0,
                              reads=("tmp0",), writes=("tmp0",))
                        p.ts(tmp[:, 1, :], glc[:], 0.0, None, ALU.min,
                             reads=("glc",), writes=("tmp1",))
                        p.tt(glc[:], tmp[:, 1, :], tmp[:, 0, :], ALU.subtract,
                             reads=("tmp1", "tmp0"), writes=("glc",))
                        for ci in range(2):
                            p.scan(Gb[:, c, ci * 128:(ci + 1) * 128],
                                   glc[:, ci * 128:(ci + 1) * 128], zb[:], 0.0,
                                   ALU.add, ALU.add, reads=("glc", "z"),
                                   writes=(f"G{c}",))
                    for c in range(8):
                        p.act(glc[:], Gb[:, c, :], AF.Exp, scale=1.0 / 16,
                              reads=(f"G{c}",), writes=("glc",))
                        for d in range(8):
                            p.mm(cx.pbA[0:128, 0:256], lwq[:, d, c * 128:(c + 1) * 128],
                                 xnfm[:, d, tsl], d == 0, d == 7,
                                 reads=("lwq", f"gxn{d}"), writes=("pbA",))
                        p.stt(qt[:, c, :], cx.pbA[0:128, 0:256], 0.0625, glc[:],
                              ALU.mult, ALU.mult, reads=("pbA", "glc"), writes=(f"qt{c}",))
                        p.act(glc[:], Gb[:, c, :], AF.Exp, scale=-1.0 / 16,
                              reads=(f"G{c}",), writes=("glc",))
                        for d in range(8):
                            p.mm(cx.pbB[0:128, 0:256], lwk[:, d, c * 128:(c + 1) * 128],
                                 xnfm[:, d, tsl], d == 0, d == 7,
                                 reads=("lwk", f"gxn{d}"), writes=("pbB",))
                        p.tt(ktl[:, c, :], cx.pbB[0:128, 0:256], glc[:], ALU.mult,
                             reads=("pbB", "glc"), writes=(f"kt{c}",))
                        for ci in range(2):
                            p.act(rsb[:, 0:1], Gb[:, c, ci * 128 + 127:ci * 128 + 128],
                                  AF.Exp, scale=1.0 / 16, reads=(f"G{c}",), writes=("E4",))
                            p.ts(khat[:, c, ci * 128:(ci + 1) * 128],
                                 ktl[:, c, ci * 128:(ci + 1) * 128], rsb[:, 0:1], None,
                                 ALU.mult, reads=(f"kt{c}", "E4"), writes=(f"kh{c}",))
                    for tt in range(2):
                        for vc in range(4):
                            for d in range(8):
                                p.mm(cx.pbA[0:128, 0:512],
                                     xnfm[:, d, t0 + tt * 128:t0 + (tt + 1) * 128],
                                     lwv[:, d, vc * 512:(vc + 1) * 512], d == 0, d == 7,
                                     reads=("lwv", f"gxn{d}"), writes=("pbA",))
                            p.copy(vb[:, tt, vc * 512:(vc + 1) * 512],
                                   cx.pbA[0:128, 0:512], reads=("pbA",), writes=(f"v{tt}",))
                    for vc in range(16):
                        for d in range(8):
                            p.mm(cx.pbB[0:128, 0:256], lwg[:, d, vc * 128:(vc + 1) * 128],
                                 xnfm[:, d, tsl], d == 0, d == 7,
                                 reads=("lwg", f"gxn{d}"), writes=("pbB",))
                        p.act(glc[:], cx.pbB[0:128, 0:256], AF.Exp, scale=-1.0,
                              reads=("pbB",), writes=("glc",))
                        p.ts(glc[:], glc[:], 1.0, None, ALU.add, reads=("glc",),
                             writes=("glc",))
                        p.recip(glc[:], glc[:], reads=("glc",), writes=("glc",))
                        p.tt(gateb[:, vc, :], cx.pbB[0:128, 0:256], glc[:], ALU.mult,
                             reads=("pbB", "glc"), writes=(f"gate{vc}",))
                    for ci in range(2):
                        csl = slice(ci * 128, (ci + 1) * 128)
                        p.copy(S0b[:], Sst[:, :, 0:2, :], reads=("S",), writes=("S0b",))
                        for h in range(4):
                            cc0, cc1 = 2 * h, 2 * h + 1
                            asl = slice(128 * h, 128 * (h + 1))
                            p.mm(cx.pbC[0:128, asl], ktl[:, cc0, csl], qt[:, cc0, csl],
                                 True, False, reads=(f"kt{cc0}", f"qt{cc0}"),
                                 writes=("pbC",))
                            p.mm(cx.pbC[0:128, asl], ktl[:, cc1, csl], qt[:, cc1, csl],
                                 False, True, reads=(f"kt{cc1}", f"qt{cc1}"),
                                 writes=("pbC",))
                            p.tt(Asb[:, h, :], cx.pbC[0:128, asl], cx.mask128[:], ALU.mult,
                                 reads=("pbC", "c_mask128"), writes=(f"Asb{h}",))
                        for h in range(4):
                            if h == 2:
                                p.copy(S0b[:], Sst[:, :, 2:4, :], reads=("S",),
                                       writes=("S0b",))
                            s0k = "S0b"
                            cc0, cc1 = 2 * h, 2 * h + 1
                            pb, pk = (cx.pbA, "pbA") if h < 2 else (cx.pbB, "pbB")
                            off = 512 * (h % 2)
                            for j in range(4):
                                jsl = slice(512 * h + 128 * j, 512 * h + 128 * (j + 1))
                                osl = slice(off + 128 * j, off + 128 * (j + 1))
                                p.mm(pb[0:128, osl], vb[:, ci, jsl], Asb[:, h, :],
                                     True, False, reads=(f"v{ci}", f"Asb{h}"), writes=(pk,))
                                p.mm(pb[0:128, osl],
                                     S0b[:, 0, h % 2, 128 * j:128 * (j + 1)], qt[:, cc0, csl],
                                     False, False, reads=(s0k, f"qt{cc0}"), writes=(pk,))
                                p.mm(pb[0:128, osl],
                                     S0b[:, 1, h % 2, 128 * j:128 * (j + 1)], qt[:, cc1, csl],
                                     False, True, reads=(s0k, f"qt{cc1}"), writes=(pk,))
                        for half in range(2):
                            pbh, pkh = (cx.pbA, "pbA") if half == 0 else (cx.pbB, "pbB")
                            p.act(sqb[:], pbh[0:128, 0:1024], AF.Square,
                                  reads=(pkh,), writes=(f"osq{half}",))
                            for hh in range(2):
                                h = 2 * half + hh
                                for j in range(4):
                                    p.mm(cx.pbD[0:1, 128 * h:128 * (h + 1)],
                                         cx.ones_bf_col[:],
                                         sqb[:, 512 * hh + 128 * j:512 * hh + 128 * (j + 1)],
                                         j == 0, j == 3,
                                         reads=(f"osq{half}", "c_ones_bf_col"),
                                         writes=("pbD",))
                        p.copy(rrow[:], cx.pbD[0:1, 0:512], reads=("pbD",), writes=("rrow",))
                        p.ts(rrow[:], rrow[:], 1.0 / 512, 1e-5, ALU.mult, ALU.add,
                             reads=("rrow",), writes=("rrow",))
                        p.act(rrow[:], rrow[:], AF.Ln, reads=("rrow",), writes=("rrow",))
                        p.act(rrow[:], rrow[:], AF.Exp, scale=-0.5, reads=("rrow",),
                              writes=("rrow",))
                        p.mm(cx.pbD[0:128, 0:512], cx.ones_f_row[:], rrow[:],
                             True, True, reads=("rrow", "c_ones_f_row"), writes=("pbD",))
                        p.copy(rstdb[:], cx.pbD[0:128, 0:512], reads=("pbD",),
                               writes=("orb",))
                        for h in range(4):
                            pb, pk = (cx.pbA, "pbA") if h < 2 else (cx.pbB, "pbB")
                            off = 512 * (h % 2)
                            for j in range(4):
                                p.stt(ogb[:, 4 * h + j, csl],
                                      pb[0:128, off + 128 * j:off + 128 * (j + 1)],
                                      rw[:, j:j + 1],
                                      rstdb[:, 128 * h:128 * (h + 1)], ALU.mult, ALU.mult,
                                      reads=(pk, "rw", "orb"), writes=(f"og{4 * h + j}",))
                                p.tt(ogb[:, 4 * h + j, csl], ogb[:, 4 * h + j, csl],
                                     gateb[:, 4 * h + j, csl], ALU.mult,
                                     reads=(f"og{4 * h + j}", f"gate{4 * h + j}"),
                                     writes=(f"og{4 * h + j}",))
                        for h in range(4):
                            cc0, cc1 = 2 * h, 2 * h + 1
                            pT, pTk = (cx.pbT, "pbT") if h % 2 == 0 else (cx.pbU, "pbU")
                            for i, cc in enumerate((cc0, cc1)):
                                p.trans(pT[0:128, 512 * i:512 * i + 128], khat[:, cc, csl],
                                        cx.id128[:], reads=(f"kh{cc}", "c_id128"),
                                        writes=(pTk,))
                            p.copy(kTb[:, 0, :], pT[0:128, 0:128], reads=(pTk,),
                                   writes=("kT0",))
                            p.copy(kTb[:, 1, :], pT[0:128, 512:640], reads=(pTk,),
                                   writes=("kT1",))
                            pb, pk = (cx.pbA, "pbA") if h < 2 else (cx.pbB, "pbB")
                            off = 512 * (h % 2)
                            for i, cc in enumerate((cc0, cc1)):
                                psl = slice(off * 0 + 512 * i, 512 * (i + 1))
                                p.mm(pb[0:128, 512 * i:512 * (i + 1)], kTb[:, i, :],
                                     vb[:, ci, 512 * h:512 * (h + 1)], True, True,
                                     reads=(f"kT{i}", f"v{ci}"), writes=(pk,))
                                p.act(rsb[:, 1:2], Gb[:, cc, ci * 128 + 127:ci * 128 + 128],
                                      AF.Exp, scale=1.0 / 16, reads=(f"G{cc}",),
                                      writes=("E4b",))
                                p.stt(Sst[:, i, h, :], Sst[:, i, h, :], rsb[:, 1:2],
                                      pb[0:128, 512 * i:512 * (i + 1)], ALU.mult,
                                      ALU.add, reads=("S", "E4b", pk), writes=("S",))
                    for vc in range(16):
                        p.dma(og_d[vc * 128:(vc + 1) * 128, t0:t0 + 256], ogb[:, vc, :],
                              reads=(f"og{vc}",), writes=(f"ogd{s}_{hb}_{vc}",))
            p.barrier()
            with (nc.sbuf_tensor(f"lwo{s}", [128, 16, 1024], BF16) as lwo,
                  nc.sbuf_tensor(f"ogs{s}", [128, 2, 16, 128], BF16) as ogs,
                  nc.sbuf_tensor(f"glo{s}", [128, 2, 1024], F32) as glo):
                with nc.sbuf_tensor(f"lwostg{s}", [128, 1024], F32) as stg2:
                    LWo_r = LWo.rearrange("(c p) n -> c p n", p=128)
                    cast_load(nc, p, stg2, [(lwo[:, c, :], LWo_r[c], 1024)
                                            for c in range(16)], "lwo", "lwostg")
                p.barrier()
                ogr = og_d.rearrange("(vc p) t -> p vc t", p=128)
                for ti in range(16):
                    b = ti % 2
                    p.dma(ogs[:, b], ogr[:, :, ti * 128:(ti + 1) * 128],
                          reads=tuple(f"ogd{s}_{ti // 2}_{vc}" for vc in range(16)),
                          writes=(f"ogs{b}",))
                    for vc in range(16):
                        p.mm(cx.pbA[0:128, 0:512], ogs[:, b, vc, :],
                             lwo[:, vc, 0:512], vc == 0, vc == 15,
                             reads=(f"ogs{b}", "lwo"), writes=("pbA",))
                        p.mm(cx.pbA[0:128, 512:1024], ogs[:, b, vc, :],
                             lwo[:, vc, 512:1024], vc == 0, vc == 15,
                             reads=(f"ogs{b}", "lwo"), writes=("pbA",))
                    p.copy(glo[:, b], cx.pbA[0:128, 0:1024], reads=("pbA",),
                           writes=(f"glo{b}",))
                    p.dma(gla_r[s * 16 + ti], glo[:, b], reads=(f"glo{b}",),
                          writes=(f"glad{s * 16 + ti}",))
            p.barrier()
    glws.close()
    p.barrier()
    return og_d


def build_ffn1(nc, p, cx, x1_ap, xn1bf_d, gla_d, w_aps, out_ap, dev=False):
    SEQ = 2048
    Wqkv, Bqkv = w_aps["L1_ffn_W_qkv"], w_aps["L1_ffn_b_qkv"]
    Wg, Bg = w_aps["L1_ffn_W_gate"], w_aps["L1_ffn_b_gate"]
    We, Be = w_aps["L1_ffn_W_experts"], w_aps["L1_ffn_b_experts"]

    f1ws = ExitStack()
    wgate = f1ws.enter_context(nc.sbuf_tensor("wgate", [128, 8, 8], BF16))
    bgr = f1ws.enter_context(nc.sbuf_tensor("bgr", [1, 8], F32))
    bgb = f1ws.enter_context(nc.sbuf_tensor("bgb", [128, 8], F32))
    bqc = f1ws.enter_context(nc.sbuf_tensor("bqc", [128, 16], F32))
    bvr = f1ws.enter_context(nc.sbuf_tensor("bvr", [1, 1024], F32))
    bvb = f1ws.enter_context(nc.sbuf_tensor("bvb", [128, 1024], F32))
    ber = f1ws.enter_context(nc.sbuf_tensor("ber", [1, 8, 1024], BF16))
    with nc.sbuf_tensor("f1stage", [128, 1024], F32) as stg:
        Wg_r = Wg.rearrange("(c p) n -> c p n", p=128)
        cast_load(nc, p, stg, [(wgate[:, c, :], Wg_r[c], 8) for c in range(8)],
                  "wgate", "f1stg")
        p.dma(bgr[:], Bg.rearrange("(o n) -> o n", o=1), writes=("bgr",))
        Bq_r = Bqkv[0:2048].rearrange("(c p) -> c p", p=128)
        for c in range(16):
            p.dma(bqc[:, c:c + 1], Bq_r[c].rearrange("(p o) -> p o", o=1), writes=("bqc",))
        p.dma(bvr[:], Bqkv[2048:3072].rearrange("(o n) -> o n", o=1), writes=("bvr",))
        with nc.sbuf_tensor("berf", [1, 8, 1024], F32) as berf:
            p.dma(berf[:], Be.rearrange("(o e) n -> o e n", o=1), writes=("berf",))
            p.copy(ber[:], berf[:], reads=("berf",), writes=("ber",))
    p.barrier()
    p.mm(cx.pbC[0:128, 0:8], cx.ones_f_row[:], bgr[:], True, True,
         reads=("bgr", "c_ones_f_row"), writes=("pbC",))
    p.copy(bgb[:], cx.pbC[0:128, 0:8], reads=("pbC",), writes=("bgb",))
    p.mm(cx.pbA[0:128, 0:512], cx.ones_f_row[:], bvr[:, 0:512], True, True,
         reads=("bvr", "c_ones_f_row"), writes=("pbA",))
    p.copy(bvb[:, 0:512], cx.pbA[0:128, 0:512], reads=("pbA",), writes=("bvb",))
    p.mm(cx.pbA[0:128, 0:512], cx.ones_f_row[:], bvr[:, 512:1024], True, True,
         reads=("bvr", "c_ones_f_row"), writes=("pbA",))
    p.copy(bvb[:, 512:1024], cx.pbA[0:128, 0:512], reads=("pbA",), writes=("bvb",))

    attn1_d = nc.dram_tensor("attn1_d", [4096, 1024], F32,
                             kind="ExternalOutput" if dev else "Internal").ap()
    if dev:
        moe_dbg = nc.dram_tensor("moe_dbg", [4096, 1024], F32, kind="ExternalOutput").ap()
        cmb_dbg = nc.dram_tensor("cmb_dbg", [4096, 8], F32, kind="ExternalOutput").ap()
    at_r = attn1_d.rearrange("(t p) n -> t p n", p=128)
    x1_r = x1_ap.rearrange("(t p) n -> t p n", p=128)
    gla_r2 = gla_d.rearrange("(t p) n -> t p n", p=128)
    out_r = out_ap.rearrange("(t p) n -> t p n", p=128)

    for s in range(2):
        with ExitStack() as sst:
            xnfm = sst.enter_context(nc.sbuf_tensor(f"fxn{s}", [128, 8, 2048], BF16))
            for c in range(8):
                p.dma(xnfm[:, c, :], xn1bf_d[s * SEQ:(s + 1) * SEQ, c * 128:(c + 1) * 128],
                      transpose=True, reads=tuple(f"N1d{s * 16 + i}" for i in range(16)),
                      writes=(f"fxn{c}",))
            qkv_st = ExitStack()
            qfm = qkv_st.enter_context(nc.sbuf_tensor(f"fq{s}", [128, 8, 2048], BF16))
            kfm = qkv_st.enter_context(nc.sbuf_tensor(f"fk{s}", [128, 8, 2048], BF16))
            vaug = qkv_st.enter_context(nc.sbuf_tensor(f"fv{s}", [128, 16, 16, 68], BF16))
            with nc.sbuf_tensor(f"fwqkv{s}", [128, 8, 3072], BF16) as wqkv:
                with nc.sbuf_tensor(f"fwqs{s}", [128, 1024], F32) as stg3:
                    Wqkv_r = Wqkv.rearrange("(c p) n -> c p n", p=128)
                    cast_load(nc, p, stg3,
                              [(wqkv[:, c, j * 1024:(j + 1) * 1024],
                                Wqkv_r[c][:, j * 1024:(j + 1) * 1024], 1024)
                               for c in range(8) for j in range(3)], "wqkv", "fwqs")
                p.barrier()
                for dstt, base, kk in ((qfm, 0, "fqc"), (kfm, 1024, "fkc")):
                    for c in range(8):
                        for qb in range(4):
                            sl = slice(qb * 512, (qb + 1) * 512)
                            for d in range(8):
                                p.mm(cx.pbA[0:128, 0:512],
                                     wqkv[:, d, base + c * 128:base + (c + 1) * 128],
                                     xnfm[:, d, sl], d == 0, d == 7,
                                     reads=("wqkv", f"fxn{d}"), writes=("pbA",))
                            bc = (base + c * 128) // 128
                            fn = (lambda eng, o=dstt[:, c, sl], i=cx.pbA[0:128, 0:512],
                                  b=bqc[:, bc:bc + 1]: eng.activation(o, i, AF.Identity,
                                                                      bias=b))
                            p.emit("scalar", fn, reads=("pbA", "bqc"), writes=(kk + str(c),))
                p.memset(vaug[:, :, :, 64:65], 1.0, writes=("fva_ones",))
                for i in range(16):
                    for d in range(8):
                        p.mm(cx.pbB[0:128, 0:512], xnfm[:, d, i * 128:(i + 1) * 128],
                             wqkv[:, d, 2048:2560], d == 0, d == 7,
                             reads=("wqkv", f"fxn{d}"), writes=("pbB",))
                        p.mm(cx.pbB[0:128, 512:1024], xnfm[:, d, i * 128:(i + 1) * 128],
                             wqkv[:, d, 2560:3072], d == 0, d == 7,
                             reads=("wqkv", f"fxn{d}"), writes=("pbB",))
                    for h in range(16):
                        p.stt(vaug[:, i, h, 0:64], cx.pbB[0:128, h * 64:(h + 1) * 64], 1.0,
                              bvb[:, h * 64:(h + 1) * 64], ALU.mult, ALU.add,
                              reads=("pbB", "bvb"), writes=(f"fva{i}",))
            p.barrier()
            with (nc.sbuf_tensor(f"feb{s}", [128, 2, 1024], BF16) as ebuf,
                  nc.sbuf_tensor(f"fpv{s}", [65, 2, 512], BF16) as pvs,
                  nc.sbuf_tensor(f"fac{s}", [128, 4, 1024], F32) as acc,
                  nc.sbuf_tensor(f"fep{s}", [128, 8], F32) as eps_t):
                for qb in range(4):
                    for hp in range(8):
                        for kt in range(16):
                            pb, pk = (cx.pbA, "pbA") if kt % 2 == 0 else (cx.pbB, "pbB")
                            eb, ek = (kt % 2), f"feb{kt % 2}"
                            ksl = slice(kt * 128, (kt + 1) * 128)
                            qsl = slice(qb * 512, (qb + 1) * 512)
                            p.mm(pb[0:128, 0:512], kfm[0:64, hp, ksl], qfm[0:64, hp, qsl],
                                 True, True, reads=(f"fkc{hp}", f"fqc{hp}"), writes=(pk,))
                            p.mm(pb[0:128, 512:1024], kfm[64:128, hp, ksl],
                                 qfm[64:128, hp, qsl],
                                 True, True, reads=(f"fkc{hp}", f"fqc{hp}"), writes=(pk,))
                            p.act(ebuf[:, eb], pb[:], AF.Exp, scale=0.125, reads=(pk,),
                                  writes=(ek,))
                            p.mm(cx.pbC[0:65, 0:512], vaug[:, kt, 2 * hp, 0:65],
                                 ebuf[:, eb, 0:512], kt == 0, kt == 15,
                                 reads=(f"fva{kt}", "fva_ones", ek), writes=("pbC",))
                            p.mm(cx.pbD[0:65, 0:512], vaug[:, kt, 2 * hp + 1, 0:65],
                                 ebuf[:, eb, 512:1024], kt == 0, kt == 15,
                                 reads=(f"fva{kt}", "fva_ones", ek), writes=("pbD",))
                        for hh in range(2):
                            pvp, pvk = (cx.pbC, "pbC") if hh == 0 else (cx.pbD, "pbD")
                            pT, pTk = (cx.pbT, "pbT") if hh == 0 else (cx.pbU, "pbU")
                            n = 2 * hp + hh
                            p.copy(pvs[:, hh], pvp[0:65, 0:512], reads=(pvk,),
                                   writes=(f"fpv{hh}",))
                            for j in range(4):
                                p.trans(pT[0:128, j * 256:j * 256 + 65],
                                        pvs[0:65, hh, j * 128:(j + 1) * 128],
                                        cx.id128[0:65, 0:65],
                                        reads=(f"fpv{hh}", "c_id128"), writes=(pTk,))
                            dcols = pT[0:128, 256 * 0 + 64:256 * 3 + 65:256]
                            p.recip(eps_t[:, 4:8], dcols, reads=(pTk,), writes=("frin",))
                            for j in range(4):
                                p.ts(acc[:, j, n * 64:(n + 1) * 64],
                                     pT[0:128, j * 256:j * 256 + 64], eps_t[:, 4 + j:5 + j],
                                     None,
                                     ALU.mult, reads=(pTk, "frin"), writes=(f"fatt{j}",))
                    for j in range(4):
                        p.dma(at_r[s * 16 + qb * 4 + j], acc[:, j, :], reads=(f"fatt{j}",),
                              writes=(f"at1_d{s * 16 + qb * 4 + j}",))
            p.barrier()
            qkv_st.close()
            comb = sst.enter_context(nc.sbuf_tensor(f"fcb{s}", [128, 16, 8], F32))
            with nc.sbuf_tensor(f"fgt{s}", [128, 6, 8], F32) as gt:
                for i in range(16):
                    for d in range(8):
                        p.mm(cx.pbC[0:128, 0:8], xnfm[:, d, i * 128:(i + 1) * 128],
                             wgate[:, d, :], d == 0, d == 7,
                             reads=(f"fxn{d}", "wgate"), writes=("pbC",))
                    p.tt(gt[:, 0, :], cx.pbC[0:128, 0:8], bgb[:], ALU.add,
                         reads=("pbC", "bgb"), writes=("fg0",))
                    p.red(gt[:, 1, 0:1], gt[:, 0, :], ALU.max, negate=True,
                          reads=("fg0",), writes=("fg1",))
                    fn = (lambda eng, o=gt[:, 2, :], i_=gt[:, 0, :], b=gt[:, 1, 0:1]:
                          eng.activation(o, i_, AF.Exp, bias=b))
                    p.emit("scalar", fn, reads=("fg0", "fg1"), writes=("fg2",))
                    p.red(gt[:, 1, 1:2], gt[:, 2, :], ALU.add, reads=("fg2",),
                          writes=("fg3",))
                    p.recip(gt[:, 1, 1:2], gt[:, 1, 1:2], reads=("fg3",), writes=("fg3",))
                    p.ts(gt[:, 2, :], gt[:, 2, :], gt[:, 1, 1:2], None, ALU.mult,
                         reads=("fg2", "fg3"), writes=("fg2",))
                    p.red(gt[:, 1, 2:3], gt[:, 2, :], ALU.max, reads=("fg2",),
                          writes=("fg4",))
                    p.ts(gt[:, 3, :], gt[:, 2, :], gt[:, 1, 2:3], None, ALU.is_ge,
                         reads=("fg2", "fg4"), writes=("fg5",))
                    p.stt(gt[:, 3, :], gt[:, 3, :], -1e30, gt[:, 2, :], ALU.mult, ALU.add,
                          reads=("fg5", "fg2"), writes=("fg5",))
                    p.red(gt[:, 1, 3:4], gt[:, 3, :], ALU.max, reads=("fg5",),
                          writes=("fg6",))
                    p.ts(gt[:, 4, :], gt[:, 2, :], gt[:, 1, 3:4], None, ALU.is_ge,
                         reads=("fg2", "fg6"), writes=("fg7",))
                    p.tt(comb[:, i, :], gt[:, 2, :], gt[:, 4, :], ALU.mult,
                         reads=("fg2", "fg7"), writes=(f"fcb{i}",))
            p.barrier()
            moe = sst.enter_context(nc.sbuf_tensor(f"fmo{s}", [128, 16, 1024], BF16))
            p.memset(moe[:], 0.0, writes=tuple(f"fmo{i}" for i in range(16)))
            with (nc.sbuf_tensor(f"fwe{s}", [128, 2, 8, 1024], BF16) as wex,
                  nc.sbuf_tensor(f"fwf{s}", [128, 2, 4, 1024], F32) as wf32):
                Wer = We.rearrange("e (c p) n -> e c p n", p=128)
                for e in range(8):
                    wb, wk_ = e % 2, f"fwe{e % 2}"
                    for half in range(2):
                        fk_ = f"fwf{wb}"
                        for c in range(4):
                            p.dma(wf32[:, wb, c, :], Wer[e, half * 4 + c], writes=(fk_,))
                        p.copy(wex[:, wb, half * 4:half * 4 + 4, :], wf32[:, wb],
                               reads=(fk_,), writes=(wk_,))
                    for i in range(16):
                        pb, pk = (cx.pbA, "pbA") if i % 2 == 0 else (cx.pbB, "pbB")
                        for d in range(8):
                            p.mm(pb[0:128, 0:512], xnfm[:, d, i * 128:(i + 1) * 128],
                                 wex[:, wb, d, 0:512], d == 0, False,
                                 reads=(wk_, f"fxn{d}"), writes=(pk,))
                            p.mm(pb[0:128, 512:1024], xnfm[:, d, i * 128:(i + 1) * 128],
                                 wex[:, wb, d, 512:1024], d == 0, False,
                                 reads=(wk_, f"fxn{d}"), writes=(pk,))
                        p.mm(pb[0:128, 0:512], cx.ones_bf_row[:], ber[0:1, e, 0:512],
                             False, True, reads=("ber", "c_ones_bf_row"), writes=(pk,))
                        p.mm(pb[0:128, 512:1024], cx.ones_bf_row[:],
                             ber[0:1, e, 512:1024],
                             False, True, reads=("ber", "c_ones_bf_row"), writes=(pk,))
                        p.stt(moe[:, i, :], pb[0:128, 0:1024], comb[:, i, e:e + 1],
                              moe[:, i, :], ALU.mult, ALU.add,
                              reads=(pk, f"fcb{i}", f"fmo{i}"), writes=(f"fmo{i}",))
            if dev:
                for i in range(16):
                    ti = s * 16 + i
                    p.dma(moe_dbg[ti * 128:(ti + 1) * 128, :], moe[:, i, :], e="gpsimd",
                          reads=(f"fmo{i}",), writes=(f"md{ti}",))
                    p.dma(cmb_dbg[ti * 128:(ti + 1) * 128, :], comb[:, i, :],
                          reads=(f"fcb{i}",), writes=(f"cd{ti}",))
            p.barrier()
            with (nc.sbuf_tensor(f"fo1{s}", [128, 1024], F32) as t1,
                  nc.sbuf_tensor(f"fo2{s}", [128, 1024], F32) as t2,
                  nc.sbuf_tensor(f"fo3{s}", [128, 1024], F32) as t3):
                for i in range(16):
                    ti = s * 16 + i
                    p.dma(t1[:], x1_r[ti], writes=("fo1",))
                    p.dma(t2[:], gla_r2[ti], reads=(f"glad{ti}",), writes=("fo2",))
                    p.dma(t3[:], at_r[ti], reads=(f"at1_d{ti}",), writes=("fo3",))
                    p.tt(t1[:], t1[:], t2[:], ALU.add, reads=("fo1", "fo2"), writes=("fo1",))
                    p.tt(t1[:], t1[:], t3[:], ALU.add, reads=("fo1", "fo3"), writes=("fo1",))
                    p.tt(t1[:], t1[:], moe[:, i, :], ALU.add, reads=("fo1", f"fmo{i}"),
                         writes=("fo1",))
                    p.dma(out_r[ti], t1[:], reads=("fo1",), writes=(f"out_d{ti}",))
            p.barrier()
    f1ws.close()
    p.barrier()


L0_WSHAPES = {
    "L0_attn_Wq": (1024, 1024), "L0_attn_Wk": (1024, 512), "L0_attn_Wv": (1024, 512),
    "L0_attn_memory": (512, 512), "L0_attn_memory_norm": (512,),
    "L0_attn_memory_weight": (1,), "L0_ffn_w_up": (4096, 1024),
    "L0_ffn_w_down": (1024, 4096),
}


def build_nc_l0(dev=False):
    nc = bass.Bass("TRN2", target_bir_lowering=False, debug=False)
    p = Prog(nc)
    cx = Ctx(nc, p)
    x = nc.dram_tensor("x", [4096, 1024], F32, kind="ExternalInput").ap()
    w_aps = {k: nc.dram_tensor(k, list(s), F32, kind="ExternalInput").ap()
             for k, s in L0_WSHAPES.items()}
    x1 = nc.dram_tensor("x1", [4096, 1024], F32, kind="ExternalOutput").ap()
    build_l0(nc, p, cx, x, w_aps, x1, dev=dev)
    p.finalize()
    return nc


def kernel_unfused(**inputs):
    cores = list(range(8))
    x = np.ascontiguousarray(np.asarray(inputs["L0_attn_x"], np.float32))
    shards = [np.ascontiguousarray(x[2 * i:2 * i + 2].reshape(4096, 1024))
              for i in range(8)]

    def wm(shapes):
        return {k: np.ascontiguousarray(np.asarray(inputs[k], np.float32))
                for k in shapes}

    w0, w1a, w1f = wm(L0_WSHAPES), wm(L1A_WSHAPES), wm(L1F_WSHAPES)
    r0 = run_bass_kernel_spmd(build_nc_l0(), [dict(x=s, **w0) for s in shards],
                              core_ids=cores)
    x1s = [r0.results[i]["x1"] for i in range(8)]
    r1 = run_bass_kernel_spmd(build_nc_gla(), [dict(x1=x1s[i], **w1a) for i in range(8)],
                              core_ids=cores)
    r2 = run_bass_kernel_spmd(
        build_nc_ffn1(),
        [dict(x1=x1s[i], gla=r1.results[i]["gla"], **w1f)
         for i in range(8)], core_ids=cores)
    out = np.stack([np.asarray(r2.results[i]["out"]).reshape(2, 2048, 1024)
                    for i in range(8)])
    return out.reshape(16, 2048, 1024).astype(np.float32)


L1A_WSHAPES = {
    "L1_attn_Wq": (1024, 1024), "L1_attn_Wk": (1024, 1024),
    "L1_attn_Wv": (1024, 2048), "L1_attn_Wg": (1024, 2048),
    "L1_attn_gk_w1": (1024, 16), "L1_attn_gk_w2": (16, 1024),
    "L1_attn_gk_b": (1024,), "L1_attn_Wo": (2048, 1024), "L1_attn_rms_w": (512,),
}


def build_nc_gla(dev=False):
    nc = bass.Bass("TRN2", target_bir_lowering=False, debug=False)
    p = Prog(nc)
    cx = Ctx(nc, p)
    x1 = nc.dram_tensor("x1", [4096, 1024], F32, kind="ExternalInput").ap()
    w_aps = {k: nc.dram_tensor(k, list(s), F32, kind="ExternalInput").ap()
             for k, s in L1A_WSHAPES.items()}
    gla = nc.dram_tensor("gla", [4096, 1024], F32, kind="ExternalOutput").ap()
    xn1bf = nc.dram_tensor("xn1bf", [4096, 1024], BF16, kind="Internal").ap()
    build_ln(nc, p, cx, x1, xn1bf, "N1")
    build_gla(nc, p, cx, xn1bf, w_aps, gla, dev=dev)
    p.finalize()
    return nc


L1F_WSHAPES = {
    "L1_ffn_W_qkv": (1024, 3072), "L1_ffn_b_qkv": (3072,),
    "L1_ffn_W_gate": (1024, 8), "L1_ffn_b_gate": (8,),
    "L1_ffn_W_experts": (8, 1024, 1024), "L1_ffn_b_experts": (8, 1024),
}


def build_nc_ffn1(dev=False):
    nc = bass.Bass("TRN2", target_bir_lowering=False, debug=False)
    p = Prog(nc)
    cx = Ctx(nc, p)
    x1 = nc.dram_tensor("x1", [4096, 1024], F32, kind="ExternalInput").ap()
    xn1bf = nc.dram_tensor("xn1bf", [4096, 1024], BF16, kind="Internal").ap()
    gla = nc.dram_tensor("gla", [4096, 1024], F32, kind="ExternalInput").ap()
    w_aps = {k: nc.dram_tensor(k, list(s), F32, kind="ExternalInput").ap()
             for k, s in L1F_WSHAPES.items()}
    out = nc.dram_tensor("out", [4096, 1024], F32, kind="ExternalOutput").ap()
    build_ln(nc, p, cx, x1, xn1bf, "N1")
    build_ffn1(nc, p, cx, x1, xn1bf, gla, w_aps, out, dev=dev)
    p.finalize()
    return nc


ALL_WSHAPES = {}
ALL_WSHAPES.update(L0_WSHAPES)
ALL_WSHAPES.update(L1A_WSHAPES)
ALL_WSHAPES.update(L1F_WSHAPES)


def build_nc_fused(dev=False):
    nc = bass.Bass("TRN2", target_bir_lowering=False, debug=False)
    p = Prog(nc)
    cx = Ctx(nc, p)
    x = nc.dram_tensor("x", [4096, 1024], F32, kind="ExternalInput").ap()
    w_aps = {k: nc.dram_tensor(k, list(s), F32, kind="ExternalInput").ap()
             for k, s in ALL_WSHAPES.items()}
    out = nc.dram_tensor("out", [4096, 1024], F32, kind="ExternalOutput").ap()
    x1 = nc.dram_tensor("x1", [4096, 1024], F32, kind="Internal").ap()
    gla = nc.dram_tensor("gla", [4096, 1024], F32, kind="Internal").ap()
    xn1bf = nc.dram_tensor("xn1bf", [4096, 1024], BF16, kind="Internal").ap()
    build_l0(nc, p, cx, x, w_aps, x1, dev=dev)
    p.barrier()
    build_ln(nc, p, cx, x1, xn1bf, "N1")
    build_gla(nc, p, cx, xn1bf, w_aps, gla, dev=dev)
    p.barrier()
    build_ffn1(nc, p, cx, x1, xn1bf, gla, w_aps, out, dev=dev)
    p.finalize()
    return nc


def kernel(**inputs):
    cores = list(range(8))
    x = np.ascontiguousarray(np.asarray(inputs["L0_attn_x"], np.float32))
    shards = [np.ascontiguousarray(x[2 * i:2 * i + 2].reshape(4096, 1024))
              for i in range(8)]
    wmap = {k: np.ascontiguousarray(np.asarray(inputs[k], np.float32))
            for k in ALL_WSHAPES}
    nc = build_nc_fused()
    r = run_bass_kernel_spmd(nc, [dict(x=s, **wmap) for s in shards], core_ids=cores)
    out = np.stack([np.asarray(r.results[i]["out"]).reshape(2, 2048, 1024)
                    for i in range(8)])
    return out.reshape(16, 2048, 1024)


# version 09
# speedup vs baseline: 1.2709x; 1.1859x over previous
from contextlib import ExitStack
import numpy as np
import ml_dtypes
import concourse.bass as bass
import concourse.mybir as mybir
from concourse.bass_utils import run_bass_kernel_spmd

F32 = mybir.dt.float32
BF16 = mybir.dt.bfloat16
AF = mybir.ActivationFunctionType
ALU = mybir.AluOpType
AX = mybir.AxisListType.X
bfdt = ml_dtypes.bfloat16

ENGINES = ("sync", "tensor", "vector", "scalar", "gpsimd")
DMA_ENG = ("sync", "scalar", "gpsimd")


class Prog:

    NLANES = 8

    def __init__(self, nc):
        self.nc = nc
        self.c_sem = {e: nc.alloc_semaphore(f"c_{e}") for e in ENGINES}
        self.d_sem = {(e, l): nc.alloc_semaphore(f"d_{e}{l}")
                      for e in DMA_ENG for l in range(self.NLANES)}
        self.ccnt = {e: 0 for e in ENGINES}
        self.dcnt = {(e, l): 0 for e in DMA_ENG for l in range(self.NLANES)}
        self.dtot = {e: 0 for e in DMA_ENG}
        self.steps = {e: [] for e in ENGINES}
        self.lw = {}
        self.readers = {}
        self.wm = {e: {} for e in ENGINES}
        self.n_inst = 0

    def _sem_of(self, kind, oe):
        return self.c_sem[oe] if kind == "c" else self.d_sem[oe]

    def _wait(self, e, waits, dep):
        kind, oe, cnt = dep
        if oe == e and kind == "c":
            if e == "tensor":
                return
        got = self.wm[e].get((kind, oe), 0)
        if cnt <= got:
            return
        waits[(kind, oe)] = max(waits.get((kind, oe), 0), cnt)

    def emit(self, e, fn, reads=(), writes=(), dma=False):
        assert e in ENGINES
        if dma:
            assert e in DMA_ENG
        waits = {}
        for k in reads:
            d = self.lw.get(k)
            if d is not None:
                self._wait(e, waits, d)
        for k in writes:
            d = self.lw.get(k)
            if d is not None:
                self._wait(e, waits, d)
            for r in self.readers.get(k, ()):
                self._wait(e, waits, r)
        if dma:
            lane = self.dtot[e] % self.NLANES
            self.dtot[e] += 1
            if self.dcnt[(e, lane)] > 0:
                self._wait(e, waits, ("d", (e, lane), self.dcnt[(e, lane)]))
        wl = []
        for (kind, oe), cnt in waits.items():
            wl.append((self._sem_of(kind, oe), cnt))
            self.wm[e][(kind, oe)] = cnt
        if dma:
            self.dcnt[(e, lane)] += 16
            inc = (self.d_sem[(e, lane)], 16)
            me = ("d", (e, lane), self.dcnt[(e, lane)])
        else:
            self.ccnt[e] += 1
            inc = (self.c_sem[e], 1)
            me = ("c", e, self.ccnt[e])
        self.steps[e].append((fn, wl, inc))
        self.n_inst += 1
        for k in writes:
            self.lw[k] = me
            self.readers[k] = []
        for k in reads:
            self.readers.setdefault(k, []).append(me)

    def barrier(self):
        tgt = [("c", oe, self.ccnt[oe]) for oe in ENGINES if self.ccnt[oe]] + \
              [("d", (oe, l), self.dcnt[(oe, l)]) for oe in DMA_ENG
               for l in range(self.NLANES) if self.dcnt[(oe, l)]]
        for e in ENGINES:
            wl = []
            for kind, oe, cnt in tgt:
                if self.wm[e].get((kind, oe), 0) >= cnt:
                    continue
                wl.append((self._sem_of(kind, oe), cnt))
                self.wm[e][(kind, oe)] = cnt
            if wl:
                self.steps[e].append((None, wl, None))

    def dma(self, dst, src, e="sync", reads=(), writes=(), transpose=False, slow=False):
        if transpose:
            fn = lambda eng, d=dst, s=src: eng.dma_start_transpose(d, s)
        elif slow:
            fn = lambda eng, d=dst, s=src: eng.dma_start(d, s, allow_slow_non_contiguous=True)
        else:
            fn = lambda eng, d=dst, s=src: eng.dma_start(d, s)
        self.emit(e, fn, reads, writes, dma=True)

    def mm(self, out, lhsT, rhs, start, stop, reads=(), writes=()):
        fn = lambda eng, o=out, l=lhsT, r=rhs, st=start, sp=stop: eng.matmul(o, l, r, start=st, stop=sp)
        self.emit("tensor", fn, reads, writes)

    def trans(self, out, in_, ident, reads=(), writes=()):
        fn = lambda eng, o=out, i=in_, d=ident: eng.transpose(o, i, d)
        self.emit("tensor", fn, reads, writes)

    def act(self, out, in_, func, bias=0.0, scale=1.0, accum_out=None, reads=(), writes=()):
        fn = lambda eng, o=out, i=in_, f=func, b=bias, s=scale, a=accum_out: \
            eng.activation(o, i, f, bias=b, scale=s, accum_out=a)
        self.emit("scalar", fn, reads, writes)

    def tt(self, out, a, b, op, e="vector", reads=(), writes=()):
        fn = lambda eng, o=out, x=a, y=b, p=op: eng.tensor_tensor(o, x, y, p)
        self.emit(e, fn, reads, writes)

    def ts(self, out, in_, s1, s2, op0, op1=ALU.bypass, e="vector", reads=(), writes=()):
        fn = lambda eng, o=out, i=in_, a=s1, b=s2, p0=op0, p1=op1: \
            eng.tensor_scalar(o, i, a, b, p0, p1)
        self.emit(e, fn, reads, writes)

    def stt(self, out, in0, scalar, in1, op0, op1, e="vector", reads=(), writes=()):
        fn = lambda eng, o=out, i0=in0, s=scalar, i1=in1, p0=op0, p1=op1: \
            eng.scalar_tensor_tensor(o, i0, s, i1, p0, p1)
        self.emit(e, fn, reads, writes)

    def copy(self, out, in_, e="vector", reads=(), writes=()):
        fn = lambda eng, o=out, i=in_: eng.tensor_copy(o, i)
        self.emit(e, fn, reads, writes)

    def red(self, out, in_, op, negate=False, absval=False, e="vector", reads=(), writes=()):
        fn = lambda eng, o=out, i=in_, p=op, n=negate, a=absval: \
            eng.tensor_reduce(o, i, AX, p, apply_absolute_value=a, negate=n)
        self.emit(e, fn, reads, writes)

    def memset(self, ap, val, e="vector", reads=(), writes=()):
        fn = lambda eng, a=ap, v=val: eng.memset(a, v)
        self.emit(e, fn, reads, writes)

    def recip(self, out, in_, reads=(), writes=()):
        fn = lambda eng, o=out, i=in_: eng.reciprocal(o, i)
        self.emit("vector", fn, reads, writes)

    def scan(self, out, d0, d1, init, op0, op1, reads=(), writes=()):
        fn = lambda eng, o=out, a=d0, b=d1, i=init, p0=op0, p1=op1: \
            eng.tensor_tensor_scan(o, a, b, i, p0, p1)
        self.emit("vector", fn, reads, writes)

    def finalize(self):
        nc = self.nc
        self.barrier()
        with nc.Block() as block:
            for e in ENGINES:
                mine = self.steps[e]
                if not mine and e != "sync":
                    continue

                def body(eng, mine=mine):
                    for fn, wl, inc in mine:
                        for sem, cnt in wl:
                            eng.wait_ge(sem, cnt)
                        if fn is not None:
                            inst = fn(eng)
                            inst.then_inc(inc[0], inc[1])
                getattr(block, e)(body)


class Ctx:
    def __init__(self, nc, p):
        self.nc, self.p = nc, p
        self.id128 = nc.alloc_sbuf_tensor("c_id128", [128, 128], BF16)
        self.ones_bf_row = nc.alloc_sbuf_tensor("c_ones_bf_row", [1, 128], BF16)
        self.ones_f_row = nc.alloc_sbuf_tensor("c_ones_f_row", [1, 128], F32)
        self.ones_bf_col = nc.alloc_sbuf_tensor("c_ones_bf_col", [128, 1], BF16)
        self.ones_f_col = nc.alloc_sbuf_tensor("c_ones_f_col", [128, 1], F32)
        self.mask128 = nc.alloc_sbuf_tensor("c_mask128", [128, 128], BF16)
        for t, arr in ((self.id128, np.eye(128, dtype=bfdt)),
                       (self.mask128, np.triu(np.ones((128, 128), bfdt))),
                       (self.ones_bf_row, np.ones((1, 128), bfdt)),
                       (self.ones_f_row, np.ones((1, 128), np.float32)),
                       (self.ones_bf_col, np.ones((128, 1), bfdt)),
                       (self.ones_f_col, np.ones((128, 1), np.float32))):
            d = nc.inline_tensor(arr, t.name + "_d").ap()
            p.dma(t[:], d, writes=(t.name,))
        self.pbA = nc.alloc_psum_tensor("pbA", [128, 1024], F32)
        self.pbB = nc.alloc_psum_tensor("pbB", [128, 1024], F32)
        self.pbC = nc.alloc_psum_tensor("pbC", [128, 512], F32)
        self.pbD = nc.alloc_psum_tensor("pbD", [128, 512], F32)
        self.pbT = nc.alloc_psum_tensor("pbT", [128, 1024], BF16)
        self.pbU = nc.alloc_psum_tensor("pbU", [128, 1024], BF16)

    def rsqrt(self, p, out, in_, key, eps=0.0, rkeys=()):
        p.act(out, in_, AF.Ln, bias=eps, reads=rkeys, writes=(key,))
        p.act(out, out, AF.Exp, scale=-0.5, reads=(key,), writes=(key,))



def cast_load(nc, p, stage, dst_slices_srcs, key, stage_key):
    for dst, srcap, cols in dst_slices_srcs:
        p.dma(stage[:, 0:cols], srcap, writes=(stage_key,))
        p.copy(dst, stage[:, 0:cols], reads=(stage_key,), writes=(key,))


def build_l0(nc, p, cx, x_ap, w_aps, x1_ap, dbg=None, dev=False):
    NT = 32
    SEQ = 2048
    Wq, Wk, Wv, Mem, Mn, Mw, Wup, Wdown = (
        w_aps["L0_attn_Wq"], w_aps["L0_attn_Wk"], w_aps["L0_attn_Wv"],
        w_aps["L0_attn_memory"], w_aps["L0_attn_memory_norm"],
        w_aps["L0_attn_memory_weight"], w_aps["L0_ffn_w_up"], w_aps["L0_ffn_w_down"])

    l0ws = ExitStack()
    wq_sb = l0ws.enter_context(nc.sbuf_tensor("wq_sb", [128, 8, 1024], BF16))
    wkd_sb = l0ws.enter_context(nc.sbuf_tensor("wkd_sb", [128, 8, 8, 128], BF16))
    wv_sb = l0ws.enter_context(nc.sbuf_tensor("wv_sb", [128, 8, 512], BF16))
    mem_sb = l0ws.enter_context(nc.sbuf_tensor("mem_sb", [128, 4, 512], BF16))
    mn_sb = l0ws.enter_context(nc.sbuf_tensor("mn_sb", [128, 4], BF16))
    with nc.sbuf_tensor("l0stage", [128, 1024], F32) as stg:
        Wq_r = Wq.rearrange("(c p) n -> c p n", p=128)
        cast_load(nc, p, stg, [(wq_sb[:, c, :], Wq_r[c], 1024) for c in range(8)],
                  "wq", "l0stg")
        wk_r = Wk.rearrange("(c p) (h i) -> c p h i", p=128, i=64)
        wk_v = wk_r.rearrange("c p h i -> c p (h i)")
        for c in range(8):
            p.dma(stg[:, 0:512], wk_v[c], writes=("l0stg",))
            sview = stg[:, 0:512].rearrange("p (h i) -> p h i", i=64)
            p.copy(wkd_sb[:, c, :, 0:64], sview, reads=("l0stg",), writes=("wkd0",))
            p.copy(wkd_sb[:, c, :, 64:128], sview, reads=("l0stg",), writes=("wkd1",))
        Wv_r = Wv.rearrange("(c p) n -> c p n", p=128)
        cast_load(nc, p, stg, [(wv_sb[:, c, :], Wv_r[c], 512) for c in range(8)],
                  "wv", "l0stg")
        Mem_r = Mem.rearrange("(c p) n -> c p n", p=128)
        cast_load(nc, p, stg, [(mem_sb[:, c, :], Mem_r[c], 512) for c in range(4)],
                  "mem", "l0stg")
        Mn_r = Mn.rearrange("(c p) -> c p", p=128)
        for c in range(4):
            p.dma(stg[:, 0:1], Mn_r[c].rearrange("(p o) -> p o", o=1), writes=("l0stg",))
            p.copy(mn_sb[:, c:c + 1], stg[:, 0:1], reads=("l0stg",), writes=("mn",))
    p.barrier()

    mw_sb = l0ws.enter_context(nc.sbuf_tensor("mw_sb", [1, 1], F32))
    wrow = l0ws.enter_context(nc.sbuf_tensor("wrow", [1, 2], F32))
    wcols = l0ws.enter_context(nc.sbuf_tensor("wcols", [128, 2], F32))
    p.dma(mw_sb[:], Mw.rearrange("(a b) -> a b", a=1), writes=("mw",))
    p.act(wrow[:, 0:1], mw_sb[:], AF.Exp, scale=-1.0, reads=("mw",), writes=("wr_w",))
    p.ts(wrow[:, 0:1], wrow[:, 0:1], 1.0, None, ALU.add, reads=("wr_w",), writes=("wr_w",))
    p.recip(wrow[:, 0:1], wrow[:, 0:1], reads=("wr_w",), writes=("wr_w",))
    p.ts(wrow[:, 1:2], wrow[:, 0:1], 1.0, -1.0, ALU.subtract, ALU.mult,
         reads=("wr_w",), writes=("wr_w1",))
    p.mm(cx.pbC[0:128, 0:2], cx.ones_f_row[:], wrow[:], True, True,
         reads=("wr_w", "wr_w1", "c_ones_f_row"), writes=("pbC",))
    p.copy(wcols[:], cx.pbC[0:128, 0:2], reads=("pbC",), writes=("wcols",))
    wcol, w1col = wcols[:, 0:1], wcols[:, 1:2]

    KIND = "ExternalOutput" if dev else "Internal"
    wtup = nc.dram_tensor("wtup", [4096, 1024], BF16, kind=KIND).ap()
    wtdown = nc.dram_tensor("wtdown", [1024, 4096], BF16, kind=KIND).ap()

    with (nc.sbuf_tensor("t_w0", [128, 4096], F32) as tw0,
          nc.sbuf_tensor("t_w1", [128, 4096], F32) as tw1,
          nc.sbuf_tensor("t_sg", [128, 4096], F32) as tsg,
          nc.sbuf_tensor("t_ab", [128, 4096], F32) as tab,
          nc.sbuf_tensor("t_wt", [128, 4096], BF16) as twt,
          nc.sbuf_tensor("t_sm", [128, 4], F32) as tsm):

        def ternarize(W, n_rows, n_cols, out_dram, tag):
            nt = n_rows // 128
            wld = (tw0[:, 0:n_cols], tw1[:, 0:n_cols])
            acc, r = tsm[:, 0:1], tsm[:, 1:2]
            thc = tsm[:, 2:3]
            th = tsm[0:1, 3:4]
            p.memset(acc, 0.0, writes=("t_acc",))
            Wr = W.rearrange("(t p) n -> t p n", p=128)
            for i in range(nt):
                b, bk = wld[i % 2], f"twl{i % 2}"
                p.dma(b, Wr[i], writes=(bk,))
                p.red(r, b, ALU.add, absval=True, reads=(bk,), writes=("t_r",))
                p.tt(acc, acc, r, ALU.add, reads=("t_r", "t_acc"), writes=("t_acc",))
            p.mm(cx.pbC[0:1, 0:1], acc, cx.ones_f_col[:], True, True,
                 reads=("t_acc", "c_ones_f_col"), writes=("pbC",))
            p.ts(th, cx.pbC[0:1, 0:1], 0.5 / (n_rows * n_cols), None, ALU.mult,
                 reads=("pbC",), writes=("t_th",))
            p.mm(cx.pbC[0:128, 0:1], cx.ones_f_row[:], th, True, True,
                 reads=("t_th", "c_ones_f_row"), writes=("pbC",))
            p.copy(thc, cx.pbC[0:128, 0:1], reads=("pbC",), writes=("t_thc",))
            for i in range(nt):
                b, bk = wld[i % 2], f"twl{i % 2}"
                p.dma(b, Wr[i], writes=(bk,))
                p.act(tsg[:, 0:n_cols], b, AF.Sign, reads=(bk,), writes=("t_sg",))
                p.act(tab[:, 0:n_cols], b, AF.Abs, reads=(bk,), writes=("t_ab",))
                p.ts(tab[:, 0:n_cols], tab[:, 0:n_cols], thc[:, 0:1], None, ALU.is_gt,
                     reads=("t_ab", "t_thc"), writes=("t_ab",))
                p.tt(twt[:, 0:n_cols], tsg[:, 0:n_cols], tab[:, 0:n_cols], ALU.mult,
                     reads=("t_sg", "t_ab"), writes=("t_wt",))
                p.dma(out_dram[i * 128:(i + 1) * 128, :], twt[:, 0:n_cols],
                      reads=("t_wt",), writes=(f"twd_{tag}{i}",))

        ternarize(Wup, 4096, 1024, wtup, "u")
        ternarize(Wdown, 1024, 4096, wtdown, "d")
        p.barrier()

    xnbf_d = nc.dram_tensor("xnbf", [4096, 1024], BF16, kind=KIND).ap()
    attn_d = nc.dram_tensor("attn0", [4096, 1024], F32, kind=KIND).ap()
    rstd_d = nc.dram_tensor("rstd0", [4096], F32, kind=KIND).ap()
    if dbg is not None:
        dbg["xnbf"] = xnbf_d
        dbg["attn0"] = attn_d

    x_r = x_ap.rearrange("(t p) n -> t p n", p=128)
    x1_r = x1_ap.rearrange("(t p) n -> t p n", p=128)
    xn_r = xnbf_d.rearrange("(t p) n -> t p n", p=128)
    at_r = attn_d.rearrange("(t p) n -> t p n", p=128)

    with (nc.sbuf_tensor("xt", [128, 2, 1024], F32) as xt,
          nc.sbuf_tensor("lnt", [128, 6], F32) as lnt,
          nc.sbuf_tensor("xnt", [128, 2, 1024], BF16) as xnt,
          nc.sbuf_tensor("sqt", [128, 1024], F32) as sqt):
        m = lnt[:, 0:1]; s2 = lnt[:, 1:2]; var = lnt[:, 2:3]
        rstd = lnt[:, 3:4]; rsf = lnt[:, 4:5]; m2 = lnt[:, 5:6]
        for i in range(NT):
            b = i % 2
            xk, nk = f"xt{b}", f"xnt{b}"
            p.dma(xt[:, b], x_r[i], writes=(xk,))
            p.red(m, xt[:, b], ALU.add, reads=(xk,), writes=("ln_m",))
            p.act(sqt[:], xt[:, b], AF.Square, reads=(xk,), writes=("ln_sq",))
            p.red(s2, sqt[:], ALU.add, reads=("ln_sq",), writes=("ln_s2",))
            p.ts(m, m, 1.0 / 1024, None, ALU.mult, reads=("ln_m",), writes=("ln_m",))
            p.ts(s2, s2, 1.0 / 1024, 1e-5, ALU.mult, ALU.add, reads=("ln_s2",), writes=("ln_s2",))
            p.tt(m2, m, m, ALU.mult, reads=("ln_m",), writes=("ln_m2",))
            p.tt(var, s2, m2, ALU.subtract, reads=("ln_s2", "ln_m2"), writes=("ln_v",))
            cx.rsqrt(p, rstd, var, "ln_rstd", rkeys=("ln_v",))
            p.ts(xnt[:, b], xt[:, b], m[:, 0:1], rstd[:, 0:1], ALU.subtract, ALU.mult,
                 reads=(xk, "ln_m", "ln_rstd"), writes=(nk,))
            p.dma(xn_r[i], xnt[:, b], reads=(nk,), writes=(f"xn_d{i}",))
            p.tt(sqt[:], xnt[:, b], xnt[:, b], ALU.mult, reads=(nk,), writes=("ln_sq",))
            p.red(s2, sqt[:], ALU.add, reads=("ln_sq",), writes=("ln_s2",))
            p.ts(s2, s2, 1.0 / 1024, None, ALU.mult, reads=("ln_s2",), writes=("ln_s2",))
            p.act(rsf, s2, AF.Ln, reads=("ln_s2",), writes=("ln_rsf",))
            p.act(rsf, rsf, AF.Exp, scale=0.5, reads=("ln_rsf",), writes=("ln_rsf",))
            p.ts(rsf, rsf, 1e-8, None, ALU.add, reads=("ln_rsf",), writes=("ln_rsf",))
            p.recip(rsf, rsf, reads=("ln_rsf",), writes=("ln_rsf",))
            p.dma(rstd_d[i * 128:(i + 1) * 128].rearrange("(p o) -> p o", o=1), rsf,
                  reads=("ln_rsf",), writes=(f"rs_d{i}",))
    p.barrier()

    for s in range(2):
        T0 = s * 16
        with nc.sbuf_tensor(f"xnfm{s}", [128, 8, 2048], BF16) as xnfm:
            for c in range(8):
                p.dma(xnfm[:, c, :], xnbf_d[s * SEQ:(s + 1) * SEQ, c * 128:(c + 1) * 128],
                      transpose=True, reads=tuple(f"xn_d{T0 + i}" for i in range(16)),
                      writes=(f"xnfm{c}",))
            with (nc.sbuf_tensor(f"kdup{s}", [128, 8, 2048], BF16) as kdup,
                  nc.sbuf_tensor(f"vaug{s}", [128, 16, 8, 68], BF16) as vaug):
                for j in range(8):
                    for qb in range(4):
                        sl = slice(qb * 512, (qb + 1) * 512)
                        for c in range(8):
                            p.mm(cx.pbA[0:128, 0:512], wkd_sb[:, c, j, :], xnfm[:, c, sl],
                                 c == 0, c == 7, reads=("wkd0", "wkd1", f"xnfm{c}"),
                                 writes=("pbA",))
                        p.copy(kdup[:, j, sl], cx.pbA[0:128, 0:512], reads=("pbA",),
                               writes=(f"kd{j}",))
                p.memset(vaug[:, :, :, 64:65], 1.0, writes=("va_ones",))
                for i in range(16):
                    for c in range(8):
                        p.mm(cx.pbB[0:128, 0:512], xnfm[:, c, i * 128:(i + 1) * 128],
                             wv_sb[:, c, :], c == 0, c == 7,
                             reads=(f"xnfm{c}", "wv"), writes=("pbB",))
                    for h in range(8):
                        p.copy(vaug[:, i, h, 0:64], cx.pbB[0:128, h * 64:(h + 1) * 64],
                               reads=("pbB",), writes=(f"va{i}",))

                with (nc.sbuf_tensor(f"qfm{s}", [128, 8, 512], BF16) as qfm,
                      nc.sbuf_tensor(f"sq_bf{s}", [128, 8, 512], BF16) as sq_bf,
                      nc.sbuf_tensor(f"elut{s}", [128, 8, 512], F32) as elut,
                      nc.sbuf_tensor(f"ebuf{s}", [128, 2, 1024], BF16) as ebuf,
                      nc.sbuf_tensor(f"pvs{s}", [65, 2, 512], BF16) as pvs,
                      nc.sbuf_tensor(f"acc{s}", [128, 4, 1024], F32) as acc,
                      nc.sbuf_tensor(f"eps_t{s}", [128, 8], F32) as eps_t):
                    for qb in range(4):
                        qsl = slice(qb * 512, (qb + 1) * 512)
                        for c in range(8):
                            for d in range(8):
                                p.mm(cx.pbA[0:128, 0:512], wq_sb[:, d, c * 128:(c + 1) * 128],
                                     xnfm[:, d, qsl], d == 0, d == 7,
                                     reads=("wq", f"xnfm{d}"), writes=("pbA",))
                            p.copy(qfm[:, c, :], cx.pbA[0:128, 0:512], reads=("pbA",),
                                   writes=(f"qfm{c}",))
                        qk = tuple(f"qfm{c}" for c in range(8))
                        p.ts(elut[:], qfm[:], 0.0, None, ALU.min, reads=qk, writes=("elut",))
                        p.act(elut[:], elut[:], AF.Exp, reads=("elut",), writes=("elut",))
                        p.ts(sq_bf[:], qfm[:], 0.0, -1.0, ALU.max, ALU.add, reads=qk,
                             writes=("sqbf",))
                        p.tt(sq_bf[:], sq_bf[:], elut[:], ALU.add, reads=("sqbf", "elut"),
                             writes=("sqbf",))
                        for g in range(2):
                            for j in range(4):
                                for c in range(4):
                                    cc = 4 * g + c
                                    p.mm(cx.pbA[0:128, 0:512],
                                         sq_bf[:, cc, j * 128:(j + 1) * 128],
                                         mem_sb[:, c, :], c == 0, c == 3,
                                         reads=("sqbf", "mem"), writes=("pbA",))
                                    p.mm(cx.pbC[0:128, 0:1],
                                         sq_bf[:, cc, j * 128:(j + 1) * 128],
                                         mn_sb[:, c:c + 1], c == 0, c == 3,
                                         reads=("sqbf", "mn"), writes=("pbC",))
                                p.recip(eps_t[:, 0:1], cx.pbC[0:128, 0:1], reads=("pbC",),
                                        writes=("rdw",))
                                p.tt(eps_t[:, 0:1], eps_t[:, 0:1], wcol, ALU.mult,
                                     reads=("rdw", "wcols"), writes=("rdw",))
                                p.ts(acc[:, j, g * 512:(g + 1) * 512], cx.pbA[0:128, 0:512],
                                     eps_t[:, 0:1], None, ALU.mult,
                                     reads=("pbA", "rdw"), writes=(f"acc{j}",))
                        for hp in range(8):
                            for kt in range(17):
                                if kt < 16:
                                    pb, pk = (cx.pbA, "pbA") if kt % 2 == 0 else (cx.pbB, "pbB")
                                    eb, ek = (kt % 2), f"eb{kt % 2}"
                                    ksl = slice(kt * 128, (kt + 1) * 128)
                                    p.mm(pb[0:128, 0:512], kdup[0:64, hp, ksl],
                                         qfm[0:64, hp, :],
                                         True, True, reads=(f"kd{hp}", f"qfm{hp}"),
                                         writes=(pk,))
                                    p.mm(pb[0:128, 512:1024], kdup[64:128, hp, ksl],
                                         qfm[64:128, hp, :],
                                         True, True, reads=(f"kd{hp}", f"qfm{hp}"),
                                         writes=(pk,))
                                    p.act(ebuf[:, eb], pb[:], AF.Exp, scale=0.125,
                                          reads=(pk,), writes=(ek,))
                                if kt >= 1:
                                    kv = kt - 1
                                    eb2, ek2 = (kv % 2), f"eb{kv % 2}"
                                    p.mm(cx.pbC[0:65, 0:512], vaug[:, kv, hp, 0:65],
                                         ebuf[:, eb2, 0:512], kv == 0, kv == 15,
                                         reads=(f"va{kv}", "va_ones", ek2), writes=("pbC",))
                                    p.mm(cx.pbD[0:65, 0:512], vaug[:, kv, hp, 0:65],
                                         ebuf[:, eb2, 512:1024], kv == 0, kv == 15,
                                         reads=(f"va{kv}", "va_ones", ek2), writes=("pbD",))
                            for hh in range(2):
                                pvp, pvk = (cx.pbC, "pbC") if hh == 0 else (cx.pbD, "pbD")
                                pT, pTk = (cx.pbT, "pbT") if hh == 0 else (cx.pbU, "pbU")
                                n = 2 * hp + hh
                                p.copy(pvs[:, hh], pvp[0:65, 0:512], reads=(pvk,),
                                       writes=(f"pvs{hh}",))
                                for j in range(4):
                                    p.trans(pT[0:128, j * 256:j * 256 + 65],
                                            pvs[0:65, hh, j * 128:(j + 1) * 128],
                                            cx.id128[0:65, 0:65],
                                            reads=(f"pvs{hh}", "c_id128"), writes=(pTk,))
                                dcols = pT[0:128, 256 * 0 + 64:256 * 3 + 65:256]
                                p.recip(eps_t[:, 4:8], dcols, reads=(pTk,), writes=("rin1",))
                                p.ts(eps_t[:, 4:8], eps_t[:, 4:8], w1col[:, 0:1], None,
                                     ALU.mult, reads=("rin1", "wcols"), writes=("rin1",))
                                for j in range(4):
                                    p.stt(acc[:, j, n * 64:(n + 1) * 64],
                                          pT[0:128, j * 256:j * 256 + 64], eps_t[:, 4 + j:5 + j],
                                          acc[:, j, n * 64:(n + 1) * 64], ALU.mult, ALU.add,
                                          reads=(pTk, "rin1", f"acc{j}"), writes=(f"acc{j}",))
                        for j in range(4):
                            p.dma(at_r[T0 + qb * 4 + j], acc[:, j, :], reads=(f"acc{j}",),
                                  writes=(f"at_d{T0 + qb * 4 + j}",))
                p.barrier()

            with (nc.sbuf_tensor(f"hstore{s}", [128, 32, 512], BF16) as hst,
                  nc.sbuf_tensor(f"xffn{s}", [128, 8, 512], BF16) as xffn,
                  nc.sbuf_tensor(f"wupT{s}", [128, 2, 8, 512], BF16) as wupT,
                  nc.sbuf_tensor(f"wdT{s}", [128, 2, 1024], BF16) as wdT,
                  nc.sbuf_tensor(f"r1row{s}", [1, 512], F32) as r1row,
                  nc.sbuf_tensor(f"r1b{s}", [128, 512], F32) as r1b,
                  nc.sbuf_tensor(f"h2{s}", [128, 2, 512], BF16) as h2t,
                  nc.sbuf_tensor(f"ff{s}", [128, 8], F32) as ff,
                  nc.sbuf_tensor(f"xres{s}", [128, 1024], F32) as xres,
                  nc.sbuf_tensor(f"ares{s}", [128, 1024], F32) as ares,
                  nc.sbuf_tensor(f"x1t{s}", [128, 1024], F32) as x1t):
                for blk in range(4):
                    t0 = s * SEQ + blk * 512
                    bt0 = blk * 512
                    p.dma(r1row[:], rstd_d[t0:t0 + 512].rearrange("(o n) -> o n", o=1),
                          reads=tuple(f"rs_d{t0 // 128 + i}" for i in range(4)),
                          writes=("r1row",))
                    p.mm(cx.pbD[0:128, 0:512], cx.ones_f_row[:], r1row[:], True, True,
                         reads=("r1row", "c_ones_f_row"), writes=("pbD",))
                    p.copy(r1b[:], cx.pbD[0:128, 0:512], reads=("pbD",), writes=("r1b",))
                    for c in range(8):
                        p.tt(xffn[:, c, :], xnfm[:, c, bt0:bt0 + 512], r1b[:], ALU.mult,
                             reads=(f"xnfm{c}", "r1b"), writes=(f"xf{c}",))
                    for fg in range(8):
                        wb, wk_ = fg % 2, f"wupT{fg % 2}"
                        for d in range(8):
                            p.dma(wupT[:, wb, d, :],
                                  wtup[fg * 512:(fg + 1) * 512, d * 128:(d + 1) * 128],
                                  transpose=True, writes=(wk_,))
                        for fs in range(4):
                            fi = fg * 4 + fs
                            pb, pk = (cx.pbA, "pbA") if fi % 2 == 0 else (cx.pbB, "pbB")
                            for d in range(8):
                                p.mm(pb[0:128, 0:512], wupT[:, wb, d, fs * 128:(fs + 1) * 128],
                                     xffn[:, d, :], d == 0, d == 7,
                                     reads=(wk_,) + tuple(f"xf{dd}" for dd in range(8)),
                                     writes=(pk,))
                            p.act(hst[:, fi, :], pb[0:128, 0:512], AF.Gelu,
                                  reads=(pk,), writes=(f"h{fi}",))
                            p.tt(h2t[:, (fi % 2), :], hst[:, fi, :], hst[:, fi, :],
                                 ALU.mult, reads=(f"h{fi}",), writes=(f"h2_{fi % 2}",))
                            if fi >= 1:
                                fj = fi - 1
                                for j in range(4):
                                    p.mm(cx.pbC[0:128, j * 32 + fj:j * 32 + fj + 1],
                                         h2t[:, (fj % 2), j * 128:(j + 1) * 128],
                                         cx.ones_bf_col[:], True, True,
                                         reads=(f"h2_{fj % 2}", "c_ones_bf_col"),
                                         writes=("pbC",))
                        if fg == 7:
                            for j in range(4):
                                p.mm(cx.pbC[0:128, j * 32 + 31:j * 32 + 32],
                                     h2t[:, 1, j * 128:(j + 1) * 128],
                                     cx.ones_bf_col[:], True, True,
                                     reads=("h2_1", "c_ones_bf_col"), writes=("pbC",))
                    hkeys = tuple(f"h{fi}" for fi in range(32))
                    for j in range(4):
                        p.red(ff[:, 4 + j:5 + j], cx.pbC[0:128, j * 32:j * 32 + 32],
                              ALU.add, reads=("pbC",), writes=("ff_ss",))
                    p.ts(ff[:, 0:4], ff[:, 4:8], 1.0 / 4096, None, ALU.mult,
                         reads=("ff_ss",), writes=("ff_r2",))
                    p.act(ff[:, 0:4], ff[:, 0:4], AF.Ln, reads=("ff_r2",), writes=("ff_r2",))
                    p.act(ff[:, 0:4], ff[:, 0:4], AF.Exp, scale=0.5, reads=("ff_r2",),
                          writes=("ff_r2",))
                    p.ts(ff[:, 0:4], ff[:, 0:4], 1e-8, None, ALU.add, reads=("ff_r2",),
                         writes=("ff_r2",))
                    p.recip(ff[:, 0:4], ff[:, 0:4], reads=("ff_r2",), writes=("ff_r2",))
                    for jp in range(2):
                        for fc in range(32):
                            wb2, wdk = fc % 2, f"wdT{fc % 2}"
                            p.dma(wdT[:, wb2, :], wtdown[0:1024, fc * 128:(fc + 1) * 128],
                                  transpose=True, writes=(wdk,))
                            for jj in range(2):
                                j = 2 * jp + jj
                                pb, pk = (cx.pbA, "pbA") if jj == 0 else (cx.pbB, "pbB")
                                p.mm(pb[0:128, 0:512], hst[:, fc, j * 128:(j + 1) * 128],
                                     wdT[:, wb2, 0:512], fc == 0, fc == 31,
                                     reads=(wdk,) + hkeys, writes=(pk,))
                                p.mm(pb[0:128, 512:1024], hst[:, fc, j * 128:(j + 1) * 128],
                                     wdT[:, wb2, 512:1024], fc == 0, fc == 31,
                                     reads=(wdk,) + hkeys, writes=(pk,))
                        for jj in range(2):
                            j = 2 * jp + jj
                            ti = t0 // 128 + j
                            pb, pk = (cx.pbA, "pbA") if jj == 0 else (cx.pbB, "pbB")
                            p.dma(xres[:], x_r[ti], writes=("xres",))
                            p.dma(ares[:], at_r[ti], reads=(f"at_d{ti}",), writes=("ares",))
                            p.tt(x1t[:], xres[:], ares[:], ALU.add, reads=("xres", "ares"),
                                 writes=("x1t",))
                            p.stt(x1t[:], pb[0:128, 0:1024], ff[:, j:j + 1], x1t[:],
                                  ALU.mult, ALU.add, reads=(pk, "ff_r2", "x1t"),
                                  writes=("x1t",))
                            p.dma(x1_r[ti], x1t[:], reads=("x1t",), writes=(f"x1_d{ti}",))
                p.barrier()
    l0ws.close()
    p.barrier()
    return {"x1_r": x1_r}


def build_ln(nc, p, cx, src_ap, dst_ap, tag, src_keys=None):
    src_r = src_ap.rearrange("(t p) n -> t p n", p=128)
    dst_r = dst_ap.rearrange("(t p) n -> t p n", p=128)
    NT = src_r.shape[0]
    with (nc.sbuf_tensor(f"lnx{tag}", [128, 2, 1024], F32) as xt,
          nc.sbuf_tensor(f"lns{tag}", [128, 6], F32) as lnt,
          nc.sbuf_tensor(f"lno{tag}", [128, 2, 1024], BF16) as xnt,
          nc.sbuf_tensor(f"lnq{tag}", [128, 1024], F32) as sqt):
        m = lnt[:, 0:1]; s2 = lnt[:, 1:2]; var = lnt[:, 2:3]
        rstd = lnt[:, 3:4]; m2 = lnt[:, 5:6]
        for i in range(NT):
            b = i % 2
            xk, nk = f"{tag}xt{b}", f"{tag}xnt{b}"
            rk = (src_keys[i],) if src_keys else ()
            p.dma(xt[:, b], src_r[i], reads=rk, writes=(xk,))
            p.red(m, xt[:, b], ALU.add, reads=(xk,), writes=(f"{tag}m",))
            p.act(sqt[:], xt[:, b], AF.Square, reads=(xk,), writes=(f"{tag}sq",))
            p.red(s2, sqt[:], ALU.add, reads=(f"{tag}sq",), writes=(f"{tag}s2",))
            p.ts(m, m, 1.0 / 1024, None, ALU.mult, reads=(f"{tag}m",), writes=(f"{tag}m",))
            p.ts(s2, s2, 1.0 / 1024, 1e-5, ALU.mult, ALU.add, reads=(f"{tag}s2",),
                 writes=(f"{tag}s2",))
            p.tt(m2, m, m, ALU.mult, reads=(f"{tag}m",), writes=(f"{tag}m2",))
            p.tt(var, s2, m2, ALU.subtract, reads=(f"{tag}s2", f"{tag}m2"),
                 writes=(f"{tag}v",))
            cx.rsqrt(p, rstd, var, f"{tag}rstd", rkeys=(f"{tag}v",))
            p.ts(xnt[:, b], xt[:, b], m[:, 0:1], rstd[:, 0:1], ALU.subtract, ALU.mult,
                 reads=(xk, f"{tag}m", f"{tag}rstd"), writes=(nk,))
            p.dma(dst_r[i], xnt[:, b], reads=(nk,), writes=(f"{tag}d{i}",))
    p.barrier()


def build_gla(nc, p, cx, xn1bf_d, w_aps, gla_d, dev=False):
    SEQ = 2048
    HB = 256
    NHB = SEQ // HB
    LWq, LWk, LWv, LWg = (w_aps["L1_attn_Wq"], w_aps["L1_attn_Wk"],
                          w_aps["L1_attn_Wv"], w_aps["L1_attn_Wg"])
    GK1, GK2, GKB = w_aps["L1_attn_gk_w1"], w_aps["L1_attn_gk_w2"], w_aps["L1_attn_gk_b"]
    LWo, RMSW = w_aps["L1_attn_Wo"], w_aps["L1_attn_rms_w"]

    glws = ExitStack()
    lwq = glws.enter_context(nc.sbuf_tensor("lwq", [128, 8, 1024], BF16))
    lwk = glws.enter_context(nc.sbuf_tensor("lwk", [128, 8, 1024], BF16))
    lwv = glws.enter_context(nc.sbuf_tensor("lwv", [128, 8, 2048], BF16))
    lwg = glws.enter_context(nc.sbuf_tensor("lwg", [128, 8, 2048], BF16))
    gk1 = glws.enter_context(nc.sbuf_tensor("gk1", [128, 8, 16], BF16))
    gk2 = glws.enter_context(nc.sbuf_tensor("gk2", [16, 1024], BF16))
    gkb = glws.enter_context(nc.sbuf_tensor("gkb", [128, 8], F32))
    rw = glws.enter_context(nc.sbuf_tensor("rw", [128, 4], F32))
    with nc.sbuf_tensor("glstage", [128, 2048], F32) as stg:
        for nm, dst, W, nch, cols in (("lwq", lwq, LWq, 8, 1024), ("lwk", lwk, LWk, 8, 1024),
                                      ("lwv", lwv, LWv, 8, 2048), ("lwg", lwg, LWg, 8, 2048),
                                      ("gk1", gk1, GK1, 8, 16)):
            Wr = W.rearrange("(c p) n -> c p n", p=128)
            cast_load(nc, p, stg, [(dst[:, c, :], Wr[c], cols) for c in range(nch)],
                      nm, "glstg")
        p.dma(stg[0:16, 0:1024], GK2, writes=("glstg",))
        p.copy(gk2[:], stg[0:16, 0:1024], reads=("glstg",), writes=("gk2",))
        GKB_r = GKB.rearrange("(c p) -> c p", p=128)
        for c in range(8):
            p.dma(gkb[:, c:c + 1], GKB_r[c].rearrange("(p o) -> p o", o=1), writes=("gkb",))
        RW_r = RMSW.rearrange("(c p) -> c p", p=128)
        for c in range(4):
            p.dma(rw[:, c:c + 1], RW_r[c].rearrange("(p o) -> p o", o=1), writes=("rw",))
    p.barrier()

    og_d = nc.dram_tensor("og_d", [2048, 2048], BF16,
                          kind="ExternalOutput" if dev else "Internal").ap()
    gla_r = gla_d.rearrange("(t p) n -> t p n", p=128)

    for s in range(2):
        with (nc.sbuf_tensor(f"gxnfm{s}", [128, 8, 2048], BF16) as xnfm,
              nc.sbuf_tensor(f"gS{s}", [128, 2, 4, 512], F32) as Sst,
              nc.sbuf_tensor(f"gS0b{s}", [128, 2, 2, 512], BF16) as S0b):
            for c in range(8):
                p.dma(xnfm[:, c, :], xn1bf_d[s * SEQ:(s + 1) * SEQ, c * 128:(c + 1) * 128],
                      transpose=True, reads=tuple(f"N1d{s * 16 + i}" for i in range(16)),
                      writes=(f"gxn{c}",))
            p.memset(Sst[:], 0.0, writes=("S",))
            for hb in range(NHB):
                t0 = hb * HB
                hbk = f"{s}_{hb}"
                with ExitStack() as _st:
                    glc = _st.enter_context(nc.sbuf_tensor(f"g_gl{hbk}", [128, 256], F32))
                    Gb = _st.enter_context(nc.sbuf_tensor(f"g_G{hbk}", [128, 8, 256], F32))
                    tmp = _st.enter_context(nc.sbuf_tensor(f"g_tmp{hbk}", [128, 2, 256], F32))

                    zb = _st.enter_context(nc.sbuf_tensor(f"g_z{hbk}", [128, 128], F32))
                    qt = _st.enter_context(nc.sbuf_tensor(f"g_qt{hbk}", [128, 8, 256], BF16))
                    ktl = _st.enter_context(nc.sbuf_tensor(f"g_kt{hbk}", [128, 8, 256], BF16))
                    khat = _st.enter_context(nc.sbuf_tensor(f"g_kh{hbk}", [128, 8, 256], BF16))
                    vb = _st.enter_context(nc.sbuf_tensor(f"g_v{hbk}", [128, 2, 2048], BF16))
                    gateb = _st.enter_context(nc.sbuf_tensor(f"g_gate{hbk}", [128, 16, 256], BF16))
                    ogb = _st.enter_context(nc.sbuf_tensor(f"g_og{hbk}", [128, 16, 256], BF16))
                    Asb = _st.enter_context(nc.sbuf_tensor(f"g_A{hbk}", [128, 4, 128], BF16))
                    kTb = _st.enter_context(nc.sbuf_tensor(f"g_kT{hbk}", [128, 2, 128], BF16))
                    sqb = _st.enter_context(nc.sbuf_tensor(f"g_sq{hbk}", [128, 1024], BF16))
                    rrow = _st.enter_context(nc.sbuf_tensor(f"g_rr{hbk}", [1, 512], F32))
                    rsb = _st.enter_context(nc.sbuf_tensor(f"g_rs{hbk}", [128, 4], F32))
                    rstdb = _st.enter_context(nc.sbuf_tensor(f"g_rb{hbk}", [128, 512], F32))
                    rlow = _st.enter_context(nc.sbuf_tensor(f"g_r{hbk}", [16, 256], BF16))
                    p.memset(zb[:], 0.0, writes=("z",))
                    tsl = slice(t0, t0 + 256)
                    for d in range(8):
                        p.mm(cx.pbC[0:16, 0:256], gk1[:, d, :], xnfm[:, d, tsl],
                             d == 0, d == 7, reads=("gk1", f"gxn{d}"), writes=("pbC",))
                    p.copy(rlow[:], cx.pbC[0:16, 0:256], reads=("pbC",), writes=("rlow",))
                    for c in range(8):
                        p.mm(cx.pbD[0:128, 0:256], gk2[:, c * 128:(c + 1) * 128], rlow[:],
                             True, True, reads=("gk2", "rlow"), writes=("pbD",))
                        fn = (lambda eng, o=glc[:], i=cx.pbD[0:128, 0:256],
                              b=gkb[:, c:c + 1]: eng.activation(o, i, AF.Identity, bias=b))
                        p.emit("scalar", fn, reads=("pbD", "gkb"), writes=("glc",))
                        p.act(tmp[:, 0, :], glc[:], AF.Abs, reads=("glc",),
                              writes=("tmp0",))
                        p.act(tmp[:, 0, :], tmp[:, 0, :], AF.Exp, scale=-1.0,
                              reads=("tmp0",), writes=("tmp0",))
                        p.act(tmp[:, 0, :], tmp[:, 0, :], AF.Ln, bias=1.0,
                              reads=("tmp0",), writes=("tmp0",))
                        p.ts(tmp[:, 1, :], glc[:], 0.0, None, ALU.min,
                             reads=("glc",), writes=("tmp1",))
                        p.tt(glc[:], tmp[:, 1, :], tmp[:, 0, :], ALU.subtract,
                             reads=("tmp1", "tmp0"), writes=("glc",))
                        for ci in range(2):
                            p.scan(Gb[:, c, ci * 128:(ci + 1) * 128],
                                   glc[:, ci * 128:(ci + 1) * 128], zb[:], 0.0,
                                   ALU.add, ALU.add, reads=("glc", "z"),
                                   writes=(f"G{c}",))
                    for c in range(8):
                        p.act(glc[:], Gb[:, c, :], AF.Exp, scale=1.0 / 16,
                              reads=(f"G{c}",), writes=("glc",))
                        for d in range(8):
                            p.mm(cx.pbA[0:128, 0:256], lwq[:, d, c * 128:(c + 1) * 128],
                                 xnfm[:, d, tsl], d == 0, d == 7,
                                 reads=("lwq", f"gxn{d}"), writes=("pbA",))
                        p.stt(qt[:, c, :], cx.pbA[0:128, 0:256], 0.0625, glc[:],
                              ALU.mult, ALU.mult, reads=("pbA", "glc"), writes=(f"qt{c}",))
                        p.act(glc[:], Gb[:, c, :], AF.Exp, scale=-1.0 / 16,
                              reads=(f"G{c}",), writes=("glc",))
                        for d in range(8):
                            p.mm(cx.pbB[0:128, 0:256], lwk[:, d, c * 128:(c + 1) * 128],
                                 xnfm[:, d, tsl], d == 0, d == 7,
                                 reads=("lwk", f"gxn{d}"), writes=("pbB",))
                        p.tt(ktl[:, c, :], cx.pbB[0:128, 0:256], glc[:], ALU.mult,
                             reads=("pbB", "glc"), writes=(f"kt{c}",))
                        for ci in range(2):
                            p.act(rsb[:, 0:1], Gb[:, c, ci * 128 + 127:ci * 128 + 128],
                                  AF.Exp, scale=1.0 / 16, reads=(f"G{c}",), writes=("E4",))
                            p.ts(khat[:, c, ci * 128:(ci + 1) * 128],
                                 ktl[:, c, ci * 128:(ci + 1) * 128], rsb[:, 0:1], None,
                                 ALU.mult, reads=(f"kt{c}", "E4"), writes=(f"kh{c}",))
                    for tt in range(2):
                        for vc in range(4):
                            for d in range(8):
                                p.mm(cx.pbA[0:128, 0:512],
                                     xnfm[:, d, t0 + tt * 128:t0 + (tt + 1) * 128],
                                     lwv[:, d, vc * 512:(vc + 1) * 512], d == 0, d == 7,
                                     reads=("lwv", f"gxn{d}"), writes=("pbA",))
                            p.copy(vb[:, tt, vc * 512:(vc + 1) * 512],
                                   cx.pbA[0:128, 0:512], reads=("pbA",), writes=(f"v{tt}",))
                    for vc in range(16):
                        for d in range(8):
                            p.mm(cx.pbB[0:128, 0:256], lwg[:, d, vc * 128:(vc + 1) * 128],
                                 xnfm[:, d, tsl], d == 0, d == 7,
                                 reads=("lwg", f"gxn{d}"), writes=("pbB",))
                        p.act(glc[:], cx.pbB[0:128, 0:256], AF.Exp, scale=-1.0,
                              reads=("pbB",), writes=("glc",))
                        p.ts(glc[:], glc[:], 1.0, None, ALU.add, reads=("glc",),
                             writes=("glc",))
                        p.recip(glc[:], glc[:], reads=("glc",), writes=("glc",))
                        p.tt(gateb[:, vc, :], cx.pbB[0:128, 0:256], glc[:], ALU.mult,
                             reads=("pbB", "glc"), writes=(f"gate{vc}",))
                    for ci in range(2):
                        csl = slice(ci * 128, (ci + 1) * 128)
                        p.copy(S0b[:], Sst[:, :, 0:2, :], reads=("S",), writes=("S0b",))
                        for h in range(4):
                            cc0, cc1 = 2 * h, 2 * h + 1
                            asl = slice(128 * h, 128 * (h + 1))
                            p.mm(cx.pbC[0:128, asl], ktl[:, cc0, csl], qt[:, cc0, csl],
                                 True, False, reads=(f"kt{cc0}", f"qt{cc0}"),
                                 writes=("pbC",))
                            p.mm(cx.pbC[0:128, asl], ktl[:, cc1, csl], qt[:, cc1, csl],
                                 False, True, reads=(f"kt{cc1}", f"qt{cc1}"),
                                 writes=("pbC",))
                            p.tt(Asb[:, h, :], cx.pbC[0:128, asl], cx.mask128[:], ALU.mult,
                                 reads=("pbC", "c_mask128"), writes=(f"Asb{h}",))
                        for h in range(4):
                            if h == 2:
                                p.copy(S0b[:], Sst[:, :, 2:4, :], reads=("S",),
                                       writes=("S0b",))
                            s0k = "S0b"
                            cc0, cc1 = 2 * h, 2 * h + 1
                            pb, pk = (cx.pbA, "pbA") if h < 2 else (cx.pbB, "pbB")
                            off = 512 * (h % 2)
                            for j in range(4):
                                jsl = slice(512 * h + 128 * j, 512 * h + 128 * (j + 1))
                                osl = slice(off + 128 * j, off + 128 * (j + 1))
                                p.mm(pb[0:128, osl], vb[:, ci, jsl], Asb[:, h, :],
                                     True, False, reads=(f"v{ci}", f"Asb{h}"), writes=(pk,))
                                p.mm(pb[0:128, osl],
                                     S0b[:, 0, h % 2, 128 * j:128 * (j + 1)], qt[:, cc0, csl],
                                     False, False, reads=(s0k, f"qt{cc0}"), writes=(pk,))
                                p.mm(pb[0:128, osl],
                                     S0b[:, 1, h % 2, 128 * j:128 * (j + 1)], qt[:, cc1, csl],
                                     False, True, reads=(s0k, f"qt{cc1}"), writes=(pk,))
                        for half in range(2):
                            pbh, pkh = (cx.pbA, "pbA") if half == 0 else (cx.pbB, "pbB")
                            p.act(sqb[:], pbh[0:128, 0:1024], AF.Square,
                                  reads=(pkh,), writes=(f"osq{half}",))
                            for hh in range(2):
                                h = 2 * half + hh
                                for j in range(4):
                                    p.mm(cx.pbD[0:1, 128 * h:128 * (h + 1)],
                                         cx.ones_bf_col[:],
                                         sqb[:, 512 * hh + 128 * j:512 * hh + 128 * (j + 1)],
                                         j == 0, j == 3,
                                         reads=(f"osq{half}", "c_ones_bf_col"),
                                         writes=("pbD",))
                        p.copy(rrow[:], cx.pbD[0:1, 0:512], reads=("pbD",), writes=("rrow",))
                        p.ts(rrow[:], rrow[:], 1.0 / 512, 1e-5, ALU.mult, ALU.add,
                             reads=("rrow",), writes=("rrow",))
                        p.act(rrow[:], rrow[:], AF.Ln, reads=("rrow",), writes=("rrow",))
                        p.act(rrow[:], rrow[:], AF.Exp, scale=-0.5, reads=("rrow",),
                              writes=("rrow",))
                        p.mm(cx.pbD[0:128, 0:512], cx.ones_f_row[:], rrow[:],
                             True, True, reads=("rrow", "c_ones_f_row"), writes=("pbD",))
                        p.copy(rstdb[:], cx.pbD[0:128, 0:512], reads=("pbD",),
                               writes=("orb",))
                        for h in range(4):
                            pb, pk = (cx.pbA, "pbA") if h < 2 else (cx.pbB, "pbB")
                            off = 512 * (h % 2)
                            for j in range(4):
                                p.stt(ogb[:, 4 * h + j, csl],
                                      pb[0:128, off + 128 * j:off + 128 * (j + 1)],
                                      rw[:, j:j + 1],
                                      rstdb[:, 128 * h:128 * (h + 1)], ALU.mult, ALU.mult,
                                      reads=(pk, "rw", "orb"), writes=(f"og{4 * h + j}",))
                                p.tt(ogb[:, 4 * h + j, csl], ogb[:, 4 * h + j, csl],
                                     gateb[:, 4 * h + j, csl], ALU.mult,
                                     reads=(f"og{4 * h + j}", f"gate{4 * h + j}"),
                                     writes=(f"og{4 * h + j}",))
                        for h in range(4):
                            cc0, cc1 = 2 * h, 2 * h + 1
                            pT, pTk = (cx.pbT, "pbT") if h % 2 == 0 else (cx.pbU, "pbU")
                            for i, cc in enumerate((cc0, cc1)):
                                p.trans(pT[0:128, 512 * i:512 * i + 128], khat[:, cc, csl],
                                        cx.id128[:], reads=(f"kh{cc}", "c_id128"),
                                        writes=(pTk,))
                            p.copy(kTb[:, 0, :], pT[0:128, 0:128], reads=(pTk,),
                                   writes=("kT0",))
                            p.copy(kTb[:, 1, :], pT[0:128, 512:640], reads=(pTk,),
                                   writes=("kT1",))
                            pb, pk = (cx.pbA, "pbA") if h < 2 else (cx.pbB, "pbB")
                            off = 512 * (h % 2)
                            for i, cc in enumerate((cc0, cc1)):
                                psl = slice(off * 0 + 512 * i, 512 * (i + 1))
                                p.mm(pb[0:128, 512 * i:512 * (i + 1)], kTb[:, i, :],
                                     vb[:, ci, 512 * h:512 * (h + 1)], True, True,
                                     reads=(f"kT{i}", f"v{ci}"), writes=(pk,))
                                p.act(rsb[:, 1:2], Gb[:, cc, ci * 128 + 127:ci * 128 + 128],
                                      AF.Exp, scale=1.0 / 16, reads=(f"G{cc}",),
                                      writes=("E4b",))
                                p.stt(Sst[:, i, h, :], Sst[:, i, h, :], rsb[:, 1:2],
                                      pb[0:128, 512 * i:512 * (i + 1)], ALU.mult,
                                      ALU.add, reads=("S", "E4b", pk), writes=("S",))
                    for vc in range(16):
                        p.dma(og_d[vc * 128:(vc + 1) * 128, t0:t0 + 256], ogb[:, vc, :],
                              reads=(f"og{vc}",), writes=(f"ogd{s}_{hb}_{vc}",))
            p.barrier()
            with (nc.sbuf_tensor(f"lwo{s}", [128, 16, 1024], BF16) as lwo,
                  nc.sbuf_tensor(f"ogs{s}", [128, 2, 16, 128], BF16) as ogs,
                  nc.sbuf_tensor(f"glo{s}", [128, 2, 1024], F32) as glo):
                with nc.sbuf_tensor(f"lwostg{s}", [128, 1024], F32) as stg2:
                    LWo_r = LWo.rearrange("(c p) n -> c p n", p=128)
                    cast_load(nc, p, stg2, [(lwo[:, c, :], LWo_r[c], 1024)
                                            for c in range(16)], "lwo", "lwostg")
                p.barrier()
                ogr = og_d.rearrange("(vc p) t -> p vc t", p=128)
                for ti in range(16):
                    b = ti % 2
                    p.dma(ogs[:, b], ogr[:, :, ti * 128:(ti + 1) * 128],
                          reads=tuple(f"ogd{s}_{ti // 2}_{vc}" for vc in range(16)),
                          writes=(f"ogs{b}",))
                    for vc in range(16):
                        p.mm(cx.pbA[0:128, 0:512], ogs[:, b, vc, :],
                             lwo[:, vc, 0:512], vc == 0, vc == 15,
                             reads=(f"ogs{b}", "lwo"), writes=("pbA",))
                        p.mm(cx.pbA[0:128, 512:1024], ogs[:, b, vc, :],
                             lwo[:, vc, 512:1024], vc == 0, vc == 15,
                             reads=(f"ogs{b}", "lwo"), writes=("pbA",))
                    p.copy(glo[:, b], cx.pbA[0:128, 0:1024], reads=("pbA",),
                           writes=(f"glo{b}",))
                    p.dma(gla_r[s * 16 + ti], glo[:, b], reads=(f"glo{b}",),
                          writes=(f"glad{s * 16 + ti}",))
            p.barrier()
    glws.close()
    p.barrier()
    return og_d


def build_ffn1(nc, p, cx, x1_ap, xn1bf_d, gla_d, w_aps, out_ap, dev=False):
    SEQ = 2048
    Wqkv, Bqkv = w_aps["L1_ffn_W_qkv"], w_aps["L1_ffn_b_qkv"]
    Wg, Bg = w_aps["L1_ffn_W_gate"], w_aps["L1_ffn_b_gate"]
    We, Be = w_aps["L1_ffn_W_experts"], w_aps["L1_ffn_b_experts"]

    f1ws = ExitStack()
    wgate = f1ws.enter_context(nc.sbuf_tensor("wgate", [128, 8, 8], BF16))
    bgr = f1ws.enter_context(nc.sbuf_tensor("bgr", [1, 8], F32))
    bgb = f1ws.enter_context(nc.sbuf_tensor("bgb", [128, 8], F32))
    bqc = f1ws.enter_context(nc.sbuf_tensor("bqc", [128, 16], F32))
    bvr = f1ws.enter_context(nc.sbuf_tensor("bvr", [1, 1024], F32))
    bvb = f1ws.enter_context(nc.sbuf_tensor("bvb", [128, 1024], F32))
    ber = f1ws.enter_context(nc.sbuf_tensor("ber", [1, 8, 1024], BF16))
    with nc.sbuf_tensor("f1stage", [128, 1024], F32) as stg:
        Wg_r = Wg.rearrange("(c p) n -> c p n", p=128)
        cast_load(nc, p, stg, [(wgate[:, c, :], Wg_r[c], 8) for c in range(8)],
                  "wgate", "f1stg")
        p.dma(bgr[:], Bg.rearrange("(o n) -> o n", o=1), writes=("bgr",))
        Bq_r = Bqkv[0:2048].rearrange("(c p) -> c p", p=128)
        for c in range(16):
            p.dma(bqc[:, c:c + 1], Bq_r[c].rearrange("(p o) -> p o", o=1), writes=("bqc",))
        p.dma(bvr[:], Bqkv[2048:3072].rearrange("(o n) -> o n", o=1), writes=("bvr",))
        with nc.sbuf_tensor("berf", [1, 8, 1024], F32) as berf:
            p.dma(berf[:], Be.rearrange("(o e) n -> o e n", o=1), writes=("berf",))
            p.copy(ber[:], berf[:], reads=("berf",), writes=("ber",))
    p.barrier()
    p.mm(cx.pbC[0:128, 0:8], cx.ones_f_row[:], bgr[:], True, True,
         reads=("bgr", "c_ones_f_row"), writes=("pbC",))
    p.copy(bgb[:], cx.pbC[0:128, 0:8], reads=("pbC",), writes=("bgb",))
    p.mm(cx.pbA[0:128, 0:512], cx.ones_f_row[:], bvr[:, 0:512], True, True,
         reads=("bvr", "c_ones_f_row"), writes=("pbA",))
    p.copy(bvb[:, 0:512], cx.pbA[0:128, 0:512], reads=("pbA",), writes=("bvb",))
    p.mm(cx.pbA[0:128, 0:512], cx.ones_f_row[:], bvr[:, 512:1024], True, True,
         reads=("bvr", "c_ones_f_row"), writes=("pbA",))
    p.copy(bvb[:, 512:1024], cx.pbA[0:128, 0:512], reads=("pbA",), writes=("bvb",))

    attn1_d = nc.dram_tensor("attn1_d", [4096, 1024], F32,
                             kind="ExternalOutput" if dev else "Internal").ap()
    if dev:
        moe_dbg = nc.dram_tensor("moe_dbg", [4096, 1024], F32, kind="ExternalOutput").ap()
        cmb_dbg = nc.dram_tensor("cmb_dbg", [4096, 8], F32, kind="ExternalOutput").ap()
    at_r = attn1_d.rearrange("(t p) n -> t p n", p=128)
    x1_r = x1_ap.rearrange("(t p) n -> t p n", p=128)
    gla_r2 = gla_d.rearrange("(t p) n -> t p n", p=128)
    out_r = out_ap.rearrange("(t p) n -> t p n", p=128)

    for s in range(2):
        with ExitStack() as sst:
            xnfm = sst.enter_context(nc.sbuf_tensor(f"fxn{s}", [128, 8, 2048], BF16))
            for c in range(8):
                p.dma(xnfm[:, c, :], xn1bf_d[s * SEQ:(s + 1) * SEQ, c * 128:(c + 1) * 128],
                      transpose=True, reads=tuple(f"N1d{s * 16 + i}" for i in range(16)),
                      writes=(f"fxn{c}",))
            qkv_st = ExitStack()
            qfm = qkv_st.enter_context(nc.sbuf_tensor(f"fq{s}", [128, 8, 2048], BF16))
            kfm = qkv_st.enter_context(nc.sbuf_tensor(f"fk{s}", [128, 8, 2048], BF16))
            vaug = qkv_st.enter_context(nc.sbuf_tensor(f"fv{s}", [128, 16, 16, 68], BF16))
            with nc.sbuf_tensor(f"fwqkv{s}", [128, 8, 3072], BF16) as wqkv:
                with nc.sbuf_tensor(f"fwqs{s}", [128, 1024], F32) as stg3:
                    Wqkv_r = Wqkv.rearrange("(c p) n -> c p n", p=128)
                    cast_load(nc, p, stg3,
                              [(wqkv[:, c, j * 1024:(j + 1) * 1024],
                                Wqkv_r[c][:, j * 1024:(j + 1) * 1024], 1024)
                               for c in range(8) for j in range(3)], "wqkv", "fwqs")
                p.barrier()
                for dstt, base, kk in ((qfm, 0, "fqc"), (kfm, 1024, "fkc")):
                    for c in range(8):
                        for qb in range(4):
                            sl = slice(qb * 512, (qb + 1) * 512)
                            for d in range(8):
                                p.mm(cx.pbA[0:128, 0:512],
                                     wqkv[:, d, base + c * 128:base + (c + 1) * 128],
                                     xnfm[:, d, sl], d == 0, d == 7,
                                     reads=("wqkv", f"fxn{d}"), writes=("pbA",))
                            bc = (base + c * 128) // 128
                            fn = (lambda eng, o=dstt[:, c, sl], i=cx.pbA[0:128, 0:512],
                                  b=bqc[:, bc:bc + 1]: eng.activation(o, i, AF.Identity,
                                                                      bias=b))
                            p.emit("scalar", fn, reads=("pbA", "bqc"), writes=(kk + str(c),))
                p.memset(vaug[:, :, :, 64:65], 1.0, writes=("fva_ones",))
                for i in range(16):
                    for d in range(8):
                        p.mm(cx.pbB[0:128, 0:512], xnfm[:, d, i * 128:(i + 1) * 128],
                             wqkv[:, d, 2048:2560], d == 0, d == 7,
                             reads=("wqkv", f"fxn{d}"), writes=("pbB",))
                        p.mm(cx.pbB[0:128, 512:1024], xnfm[:, d, i * 128:(i + 1) * 128],
                             wqkv[:, d, 2560:3072], d == 0, d == 7,
                             reads=("wqkv", f"fxn{d}"), writes=("pbB",))
                    for h in range(16):
                        p.stt(vaug[:, i, h, 0:64], cx.pbB[0:128, h * 64:(h + 1) * 64], 1.0,
                              bvb[:, h * 64:(h + 1) * 64], ALU.mult, ALU.add,
                              reads=("pbB", "bvb"), writes=(f"fva{i}",))
            p.barrier()
            with (nc.sbuf_tensor(f"feb{s}", [128, 2, 1024], BF16) as ebuf,
                  nc.sbuf_tensor(f"fpv{s}", [65, 2, 512], BF16) as pvs,
                  nc.sbuf_tensor(f"fac{s}", [128, 4, 1024], F32) as acc,
                  nc.sbuf_tensor(f"fep{s}", [128, 8], F32) as eps_t):
                for qb in range(4):
                    for hp in range(8):
                        qsl = slice(qb * 512, (qb + 1) * 512)
                        for kt in range(17):
                            if kt < 16:
                                pb, pk = (cx.pbA, "pbA") if kt % 2 == 0 else (cx.pbB, "pbB")
                                eb, ek = (kt % 2), f"feb{kt % 2}"
                                ksl = slice(kt * 128, (kt + 1) * 128)
                                p.mm(pb[0:128, 0:512], kfm[0:64, hp, ksl],
                                     qfm[0:64, hp, qsl],
                                     True, True, reads=(f"fkc{hp}", f"fqc{hp}"),
                                     writes=(pk,))
                                p.mm(pb[0:128, 512:1024], kfm[64:128, hp, ksl],
                                     qfm[64:128, hp, qsl],
                                     True, True, reads=(f"fkc{hp}", f"fqc{hp}"),
                                     writes=(pk,))
                                p.act(ebuf[:, eb], pb[:], AF.Exp, scale=0.125, reads=(pk,),
                                      writes=(ek,))
                            if kt >= 1:
                                kv = kt - 1
                                eb2, ek2 = (kv % 2), f"feb{kv % 2}"
                                p.mm(cx.pbC[0:65, 0:512], vaug[:, kv, 2 * hp, 0:65],
                                     ebuf[:, eb2, 0:512], kv == 0, kv == 15,
                                     reads=(f"fva{kv}", "fva_ones", ek2), writes=("pbC",))
                                p.mm(cx.pbD[0:65, 0:512], vaug[:, kv, 2 * hp + 1, 0:65],
                                     ebuf[:, eb2, 512:1024], kv == 0, kv == 15,
                                     reads=(f"fva{kv}", "fva_ones", ek2), writes=("pbD",))
                        for hh in range(2):
                            pvp, pvk = (cx.pbC, "pbC") if hh == 0 else (cx.pbD, "pbD")
                            pT, pTk = (cx.pbT, "pbT") if hh == 0 else (cx.pbU, "pbU")
                            n = 2 * hp + hh
                            p.copy(pvs[:, hh], pvp[0:65, 0:512], reads=(pvk,),
                                   writes=(f"fpv{hh}",))
                            for j in range(4):
                                p.trans(pT[0:128, j * 256:j * 256 + 65],
                                        pvs[0:65, hh, j * 128:(j + 1) * 128],
                                        cx.id128[0:65, 0:65],
                                        reads=(f"fpv{hh}", "c_id128"), writes=(pTk,))
                            dcols = pT[0:128, 256 * 0 + 64:256 * 3 + 65:256]
                            p.recip(eps_t[:, 4:8], dcols, reads=(pTk,), writes=("frin",))
                            for j in range(4):
                                p.ts(acc[:, j, n * 64:(n + 1) * 64],
                                     pT[0:128, j * 256:j * 256 + 64], eps_t[:, 4 + j:5 + j],
                                     None,
                                     ALU.mult, reads=(pTk, "frin"), writes=(f"fatt{j}",))
                    for j in range(4):
                        p.dma(at_r[s * 16 + qb * 4 + j], acc[:, j, :], reads=(f"fatt{j}",),
                              writes=(f"at1_d{s * 16 + qb * 4 + j}",))
            p.barrier()
            qkv_st.close()
            comb = sst.enter_context(nc.sbuf_tensor(f"fcb{s}", [128, 16, 8], F32))
            with nc.sbuf_tensor(f"fgt{s}", [128, 6, 8], F32) as gt:
                for i in range(16):
                    for d in range(8):
                        p.mm(cx.pbC[0:128, 0:8], xnfm[:, d, i * 128:(i + 1) * 128],
                             wgate[:, d, :], d == 0, d == 7,
                             reads=(f"fxn{d}", "wgate"), writes=("pbC",))
                    p.tt(gt[:, 0, :], cx.pbC[0:128, 0:8], bgb[:], ALU.add,
                         reads=("pbC", "bgb"), writes=("fg0",))
                    p.red(gt[:, 1, 0:1], gt[:, 0, :], ALU.max, negate=True,
                          reads=("fg0",), writes=("fg1",))
                    fn = (lambda eng, o=gt[:, 2, :], i_=gt[:, 0, :], b=gt[:, 1, 0:1]:
                          eng.activation(o, i_, AF.Exp, bias=b))
                    p.emit("scalar", fn, reads=("fg0", "fg1"), writes=("fg2",))
                    p.red(gt[:, 1, 1:2], gt[:, 2, :], ALU.add, reads=("fg2",),
                          writes=("fg3",))
                    p.recip(gt[:, 1, 1:2], gt[:, 1, 1:2], reads=("fg3",), writes=("fg3",))
                    p.ts(gt[:, 2, :], gt[:, 2, :], gt[:, 1, 1:2], None, ALU.mult,
                         reads=("fg2", "fg3"), writes=("fg2",))
                    p.red(gt[:, 1, 2:3], gt[:, 2, :], ALU.max, reads=("fg2",),
                          writes=("fg4",))
                    p.ts(gt[:, 3, :], gt[:, 2, :], gt[:, 1, 2:3], None, ALU.is_ge,
                         reads=("fg2", "fg4"), writes=("fg5",))
                    p.stt(gt[:, 3, :], gt[:, 3, :], -1e30, gt[:, 2, :], ALU.mult, ALU.add,
                          reads=("fg5", "fg2"), writes=("fg5",))
                    p.red(gt[:, 1, 3:4], gt[:, 3, :], ALU.max, reads=("fg5",),
                          writes=("fg6",))
                    p.ts(gt[:, 4, :], gt[:, 2, :], gt[:, 1, 3:4], None, ALU.is_ge,
                         reads=("fg2", "fg6"), writes=("fg7",))
                    p.tt(comb[:, i, :], gt[:, 2, :], gt[:, 4, :], ALU.mult,
                         reads=("fg2", "fg7"), writes=(f"fcb{i}",))
            p.barrier()
            moe = sst.enter_context(nc.sbuf_tensor(f"fmo{s}", [128, 16, 1024], BF16))
            p.memset(moe[:], 0.0, writes=tuple(f"fmo{i}" for i in range(16)))
            with (nc.sbuf_tensor(f"fwe{s}", [128, 2, 8, 1024], BF16) as wex,
                  nc.sbuf_tensor(f"fwf{s}", [128, 2, 4, 1024], F32) as wf32):
                Wer = We.rearrange("e (c p) n -> e c p n", p=128)
                for e in range(8):
                    wb, wk_ = e % 2, f"fwe{e % 2}"
                    for half in range(2):
                        fk_ = f"fwf{wb}"
                        for c in range(4):
                            p.dma(wf32[:, wb, c, :], Wer[e, half * 4 + c], writes=(fk_,))
                        p.copy(wex[:, wb, half * 4:half * 4 + 4, :], wf32[:, wb],
                               reads=(fk_,), writes=(wk_,))
                    for i in range(16):
                        pb, pk = (cx.pbA, "pbA") if i % 2 == 0 else (cx.pbB, "pbB")
                        for d in range(8):
                            p.mm(pb[0:128, 0:512], xnfm[:, d, i * 128:(i + 1) * 128],
                                 wex[:, wb, d, 0:512], d == 0, False,
                                 reads=(wk_, f"fxn{d}"), writes=(pk,))
                            p.mm(pb[0:128, 512:1024], xnfm[:, d, i * 128:(i + 1) * 128],
                                 wex[:, wb, d, 512:1024], d == 0, False,
                                 reads=(wk_, f"fxn{d}"), writes=(pk,))
                        p.mm(pb[0:128, 0:512], cx.ones_bf_row[:], ber[0:1, e, 0:512],
                             False, True, reads=("ber", "c_ones_bf_row"), writes=(pk,))
                        p.mm(pb[0:128, 512:1024], cx.ones_bf_row[:],
                             ber[0:1, e, 512:1024],
                             False, True, reads=("ber", "c_ones_bf_row"), writes=(pk,))
                        p.stt(moe[:, i, :], pb[0:128, 0:1024], comb[:, i, e:e + 1],
                              moe[:, i, :], ALU.mult, ALU.add,
                              reads=(pk, f"fcb{i}", f"fmo{i}"), writes=(f"fmo{i}",))
            if dev:
                for i in range(16):
                    ti = s * 16 + i
                    p.dma(moe_dbg[ti * 128:(ti + 1) * 128, :], moe[:, i, :], e="gpsimd",
                          reads=(f"fmo{i}",), writes=(f"md{ti}",))
                    p.dma(cmb_dbg[ti * 128:(ti + 1) * 128, :], comb[:, i, :],
                          reads=(f"fcb{i}",), writes=(f"cd{ti}",))
            p.barrier()
            with (nc.sbuf_tensor(f"fo1{s}", [128, 1024], F32) as t1,
                  nc.sbuf_tensor(f"fo2{s}", [128, 1024], F32) as t2,
                  nc.sbuf_tensor(f"fo3{s}", [128, 1024], F32) as t3):
                for i in range(16):
                    ti = s * 16 + i
                    p.dma(t1[:], x1_r[ti], writes=("fo1",))
                    p.dma(t2[:], gla_r2[ti], reads=(f"glad{ti}",), writes=("fo2",))
                    p.dma(t3[:], at_r[ti], reads=(f"at1_d{ti}",), writes=("fo3",))
                    p.tt(t1[:], t1[:], t2[:], ALU.add, reads=("fo1", "fo2"), writes=("fo1",))
                    p.tt(t1[:], t1[:], t3[:], ALU.add, reads=("fo1", "fo3"), writes=("fo1",))
                    p.tt(t1[:], t1[:], moe[:, i, :], ALU.add, reads=("fo1", f"fmo{i}"),
                         writes=("fo1",))
                    p.dma(out_r[ti], t1[:], reads=("fo1",), writes=(f"out_d{ti}",))
            p.barrier()
    f1ws.close()
    p.barrier()


L0_WSHAPES = {
    "L0_attn_Wq": (1024, 1024), "L0_attn_Wk": (1024, 512), "L0_attn_Wv": (1024, 512),
    "L0_attn_memory": (512, 512), "L0_attn_memory_norm": (512,),
    "L0_attn_memory_weight": (1,), "L0_ffn_w_up": (4096, 1024),
    "L0_ffn_w_down": (1024, 4096),
}


def build_nc_l0(dev=False):
    nc = bass.Bass("TRN2", target_bir_lowering=False, debug=False)
    p = Prog(nc)
    cx = Ctx(nc, p)
    x = nc.dram_tensor("x", [4096, 1024], F32, kind="ExternalInput").ap()
    w_aps = {k: nc.dram_tensor(k, list(s), F32, kind="ExternalInput").ap()
             for k, s in L0_WSHAPES.items()}
    x1 = nc.dram_tensor("x1", [4096, 1024], F32, kind="ExternalOutput").ap()
    build_l0(nc, p, cx, x, w_aps, x1, dev=dev)
    p.finalize()
    return nc


def kernel_unfused(**inputs):
    cores = list(range(8))
    x = np.ascontiguousarray(np.asarray(inputs["L0_attn_x"], np.float32))
    shards = [np.ascontiguousarray(x[2 * i:2 * i + 2].reshape(4096, 1024))
              for i in range(8)]

    def wm(shapes):
        return {k: np.ascontiguousarray(np.asarray(inputs[k], np.float32))
                for k in shapes}

    w0, w1a, w1f = wm(L0_WSHAPES), wm(L1A_WSHAPES), wm(L1F_WSHAPES)
    r0 = run_bass_kernel_spmd(build_nc_l0(), [dict(x=s, **w0) for s in shards],
                              core_ids=cores)
    x1s = [r0.results[i]["x1"] for i in range(8)]
    r1 = run_bass_kernel_spmd(build_nc_gla(), [dict(x1=x1s[i], **w1a) for i in range(8)],
                              core_ids=cores)
    r2 = run_bass_kernel_spmd(
        build_nc_ffn1(),
        [dict(x1=x1s[i], gla=r1.results[i]["gla"], **w1f)
         for i in range(8)], core_ids=cores)
    out = np.stack([np.asarray(r2.results[i]["out"]).reshape(2, 2048, 1024)
                    for i in range(8)])
    return out.reshape(16, 2048, 1024).astype(np.float32)


L1A_WSHAPES = {
    "L1_attn_Wq": (1024, 1024), "L1_attn_Wk": (1024, 1024),
    "L1_attn_Wv": (1024, 2048), "L1_attn_Wg": (1024, 2048),
    "L1_attn_gk_w1": (1024, 16), "L1_attn_gk_w2": (16, 1024),
    "L1_attn_gk_b": (1024,), "L1_attn_Wo": (2048, 1024), "L1_attn_rms_w": (512,),
}


def build_nc_gla(dev=False):
    nc = bass.Bass("TRN2", target_bir_lowering=False, debug=False)
    p = Prog(nc)
    cx = Ctx(nc, p)
    x1 = nc.dram_tensor("x1", [4096, 1024], F32, kind="ExternalInput").ap()
    w_aps = {k: nc.dram_tensor(k, list(s), F32, kind="ExternalInput").ap()
             for k, s in L1A_WSHAPES.items()}
    gla = nc.dram_tensor("gla", [4096, 1024], F32, kind="ExternalOutput").ap()
    xn1bf = nc.dram_tensor("xn1bf", [4096, 1024], BF16, kind="Internal").ap()
    build_ln(nc, p, cx, x1, xn1bf, "N1")
    build_gla(nc, p, cx, xn1bf, w_aps, gla, dev=dev)
    p.finalize()
    return nc


L1F_WSHAPES = {
    "L1_ffn_W_qkv": (1024, 3072), "L1_ffn_b_qkv": (3072,),
    "L1_ffn_W_gate": (1024, 8), "L1_ffn_b_gate": (8,),
    "L1_ffn_W_experts": (8, 1024, 1024), "L1_ffn_b_experts": (8, 1024),
}


def build_nc_ffn1(dev=False):
    nc = bass.Bass("TRN2", target_bir_lowering=False, debug=False)
    p = Prog(nc)
    cx = Ctx(nc, p)
    x1 = nc.dram_tensor("x1", [4096, 1024], F32, kind="ExternalInput").ap()
    xn1bf = nc.dram_tensor("xn1bf", [4096, 1024], BF16, kind="Internal").ap()
    gla = nc.dram_tensor("gla", [4096, 1024], F32, kind="ExternalInput").ap()
    w_aps = {k: nc.dram_tensor(k, list(s), F32, kind="ExternalInput").ap()
             for k, s in L1F_WSHAPES.items()}
    out = nc.dram_tensor("out", [4096, 1024], F32, kind="ExternalOutput").ap()
    build_ln(nc, p, cx, x1, xn1bf, "N1")
    build_ffn1(nc, p, cx, x1, xn1bf, gla, w_aps, out, dev=dev)
    p.finalize()
    return nc


ALL_WSHAPES = {}
ALL_WSHAPES.update(L0_WSHAPES)
ALL_WSHAPES.update(L1A_WSHAPES)
ALL_WSHAPES.update(L1F_WSHAPES)


def build_nc_fused(dev=False):
    nc = bass.Bass("TRN2", target_bir_lowering=False, debug=False)
    p = Prog(nc)
    cx = Ctx(nc, p)
    x = nc.dram_tensor("x", [4096, 1024], F32, kind="ExternalInput").ap()
    w_aps = {k: nc.dram_tensor(k, list(s), F32, kind="ExternalInput").ap()
             for k, s in ALL_WSHAPES.items()}
    out = nc.dram_tensor("out", [4096, 1024], F32, kind="ExternalOutput").ap()
    x1 = nc.dram_tensor("x1", [4096, 1024], F32, kind="Internal").ap()
    gla = nc.dram_tensor("gla", [4096, 1024], F32, kind="Internal").ap()
    xn1bf = nc.dram_tensor("xn1bf", [4096, 1024], BF16, kind="Internal").ap()
    build_l0(nc, p, cx, x, w_aps, x1, dev=dev)
    p.barrier()
    build_ln(nc, p, cx, x1, xn1bf, "N1")
    build_gla(nc, p, cx, xn1bf, w_aps, gla, dev=dev)
    p.barrier()
    build_ffn1(nc, p, cx, x1, xn1bf, gla, w_aps, out, dev=dev)
    p.finalize()
    return nc


def kernel(**inputs):
    cores = list(range(8))
    x = np.ascontiguousarray(np.asarray(inputs["L0_attn_x"], np.float32))
    shards = [np.ascontiguousarray(x[2 * i:2 * i + 2].reshape(4096, 1024))
              for i in range(8)]
    wmap = {k: np.ascontiguousarray(np.asarray(inputs[k], np.float32))
            for k in ALL_WSHAPES}
    nc = build_nc_fused()
    r = run_bass_kernel_spmd(nc, [dict(x=s, **wmap) for s in shards], core_ids=cores)
    out = np.stack([np.asarray(r.results[i]["out"]).reshape(2, 2048, 1024)
                    for i in range(8)])
    return out.reshape(16, 2048, 1024)


# version 12
# speedup vs baseline: 1.4193x; 1.1168x over previous
from contextlib import ExitStack
import numpy as np
import ml_dtypes
import concourse.bass as bass
import concourse.mybir as mybir
from concourse.bass_utils import run_bass_kernel_spmd

F32 = mybir.dt.float32
BF16 = mybir.dt.bfloat16
AF = mybir.ActivationFunctionType
ALU = mybir.AluOpType
AX = mybir.AxisListType.X
bfdt = ml_dtypes.bfloat16

ENGINES = ("sync", "tensor", "vector", "scalar", "gpsimd")
DMA_ENG = ("sync", "scalar", "gpsimd")


class Prog:

    NLANES = 8

    def __init__(self, nc):
        self.nc = nc
        self.c_sem = {e: nc.alloc_semaphore(f"c_{e}") for e in ENGINES}
        self.d_sem = {(e, l): nc.alloc_semaphore(f"d_{e}{l}")
                      for e in DMA_ENG for l in range(self.NLANES)}
        self.ccnt = {e: 0 for e in ENGINES}
        self.dcnt = {(e, l): 0 for e in DMA_ENG for l in range(self.NLANES)}
        self.dtot = {e: 0 for e in DMA_ENG}
        self.steps = {e: [] for e in ENGINES}
        self.lw = {}
        self.readers = {}
        self.wm = {e: {} for e in ENGINES}
        self.n_inst = 0

    def _sem_of(self, kind, oe):
        return self.c_sem[oe] if kind == "c" else self.d_sem[oe]

    def _wait(self, e, waits, dep):
        kind, oe, cnt = dep
        if oe == e and kind == "c":
            if e == "tensor":
                return
        got = self.wm[e].get((kind, oe), 0)
        if cnt <= got:
            return
        waits[(kind, oe)] = max(waits.get((kind, oe), 0), cnt)

    def emit(self, e, fn, reads=(), writes=(), dma=False):
        assert e in ENGINES
        if dma:
            assert e in DMA_ENG
        waits = {}
        for k in reads:
            d = self.lw.get(k)
            if d is not None:
                self._wait(e, waits, d)
        for k in writes:
            d = self.lw.get(k)
            if d is not None:
                self._wait(e, waits, d)
            for r in self.readers.get(k, ()):
                self._wait(e, waits, r)
        if dma:
            lane = self.dtot[e] % self.NLANES
            self.dtot[e] += 1
            if self.dcnt[(e, lane)] > 0:
                self._wait(e, waits, ("d", (e, lane), self.dcnt[(e, lane)]))
        wl = []
        for (kind, oe), cnt in waits.items():
            wl.append((self._sem_of(kind, oe), cnt))
            self.wm[e][(kind, oe)] = cnt
        if dma:
            self.dcnt[(e, lane)] += 16
            inc = (self.d_sem[(e, lane)], 16)
            me = ("d", (e, lane), self.dcnt[(e, lane)])
        else:
            self.ccnt[e] += 1
            inc = (self.c_sem[e], 1)
            me = ("c", e, self.ccnt[e])
        self.steps[e].append((fn, wl, inc))
        self.n_inst += 1
        for k in writes:
            self.lw[k] = me
            self.readers[k] = []
        for k in reads:
            self.readers.setdefault(k, []).append(me)

    def barrier(self):
        tgt = [("c", oe, self.ccnt[oe]) for oe in ENGINES if self.ccnt[oe]] + \
              [("d", (oe, l), self.dcnt[(oe, l)]) for oe in DMA_ENG
               for l in range(self.NLANES) if self.dcnt[(oe, l)]]
        for e in ENGINES:
            wl = []
            for kind, oe, cnt in tgt:
                if self.wm[e].get((kind, oe), 0) >= cnt:
                    continue
                wl.append((self._sem_of(kind, oe), cnt))
                self.wm[e][(kind, oe)] = cnt
            if wl:
                self.steps[e].append((None, wl, None))

    def dma(self, dst, src, e="sync", reads=(), writes=(), transpose=False, slow=False):
        if transpose:
            fn = lambda eng, d=dst, s=src: eng.dma_start_transpose(d, s)
        elif slow:
            fn = lambda eng, d=dst, s=src: eng.dma_start(d, s, allow_slow_non_contiguous=True)
        else:
            fn = lambda eng, d=dst, s=src: eng.dma_start(d, s)
        self.emit(e, fn, reads, writes, dma=True)

    def mm(self, out, lhsT, rhs, start, stop, reads=(), writes=()):
        fn = lambda eng, o=out, l=lhsT, r=rhs, st=start, sp=stop: eng.matmul(o, l, r, start=st, stop=sp)
        self.emit("tensor", fn, reads, writes)

    def trans(self, out, in_, ident, reads=(), writes=()):
        fn = lambda eng, o=out, i=in_, d=ident: eng.transpose(o, i, d)
        self.emit("tensor", fn, reads, writes)

    def act(self, out, in_, func, bias=0.0, scale=1.0, accum_out=None, reads=(), writes=()):
        fn = lambda eng, o=out, i=in_, f=func, b=bias, s=scale, a=accum_out: \
            eng.activation(o, i, f, bias=b, scale=s, accum_out=a)
        self.emit("scalar", fn, reads, writes)

    def tt(self, out, a, b, op, e="vector", reads=(), writes=()):
        fn = lambda eng, o=out, x=a, y=b, p=op: eng.tensor_tensor(o, x, y, p)
        self.emit(e, fn, reads, writes)

    def ts(self, out, in_, s1, s2, op0, op1=ALU.bypass, e="vector", reads=(), writes=()):
        fn = lambda eng, o=out, i=in_, a=s1, b=s2, p0=op0, p1=op1: \
            eng.tensor_scalar(o, i, a, b, p0, p1)
        self.emit(e, fn, reads, writes)

    def stt(self, out, in0, scalar, in1, op0, op1, e="vector", reads=(), writes=()):
        fn = lambda eng, o=out, i0=in0, s=scalar, i1=in1, p0=op0, p1=op1: \
            eng.scalar_tensor_tensor(o, i0, s, i1, p0, p1)
        self.emit(e, fn, reads, writes)

    def copy(self, out, in_, e="vector", reads=(), writes=()):
        fn = lambda eng, o=out, i=in_: eng.tensor_copy(o, i)
        self.emit(e, fn, reads, writes)

    def red(self, out, in_, op, negate=False, absval=False, e="vector", reads=(), writes=()):
        fn = lambda eng, o=out, i=in_, p=op, n=negate, a=absval: \
            eng.tensor_reduce(o, i, AX, p, apply_absolute_value=a, negate=n)
        self.emit(e, fn, reads, writes)

    def memset(self, ap, val, e="vector", reads=(), writes=()):
        fn = lambda eng, a=ap, v=val: eng.memset(a, v)
        self.emit(e, fn, reads, writes)

    def recip(self, out, in_, reads=(), writes=()):
        fn = lambda eng, o=out, i=in_: eng.reciprocal(o, i)
        self.emit("vector", fn, reads, writes)

    def scan(self, out, d0, d1, init, op0, op1, reads=(), writes=()):
        fn = lambda eng, o=out, a=d0, b=d1, i=init, p0=op0, p1=op1: \
            eng.tensor_tensor_scan(o, a, b, i, p0, p1)
        self.emit("vector", fn, reads, writes)

    def finalize(self):
        nc = self.nc
        self.barrier()
        with nc.Block() as block:
            for e in ENGINES:
                mine = self.steps[e]
                if not mine and e != "sync":
                    continue

                def body(eng, mine=mine):
                    for fn, wl, inc in mine:
                        for sem, cnt in wl:
                            eng.wait_ge(sem, cnt)
                        if fn is not None:
                            inst = fn(eng)
                            inst.then_inc(inc[0], inc[1])
                getattr(block, e)(body)


class Ctx:
    def __init__(self, nc, p):
        self.nc, self.p = nc, p
        self.id128 = nc.alloc_sbuf_tensor("c_id128", [128, 128], BF16)
        self.ones_bf_row = nc.alloc_sbuf_tensor("c_ones_bf_row", [1, 128], BF16)
        self.ones_f_row = nc.alloc_sbuf_tensor("c_ones_f_row", [1, 128], F32)
        self.ones_bf_col = nc.alloc_sbuf_tensor("c_ones_bf_col", [128, 1], BF16)
        self.ones_f_col = nc.alloc_sbuf_tensor("c_ones_f_col", [128, 1], F32)
        self.mask128 = nc.alloc_sbuf_tensor("c_mask128", [128, 128], BF16)
        for t, arr in ((self.id128, np.eye(128, dtype=bfdt)),
                       (self.mask128, np.triu(np.ones((128, 128), bfdt))),
                       (self.ones_bf_row, np.ones((1, 128), bfdt)),
                       (self.ones_f_row, np.ones((1, 128), np.float32)),
                       (self.ones_bf_col, np.ones((128, 1), bfdt)),
                       (self.ones_f_col, np.ones((128, 1), np.float32))):
            d = nc.inline_tensor(arr, t.name + "_d").ap()
            p.dma(t[:], d, writes=(t.name,))
        self.pbA = nc.alloc_psum_tensor("pbA", [128, 1024], F32)
        self.pbB = nc.alloc_psum_tensor("pbB", [128, 1024], F32)
        self.pbC = nc.alloc_psum_tensor("pbC", [128, 512], F32)
        self.pbD = nc.alloc_psum_tensor("pbD", [128, 512], F32)
        self.pbT = nc.alloc_psum_tensor("pbT", [128, 1024], BF16)
        self.pbU = nc.alloc_psum_tensor("pbU", [128, 1024], BF16)

    def rsqrt(self, p, out, in_, key, eps=0.0, rkeys=()):
        p.act(out, in_, AF.Ln, bias=eps, reads=rkeys, writes=(key,))
        p.act(out, out, AF.Exp, scale=-0.5, reads=(key,), writes=(key,))



def cast_load(nc, p, stage, dst_slices_srcs, key, stage_key):
    for dst, srcap, cols in dst_slices_srcs:
        p.dma(stage[:, 0:cols], srcap, writes=(stage_key,))
        p.copy(dst, stage[:, 0:cols], reads=(stage_key,), writes=(key,))


def build_l0(nc, p, cx, x_ap, w_aps, x1_ap, dbg=None, dev=False, xn1bf_ap=None):
    NT = 32
    SEQ = 2048
    Wq, Wk, Wv, Mem, Mn, Mw, Wup, Wdown = (
        w_aps["L0_attn_Wq"], w_aps["L0_attn_Wk"], w_aps["L0_attn_Wv"],
        w_aps["L0_attn_memory"], w_aps["L0_attn_memory_norm"],
        w_aps["L0_attn_memory_weight"], w_aps["L0_ffn_w_up"], w_aps["L0_ffn_w_down"])

    l0ws = ExitStack()
    wq_sb = l0ws.enter_context(nc.sbuf_tensor("wq_sb", [128, 8, 1024], BF16))
    wkd_sb = l0ws.enter_context(nc.sbuf_tensor("wkd_sb", [128, 8, 8, 128], BF16))
    wv_sb = l0ws.enter_context(nc.sbuf_tensor("wv_sb", [128, 8, 512], BF16))
    mem_sb = l0ws.enter_context(nc.sbuf_tensor("mem_sb", [128, 4, 512], BF16))
    mn_sb = l0ws.enter_context(nc.sbuf_tensor("mn_sb", [128, 4], BF16))
    with nc.sbuf_tensor("l0stage", [128, 1024], F32) as stg:
        Wq_r = Wq.rearrange("(c p) n -> c p n", p=128)
        cast_load(nc, p, stg, [(wq_sb[:, c, :], Wq_r[c], 1024) for c in range(8)],
                  "wq", "l0stg")
        wk_r = Wk.rearrange("(c p) (h i) -> c p h i", p=128, i=64)
        wk_v = wk_r.rearrange("c p h i -> c p (h i)")
        for c in range(8):
            p.dma(stg[:, 0:512], wk_v[c], writes=("l0stg",))
            sview = stg[:, 0:512].rearrange("p (h i) -> p h i", i=64)
            p.copy(wkd_sb[:, c, :, 0:64], sview, reads=("l0stg",), writes=("wkd0",))
            p.copy(wkd_sb[:, c, :, 64:128], sview, reads=("l0stg",), writes=("wkd1",))
        Wv_r = Wv.rearrange("(c p) n -> c p n", p=128)
        cast_load(nc, p, stg, [(wv_sb[:, c, :], Wv_r[c], 512) for c in range(8)],
                  "wv", "l0stg")
        Mem_r = Mem.rearrange("(c p) n -> c p n", p=128)
        cast_load(nc, p, stg, [(mem_sb[:, c, :], Mem_r[c], 512) for c in range(4)],
                  "mem", "l0stg")
        Mn_r = Mn.rearrange("(c p) -> c p", p=128)
        for c in range(4):
            p.dma(stg[:, 0:1], Mn_r[c].rearrange("(p o) -> p o", o=1), writes=("l0stg",))
            p.copy(mn_sb[:, c:c + 1], stg[:, 0:1], reads=("l0stg",), writes=("mn",))
    p.barrier()

    mw_sb = l0ws.enter_context(nc.sbuf_tensor("mw_sb", [1, 1], F32))
    wrow = l0ws.enter_context(nc.sbuf_tensor("wrow", [1, 2], F32))
    wcols = l0ws.enter_context(nc.sbuf_tensor("wcols", [128, 2], F32))
    p.dma(mw_sb[:], Mw.rearrange("(a b) -> a b", a=1), writes=("mw",))
    p.act(wrow[:, 0:1], mw_sb[:], AF.Exp, scale=-1.0, reads=("mw",), writes=("wr_w",))
    p.ts(wrow[:, 0:1], wrow[:, 0:1], 1.0, None, ALU.add, reads=("wr_w",), writes=("wr_w",))
    p.recip(wrow[:, 0:1], wrow[:, 0:1], reads=("wr_w",), writes=("wr_w",))
    p.ts(wrow[:, 1:2], wrow[:, 0:1], 1.0, -1.0, ALU.subtract, ALU.mult,
         reads=("wr_w",), writes=("wr_w1",))
    p.mm(cx.pbC[0:128, 0:2], cx.ones_f_row[:], wrow[:], True, True,
         reads=("wr_w", "wr_w1", "c_ones_f_row"), writes=("pbC",))
    p.copy(wcols[:], cx.pbC[0:128, 0:2], reads=("pbC",), writes=("wcols",))
    wcol, w1col = wcols[:, 0:1], wcols[:, 1:2]

    KIND = "ExternalOutput" if dev else "Internal"
    wtup = nc.dram_tensor("wtup", [4096, 1024], BF16, kind=KIND).ap()
    wtdown = nc.dram_tensor("wtdown", [1024, 4096], BF16, kind=KIND).ap()

    tws = ExitStack()
    tw0 = tws.enter_context(nc.sbuf_tensor("t_w0", [128, 2048], F32))
    tw1 = tws.enter_context(nc.sbuf_tensor("t_w1", [128, 2048], F32))
    tcm = tws.enter_context(nc.sbuf_tensor("t_cm", [128, 2048], BF16))
    twt = tws.enter_context(nc.sbuf_tensor("t_wt", [128, 2048], BF16))
    tsm = tws.enter_context(nc.sbuf_tensor("t_sm", [128, 6], F32))

    def ternarize(W, n_rows, n_cols, out_dram, tag):
        nt = n_rows // 128
        ncc = n_cols // 2048 if n_cols > 2048 else 1
        cw = n_cols // ncc
        wld = (tw0[:, 0:cw], tw1[:, 0:cw])
        acc, r = tsm[:, 0:1], tsm[:, 1:2]
        thc = tsm[:, 2:3]
        th = tsm[0:1, 4:5]
        p.memset(acc, 0.0, writes=("t_acc",))
        Wr = W.rearrange("(t p) n -> t p n", p=128)
        nch = 0
        for i in range(nt):
            for cc in range(ncc):
                b, bk = wld[nch % 2], f"twl{nch % 2}"
                nch += 1
                p.dma(b, Wr[i][:, cc * cw:(cc + 1) * cw], writes=(bk,))
                p.red(r, b, ALU.add, absval=True, reads=(bk,), writes=("t_r",))
                p.tt(acc, acc, r, ALU.add,
                     reads=("t_r", "t_acc"), writes=("t_acc",))
        p.mm(cx.pbC[0:1, 0:1], acc, cx.ones_f_col[:], True, True,
             reads=("t_acc", "c_ones_f_col"), writes=("pbC",))
        p.ts(th, cx.pbC[0:1, 0:1], 0.5 / (n_rows * n_cols), None, ALU.mult,
             reads=("pbC",), writes=("t_th",))
        p.mm(cx.pbC[0:128, 0:1], cx.ones_f_row[:], th, True, True,
             reads=("t_th", "c_ones_f_row"), writes=("pbC",))
        p.copy(thc, cx.pbC[0:128, 0:1], reads=("pbC",), writes=("t_thc",))
        p.ts(tsm[:, 3:4], thc, -1.0, None, ALU.mult,
             reads=("t_thc",), writes=("t_nthc",))
        nch = 0
        for i in range(nt):
            for cc in range(ncc):
                b, bk = wld[nch % 2], f"twl{nch % 2}"
                nch += 1
                p.dma(b, Wr[i][:, cc * cw:(cc + 1) * cw], writes=(bk,))
                p.ts(tcm[:, 0:cw], b, thc[:, 0:1], None, ALU.is_gt,
                     reads=(bk, "t_thc"), writes=("t_cm",))
                p.ts(twt[:, 0:cw], b, tsm[:, 3:4], None, ALU.is_lt,
                     reads=(bk, "t_nthc"), writes=("t_wt",))
                p.tt(twt[:, 0:cw], tcm[:, 0:cw], twt[:, 0:cw], ALU.subtract,
                     reads=("t_cm", "t_wt"), writes=("t_wt",))
                p.dma(out_dram[i * 128:(i + 1) * 128, cc * cw:(cc + 1) * cw],
                      twt[:, 0:cw], e="scalar", reads=("t_wt",),
                      writes=(f"twd_{tag}{i}_{cc}",))

    ternarize(Wup, 4096, 1024, wtup, "u")
    ternarize(Wdown, 1024, 4096, wtdown, "d")

    xnbf_d = nc.dram_tensor("xnbf", [4096, 1024], BF16, kind=KIND).ap()
    attn_d = nc.dram_tensor("attn0", [4096, 1024], F32, kind=KIND).ap()
    rstd_d = nc.dram_tensor("rstd0", [4096], F32, kind=KIND).ap()
    if dbg is not None:
        dbg["xnbf"] = xnbf_d
        dbg["attn0"] = attn_d

    x_r = x_ap.rearrange("(t p) n -> t p n", p=128)
    x1_r = x1_ap.rearrange("(t p) n -> t p n", p=128)
    xn_r = xnbf_d.rearrange("(t p) n -> t p n", p=128)
    at_r = attn_d.rearrange("(t p) n -> t p n", p=128)

    with (nc.sbuf_tensor("xt", [128, 2, 1024], F32) as xt,
          nc.sbuf_tensor("lnt", [128, 6], F32) as lnt,
          nc.sbuf_tensor("xnt", [128, 2, 1024], BF16) as xnt,
          nc.sbuf_tensor("sqt", [128, 1024], F32) as sqt):
        m = lnt[:, 0:1]; s2 = lnt[:, 1:2]; var = lnt[:, 2:3]
        rstd = lnt[:, 3:4]; rsf = lnt[:, 4:5]; m2 = lnt[:, 5:6]
        for i in range(NT):
            b = i % 2
            xk, nk = f"xt{b}", f"xnt{b}"
            p.dma(xt[:, b], x_r[i], writes=(xk,))
            p.red(m, xt[:, b], ALU.add, reads=(xk,), writes=("ln_m",))
            p.act(sqt[:], xt[:, b], AF.Square, reads=(xk,), writes=("ln_sq",))
            p.red(s2, sqt[:], ALU.add, reads=("ln_sq",), writes=("ln_s2",))
            p.ts(m, m, 1.0 / 1024, None, ALU.mult, reads=("ln_m",), writes=("ln_m",))
            p.ts(s2, s2, 1.0 / 1024, 1e-5, ALU.mult, ALU.add, reads=("ln_s2",), writes=("ln_s2",))
            p.tt(m2, m, m, ALU.mult, reads=("ln_m",), writes=("ln_m2",))
            p.tt(var, s2, m2, ALU.subtract, reads=("ln_s2", "ln_m2"), writes=("ln_v",))
            cx.rsqrt(p, rstd, var, "ln_rstd", rkeys=("ln_v",))
            p.ts(xnt[:, b], xt[:, b], m[:, 0:1], rstd[:, 0:1], ALU.subtract, ALU.mult,
                 reads=(xk, "ln_m", "ln_rstd"), writes=(nk,))
            p.dma(xn_r[i], xnt[:, b], reads=(nk,), writes=(f"xn_d{i}",))
            p.tt(sqt[:], xnt[:, b], xnt[:, b], ALU.mult, reads=(nk,), writes=("ln_sq",))
            p.red(s2, sqt[:], ALU.add, reads=("ln_sq",), writes=("ln_s2",))
            p.ts(s2, s2, 1.0 / 1024, None, ALU.mult, reads=("ln_s2",), writes=("ln_s2",))
            p.act(rsf, s2, AF.Ln, reads=("ln_s2",), writes=("ln_rsf",))
            p.act(rsf, rsf, AF.Exp, scale=0.5, reads=("ln_rsf",), writes=("ln_rsf",))
            p.ts(rsf, rsf, 1e-8, None, ALU.add, reads=("ln_rsf",), writes=("ln_rsf",))
            p.recip(rsf, rsf, reads=("ln_rsf",), writes=("ln_rsf",))
            p.dma(rstd_d[i * 128:(i + 1) * 128].rearrange("(p o) -> p o", o=1), rsf,
                  reads=("ln_rsf",), writes=(f"rs_d{i}",))
    p.barrier()

    for s in range(2):
        T0 = s * 16
        with nc.sbuf_tensor(f"xnfm{s}", [128, 8, 2048], BF16) as xnfm:
            for c in range(8):
                p.dma(xnfm[:, c, :], xnbf_d[s * SEQ:(s + 1) * SEQ, c * 128:(c + 1) * 128],
                      transpose=True, reads=tuple(f"xn_d{T0 + i}" for i in range(16)),
                      writes=(f"xnfm{c}",))
            with (nc.sbuf_tensor(f"kdup{s}", [128, 8, 2048], BF16) as kdup,
                  nc.sbuf_tensor(f"vaug{s}", [128, 16, 8, 68], BF16) as vaug):
                for j in range(8):
                    for qb in range(4):
                        sl = slice(qb * 512, (qb + 1) * 512)
                        for c in range(8):
                            p.mm(cx.pbA[0:128, 0:512], wkd_sb[:, c, j, :], xnfm[:, c, sl],
                                 c == 0, c == 7, reads=("wkd0", "wkd1", f"xnfm{c}"),
                                 writes=("pbA",))
                        p.copy(kdup[:, j, sl], cx.pbA[0:128, 0:512], reads=("pbA",),
                               writes=(f"kd{j}",))
                p.memset(vaug[:, :, :, 64:65], 1.0, writes=("va_ones",))
                for i in range(16):
                    for c in range(8):
                        p.mm(cx.pbB[0:128, 0:512], xnfm[:, c, i * 128:(i + 1) * 128],
                             wv_sb[:, c, :], c == 0, c == 7,
                             reads=(f"xnfm{c}", "wv"), writes=("pbB",))
                    for h in range(8):
                        p.copy(vaug[:, i, h, 0:64], cx.pbB[0:128, h * 64:(h + 1) * 64],
                               reads=("pbB",), writes=(f"va{i}",))

                with (nc.sbuf_tensor(f"qfm{s}", [128, 8, 512], BF16) as qfm,
                      nc.sbuf_tensor(f"sq_bf{s}", [128, 8, 512], BF16) as sq_bf,
                      nc.sbuf_tensor(f"elut{s}", [128, 8, 512], F32) as elut,
                      nc.sbuf_tensor(f"ebuf{s}", [128, 2, 1024], BF16) as ebuf,
                      nc.sbuf_tensor(f"pvs{s}", [65, 2, 512], BF16) as pvs,
                      nc.sbuf_tensor(f"acc{s}", [128, 4, 1024], F32) as acc,
                      nc.sbuf_tensor(f"eps_t{s}", [128, 8], F32) as eps_t):
                    for qb in range(4):
                        qsl = slice(qb * 512, (qb + 1) * 512)
                        for c in range(8):
                            for d in range(8):
                                p.mm(cx.pbA[0:128, 0:512], wq_sb[:, d, c * 128:(c + 1) * 128],
                                     xnfm[:, d, qsl], d == 0, d == 7,
                                     reads=("wq", f"xnfm{d}"), writes=("pbA",))
                            p.copy(qfm[:, c, :], cx.pbA[0:128, 0:512], reads=("pbA",),
                                   writes=(f"qfm{c}",))
                        qk = tuple(f"qfm{c}" for c in range(8))
                        p.ts(elut[:], qfm[:], 0.0, None, ALU.min, reads=qk, writes=("elut",))
                        p.act(elut[:], elut[:], AF.Exp, reads=("elut",), writes=("elut",))
                        p.ts(sq_bf[:], qfm[:], 0.0, -1.0, ALU.max, ALU.add, reads=qk,
                             writes=("sqbf",))
                        p.tt(sq_bf[:], sq_bf[:], elut[:], ALU.add, reads=("sqbf", "elut"),
                             writes=("sqbf",))
                        for g in range(2):
                            for j in range(4):
                                for c in range(4):
                                    cc = 4 * g + c
                                    p.mm(cx.pbA[0:128, 0:512],
                                         sq_bf[:, cc, j * 128:(j + 1) * 128],
                                         mem_sb[:, c, :], c == 0, c == 3,
                                         reads=("sqbf", "mem"), writes=("pbA",))
                                    p.mm(cx.pbC[0:128, 0:1],
                                         sq_bf[:, cc, j * 128:(j + 1) * 128],
                                         mn_sb[:, c:c + 1], c == 0, c == 3,
                                         reads=("sqbf", "mn"), writes=("pbC",))
                                p.recip(eps_t[:, 0:1], cx.pbC[0:128, 0:1], reads=("pbC",),
                                        writes=("rdw",))
                                p.tt(eps_t[:, 0:1], eps_t[:, 0:1], wcol, ALU.mult,
                                     reads=("rdw", "wcols"), writes=("rdw",))
                                p.ts(acc[:, j, g * 512:(g + 1) * 512], cx.pbA[0:128, 0:512],
                                     eps_t[:, 0:1], None, ALU.mult,
                                     reads=("pbA", "rdw"), writes=(f"acc{j}",))
                        for hp in range(8):
                            for kt in range(17):
                                if kt < 16:
                                    pb, pk = (cx.pbA, "pbA") if kt % 2 == 0 else (cx.pbB, "pbB")
                                    eb, ek = (kt % 2), f"eb{kt % 2}"
                                    ksl = slice(kt * 128, (kt + 1) * 128)
                                    p.mm(pb[0:128, 0:512], kdup[0:64, hp, ksl],
                                         qfm[0:64, hp, :],
                                         True, True, reads=(f"kd{hp}", f"qfm{hp}"),
                                         writes=(pk,))
                                    p.mm(pb[0:128, 512:1024], kdup[64:128, hp, ksl],
                                         qfm[64:128, hp, :],
                                         True, True, reads=(f"kd{hp}", f"qfm{hp}"),
                                         writes=(pk,))
                                    p.act(ebuf[:, eb], pb[:], AF.Exp, scale=0.125,
                                          reads=(pk,), writes=(ek,))
                                if kt >= 1:
                                    kv = kt - 1
                                    eb2, ek2 = (kv % 2), f"eb{kv % 2}"
                                    p.mm(cx.pbC[0:65, 0:512], vaug[:, kv, hp, 0:65],
                                         ebuf[:, eb2, 0:512], kv == 0, kv == 15,
                                         reads=(f"va{kv}", "va_ones", ek2), writes=("pbC",))
                                    p.mm(cx.pbD[0:65, 0:512], vaug[:, kv, hp, 0:65],
                                         ebuf[:, eb2, 512:1024], kv == 0, kv == 15,
                                         reads=(f"va{kv}", "va_ones", ek2), writes=("pbD",))
                            for hh in range(2):
                                pvp, pvk = (cx.pbC, "pbC") if hh == 0 else (cx.pbD, "pbD")
                                pT, pTk = (cx.pbT, "pbT") if hh == 0 else (cx.pbU, "pbU")
                                n = 2 * hp + hh
                                p.copy(pvs[:, hh], pvp[0:65, 0:512], reads=(pvk,),
                                       writes=(f"pvs{hh}",))
                                for j in range(4):
                                    p.trans(pT[0:128, j * 256:j * 256 + 65],
                                            pvs[0:65, hh, j * 128:(j + 1) * 128],
                                            cx.id128[0:65, 0:65],
                                            reads=(f"pvs{hh}", "c_id128"), writes=(pTk,))
                                dcols = pT[0:128, 256 * 0 + 64:256 * 3 + 65:256]
                                p.recip(eps_t[:, 4:8], dcols, reads=(pTk,), writes=("rin1",))
                                p.ts(eps_t[:, 4:8], eps_t[:, 4:8], w1col[:, 0:1], None,
                                     ALU.mult, reads=("rin1", "wcols"), writes=("rin1",))
                                for j in range(4):
                                    p.stt(acc[:, j, n * 64:(n + 1) * 64],
                                          pT[0:128, j * 256:j * 256 + 64], eps_t[:, 4 + j:5 + j],
                                          acc[:, j, n * 64:(n + 1) * 64], ALU.mult, ALU.add,
                                          reads=(pTk, "rin1", f"acc{j}"), writes=(f"acc{j}",))
                        for j in range(4):
                            p.dma(at_r[T0 + qb * 4 + j], acc[:, j, :], reads=(f"acc{j}",),
                                  writes=(f"at_d{T0 + qb * 4 + j}",))
                p.barrier()

            with (nc.sbuf_tensor(f"hstore{s}", [128, 32, 512], BF16) as hst,
                  nc.sbuf_tensor(f"xffn{s}", [128, 8, 512], BF16) as xffn,
                  nc.sbuf_tensor(f"wupT{s}", [128, 2, 8, 512], BF16) as wupT,
                  nc.sbuf_tensor(f"wdT{s}", [128, 4, 1024], BF16) as wdT,
                  nc.sbuf_tensor(f"r1row{s}", [1, 512], F32) as r1row,
                  nc.sbuf_tensor(f"r1b{s}", [128, 512], F32) as r1b,
                  nc.sbuf_tensor(f"h2{s}", [128, 2, 512], BF16) as h2t,
                  nc.sbuf_tensor(f"ff{s}", [128, 8], F32) as ff,
                  nc.sbuf_tensor(f"xres{s}", [128, 1024], F32) as xres,
                  nc.sbuf_tensor(f"ares{s}", [128, 1024], F32) as ares,
                  nc.sbuf_tensor(f"lnt2{s}", [128, 8], F32) as lnt2,
                  nc.sbuf_tensor(f"xnt2{s}", [128, 1024], BF16) as xnt2,
                  nc.sbuf_tensor(f"x1t{s}", [128, 1024], F32) as x1t):
                for blk in range(4):
                    t0 = s * SEQ + blk * 512
                    bt0 = blk * 512
                    p.dma(r1row[:], rstd_d[t0:t0 + 512].rearrange("(o n) -> o n", o=1),
                          reads=tuple(f"rs_d{t0 // 128 + i}" for i in range(4)),
                          writes=("r1row",))
                    p.mm(cx.pbD[0:128, 0:512], cx.ones_f_row[:], r1row[:], True, True,
                         reads=("r1row", "c_ones_f_row"), writes=("pbD",))
                    p.copy(r1b[:], cx.pbD[0:128, 0:512], reads=("pbD",), writes=("r1b",))
                    for c in range(8):
                        p.tt(xffn[:, c, :], xnfm[:, c, bt0:bt0 + 512], r1b[:], ALU.mult,
                             reads=(f"xnfm{c}", "r1b"), writes=(f"xf{c}",))
                    for fg in range(8):
                        wb, wk_ = fg % 2, f"wupT{fg % 2}"
                        for d in range(8):
                            p.dma(wupT[:, wb, d, :],
                                  wtup[fg * 512:(fg + 1) * 512, d * 128:(d + 1) * 128],
                                  transpose=True, writes=(wk_,))
                        for fs in range(4):
                            fi = fg * 4 + fs
                            pb, pk = (cx.pbA, "pbA") if fi % 2 == 0 else (cx.pbB, "pbB")
                            for d in range(8):
                                p.mm(pb[0:128, 0:512], wupT[:, wb, d, fs * 128:(fs + 1) * 128],
                                     xffn[:, d, :], d == 0, d == 7,
                                     reads=(wk_,) + tuple(f"xf{dd}" for dd in range(8)),
                                     writes=(pk,))
                            p.act(hst[:, fi, :], pb[0:128, 0:512], AF.Gelu,
                                  reads=(pk,), writes=(f"h{fi}",))
                            p.tt(h2t[:, (fi % 2), :], hst[:, fi, :], hst[:, fi, :],
                                 ALU.mult, reads=(f"h{fi}",), writes=(f"h2_{fi % 2}",))
                            if fi >= 1:
                                fj = fi - 1
                                for j in range(4):
                                    p.mm(cx.pbC[0:128, j * 32 + fj:j * 32 + fj + 1],
                                         h2t[:, (fj % 2), j * 128:(j + 1) * 128],
                                         cx.ones_bf_col[:], True, True,
                                         reads=(f"h2_{fj % 2}", "c_ones_bf_col"),
                                         writes=("pbC",))
                        if fg == 7:
                            for j in range(4):
                                p.mm(cx.pbC[0:128, j * 32 + 31:j * 32 + 32],
                                     h2t[:, 1, j * 128:(j + 1) * 128],
                                     cx.ones_bf_col[:], True, True,
                                     reads=("h2_1", "c_ones_bf_col"), writes=("pbC",))
                    hkeys = tuple(f"h{fi}" for fi in range(32))
                    for j in range(4):
                        p.red(ff[:, 4 + j:5 + j], cx.pbC[0:128, j * 32:j * 32 + 32],
                              ALU.add, reads=("pbC",), writes=("ff_ss",))
                    p.ts(ff[:, 0:4], ff[:, 4:8], 1.0 / 4096, None, ALU.mult,
                         reads=("ff_ss",), writes=("ff_r2",))
                    p.act(ff[:, 0:4], ff[:, 0:4], AF.Ln, reads=("ff_r2",), writes=("ff_r2",))
                    p.act(ff[:, 0:4], ff[:, 0:4], AF.Exp, scale=0.5, reads=("ff_r2",),
                          writes=("ff_r2",))
                    p.ts(ff[:, 0:4], ff[:, 0:4], 1e-8, None, ALU.add, reads=("ff_r2",),
                         writes=("ff_r2",))
                    p.recip(ff[:, 0:4], ff[:, 0:4], reads=("ff_r2",), writes=("ff_r2",))
                    for jp in range(2):
                        for fc in range(32):
                            wb2, wdk = fc % 4, f"wdT{fc % 4}"
                            p.dma(wdT[:, wb2, :], wtdown[0:1024, fc * 128:(fc + 1) * 128],
                                  transpose=True, writes=(wdk,))
                            for jj in range(2):
                                j = 2 * jp + jj
                                pb, pk = (cx.pbA, "pbA") if jj == 0 else (cx.pbB, "pbB")
                                p.mm(pb[0:128, 0:512], hst[:, fc, j * 128:(j + 1) * 128],
                                     wdT[:, wb2, 0:512], fc == 0, fc == 31,
                                     reads=(wdk,) + hkeys, writes=(pk,))
                                p.mm(pb[0:128, 512:1024], hst[:, fc, j * 128:(j + 1) * 128],
                                     wdT[:, wb2, 512:1024], fc == 0, fc == 31,
                                     reads=(wdk,) + hkeys, writes=(pk,))
                        for jj in range(2):
                            j = 2 * jp + jj
                            ti = t0 // 128 + j
                            pb, pk = (cx.pbA, "pbA") if jj == 0 else (cx.pbB, "pbB")
                            p.dma(xres[:], x_r[ti], writes=("xres",))
                            p.dma(ares[:], at_r[ti], reads=(f"at_d{ti}",), writes=("ares",))
                            p.tt(x1t[:], xres[:], ares[:], ALU.add, reads=("xres", "ares"),
                                 writes=("x1t",))
                            p.stt(x1t[:], pb[0:128, 0:1024], ff[:, j:j + 1], x1t[:],
                                  ALU.mult, ALU.add, reads=(pk, "ff_r2", "x1t"),
                                  writes=("x1t",))
                            p.dma(x1_r[ti], x1t[:], reads=("x1t",), writes=(f"x1_d{ti}",))
                            if xn1bf_ap is not None:
                                m2_ = lnt2[:, 0:1]; s2_ = lnt2[:, 1:2]
                                v_ = lnt2[:, 2:3]; rs_ = lnt2[:, 3:4]; mm_ = lnt2[:, 4:5]
                                p.red(m2_, x1t[:], ALU.add, reads=("x1t",),
                                      writes=("l1m",))
                                p.act(xres[:], x1t[:], AF.Square, reads=("x1t",),
                                      writes=("xres",))
                                p.red(s2_, xres[:], ALU.add, reads=("xres",),
                                      writes=("l1s",))
                                p.ts(m2_, m2_, 1.0 / 1024, None, ALU.mult,
                                     reads=("l1m",), writes=("l1m",))
                                p.ts(s2_, s2_, 1.0 / 1024, 1e-5, ALU.mult, ALU.add,
                                     reads=("l1s",), writes=("l1s",))
                                p.tt(mm_, m2_, m2_, ALU.mult, reads=("l1m",),
                                     writes=("l1mm",))
                                p.tt(v_, s2_, mm_, ALU.subtract, reads=("l1s", "l1mm"),
                                     writes=("l1v",))
                                cx.rsqrt(p, rs_, v_, "l1rs", rkeys=("l1v",))
                                p.ts(xnt2[:], x1t[:], m2_[:, 0:1], rs_[:, 0:1],
                                     ALU.subtract, ALU.mult,
                                     reads=("x1t", "l1m", "l1rs"), writes=("xnt2",))
                                p.dma(xn1bf_ap[ti * 128:(ti + 1) * 128, :], xnt2[:],
                                      reads=("xnt2",), writes=(f"N1d{ti}",))
                p.barrier()
    tws.close()
    l0ws.close()
    p.barrier()
    return {"x1_r": x1_r}


def build_ln(nc, p, cx, src_ap, dst_ap, tag, src_keys=None):
    src_r = src_ap.rearrange("(t p) n -> t p n", p=128)
    dst_r = dst_ap.rearrange("(t p) n -> t p n", p=128)
    NT = src_r.shape[0]
    with (nc.sbuf_tensor(f"lnx{tag}", [128, 2, 1024], F32) as xt,
          nc.sbuf_tensor(f"lns{tag}", [128, 6], F32) as lnt,
          nc.sbuf_tensor(f"lno{tag}", [128, 2, 1024], BF16) as xnt,
          nc.sbuf_tensor(f"lnq{tag}", [128, 1024], F32) as sqt):
        m = lnt[:, 0:1]; s2 = lnt[:, 1:2]; var = lnt[:, 2:3]
        rstd = lnt[:, 3:4]; m2 = lnt[:, 5:6]
        for i in range(NT):
            b = i % 2
            xk, nk = f"{tag}xt{b}", f"{tag}xnt{b}"
            rk = (src_keys[i],) if src_keys else ()
            p.dma(xt[:, b], src_r[i], reads=rk, writes=(xk,))
            p.red(m, xt[:, b], ALU.add, reads=(xk,), writes=(f"{tag}m",))
            p.act(sqt[:], xt[:, b], AF.Square, reads=(xk,), writes=(f"{tag}sq",))
            p.red(s2, sqt[:], ALU.add, reads=(f"{tag}sq",), writes=(f"{tag}s2",))
            p.ts(m, m, 1.0 / 1024, None, ALU.mult, reads=(f"{tag}m",), writes=(f"{tag}m",))
            p.ts(s2, s2, 1.0 / 1024, 1e-5, ALU.mult, ALU.add, reads=(f"{tag}s2",),
                 writes=(f"{tag}s2",))
            p.tt(m2, m, m, ALU.mult, reads=(f"{tag}m",), writes=(f"{tag}m2",))
            p.tt(var, s2, m2, ALU.subtract, reads=(f"{tag}s2", f"{tag}m2"),
                 writes=(f"{tag}v",))
            cx.rsqrt(p, rstd, var, f"{tag}rstd", rkeys=(f"{tag}v",))
            p.ts(xnt[:, b], xt[:, b], m[:, 0:1], rstd[:, 0:1], ALU.subtract, ALU.mult,
                 reads=(xk, f"{tag}m", f"{tag}rstd"), writes=(nk,))
            p.dma(dst_r[i], xnt[:, b], reads=(nk,), writes=(f"{tag}d{i}",))
    p.barrier()


def build_gla(nc, p, cx, xn1bf_d, w_aps, gla_d, dev=False):
    SEQ = 2048
    HB = 256
    NHB = SEQ // HB
    LWq, LWk, LWv, LWg = (w_aps["L1_attn_Wq"], w_aps["L1_attn_Wk"],
                          w_aps["L1_attn_Wv"], w_aps["L1_attn_Wg"])
    GK1, GK2, GKB = w_aps["L1_attn_gk_w1"], w_aps["L1_attn_gk_w2"], w_aps["L1_attn_gk_b"]
    LWo, RMSW = w_aps["L1_attn_Wo"], w_aps["L1_attn_rms_w"]

    glws = ExitStack()
    lwq = glws.enter_context(nc.sbuf_tensor("lwq", [128, 8, 1024], BF16))
    lwk = glws.enter_context(nc.sbuf_tensor("lwk", [128, 8, 1024], BF16))
    lwv = glws.enter_context(nc.sbuf_tensor("lwv", [128, 8, 2048], BF16))
    lwg = glws.enter_context(nc.sbuf_tensor("lwg", [128, 8, 2048], BF16))
    gk1 = glws.enter_context(nc.sbuf_tensor("gk1", [128, 8, 16], BF16))
    gk2 = glws.enter_context(nc.sbuf_tensor("gk2", [16, 1024], BF16))
    gkb = glws.enter_context(nc.sbuf_tensor("gkb", [128, 8], F32))
    rw = glws.enter_context(nc.sbuf_tensor("rw", [128, 4], F32))
    with nc.sbuf_tensor("glstage", [128, 2048], F32) as stg:
        for nm, dst, W, nch, cols in (("lwq", lwq, LWq, 8, 1024), ("lwk", lwk, LWk, 8, 1024),
                                      ("lwv", lwv, LWv, 8, 2048), ("lwg", lwg, LWg, 8, 2048),
                                      ("gk1", gk1, GK1, 8, 16)):
            Wr = W.rearrange("(c p) n -> c p n", p=128)
            cast_load(nc, p, stg, [(dst[:, c, :], Wr[c], cols) for c in range(nch)],
                      nm, "glstg")
        p.dma(stg[0:16, 0:1024], GK2, writes=("glstg",))
        p.copy(gk2[:], stg[0:16, 0:1024], reads=("glstg",), writes=("gk2",))
        GKB_r = GKB.rearrange("(c p) -> c p", p=128)
        for c in range(8):
            p.dma(gkb[:, c:c + 1], GKB_r[c].rearrange("(p o) -> p o", o=1), writes=("gkb",))
        RW_r = RMSW.rearrange("(c p) -> c p", p=128)
        for c in range(4):
            p.dma(rw[:, c:c + 1], RW_r[c].rearrange("(p o) -> p o", o=1), writes=("rw",))
    p.barrier()

    og_d = nc.dram_tensor("og_d", [2048, 2048], BF16,
                          kind="ExternalOutput" if dev else "Internal").ap()
    gla_r = gla_d.rearrange("(t p) n -> t p n", p=128)

    for s in range(2):
        with (nc.sbuf_tensor(f"gxnfm{s}", [128, 8, 2048], BF16) as xnfm,
              nc.sbuf_tensor(f"gS{s}", [128, 2, 4, 512], F32) as Sst,
              nc.sbuf_tensor(f"gS0b{s}", [128, 2, 512], BF16) as S0b):
            for c in range(8):
                p.dma(xnfm[:, c, :], xn1bf_d[s * SEQ:(s + 1) * SEQ, c * 128:(c + 1) * 128],
                      transpose=True, reads=tuple(f"N1d{s * 16 + i}" for i in range(16)),
                      writes=(f"gxn{c}",))
            p.memset(Sst[:], 0.0, writes=("S",))
            for hb in range(NHB):
                t0 = hb * HB
                hbk = f"{s}_{hb}"
                with ExitStack() as _st:
                    glc = _st.enter_context(nc.sbuf_tensor(f"g_gl{hbk}", [128, 2, 256], F32))
                    Gb = _st.enter_context(nc.sbuf_tensor(f"g_G{hbk}", [128, 8, 256], F32))
                    tmp = _st.enter_context(nc.sbuf_tensor(f"g_tmp{hbk}", [128, 4, 256], F32))

                    zb = _st.enter_context(nc.sbuf_tensor(f"g_z{hbk}", [128, 128], F32))
                    qt = _st.enter_context(nc.sbuf_tensor(f"g_qt{hbk}", [128, 8, 256], BF16))
                    ktl = _st.enter_context(nc.sbuf_tensor(f"g_kt{hbk}", [128, 8, 256], BF16))
                    khat = _st.enter_context(nc.sbuf_tensor(f"g_kh{hbk}", [128, 8, 256], BF16))
                    vb = _st.enter_context(nc.sbuf_tensor(f"g_v{hbk}", [128, 2, 2048], BF16))
                    gateb = _st.enter_context(nc.sbuf_tensor(f"g_gate{hbk}", [128, 16, 256], BF16))
                    ogb = _st.enter_context(nc.sbuf_tensor(f"g_og{hbk}", [128, 16, 256], BF16))
                    Asb = _st.enter_context(nc.sbuf_tensor(f"g_A{hbk}", [128, 4, 128], BF16))
                    kTb = _st.enter_context(nc.sbuf_tensor(f"g_kT{hbk}", [128, 2, 128], BF16))
                    sqb = _st.enter_context(nc.sbuf_tensor(f"g_sq{hbk}", [128, 1024], BF16))
                    rrow = _st.enter_context(nc.sbuf_tensor(f"g_rr{hbk}", [1, 512], F32))
                    rsb = _st.enter_context(nc.sbuf_tensor(f"g_rs{hbk}", [128, 4], F32))
                    rstdb = _st.enter_context(nc.sbuf_tensor(f"g_rb{hbk}", [128, 512], BF16))
                    rlow = _st.enter_context(nc.sbuf_tensor(f"g_r{hbk}", [16, 256], BF16))
                    p.memset(zb[:], 0.0, writes=("z",))
                    tsl = slice(t0, t0 + 256)
                    for d in range(8):
                        p.mm(cx.pbC[0:16, 0:256], gk1[:, d, :], xnfm[:, d, tsl],
                             d == 0, d == 7, reads=("gk1", f"gxn{d}"), writes=("pbC",))
                    p.copy(rlow[:], cx.pbC[0:16, 0:256], reads=("pbC",), writes=("rlow",))
                    for c in range(8):
                        r = c % 2
                        pb, pk = (cx.pbC, "pbC") if r == 0 else (cx.pbD, "pbD")
                        p.mm(pb[0:128, 0:256], gk2[:, c * 128:(c + 1) * 128], rlow[:],
                             True, True, reads=("gk2", "rlow"), writes=(pk,))
                        fn = (lambda eng, o=glc[:, r, :], i=pb[0:128, 0:256],
                              b=gkb[:, c:c + 1]: eng.activation(o, i, AF.Identity, bias=b))
                        p.emit("scalar", fn, reads=(pk, "gkb"), writes=(f"glc{r}",))
                        p.act(tmp[:, r, :], glc[:, r, :], AF.Abs, reads=(f"glc{r}",),
                              writes=(f"tA{r}",))
                        p.act(tmp[:, r, :], tmp[:, r, :], AF.Exp, scale=-1.0,
                              reads=(f"tA{r}",), writes=(f"tA{r}",))
                        p.act(tmp[:, r, :], tmp[:, r, :], AF.Ln, bias=1.0,
                              reads=(f"tA{r}",), writes=(f"tA{r}",))
                        p.ts(tmp[:, 2 + r, :], glc[:, r, :], 0.0, None, ALU.min,
                             reads=(f"glc{r}",), writes=(f"tM{r}",))
                        p.tt(glc[:, r, :], tmp[:, 2 + r, :], tmp[:, r, :], ALU.subtract,
                             reads=(f"tM{r}", f"tA{r}"), writes=(f"glc{r}",))
                        for ci in range(2):
                            p.scan(Gb[:, c, ci * 128:(ci + 1) * 128],
                                   glc[:, r, ci * 128:(ci + 1) * 128], zb[:], 0.0,
                                   ALU.add, ALU.add, reads=(f"glc{r}", "z"),
                                   writes=(f"G{c}",))
                    for c in range(8):
                        r = c % 2
                        p.act(glc[:, r, :], Gb[:, c, :], AF.Exp, scale=1.0 / 16,
                              reads=(f"G{c}",), writes=(f"glc{r}",))
                        pb, pk = (cx.pbA, "pbA") if r == 0 else (cx.pbB, "pbB")
                        for d in range(8):
                            p.mm(pb[0:128, 0:256], lwq[:, d, c * 128:(c + 1) * 128],
                                 xnfm[:, d, tsl], d == 0, d == 7,
                                 reads=("lwq", f"gxn{d}"), writes=(pk,))
                        p.stt(qt[:, c, :], pb[0:128, 0:256], 0.0625, glc[:, r, :],
                              ALU.mult, ALU.mult, reads=(pk, f"glc{r}"), writes=(f"qt{c}",))
                        p.act(glc[:, r, :], Gb[:, c, :], AF.Exp, scale=-1.0 / 16,
                              reads=(f"G{c}", f"qt{c}"), writes=(f"glc{r}",))
                        for d in range(8):
                            p.mm(pb[0:128, 256:512], lwk[:, d, c * 128:(c + 1) * 128],
                                 xnfm[:, d, tsl], d == 0, d == 7,
                                 reads=("lwk", f"gxn{d}"), writes=(pk,))
                        p.tt(ktl[:, c, :], pb[0:128, 256:512], glc[:, r, :], ALU.mult,
                             reads=(pk, f"glc{r}"), writes=(f"kt{c}",))
                        for ci in range(2):
                            p.act(rsb[:, 0:1], Gb[:, c, ci * 128 + 127:ci * 128 + 128],
                                  AF.Exp, scale=1.0 / 16, reads=(f"G{c}",), writes=("E4",))
                            p.ts(khat[:, c, ci * 128:(ci + 1) * 128],
                                 ktl[:, c, ci * 128:(ci + 1) * 128], rsb[:, 0:1], None,
                                 ALU.mult, reads=(f"kt{c}", "E4"), writes=(f"kh{c}",))
                    for tt in range(2):
                        for vc in range(4):
                            r = (tt * 4 + vc) % 2
                            pb, pk = (cx.pbA, "pbA") if r == 0 else (cx.pbB, "pbB")
                            for d in range(8):
                                p.mm(pb[0:128, 0:512],
                                     xnfm[:, d, t0 + tt * 128:t0 + (tt + 1) * 128],
                                     lwv[:, d, vc * 512:(vc + 1) * 512], d == 0, d == 7,
                                     reads=("lwv", f"gxn{d}"), writes=(pk,))
                            p.copy(vb[:, tt, vc * 512:(vc + 1) * 512],
                                   pb[0:128, 0:512], reads=(pk,), writes=(f"v{tt}",))
                    for vc in range(16):
                        r = vc % 2
                        pb, pk = (cx.pbA, "pbA") if r == 0 else (cx.pbB, "pbB")
                        for d in range(8):
                            p.mm(pb[0:128, 0:256], lwg[:, d, vc * 128:(vc + 1) * 128],
                                 xnfm[:, d, tsl], d == 0, d == 7,
                                 reads=("lwg", f"gxn{d}"), writes=(pk,))
                        p.act(glc[:, r, :], pb[0:128, 0:256], AF.Exp, scale=-1.0,
                              reads=(pk,), writes=(f"glc{r}",))
                        p.ts(glc[:, r, :], glc[:, r, :], 1.0, None, ALU.add,
                             reads=(f"glc{r}",), writes=(f"glc{r}",))
                        p.recip(glc[:, r, :], glc[:, r, :], reads=(f"glc{r}",),
                                writes=(f"glc{r}",))
                        p.tt(gateb[:, vc, :], pb[0:128, 0:256], glc[:, r, :], ALU.mult,
                             reads=(pk, f"glc{r}"), writes=(f"gate{vc}",))
                    for ci in range(2):
                        csl = slice(ci * 128, (ci + 1) * 128)

                        for h in range(4):
                            cc0, cc1 = 2 * h, 2 * h + 1
                            asl = slice(128 * h, 128 * (h + 1))
                            p.mm(cx.pbC[0:128, asl], ktl[:, cc0, csl], qt[:, cc0, csl],
                                 True, False, reads=(f"kt{cc0}", f"qt{cc0}"),
                                 writes=("pbC",))
                            p.mm(cx.pbC[0:128, asl], ktl[:, cc1, csl], qt[:, cc1, csl],
                                 False, True, reads=(f"kt{cc1}", f"qt{cc1}"),
                                 writes=("pbC",))
                            p.tt(Asb[:, h, :], cx.pbC[0:128, asl], cx.mask128[:], ALU.mult,
                                 reads=("pbC", "c_mask128"), writes=(f"Asb{h}",))
                        for h in range(4):
                            p.copy(S0b[:], Sst[:, :, h, :], reads=("S",), writes=("S0b",))
                            s0k = "S0b"
                            cc0, cc1 = 2 * h, 2 * h + 1
                            pb, pk = (cx.pbA, "pbA") if h < 2 else (cx.pbB, "pbB")
                            off = 512 * (h % 2)
                            for j in range(4):
                                jsl = slice(512 * h + 128 * j, 512 * h + 128 * (j + 1))
                                osl = slice(off + 128 * j, off + 128 * (j + 1))
                                p.mm(pb[0:128, osl], vb[:, ci, jsl], Asb[:, h, :],
                                     True, False, reads=(f"v{ci}", f"Asb{h}"), writes=(pk,))
                                p.mm(pb[0:128, osl],
                                     S0b[:, 0, 128 * j:128 * (j + 1)], qt[:, cc0, csl],
                                     False, False, reads=(s0k, f"qt{cc0}"), writes=(pk,))
                                p.mm(pb[0:128, osl],
                                     S0b[:, 1, 128 * j:128 * (j + 1)], qt[:, cc1, csl],
                                     False, True, reads=(s0k, f"qt{cc1}"), writes=(pk,))
                        for half in range(2):
                            pbh, pkh = (cx.pbA, "pbA") if half == 0 else (cx.pbB, "pbB")
                            p.act(sqb[:], pbh[0:128, 0:1024], AF.Square,
                                  reads=(pkh,), writes=(f"osq{half}",))
                            for hh in range(2):
                                h = 2 * half + hh
                                for j in range(4):
                                    p.mm(cx.pbD[0:1, 128 * h:128 * (h + 1)],
                                         cx.ones_bf_col[:],
                                         sqb[:, 512 * hh + 128 * j:512 * hh + 128 * (j + 1)],
                                         j == 0, j == 3,
                                         reads=(f"osq{half}", "c_ones_bf_col"),
                                         writes=("pbD",))
                        p.copy(rrow[:], cx.pbD[0:1, 0:512], reads=("pbD",), writes=("rrow",))
                        p.ts(rrow[:], rrow[:], 1.0 / 512, 1e-5, ALU.mult, ALU.add,
                             reads=("rrow",), writes=("rrow",))
                        p.act(rrow[:], rrow[:], AF.Ln, reads=("rrow",), writes=("rrow",))
                        p.act(rrow[:], rrow[:], AF.Exp, scale=-0.5, reads=("rrow",),
                              writes=("rrow",))
                        p.mm(cx.pbD[0:128, 0:512], cx.ones_f_row[:], rrow[:],
                             True, True, reads=("rrow", "c_ones_f_row"), writes=("pbD",))
                        p.copy(rstdb[:], cx.pbD[0:128, 0:512], reads=("pbD",),
                               writes=("orb",))
                        for h in range(4):
                            pb, pk = (cx.pbA, "pbA") if h < 2 else (cx.pbB, "pbB")
                            off = 512 * (h % 2)
                            for j in range(4):
                                p.stt(ogb[:, 4 * h + j, csl],
                                      pb[0:128, off + 128 * j:off + 128 * (j + 1)],
                                      rw[:, j:j + 1],
                                      rstdb[:, 128 * h:128 * (h + 1)], ALU.mult, ALU.mult,
                                      reads=(pk, "rw", "orb"), writes=(f"og{4 * h + j}",))
                                p.tt(ogb[:, 4 * h + j, csl], ogb[:, 4 * h + j, csl],
                                     gateb[:, 4 * h + j, csl], ALU.mult,
                                     reads=(f"og{4 * h + j}", f"gate{4 * h + j}"),
                                     writes=(f"og{4 * h + j}",))
                        for h in range(4):
                            cc0, cc1 = 2 * h, 2 * h + 1
                            pT, pTk = (cx.pbT, "pbT") if h % 2 == 0 else (cx.pbU, "pbU")
                            for i, cc in enumerate((cc0, cc1)):
                                p.trans(pT[0:128, 512 * i:512 * i + 128], khat[:, cc, csl],
                                        cx.id128[:], reads=(f"kh{cc}", "c_id128"),
                                        writes=(pTk,))
                            p.copy(kTb[:, 0, :], pT[0:128, 0:128], reads=(pTk,),
                                   writes=("kT0",))
                            p.copy(kTb[:, 1, :], pT[0:128, 512:640], reads=(pTk,),
                                   writes=("kT1",))
                            pb, pk = (cx.pbA, "pbA") if h < 2 else (cx.pbB, "pbB")
                            off = 512 * (h % 2)
                            for i, cc in enumerate((cc0, cc1)):
                                psl = slice(off * 0 + 512 * i, 512 * (i + 1))
                                p.mm(pb[0:128, 512 * i:512 * (i + 1)], kTb[:, i, :],
                                     vb[:, ci, 512 * h:512 * (h + 1)], True, True,
                                     reads=(f"kT{i}", f"v{ci}"), writes=(pk,))
                                p.act(rsb[:, 1:2], Gb[:, cc, ci * 128 + 127:ci * 128 + 128],
                                      AF.Exp, scale=1.0 / 16, reads=(f"G{cc}",),
                                      writes=("E4b",))
                                p.stt(Sst[:, i, h, :], Sst[:, i, h, :], rsb[:, 1:2],
                                      pb[0:128, 512 * i:512 * (i + 1)], ALU.mult,
                                      ALU.add, reads=("S", "E4b", pk), writes=("S",))
                    for vc in range(16):
                        p.dma(og_d[vc * 128:(vc + 1) * 128, t0:t0 + 256], ogb[:, vc, :],
                              reads=(f"og{vc}",), writes=(f"ogd{s}_{hb}_{vc}",))
            p.barrier()
            with (nc.sbuf_tensor(f"lwo{s}", [128, 16, 1024], BF16) as lwo,
                  nc.sbuf_tensor(f"ogs{s}", [128, 2, 16, 128], BF16) as ogs,
                  nc.sbuf_tensor(f"glo{s}", [128, 2, 1024], F32) as glo):
                with nc.sbuf_tensor(f"lwostg{s}", [128, 1024], F32) as stg2:
                    LWo_r = LWo.rearrange("(c p) n -> c p n", p=128)
                    cast_load(nc, p, stg2, [(lwo[:, c, :], LWo_r[c], 1024)
                                            for c in range(16)], "lwo", "lwostg")
                p.barrier()
                ogr = og_d.rearrange("(vc p) t -> p vc t", p=128)
                for ti in range(16):
                    b = ti % 2
                    p.dma(ogs[:, b], ogr[:, :, ti * 128:(ti + 1) * 128],
                          reads=tuple(f"ogd{s}_{ti // 2}_{vc}" for vc in range(16)),
                          writes=(f"ogs{b}",))
                    for vc in range(16):
                        p.mm(cx.pbA[0:128, 0:512], ogs[:, b, vc, :],
                             lwo[:, vc, 0:512], vc == 0, vc == 15,
                             reads=(f"ogs{b}", "lwo"), writes=("pbA",))
                        p.mm(cx.pbA[0:128, 512:1024], ogs[:, b, vc, :],
                             lwo[:, vc, 512:1024], vc == 0, vc == 15,
                             reads=(f"ogs{b}", "lwo"), writes=("pbA",))
                    p.copy(glo[:, b], cx.pbA[0:128, 0:1024], reads=("pbA",),
                           writes=(f"glo{b}",))
                    p.dma(gla_r[s * 16 + ti], glo[:, b], reads=(f"glo{b}",),
                          writes=(f"glad{s * 16 + ti}",))
            p.barrier()
    glws.close()
    p.barrier()
    return og_d


def build_ffn1(nc, p, cx, x1_ap, xn1bf_d, gla_d, w_aps, out_ap, dev=False):
    SEQ = 2048
    Wqkv, Bqkv = w_aps["L1_ffn_W_qkv"], w_aps["L1_ffn_b_qkv"]
    Wg, Bg = w_aps["L1_ffn_W_gate"], w_aps["L1_ffn_b_gate"]
    We, Be = w_aps["L1_ffn_W_experts"], w_aps["L1_ffn_b_experts"]

    f1ws = ExitStack()
    wgate = f1ws.enter_context(nc.sbuf_tensor("wgate", [128, 8, 8], BF16))
    bgr = f1ws.enter_context(nc.sbuf_tensor("bgr", [1, 8], F32))
    bgb = f1ws.enter_context(nc.sbuf_tensor("bgb", [128, 8], F32))
    bqc = f1ws.enter_context(nc.sbuf_tensor("bqc", [128, 16], F32))
    bvr = f1ws.enter_context(nc.sbuf_tensor("bvr", [1, 1024], F32))
    bvb = f1ws.enter_context(nc.sbuf_tensor("bvb", [128, 1024], F32))
    ber = f1ws.enter_context(nc.sbuf_tensor("ber", [1, 8, 1024], BF16))
    with nc.sbuf_tensor("f1stage", [128, 1024], F32) as stg:
        Wg_r = Wg.rearrange("(c p) n -> c p n", p=128)
        cast_load(nc, p, stg, [(wgate[:, c, :], Wg_r[c], 8) for c in range(8)],
                  "wgate", "f1stg")
        p.dma(bgr[:], Bg.rearrange("(o n) -> o n", o=1), writes=("bgr",))
        Bq_r = Bqkv[0:2048].rearrange("(c p) -> c p", p=128)
        for c in range(16):
            p.dma(bqc[:, c:c + 1], Bq_r[c].rearrange("(p o) -> p o", o=1), writes=("bqc",))
        p.dma(bvr[:], Bqkv[2048:3072].rearrange("(o n) -> o n", o=1), writes=("bvr",))
        with nc.sbuf_tensor("berf", [1, 8, 1024], F32) as berf:
            p.dma(berf[:], Be.rearrange("(o e) n -> o e n", o=1), writes=("berf",))
            p.copy(ber[:], berf[:], reads=("berf",), writes=("ber",))
    p.barrier()
    p.mm(cx.pbC[0:128, 0:8], cx.ones_f_row[:], bgr[:], True, True,
         reads=("bgr", "c_ones_f_row"), writes=("pbC",))
    p.copy(bgb[:], cx.pbC[0:128, 0:8], reads=("pbC",), writes=("bgb",))
    p.mm(cx.pbA[0:128, 0:512], cx.ones_f_row[:], bvr[:, 0:512], True, True,
         reads=("bvr", "c_ones_f_row"), writes=("pbA",))
    p.copy(bvb[:, 0:512], cx.pbA[0:128, 0:512], reads=("pbA",), writes=("bvb",))
    p.mm(cx.pbA[0:128, 0:512], cx.ones_f_row[:], bvr[:, 512:1024], True, True,
         reads=("bvr", "c_ones_f_row"), writes=("pbA",))
    p.copy(bvb[:, 512:1024], cx.pbA[0:128, 0:512], reads=("pbA",), writes=("bvb",))

    attn1_d = nc.dram_tensor("attn1_d", [4096, 1024], F32,
                             kind="ExternalOutput" if dev else "Internal").ap()
    if dev:
        moe_dbg = nc.dram_tensor("moe_dbg", [4096, 1024], F32, kind="ExternalOutput").ap()
        cmb_dbg = nc.dram_tensor("cmb_dbg", [4096, 8], F32, kind="ExternalOutput").ap()
    at_r = attn1_d.rearrange("(t p) n -> t p n", p=128)
    x1_r = x1_ap.rearrange("(t p) n -> t p n", p=128)
    gla_r2 = gla_d.rearrange("(t p) n -> t p n", p=128)
    out_r = out_ap.rearrange("(t p) n -> t p n", p=128)

    for s in range(2):
        with ExitStack() as sst:
            xnfm = sst.enter_context(nc.sbuf_tensor(f"fxn{s}", [128, 8, 2048], BF16))
            for c in range(8):
                p.dma(xnfm[:, c, :], xn1bf_d[s * SEQ:(s + 1) * SEQ, c * 128:(c + 1) * 128],
                      transpose=True, reads=tuple(f"N1d{s * 16 + i}" for i in range(16)),
                      writes=(f"fxn{c}",))
            qkv_st = ExitStack()
            qfm = qkv_st.enter_context(nc.sbuf_tensor(f"fq{s}", [128, 8, 2048], BF16))
            kfm = qkv_st.enter_context(nc.sbuf_tensor(f"fk{s}", [128, 8, 2048], BF16))
            vaug = qkv_st.enter_context(nc.sbuf_tensor(f"fv{s}", [128, 16, 16, 68], BF16))
            with nc.sbuf_tensor(f"fwqkv{s}", [128, 8, 3072], BF16) as wqkv:
                with nc.sbuf_tensor(f"fwqs{s}", [128, 1024], F32) as stg3:
                    Wqkv_r = Wqkv.rearrange("(c p) n -> c p n", p=128)
                    cast_load(nc, p, stg3,
                              [(wqkv[:, c, j * 1024:(j + 1) * 1024],
                                Wqkv_r[c][:, j * 1024:(j + 1) * 1024], 1024)
                               for c in range(8) for j in range(3)], "wqkv", "fwqs")
                p.barrier()
                for dstt, base, kk in ((qfm, 0, "fqc"), (kfm, 1024, "fkc")):
                    for c in range(8):
                        for qb in range(4):
                            sl = slice(qb * 512, (qb + 1) * 512)
                            for d in range(8):
                                p.mm(cx.pbA[0:128, 0:512],
                                     wqkv[:, d, base + c * 128:base + (c + 1) * 128],
                                     xnfm[:, d, sl], d == 0, d == 7,
                                     reads=("wqkv", f"fxn{d}"), writes=("pbA",))
                            bc = (base + c * 128) // 128
                            fn = (lambda eng, o=dstt[:, c, sl], i=cx.pbA[0:128, 0:512],
                                  b=bqc[:, bc:bc + 1]: eng.activation(o, i, AF.Identity,
                                                                      bias=b))
                            p.emit("scalar", fn, reads=("pbA", "bqc"), writes=(kk + str(c),))
                p.memset(vaug[:, :, :, 64:65], 1.0, writes=("fva_ones",))
                for i in range(16):
                    for d in range(8):
                        p.mm(cx.pbB[0:128, 0:512], xnfm[:, d, i * 128:(i + 1) * 128],
                             wqkv[:, d, 2048:2560], d == 0, d == 7,
                             reads=("wqkv", f"fxn{d}"), writes=("pbB",))
                        p.mm(cx.pbB[0:128, 512:1024], xnfm[:, d, i * 128:(i + 1) * 128],
                             wqkv[:, d, 2560:3072], d == 0, d == 7,
                             reads=("wqkv", f"fxn{d}"), writes=("pbB",))
                    for h in range(16):
                        p.stt(vaug[:, i, h, 0:64], cx.pbB[0:128, h * 64:(h + 1) * 64], 1.0,
                              bvb[:, h * 64:(h + 1) * 64], ALU.mult, ALU.add,
                              reads=("pbB", "bvb"), writes=(f"fva{i}",))
            p.barrier()
            with (nc.sbuf_tensor(f"feb{s}", [128, 2, 1024], BF16) as ebuf,
                  nc.sbuf_tensor(f"fpv{s}", [65, 2, 512], BF16) as pvs,
                  nc.sbuf_tensor(f"fac{s}", [128, 4, 1024], F32) as acc,
                  nc.sbuf_tensor(f"fep{s}", [128, 8], F32) as eps_t):
                for qb in range(4):
                    for hp in range(8):
                        qsl = slice(qb * 512, (qb + 1) * 512)
                        for kt in range(17):
                            if kt < 16:
                                pb, pk = (cx.pbA, "pbA") if kt % 2 == 0 else (cx.pbB, "pbB")
                                eb, ek = (kt % 2), f"feb{kt % 2}"
                                ksl = slice(kt * 128, (kt + 1) * 128)
                                p.mm(pb[0:128, 0:512], kfm[0:64, hp, ksl],
                                     qfm[0:64, hp, qsl],
                                     True, True, reads=(f"fkc{hp}", f"fqc{hp}"),
                                     writes=(pk,))
                                p.mm(pb[0:128, 512:1024], kfm[64:128, hp, ksl],
                                     qfm[64:128, hp, qsl],
                                     True, True, reads=(f"fkc{hp}", f"fqc{hp}"),
                                     writes=(pk,))
                                p.act(ebuf[:, eb], pb[:], AF.Exp, scale=0.125, reads=(pk,),
                                      writes=(ek,))
                            if kt >= 1:
                                kv = kt - 1
                                eb2, ek2 = (kv % 2), f"feb{kv % 2}"
                                p.mm(cx.pbC[0:65, 0:512], vaug[:, kv, 2 * hp, 0:65],
                                     ebuf[:, eb2, 0:512], kv == 0, kv == 15,
                                     reads=(f"fva{kv}", "fva_ones", ek2), writes=("pbC",))
                                p.mm(cx.pbD[0:65, 0:512], vaug[:, kv, 2 * hp + 1, 0:65],
                                     ebuf[:, eb2, 512:1024], kv == 0, kv == 15,
                                     reads=(f"fva{kv}", "fva_ones", ek2), writes=("pbD",))
                        for hh in range(2):
                            pvp, pvk = (cx.pbC, "pbC") if hh == 0 else (cx.pbD, "pbD")
                            pT, pTk = (cx.pbT, "pbT") if hh == 0 else (cx.pbU, "pbU")
                            n = 2 * hp + hh
                            p.copy(pvs[:, hh], pvp[0:65, 0:512], reads=(pvk,),
                                   writes=(f"fpv{hh}",))
                            for j in range(4):
                                p.trans(pT[0:128, j * 256:j * 256 + 65],
                                        pvs[0:65, hh, j * 128:(j + 1) * 128],
                                        cx.id128[0:65, 0:65],
                                        reads=(f"fpv{hh}", "c_id128"), writes=(pTk,))
                            dcols = pT[0:128, 256 * 0 + 64:256 * 3 + 65:256]
                            p.recip(eps_t[:, 4:8], dcols, reads=(pTk,), writes=("frin",))
                            for j in range(4):
                                p.ts(acc[:, j, n * 64:(n + 1) * 64],
                                     pT[0:128, j * 256:j * 256 + 64], eps_t[:, 4 + j:5 + j],
                                     None,
                                     ALU.mult, reads=(pTk, "frin"), writes=(f"fatt{j}",))
                    for j in range(4):
                        p.dma(at_r[s * 16 + qb * 4 + j], acc[:, j, :], reads=(f"fatt{j}",),
                              writes=(f"at1_d{s * 16 + qb * 4 + j}",))
            p.barrier()
            qkv_st.close()
            comb = sst.enter_context(nc.sbuf_tensor(f"fcb{s}", [128, 16, 8], F32))
            with nc.sbuf_tensor(f"fgt{s}", [128, 6, 8], F32) as gt:
                for i in range(16):
                    for d in range(8):
                        p.mm(cx.pbC[0:128, 0:8], xnfm[:, d, i * 128:(i + 1) * 128],
                             wgate[:, d, :], d == 0, d == 7,
                             reads=(f"fxn{d}", "wgate"), writes=("pbC",))
                    p.tt(gt[:, 0, :], cx.pbC[0:128, 0:8], bgb[:], ALU.add,
                         reads=("pbC", "bgb"), writes=("fg0",))
                    p.red(gt[:, 1, 0:1], gt[:, 0, :], ALU.max, negate=True,
                          reads=("fg0",), writes=("fg1",))
                    fn = (lambda eng, o=gt[:, 2, :], i_=gt[:, 0, :], b=gt[:, 1, 0:1]:
                          eng.activation(o, i_, AF.Exp, bias=b))
                    p.emit("scalar", fn, reads=("fg0", "fg1"), writes=("fg2",))
                    p.red(gt[:, 1, 1:2], gt[:, 2, :], ALU.add, reads=("fg2",),
                          writes=("fg3",))
                    p.recip(gt[:, 1, 1:2], gt[:, 1, 1:2], reads=("fg3",), writes=("fg3",))
                    p.ts(gt[:, 2, :], gt[:, 2, :], gt[:, 1, 1:2], None, ALU.mult,
                         reads=("fg2", "fg3"), writes=("fg2",))
                    p.red(gt[:, 1, 2:3], gt[:, 2, :], ALU.max, reads=("fg2",),
                          writes=("fg4",))
                    p.ts(gt[:, 3, :], gt[:, 2, :], gt[:, 1, 2:3], None, ALU.is_ge,
                         reads=("fg2", "fg4"), writes=("fg5",))
                    p.stt(gt[:, 3, :], gt[:, 3, :], -1e30, gt[:, 2, :], ALU.mult, ALU.add,
                          reads=("fg5", "fg2"), writes=("fg5",))
                    p.red(gt[:, 1, 3:4], gt[:, 3, :], ALU.max, reads=("fg5",),
                          writes=("fg6",))
                    p.ts(gt[:, 4, :], gt[:, 2, :], gt[:, 1, 3:4], None, ALU.is_ge,
                         reads=("fg2", "fg6"), writes=("fg7",))
                    p.tt(comb[:, i, :], gt[:, 2, :], gt[:, 4, :], ALU.mult,
                         reads=("fg2", "fg7"), writes=(f"fcb{i}",))
            p.barrier()
            moe = sst.enter_context(nc.sbuf_tensor(f"fmo{s}", [128, 16, 1024], BF16))
            p.memset(moe[:], 0.0, writes=tuple(f"fmo{i}" for i in range(16)))
            with (nc.sbuf_tensor(f"fwe{s}", [128, 2, 8, 1024], BF16) as wex,
                  nc.sbuf_tensor(f"fwf{s}", [128, 2, 4, 1024], F32) as wf32):
                Wer = We.rearrange("e (c p) n -> e c p n", p=128)
                for e in range(8):
                    wb, wk_ = e % 2, f"fwe{e % 2}"
                    for half in range(2):
                        fk_ = f"fwf{wb}"
                        for c in range(4):
                            p.dma(wf32[:, wb, c, :], Wer[e, half * 4 + c], writes=(fk_,))
                        p.copy(wex[:, wb, half * 4:half * 4 + 4, :], wf32[:, wb],
                               reads=(fk_,), writes=(wk_,))
                    for i in range(16):
                        pb, pk = (cx.pbA, "pbA") if i % 2 == 0 else (cx.pbB, "pbB")
                        for d in range(8):
                            p.mm(pb[0:128, 0:512], xnfm[:, d, i * 128:(i + 1) * 128],
                                 wex[:, wb, d, 0:512], d == 0, False,
                                 reads=(wk_, f"fxn{d}"), writes=(pk,))
                            p.mm(pb[0:128, 512:1024], xnfm[:, d, i * 128:(i + 1) * 128],
                                 wex[:, wb, d, 512:1024], d == 0, False,
                                 reads=(wk_, f"fxn{d}"), writes=(pk,))
                        p.mm(pb[0:128, 0:512], cx.ones_bf_row[:], ber[0:1, e, 0:512],
                             False, True, reads=("ber", "c_ones_bf_row"), writes=(pk,))
                        p.mm(pb[0:128, 512:1024], cx.ones_bf_row[:],
                             ber[0:1, e, 512:1024],
                             False, True, reads=("ber", "c_ones_bf_row"), writes=(pk,))
                        p.stt(moe[:, i, :], pb[0:128, 0:1024], comb[:, i, e:e + 1],
                              moe[:, i, :], ALU.mult, ALU.add,
                              reads=(pk, f"fcb{i}", f"fmo{i}"), writes=(f"fmo{i}",))
            if dev:
                for i in range(16):
                    ti = s * 16 + i
                    p.dma(moe_dbg[ti * 128:(ti + 1) * 128, :], moe[:, i, :], e="gpsimd",
                          reads=(f"fmo{i}",), writes=(f"md{ti}",))
                    p.dma(cmb_dbg[ti * 128:(ti + 1) * 128, :], comb[:, i, :],
                          reads=(f"fcb{i}",), writes=(f"cd{ti}",))
            p.barrier()
            with (nc.sbuf_tensor(f"fo1{s}", [128, 1024], F32) as t1,
                  nc.sbuf_tensor(f"fo2{s}", [128, 1024], F32) as t2,
                  nc.sbuf_tensor(f"fo3{s}", [128, 1024], F32) as t3):
                for i in range(16):
                    ti = s * 16 + i
                    p.dma(t1[:], x1_r[ti], writes=("fo1",))
                    p.dma(t2[:], gla_r2[ti], reads=(f"glad{ti}",), writes=("fo2",))
                    p.dma(t3[:], at_r[ti], reads=(f"at1_d{ti}",), writes=("fo3",))
                    p.tt(t1[:], t1[:], t2[:], ALU.add, reads=("fo1", "fo2"), writes=("fo1",))
                    p.tt(t1[:], t1[:], t3[:], ALU.add, reads=("fo1", "fo3"), writes=("fo1",))
                    p.tt(t1[:], t1[:], moe[:, i, :], ALU.add, reads=("fo1", f"fmo{i}"),
                         writes=("fo1",))
                    p.dma(out_r[ti], t1[:], reads=("fo1",), writes=(f"out_d{ti}",))
            p.barrier()
    f1ws.close()
    p.barrier()


L0_WSHAPES = {
    "L0_attn_Wq": (1024, 1024), "L0_attn_Wk": (1024, 512), "L0_attn_Wv": (1024, 512),
    "L0_attn_memory": (512, 512), "L0_attn_memory_norm": (512,),
    "L0_attn_memory_weight": (1,), "L0_ffn_w_up": (4096, 1024),
    "L0_ffn_w_down": (1024, 4096),
}


def build_nc_l0(dev=False):
    nc = bass.Bass("TRN2", target_bir_lowering=False, debug=False)
    p = Prog(nc)
    cx = Ctx(nc, p)
    x = nc.dram_tensor("x", [4096, 1024], F32, kind="ExternalInput").ap()
    w_aps = {k: nc.dram_tensor(k, list(s), F32, kind="ExternalInput").ap()
             for k, s in L0_WSHAPES.items()}
    x1 = nc.dram_tensor("x1", [4096, 1024], F32, kind="ExternalOutput").ap()
    build_l0(nc, p, cx, x, w_aps, x1, dev=dev)
    p.finalize()
    return nc


def kernel_unfused(**inputs):
    cores = list(range(8))
    x = np.ascontiguousarray(np.asarray(inputs["L0_attn_x"], np.float32))
    shards = [np.ascontiguousarray(x[2 * i:2 * i + 2].reshape(4096, 1024))
              for i in range(8)]

    def wm(shapes):
        return {k: np.ascontiguousarray(np.asarray(inputs[k], np.float32))
                for k in shapes}

    w0, w1a, w1f = wm(L0_WSHAPES), wm(L1A_WSHAPES), wm(L1F_WSHAPES)
    r0 = run_bass_kernel_spmd(build_nc_l0(), [dict(x=s, **w0) for s in shards],
                              core_ids=cores)
    x1s = [r0.results[i]["x1"] for i in range(8)]
    r1 = run_bass_kernel_spmd(build_nc_gla(), [dict(x1=x1s[i], **w1a) for i in range(8)],
                              core_ids=cores)
    r2 = run_bass_kernel_spmd(
        build_nc_ffn1(),
        [dict(x1=x1s[i], gla=r1.results[i]["gla"], **w1f)
         for i in range(8)], core_ids=cores)
    out = np.stack([np.asarray(r2.results[i]["out"]).reshape(2, 2048, 1024)
                    for i in range(8)])
    return out.reshape(16, 2048, 1024).astype(np.float32)


L1A_WSHAPES = {
    "L1_attn_Wq": (1024, 1024), "L1_attn_Wk": (1024, 1024),
    "L1_attn_Wv": (1024, 2048), "L1_attn_Wg": (1024, 2048),
    "L1_attn_gk_w1": (1024, 16), "L1_attn_gk_w2": (16, 1024),
    "L1_attn_gk_b": (1024,), "L1_attn_Wo": (2048, 1024), "L1_attn_rms_w": (512,),
}


def build_nc_gla(dev=False):
    nc = bass.Bass("TRN2", target_bir_lowering=False, debug=False)
    p = Prog(nc)
    cx = Ctx(nc, p)
    x1 = nc.dram_tensor("x1", [4096, 1024], F32, kind="ExternalInput").ap()
    w_aps = {k: nc.dram_tensor(k, list(s), F32, kind="ExternalInput").ap()
             for k, s in L1A_WSHAPES.items()}
    gla = nc.dram_tensor("gla", [4096, 1024], F32, kind="ExternalOutput").ap()
    xn1bf = nc.dram_tensor("xn1bf", [4096, 1024], BF16, kind="Internal").ap()
    build_ln(nc, p, cx, x1, xn1bf, "N1")
    build_gla(nc, p, cx, xn1bf, w_aps, gla, dev=dev)
    p.finalize()
    return nc


L1F_WSHAPES = {
    "L1_ffn_W_qkv": (1024, 3072), "L1_ffn_b_qkv": (3072,),
    "L1_ffn_W_gate": (1024, 8), "L1_ffn_b_gate": (8,),
    "L1_ffn_W_experts": (8, 1024, 1024), "L1_ffn_b_experts": (8, 1024),
}


def build_nc_ffn1(dev=False):
    nc = bass.Bass("TRN2", target_bir_lowering=False, debug=False)
    p = Prog(nc)
    cx = Ctx(nc, p)
    x1 = nc.dram_tensor("x1", [4096, 1024], F32, kind="ExternalInput").ap()
    xn1bf = nc.dram_tensor("xn1bf", [4096, 1024], BF16, kind="Internal").ap()
    gla = nc.dram_tensor("gla", [4096, 1024], F32, kind="ExternalInput").ap()
    w_aps = {k: nc.dram_tensor(k, list(s), F32, kind="ExternalInput").ap()
             for k, s in L1F_WSHAPES.items()}
    out = nc.dram_tensor("out", [4096, 1024], F32, kind="ExternalOutput").ap()
    build_ln(nc, p, cx, x1, xn1bf, "N1")
    build_ffn1(nc, p, cx, x1, xn1bf, gla, w_aps, out, dev=dev)
    p.finalize()
    return nc


ALL_WSHAPES = {}
ALL_WSHAPES.update(L0_WSHAPES)
ALL_WSHAPES.update(L1A_WSHAPES)
ALL_WSHAPES.update(L1F_WSHAPES)


def build_nc_fused(dev=False):
    nc = bass.Bass("TRN2", target_bir_lowering=False, debug=False)
    p = Prog(nc)
    cx = Ctx(nc, p)
    x = nc.dram_tensor("x", [4096, 1024], F32, kind="ExternalInput").ap()
    w_aps = {k: nc.dram_tensor(k, list(s), F32, kind="ExternalInput").ap()
             for k, s in ALL_WSHAPES.items()}
    out = nc.dram_tensor("out", [4096, 1024], F32, kind="ExternalOutput").ap()
    x1 = nc.dram_tensor("x1", [4096, 1024], F32, kind="Internal").ap()
    gla = nc.dram_tensor("gla", [4096, 1024], F32, kind="Internal").ap()
    xn1bf = nc.dram_tensor("xn1bf", [4096, 1024], BF16, kind="Internal").ap()
    build_l0(nc, p, cx, x, w_aps, x1, dev=dev, xn1bf_ap=xn1bf)
    p.barrier()
    build_gla(nc, p, cx, xn1bf, w_aps, gla, dev=dev)
    p.barrier()
    build_ffn1(nc, p, cx, x1, xn1bf, gla, w_aps, out, dev=dev)
    p.finalize()
    return nc


def kernel(**inputs):
    cores = list(range(8))
    x = np.ascontiguousarray(np.asarray(inputs["L0_attn_x"], np.float32))
    shards = [np.ascontiguousarray(x[2 * i:2 * i + 2].reshape(4096, 1024))
              for i in range(8)]
    wmap = {k: np.ascontiguousarray(np.asarray(inputs[k], np.float32))
            for k in ALL_WSHAPES}
    nc = build_nc_fused()
    r = run_bass_kernel_spmd(nc, [dict(x=s, **wmap) for s in shards], core_ids=cores)
    out = np.stack([np.asarray(r.results[i]["out"]).reshape(2, 2048, 1024)
                    for i in range(8)])
    return out.reshape(16, 2048, 1024)


# version 13
# speedup vs baseline: 1.4408x; 1.0152x over previous
from contextlib import ExitStack
import numpy as np
import ml_dtypes
import concourse.bass as bass
import concourse.mybir as mybir
from concourse.bass_utils import run_bass_kernel_spmd

F32 = mybir.dt.float32
BF16 = mybir.dt.bfloat16
AF = mybir.ActivationFunctionType
ALU = mybir.AluOpType
AX = mybir.AxisListType.X
bfdt = ml_dtypes.bfloat16

ENGINES = ("sync", "tensor", "vector", "scalar", "gpsimd")
DMA_ENG = ("sync", "scalar", "gpsimd")


class Prog:

    NLANES = 8

    def __init__(self, nc):
        self.nc = nc
        self.c_sem = {e: nc.alloc_semaphore(f"c_{e}") for e in ENGINES}
        self.d_sem = {(e, l): nc.alloc_semaphore(f"d_{e}{l}")
                      for e in DMA_ENG for l in range(self.NLANES)}
        self.ccnt = {e: 0 for e in ENGINES}
        self.dcnt = {(e, l): 0 for e in DMA_ENG for l in range(self.NLANES)}
        self.dtot = {e: 0 for e in DMA_ENG}
        self.steps = {e: [] for e in ENGINES}
        self.lw = {}
        self.readers = {}
        self.wm = {e: {} for e in ENGINES}
        self.n_inst = 0

    def _sem_of(self, kind, oe):
        return self.c_sem[oe] if kind == "c" else self.d_sem[oe]

    def _wait(self, e, waits, dep):
        kind, oe, cnt = dep
        if oe == e and kind == "c":
            if e == "tensor":
                return
        got = self.wm[e].get((kind, oe), 0)
        if cnt <= got:
            return
        waits[(kind, oe)] = max(waits.get((kind, oe), 0), cnt)

    def emit(self, e, fn, reads=(), writes=(), dma=False):
        assert e in ENGINES
        if dma:
            assert e in DMA_ENG
        waits = {}
        for k in reads:
            d = self.lw.get(k)
            if d is not None:
                self._wait(e, waits, d)
        for k in writes:
            d = self.lw.get(k)
            if d is not None:
                self._wait(e, waits, d)
            for r in self.readers.get(k, ()):
                self._wait(e, waits, r)
        if dma:
            lane = self.dtot[e] % self.NLANES
            self.dtot[e] += 1
            if self.dcnt[(e, lane)] > 0:
                self._wait(e, waits, ("d", (e, lane), self.dcnt[(e, lane)]))
        wl = []
        for (kind, oe), cnt in waits.items():
            wl.append((self._sem_of(kind, oe), cnt))
            self.wm[e][(kind, oe)] = cnt
        if dma:
            self.dcnt[(e, lane)] += 16
            inc = (self.d_sem[(e, lane)], 16)
            me = ("d", (e, lane), self.dcnt[(e, lane)])
        else:
            self.ccnt[e] += 1
            inc = (self.c_sem[e], 1)
            me = ("c", e, self.ccnt[e])
        self.steps[e].append((fn, wl, inc))
        self.n_inst += 1
        for k in writes:
            self.lw[k] = me
            self.readers[k] = []
        for k in reads:
            self.readers.setdefault(k, []).append(me)

    def barrier(self):
        tgt = [("c", oe, self.ccnt[oe]) for oe in ENGINES if self.ccnt[oe]] + \
              [("d", (oe, l), self.dcnt[(oe, l)]) for oe in DMA_ENG
               for l in range(self.NLANES) if self.dcnt[(oe, l)]]
        for e in ENGINES:
            wl = []
            for kind, oe, cnt in tgt:
                if self.wm[e].get((kind, oe), 0) >= cnt:
                    continue
                wl.append((self._sem_of(kind, oe), cnt))
                self.wm[e][(kind, oe)] = cnt
            if wl:
                self.steps[e].append((None, wl, None))

    def dma(self, dst, src, e="sync", reads=(), writes=(), transpose=False, slow=False):
        if transpose:
            fn = lambda eng, d=dst, s=src: eng.dma_start_transpose(d, s)
        elif slow:
            fn = lambda eng, d=dst, s=src: eng.dma_start(d, s, allow_slow_non_contiguous=True)
        else:
            fn = lambda eng, d=dst, s=src: eng.dma_start(d, s)
        self.emit(e, fn, reads, writes, dma=True)

    def mm(self, out, lhsT, rhs, start, stop, reads=(), writes=()):
        fn = lambda eng, o=out, l=lhsT, r=rhs, st=start, sp=stop: eng.matmul(o, l, r, start=st, stop=sp)
        self.emit("tensor", fn, reads, writes)

    def trans(self, out, in_, ident, reads=(), writes=()):
        fn = lambda eng, o=out, i=in_, d=ident: eng.transpose(o, i, d)
        self.emit("tensor", fn, reads, writes)

    def act(self, out, in_, func, bias=0.0, scale=1.0, accum_out=None, reads=(), writes=()):
        fn = lambda eng, o=out, i=in_, f=func, b=bias, s=scale, a=accum_out: \
            eng.activation(o, i, f, bias=b, scale=s, accum_out=a)
        self.emit("scalar", fn, reads, writes)

    def tt(self, out, a, b, op, e="vector", reads=(), writes=()):
        fn = lambda eng, o=out, x=a, y=b, p=op: eng.tensor_tensor(o, x, y, p)
        self.emit(e, fn, reads, writes)

    def ts(self, out, in_, s1, s2, op0, op1=ALU.bypass, e="vector", reads=(), writes=()):
        fn = lambda eng, o=out, i=in_, a=s1, b=s2, p0=op0, p1=op1: \
            eng.tensor_scalar(o, i, a, b, p0, p1)
        self.emit(e, fn, reads, writes)

    def stt(self, out, in0, scalar, in1, op0, op1, e="vector", reads=(), writes=()):
        fn = lambda eng, o=out, i0=in0, s=scalar, i1=in1, p0=op0, p1=op1: \
            eng.scalar_tensor_tensor(o, i0, s, i1, p0, p1)
        self.emit(e, fn, reads, writes)

    def copy(self, out, in_, e="vector", reads=(), writes=()):
        fn = lambda eng, o=out, i=in_: eng.tensor_copy(o, i)
        self.emit(e, fn, reads, writes)

    def red(self, out, in_, op, negate=False, absval=False, e="vector", reads=(), writes=()):
        fn = lambda eng, o=out, i=in_, p=op, n=negate, a=absval: \
            eng.tensor_reduce(o, i, AX, p, apply_absolute_value=a, negate=n)
        self.emit(e, fn, reads, writes)

    def memset(self, ap, val, e="vector", reads=(), writes=()):
        fn = lambda eng, a=ap, v=val: eng.memset(a, v)
        self.emit(e, fn, reads, writes)

    def recip(self, out, in_, reads=(), writes=()):
        fn = lambda eng, o=out, i=in_: eng.reciprocal(o, i)
        self.emit("vector", fn, reads, writes)

    def scan(self, out, d0, d1, init, op0, op1, reads=(), writes=()):
        fn = lambda eng, o=out, a=d0, b=d1, i=init, p0=op0, p1=op1: \
            eng.tensor_tensor_scan(o, a, b, i, p0, p1)
        self.emit("vector", fn, reads, writes)

    def finalize(self):
        nc = self.nc
        self.barrier()
        with nc.Block() as block:
            for e in ENGINES:
                mine = self.steps[e]
                if not mine and e != "sync":
                    continue

                def body(eng, mine=mine):
                    for fn, wl, inc in mine:
                        for sem, cnt in wl:
                            eng.wait_ge(sem, cnt)
                        if fn is not None:
                            inst = fn(eng)
                            inst.then_inc(inc[0], inc[1])
                getattr(block, e)(body)


class Ctx:
    def __init__(self, nc, p):
        self.nc, self.p = nc, p
        self.id128 = nc.alloc_sbuf_tensor("c_id128", [128, 128], BF16)
        self.ones_bf_row = nc.alloc_sbuf_tensor("c_ones_bf_row", [1, 128], BF16)
        self.ones_f_row = nc.alloc_sbuf_tensor("c_ones_f_row", [1, 128], F32)
        self.ones_bf_col = nc.alloc_sbuf_tensor("c_ones_bf_col", [128, 1], BF16)
        self.ones_f_col = nc.alloc_sbuf_tensor("c_ones_f_col", [128, 1], F32)
        self.mask128 = nc.alloc_sbuf_tensor("c_mask128", [128, 128], BF16)
        for t, arr in ((self.id128, np.eye(128, dtype=bfdt)),
                       (self.mask128, np.triu(np.ones((128, 128), bfdt))),
                       (self.ones_bf_row, np.ones((1, 128), bfdt)),
                       (self.ones_f_row, np.ones((1, 128), np.float32)),
                       (self.ones_bf_col, np.ones((128, 1), bfdt)),
                       (self.ones_f_col, np.ones((128, 1), np.float32))):
            d = nc.inline_tensor(arr, t.name + "_d").ap()
            p.dma(t[:], d, writes=(t.name,))
        self.pbA = nc.alloc_psum_tensor("pbA", [128, 1024], F32)
        self.pbB = nc.alloc_psum_tensor("pbB", [128, 1024], F32)
        self.pbC = nc.alloc_psum_tensor("pbC", [128, 512], F32)
        self.pbD = nc.alloc_psum_tensor("pbD", [128, 512], F32)
        self.pbT = nc.alloc_psum_tensor("pbT", [128, 1024], BF16)
        self.pbU = nc.alloc_psum_tensor("pbU", [128, 1024], BF16)

    def rsqrt(self, p, out, in_, key, eps=0.0, rkeys=()):
        p.act(out, in_, AF.Ln, bias=eps, reads=rkeys, writes=(key,))
        p.act(out, out, AF.Exp, scale=-0.5, reads=(key,), writes=(key,))



def cast_load(nc, p, stage, dst_slices_srcs, key, stage_key):
    for dst, srcap, cols in dst_slices_srcs:
        p.dma(stage[:, 0:cols], srcap, writes=(stage_key,))
        p.copy(dst, stage[:, 0:cols], reads=(stage_key,), writes=(key,))


def build_l0(nc, p, cx, x_ap, w_aps, x1_ap, dbg=None, dev=False, xn1bf_ap=None):
    NT = 32
    SEQ = 2048
    Wq, Wk, Wv, Mem, Mn, Mw, Wup, Wdown = (
        w_aps["L0_attn_Wq"], w_aps["L0_attn_Wk"], w_aps["L0_attn_Wv"],
        w_aps["L0_attn_memory"], w_aps["L0_attn_memory_norm"],
        w_aps["L0_attn_memory_weight"], w_aps["L0_ffn_w_up"], w_aps["L0_ffn_w_down"])

    l0ws = ExitStack()
    wq_sb = l0ws.enter_context(nc.sbuf_tensor("wq_sb", [128, 8, 1024], BF16))
    wkd_sb = l0ws.enter_context(nc.sbuf_tensor("wkd_sb", [128, 8, 8, 128], BF16))
    wv_sb = l0ws.enter_context(nc.sbuf_tensor("wv_sb", [128, 8, 512], BF16))
    mem_sb = l0ws.enter_context(nc.sbuf_tensor("mem_sb", [128, 4, 512], BF16))
    mn_sb = l0ws.enter_context(nc.sbuf_tensor("mn_sb", [128, 4], BF16))
    with nc.sbuf_tensor("l0stage", [128, 1024], F32) as stg:
        Wq_r = Wq.rearrange("(c p) n -> c p n", p=128)
        cast_load(nc, p, stg, [(wq_sb[:, c, :], Wq_r[c], 1024) for c in range(8)],
                  "wq", "l0stg")
        wk_r = Wk.rearrange("(c p) (h i) -> c p h i", p=128, i=64)
        wk_v = wk_r.rearrange("c p h i -> c p (h i)")
        for c in range(8):
            p.dma(stg[:, 0:512], wk_v[c], writes=("l0stg",))
            sview = stg[:, 0:512].rearrange("p (h i) -> p h i", i=64)
            p.copy(wkd_sb[:, c, :, 0:64], sview, reads=("l0stg",), writes=("wkd0",))
            p.copy(wkd_sb[:, c, :, 64:128], sview, reads=("l0stg",), writes=("wkd1",))
        Wv_r = Wv.rearrange("(c p) n -> c p n", p=128)
        cast_load(nc, p, stg, [(wv_sb[:, c, :], Wv_r[c], 512) for c in range(8)],
                  "wv", "l0stg")
        Mem_r = Mem.rearrange("(c p) n -> c p n", p=128)
        cast_load(nc, p, stg, [(mem_sb[:, c, :], Mem_r[c], 512) for c in range(4)],
                  "mem", "l0stg")
        Mn_r = Mn.rearrange("(c p) -> c p", p=128)
        for c in range(4):
            p.dma(stg[:, 0:1], Mn_r[c].rearrange("(p o) -> p o", o=1), writes=("l0stg",))
            p.copy(mn_sb[:, c:c + 1], stg[:, 0:1], reads=("l0stg",), writes=("mn",))
    p.barrier()

    mw_sb = l0ws.enter_context(nc.sbuf_tensor("mw_sb", [1, 1], F32))
    wrow = l0ws.enter_context(nc.sbuf_tensor("wrow", [1, 2], F32))
    wcols = l0ws.enter_context(nc.sbuf_tensor("wcols", [128, 2], F32))
    p.dma(mw_sb[:], Mw.rearrange("(a b) -> a b", a=1), writes=("mw",))
    p.act(wrow[:, 0:1], mw_sb[:], AF.Exp, scale=-1.0, reads=("mw",), writes=("wr_w",))
    p.ts(wrow[:, 0:1], wrow[:, 0:1], 1.0, None, ALU.add, reads=("wr_w",), writes=("wr_w",))
    p.recip(wrow[:, 0:1], wrow[:, 0:1], reads=("wr_w",), writes=("wr_w",))
    p.ts(wrow[:, 1:2], wrow[:, 0:1], 1.0, -1.0, ALU.subtract, ALU.mult,
         reads=("wr_w",), writes=("wr_w1",))
    p.mm(cx.pbC[0:128, 0:2], cx.ones_f_row[:], wrow[:], True, True,
         reads=("wr_w", "wr_w1", "c_ones_f_row"), writes=("pbC",))
    p.copy(wcols[:], cx.pbC[0:128, 0:2], reads=("pbC",), writes=("wcols",))
    wcol, w1col = wcols[:, 0:1], wcols[:, 1:2]

    KIND = "ExternalOutput" if dev else "Internal"
    wtup = nc.dram_tensor("wtup", [4096, 1024], BF16, kind=KIND).ap()
    wtdown = nc.dram_tensor("wtdown", [1024, 4096], BF16, kind=KIND).ap()

    tws = ExitStack()
    tw0 = tws.enter_context(nc.sbuf_tensor("t_w0", [128, 2048], F32))
    tw1 = tws.enter_context(nc.sbuf_tensor("t_w1", [128, 2048], F32))
    tcm = tws.enter_context(nc.sbuf_tensor("t_cm", [128, 2048], BF16))
    twt = tws.enter_context(nc.sbuf_tensor("t_wt", [128, 2048], BF16))
    tsm = tws.enter_context(nc.sbuf_tensor("t_sm", [128, 6], F32))

    def ternarize(W, n_rows, n_cols, out_dram, tag):
        nt = n_rows // 128
        ncc = n_cols // 2048 if n_cols > 2048 else 1
        cw = n_cols // ncc
        wld = (tw0[:, 0:cw], tw1[:, 0:cw])
        acc, r = tsm[:, 0:1], tsm[:, 1:2]
        thc = tsm[:, 2:3]
        th = tsm[0:1, 4:5]
        p.memset(acc, 0.0, writes=("t_acc",))
        Wr = W.rearrange("(t p) n -> t p n", p=128)
        nch = 0
        for i in range(nt):
            for cc in range(ncc):
                b, bk = wld[nch % 2], f"twl{nch % 2}"
                nch += 1
                p.dma(b, Wr[i][:, cc * cw:(cc + 1) * cw], writes=(bk,))
                p.red(r, b, ALU.add, absval=True, reads=(bk,), writes=("t_r",))
                p.tt(acc, acc, r, ALU.add,
                     reads=("t_r", "t_acc"), writes=("t_acc",))
        p.mm(cx.pbC[0:1, 0:1], acc, cx.ones_f_col[:], True, True,
             reads=("t_acc", "c_ones_f_col"), writes=("pbC",))
        p.ts(th, cx.pbC[0:1, 0:1], 0.5 / (n_rows * n_cols), None, ALU.mult,
             reads=("pbC",), writes=("t_th",))
        p.mm(cx.pbC[0:128, 0:1], cx.ones_f_row[:], th, True, True,
             reads=("t_th", "c_ones_f_row"), writes=("pbC",))
        p.copy(thc, cx.pbC[0:128, 0:1], reads=("pbC",), writes=("t_thc",))
        p.ts(tsm[:, 3:4], thc, -1.0, None, ALU.mult,
             reads=("t_thc",), writes=("t_nthc",))
        nch = 0
        for i in range(nt):
            for cc in range(ncc):
                b, bk = wld[nch % 2], f"twl{nch % 2}"
                nch += 1
                p.dma(b, Wr[i][:, cc * cw:(cc + 1) * cw], writes=(bk,))
                p.ts(tcm[:, 0:cw], b, thc[:, 0:1], None, ALU.is_gt,
                     reads=(bk, "t_thc"), writes=("t_cm",))
                p.ts(twt[:, 0:cw], b, tsm[:, 3:4], None, ALU.is_lt,
                     reads=(bk, "t_nthc"), writes=("t_wt",))
                p.tt(twt[:, 0:cw], tcm[:, 0:cw], twt[:, 0:cw], ALU.subtract,
                     reads=("t_cm", "t_wt"), writes=("t_wt",))
                p.dma(out_dram[i * 128:(i + 1) * 128, cc * cw:(cc + 1) * cw],
                      twt[:, 0:cw], e="scalar", reads=("t_wt",),
                      writes=(f"twd_{tag}{i}_{cc}",))

    ternarize(Wup, 4096, 1024, wtup, "u")
    ternarize(Wdown, 1024, 4096, wtdown, "d")

    xnbf_d = nc.dram_tensor("xnbf", [4096, 1024], BF16, kind=KIND).ap()
    attn_d = nc.dram_tensor("attn0", [4096, 1024], F32, kind=KIND).ap()
    rstd_d = nc.dram_tensor("rstd0", [4096], F32, kind=KIND).ap()
    if dbg is not None:
        dbg["xnbf"] = xnbf_d
        dbg["attn0"] = attn_d

    x_r = x_ap.rearrange("(t p) n -> t p n", p=128)
    x1_r = x1_ap.rearrange("(t p) n -> t p n", p=128)
    xn_r = xnbf_d.rearrange("(t p) n -> t p n", p=128)
    at_r = attn_d.rearrange("(t p) n -> t p n", p=128)

    with (nc.sbuf_tensor("xt", [128, 2, 1024], F32) as xt,
          nc.sbuf_tensor("lnt", [128, 6], F32) as lnt,
          nc.sbuf_tensor("xnt", [128, 2, 1024], BF16) as xnt,
          nc.sbuf_tensor("sqt", [128, 1024], F32) as sqt):
        m = lnt[:, 0:1]; s2 = lnt[:, 1:2]; var = lnt[:, 2:3]
        rstd = lnt[:, 3:4]; rsf = lnt[:, 4:5]; m2 = lnt[:, 5:6]
        for i in range(NT):
            b = i % 2
            xk, nk = f"xt{b}", f"xnt{b}"
            p.dma(xt[:, b], x_r[i], writes=(xk,))
            p.red(m, xt[:, b], ALU.add, reads=(xk,), writes=("ln_m",))
            p.act(sqt[:], xt[:, b], AF.Square, reads=(xk,), writes=("ln_sq",))
            p.red(s2, sqt[:], ALU.add, reads=("ln_sq",), writes=("ln_s2",))
            p.ts(m, m, 1.0 / 1024, None, ALU.mult, reads=("ln_m",), writes=("ln_m",))
            p.ts(s2, s2, 1.0 / 1024, 1e-5, ALU.mult, ALU.add, reads=("ln_s2",), writes=("ln_s2",))
            p.tt(m2, m, m, ALU.mult, reads=("ln_m",), writes=("ln_m2",))
            p.tt(var, s2, m2, ALU.subtract, reads=("ln_s2", "ln_m2"), writes=("ln_v",))
            cx.rsqrt(p, rstd, var, "ln_rstd", rkeys=("ln_v",))
            p.ts(xnt[:, b], xt[:, b], m[:, 0:1], rstd[:, 0:1], ALU.subtract, ALU.mult,
                 reads=(xk, "ln_m", "ln_rstd"), writes=(nk,))
            p.dma(xn_r[i], xnt[:, b], reads=(nk,), writes=(f"xn_d{i}",))
            p.tt(sqt[:], xnt[:, b], xnt[:, b], ALU.mult, reads=(nk,), writes=("ln_sq",))
            p.red(s2, sqt[:], ALU.add, reads=("ln_sq",), writes=("ln_s2",))
            p.ts(s2, s2, 1.0 / 1024, None, ALU.mult, reads=("ln_s2",), writes=("ln_s2",))
            p.act(rsf, s2, AF.Ln, reads=("ln_s2",), writes=("ln_rsf",))
            p.act(rsf, rsf, AF.Exp, scale=0.5, reads=("ln_rsf",), writes=("ln_rsf",))
            p.ts(rsf, rsf, 1e-8, None, ALU.add, reads=("ln_rsf",), writes=("ln_rsf",))
            p.recip(rsf, rsf, reads=("ln_rsf",), writes=("ln_rsf",))
            p.dma(rstd_d[i * 128:(i + 1) * 128].rearrange("(p o) -> p o", o=1), rsf,
                  reads=("ln_rsf",), writes=(f"rs_d{i}",))
    p.barrier()

    for s in range(2):
        T0 = s * 16
        with nc.sbuf_tensor(f"xnfm{s}", [128, 8, 2048], BF16) as xnfm:
            for c in range(8):
                p.dma(xnfm[:, c, :], xnbf_d[s * SEQ:(s + 1) * SEQ, c * 128:(c + 1) * 128],
                      transpose=True, reads=tuple(f"xn_d{T0 + i}" for i in range(16)),
                      writes=(f"xnfm{c}",))
            with (nc.sbuf_tensor(f"kdup{s}", [128, 8, 2048], BF16) as kdup,
                  nc.sbuf_tensor(f"vaug{s}", [128, 16, 8, 68], BF16) as vaug):
                for j in range(8):
                    for qb in range(4):
                        sl = slice(qb * 512, (qb + 1) * 512)
                        for c in range(8):
                            p.mm(cx.pbA[0:128, 0:512], wkd_sb[:, c, j, :], xnfm[:, c, sl],
                                 c == 0, c == 7, reads=("wkd0", "wkd1", f"xnfm{c}"),
                                 writes=("pbA",))
                        p.copy(kdup[:, j, sl], cx.pbA[0:128, 0:512], reads=("pbA",),
                               writes=(f"kd{j}",))
                p.memset(vaug[:, :, :, 64:65], 1.0, writes=("va_ones",))
                for i in range(16):
                    for c in range(8):
                        p.mm(cx.pbB[0:128, 0:512], xnfm[:, c, i * 128:(i + 1) * 128],
                             wv_sb[:, c, :], c == 0, c == 7,
                             reads=(f"xnfm{c}", "wv"), writes=("pbB",))
                    for h in range(8):
                        p.copy(vaug[:, i, h, 0:64], cx.pbB[0:128, h * 64:(h + 1) * 64],
                               reads=("pbB",), writes=(f"va{i}",))

                with (nc.sbuf_tensor(f"qfm{s}", [128, 8, 512], BF16) as qfm,
                      nc.sbuf_tensor(f"sq_bf{s}", [128, 8, 512], BF16) as sq_bf,
                      nc.sbuf_tensor(f"elut{s}", [128, 8, 512], F32) as elut,
                      nc.sbuf_tensor(f"ebuf{s}", [128, 2, 1024], BF16) as ebuf,
                      nc.sbuf_tensor(f"pvs{s}", [65, 2, 512], BF16) as pvs,
                      nc.sbuf_tensor(f"acc{s}", [128, 4, 1024], F32) as acc,
                      nc.sbuf_tensor(f"eps_t{s}", [128, 8], F32) as eps_t):
                    for qb in range(4):
                        qsl = slice(qb * 512, (qb + 1) * 512)
                        for c in range(8):
                            for d in range(8):
                                p.mm(cx.pbA[0:128, 0:512], wq_sb[:, d, c * 128:(c + 1) * 128],
                                     xnfm[:, d, qsl], d == 0, d == 7,
                                     reads=("wq", f"xnfm{d}"), writes=("pbA",))
                            p.copy(qfm[:, c, :], cx.pbA[0:128, 0:512], reads=("pbA",),
                                   writes=(f"qfm{c}",))
                        qk = tuple(f"qfm{c}" for c in range(8))
                        p.ts(elut[:], qfm[:], 0.0, None, ALU.min, reads=qk, writes=("elut",))
                        p.act(elut[:], elut[:], AF.Exp, reads=("elut",), writes=("elut",))
                        p.ts(sq_bf[:], qfm[:], 0.0, -1.0, ALU.max, ALU.add, reads=qk,
                             writes=("sqbf",))
                        p.tt(sq_bf[:], sq_bf[:], elut[:], ALU.add, reads=("sqbf", "elut"),
                             writes=("sqbf",))
                        for g in range(2):
                            for j in range(4):
                                for c in range(4):
                                    cc = 4 * g + c
                                    p.mm(cx.pbA[0:128, 0:512],
                                         sq_bf[:, cc, j * 128:(j + 1) * 128],
                                         mem_sb[:, c, :], c == 0, c == 3,
                                         reads=("sqbf", "mem"), writes=("pbA",))
                                    p.mm(cx.pbC[0:128, 0:1],
                                         sq_bf[:, cc, j * 128:(j + 1) * 128],
                                         mn_sb[:, c:c + 1], c == 0, c == 3,
                                         reads=("sqbf", "mn"), writes=("pbC",))
                                p.recip(eps_t[:, 0:1], cx.pbC[0:128, 0:1], reads=("pbC",),
                                        writes=("rdw",))
                                p.tt(eps_t[:, 0:1], eps_t[:, 0:1], wcol, ALU.mult,
                                     reads=("rdw", "wcols"), writes=("rdw",))
                                p.ts(acc[:, j, g * 512:(g + 1) * 512], cx.pbA[0:128, 0:512],
                                     eps_t[:, 0:1], None, ALU.mult,
                                     reads=("pbA", "rdw"), writes=(f"acc{j}",))
                        for hp in range(8):
                            for kt in range(17):
                                if kt < 16:
                                    pb, pk = (cx.pbA, "pbA") if kt % 2 == 0 else (cx.pbB, "pbB")
                                    eb, ek = (kt % 2), f"eb{kt % 2}"
                                    ksl = slice(kt * 128, (kt + 1) * 128)
                                    p.mm(pb[0:128, 0:512], kdup[0:64, hp, ksl],
                                         qfm[0:64, hp, :],
                                         True, True, reads=(f"kd{hp}", f"qfm{hp}"),
                                         writes=(pk,))
                                    p.mm(pb[0:128, 512:1024], kdup[64:128, hp, ksl],
                                         qfm[64:128, hp, :],
                                         True, True, reads=(f"kd{hp}", f"qfm{hp}"),
                                         writes=(pk,))
                                    p.act(ebuf[:, eb], pb[:], AF.Exp, scale=0.125,
                                          reads=(pk,), writes=(ek,))
                                if kt >= 1:
                                    kv = kt - 1
                                    eb2, ek2 = (kv % 2), f"eb{kv % 2}"
                                    p.mm(cx.pbC[0:65, 0:512], vaug[:, kv, hp, 0:65],
                                         ebuf[:, eb2, 0:512], kv == 0, kv == 15,
                                         reads=(f"va{kv}", "va_ones", ek2), writes=("pbC",))
                                    p.mm(cx.pbD[0:65, 0:512], vaug[:, kv, hp, 0:65],
                                         ebuf[:, eb2, 512:1024], kv == 0, kv == 15,
                                         reads=(f"va{kv}", "va_ones", ek2), writes=("pbD",))
                            for hh in range(2):
                                pvp, pvk = (cx.pbC, "pbC") if hh == 0 else (cx.pbD, "pbD")
                                pT, pTk = (cx.pbT, "pbT") if hh == 0 else (cx.pbU, "pbU")
                                n = 2 * hp + hh
                                p.copy(pvs[:, hh], pvp[0:65, 0:512], reads=(pvk,),
                                       writes=(f"pvs{hh}",))
                                for j in range(4):
                                    p.trans(pT[0:128, j * 256:j * 256 + 65],
                                            pvs[0:65, hh, j * 128:(j + 1) * 128],
                                            cx.id128[0:65, 0:65],
                                            reads=(f"pvs{hh}", "c_id128"), writes=(pTk,))
                                dcols = pT[0:128, 256 * 0 + 64:256 * 3 + 65:256]
                                p.recip(eps_t[:, 4:8], dcols, reads=(pTk,), writes=("rin1",))
                                p.ts(eps_t[:, 4:8], eps_t[:, 4:8], w1col[:, 0:1], None,
                                     ALU.mult, reads=("rin1", "wcols"), writes=("rin1",))
                                for j in range(4):
                                    p.stt(acc[:, j, n * 64:(n + 1) * 64],
                                          pT[0:128, j * 256:j * 256 + 64], eps_t[:, 4 + j:5 + j],
                                          acc[:, j, n * 64:(n + 1) * 64], ALU.mult, ALU.add,
                                          reads=(pTk, "rin1", f"acc{j}"), writes=(f"acc{j}",))
                        for j in range(4):
                            p.dma(at_r[T0 + qb * 4 + j], acc[:, j, :], reads=(f"acc{j}",),
                                  writes=(f"at_d{T0 + qb * 4 + j}",))
                p.barrier()

            with (nc.sbuf_tensor(f"hstore{s}", [128, 32, 512], BF16) as hst,
                  nc.sbuf_tensor(f"xffn{s}", [128, 8, 512], BF16) as xffn,
                  nc.sbuf_tensor(f"wupT{s}", [128, 2, 8, 512], BF16) as wupT,
                  nc.sbuf_tensor(f"wdT{s}", [128, 4, 1024], BF16) as wdT,
                  nc.sbuf_tensor(f"r1row{s}", [1, 512], F32) as r1row,
                  nc.sbuf_tensor(f"r1b{s}", [128, 512], F32) as r1b,
                  nc.sbuf_tensor(f"h2{s}", [128, 2, 512], BF16) as h2t,
                  nc.sbuf_tensor(f"ff{s}", [128, 8], F32) as ff,
                  nc.sbuf_tensor(f"xres{s}", [128, 2, 1024], F32) as xres,
                  nc.sbuf_tensor(f"ares{s}", [128, 2, 1024], F32) as ares,
                  nc.sbuf_tensor(f"lnt2{s}", [128, 8], F32) as lnt2,
                  nc.sbuf_tensor(f"xnt2{s}", [128, 1024], BF16) as xnt2,
                  nc.sbuf_tensor(f"x1t{s}", [128, 1024], F32) as x1t):
                for blk in range(4):
                    t0 = s * SEQ + blk * 512
                    bt0 = blk * 512
                    p.dma(r1row[:], rstd_d[t0:t0 + 512].rearrange("(o n) -> o n", o=1),
                          reads=tuple(f"rs_d{t0 // 128 + i}" for i in range(4)),
                          writes=("r1row",))
                    p.mm(cx.pbD[0:128, 0:512], cx.ones_f_row[:], r1row[:], True, True,
                         reads=("r1row", "c_ones_f_row"), writes=("pbD",))
                    p.copy(r1b[:], cx.pbD[0:128, 0:512], reads=("pbD",), writes=("r1b",))
                    for c in range(8):
                        p.tt(xffn[:, c, :], xnfm[:, c, bt0:bt0 + 512], r1b[:], ALU.mult,
                             reads=(f"xnfm{c}", "r1b"), writes=(f"xf{c}",))
                    for fg in range(8):
                        wb, wk_ = fg % 2, f"wupT{fg % 2}"
                        for d in range(8):
                            p.dma(wupT[:, wb, d, :],
                                  wtup[fg * 512:(fg + 1) * 512, d * 128:(d + 1) * 128],
                                  transpose=True, writes=(wk_,))
                        for fs in range(4):
                            fi = fg * 4 + fs
                            pb, pk = (cx.pbA, "pbA") if fi % 2 == 0 else (cx.pbB, "pbB")
                            for d in range(8):
                                p.mm(pb[0:128, 0:512], wupT[:, wb, d, fs * 128:(fs + 1) * 128],
                                     xffn[:, d, :], d == 0, d == 7,
                                     reads=(wk_,) + tuple(f"xf{dd}" for dd in range(8)),
                                     writes=(pk,))
                            p.act(hst[:, fi, :], pb[0:128, 0:512], AF.Gelu,
                                  reads=(pk,), writes=(f"h{fi}",))
                            p.tt(h2t[:, (fi % 2), :], hst[:, fi, :], hst[:, fi, :],
                                 ALU.mult, reads=(f"h{fi}",), writes=(f"h2_{fi % 2}",))
                            if fi >= 1:
                                fj = fi - 1
                                for j in range(4):
                                    p.mm(cx.pbC[0:128, j * 32 + fj:j * 32 + fj + 1],
                                         h2t[:, (fj % 2), j * 128:(j + 1) * 128],
                                         cx.ones_bf_col[:], True, True,
                                         reads=(f"h2_{fj % 2}", "c_ones_bf_col"),
                                         writes=("pbC",))
                        if fg == 7:
                            for j in range(4):
                                p.mm(cx.pbC[0:128, j * 32 + 31:j * 32 + 32],
                                     h2t[:, 1, j * 128:(j + 1) * 128],
                                     cx.ones_bf_col[:], True, True,
                                     reads=("h2_1", "c_ones_bf_col"), writes=("pbC",))
                    hkeys = tuple(f"h{fi}" for fi in range(32))
                    for j in range(4):
                        p.red(ff[:, 4 + j:5 + j], cx.pbC[0:128, j * 32:j * 32 + 32],
                              ALU.add, reads=("pbC",), writes=("ff_ss",))
                    p.ts(ff[:, 0:4], ff[:, 4:8], 1.0 / 4096, None, ALU.mult,
                         reads=("ff_ss",), writes=("ff_r2",))
                    p.act(ff[:, 0:4], ff[:, 0:4], AF.Ln, reads=("ff_r2",), writes=("ff_r2",))
                    p.act(ff[:, 0:4], ff[:, 0:4], AF.Exp, scale=0.5, reads=("ff_r2",),
                          writes=("ff_r2",))
                    p.ts(ff[:, 0:4], ff[:, 0:4], 1e-8, None, ALU.add, reads=("ff_r2",),
                         writes=("ff_r2",))
                    p.recip(ff[:, 0:4], ff[:, 0:4], reads=("ff_r2",), writes=("ff_r2",))
                    for jp in range(2):
                        for fc in range(32):
                            wb2, wdk = fc % 4, f"wdT{fc % 4}"
                            p.dma(wdT[:, wb2, :], wtdown[0:1024, fc * 128:(fc + 1) * 128],
                                  transpose=True, writes=(wdk,))
                            if fc == 0:
                                for jj in range(2):
                                    ti2 = t0 // 128 + 2 * jp + jj
                                    p.dma(xres[:, jj], x_r[ti2], writes=(f"xres{jj}",))
                                    p.dma(ares[:, jj], at_r[ti2], reads=(f"at_d{ti2}",),
                                          writes=(f"ares{jj}",))
                            for jj in range(2):
                                j = 2 * jp + jj
                                pb, pk = (cx.pbA, "pbA") if jj == 0 else (cx.pbB, "pbB")
                                p.mm(pb[0:128, 0:512], hst[:, fc, j * 128:(j + 1) * 128],
                                     wdT[:, wb2, 0:512], fc == 0, fc == 31,
                                     reads=(wdk,) + hkeys, writes=(pk,))
                                p.mm(pb[0:128, 512:1024], hst[:, fc, j * 128:(j + 1) * 128],
                                     wdT[:, wb2, 512:1024], fc == 0, fc == 31,
                                     reads=(wdk,) + hkeys, writes=(pk,))
                        for jj in range(2):
                            j = 2 * jp + jj
                            ti = t0 // 128 + j
                            pb, pk = (cx.pbA, "pbA") if jj == 0 else (cx.pbB, "pbB")
                            p.tt(x1t[:], xres[:, jj], ares[:, jj], ALU.add,
                                 reads=(f"xres{jj}", f"ares{jj}"), writes=("x1t",))
                            p.stt(x1t[:], pb[0:128, 0:1024], ff[:, j:j + 1], x1t[:],
                                  ALU.mult, ALU.add, reads=(pk, "ff_r2", "x1t"),
                                  writes=("x1t",))
                            p.dma(x1_r[ti], x1t[:], reads=("x1t",), writes=(f"x1_d{ti}",))
                            if xn1bf_ap is not None:
                                m2_ = lnt2[:, 0:1]; s2_ = lnt2[:, 1:2]
                                v_ = lnt2[:, 2:3]; rs_ = lnt2[:, 3:4]; mm_ = lnt2[:, 4:5]
                                p.red(m2_, x1t[:], ALU.add, reads=("x1t",),
                                      writes=("l1m",))
                                p.act(xres[:, jj], x1t[:], AF.Square, reads=("x1t",),
                                      writes=(f"xres{jj}",))
                                p.red(s2_, xres[:, jj], ALU.add, reads=(f"xres{jj}",),
                                      writes=("l1s",))
                                p.ts(m2_, m2_, 1.0 / 1024, None, ALU.mult,
                                     reads=("l1m",), writes=("l1m",))
                                p.ts(s2_, s2_, 1.0 / 1024, 1e-5, ALU.mult, ALU.add,
                                     reads=("l1s",), writes=("l1s",))
                                p.tt(mm_, m2_, m2_, ALU.mult, reads=("l1m",),
                                     writes=("l1mm",))
                                p.tt(v_, s2_, mm_, ALU.subtract, reads=("l1s", "l1mm"),
                                     writes=("l1v",))
                                cx.rsqrt(p, rs_, v_, "l1rs", rkeys=("l1v",))
                                p.ts(xnt2[:], x1t[:], m2_[:, 0:1], rs_[:, 0:1],
                                     ALU.subtract, ALU.mult,
                                     reads=("x1t", "l1m", "l1rs"), writes=("xnt2",))
                                p.dma(xn1bf_ap[ti * 128:(ti + 1) * 128, :], xnt2[:],
                                      reads=("xnt2",), writes=(f"N1d{ti}",))
                p.barrier()
    tws.close()
    l0ws.close()
    p.barrier()
    return {"x1_r": x1_r}


def build_ln(nc, p, cx, src_ap, dst_ap, tag, src_keys=None):
    src_r = src_ap.rearrange("(t p) n -> t p n", p=128)
    dst_r = dst_ap.rearrange("(t p) n -> t p n", p=128)
    NT = src_r.shape[0]
    with (nc.sbuf_tensor(f"lnx{tag}", [128, 2, 1024], F32) as xt,
          nc.sbuf_tensor(f"lns{tag}", [128, 6], F32) as lnt,
          nc.sbuf_tensor(f"lno{tag}", [128, 2, 1024], BF16) as xnt,
          nc.sbuf_tensor(f"lnq{tag}", [128, 1024], F32) as sqt):
        m = lnt[:, 0:1]; s2 = lnt[:, 1:2]; var = lnt[:, 2:3]
        rstd = lnt[:, 3:4]; m2 = lnt[:, 5:6]
        for i in range(NT):
            b = i % 2
            xk, nk = f"{tag}xt{b}", f"{tag}xnt{b}"
            rk = (src_keys[i],) if src_keys else ()
            p.dma(xt[:, b], src_r[i], reads=rk, writes=(xk,))
            p.red(m, xt[:, b], ALU.add, reads=(xk,), writes=(f"{tag}m",))
            p.act(sqt[:], xt[:, b], AF.Square, reads=(xk,), writes=(f"{tag}sq",))
            p.red(s2, sqt[:], ALU.add, reads=(f"{tag}sq",), writes=(f"{tag}s2",))
            p.ts(m, m, 1.0 / 1024, None, ALU.mult, reads=(f"{tag}m",), writes=(f"{tag}m",))
            p.ts(s2, s2, 1.0 / 1024, 1e-5, ALU.mult, ALU.add, reads=(f"{tag}s2",),
                 writes=(f"{tag}s2",))
            p.tt(m2, m, m, ALU.mult, reads=(f"{tag}m",), writes=(f"{tag}m2",))
            p.tt(var, s2, m2, ALU.subtract, reads=(f"{tag}s2", f"{tag}m2"),
                 writes=(f"{tag}v",))
            cx.rsqrt(p, rstd, var, f"{tag}rstd", rkeys=(f"{tag}v",))
            p.ts(xnt[:, b], xt[:, b], m[:, 0:1], rstd[:, 0:1], ALU.subtract, ALU.mult,
                 reads=(xk, f"{tag}m", f"{tag}rstd"), writes=(nk,))
            p.dma(dst_r[i], xnt[:, b], reads=(nk,), writes=(f"{tag}d{i}",))
    p.barrier()


def build_gla(nc, p, cx, xn1bf_d, w_aps, gla_d, dev=False):
    SEQ = 2048
    HB = 256
    NHB = SEQ // HB
    LWq, LWk, LWv, LWg = (w_aps["L1_attn_Wq"], w_aps["L1_attn_Wk"],
                          w_aps["L1_attn_Wv"], w_aps["L1_attn_Wg"])
    GK1, GK2, GKB = w_aps["L1_attn_gk_w1"], w_aps["L1_attn_gk_w2"], w_aps["L1_attn_gk_b"]
    LWo, RMSW = w_aps["L1_attn_Wo"], w_aps["L1_attn_rms_w"]

    glws = ExitStack()
    lwq = glws.enter_context(nc.sbuf_tensor("lwq", [128, 8, 1024], BF16))
    lwk = glws.enter_context(nc.sbuf_tensor("lwk", [128, 8, 1024], BF16))
    lwv = glws.enter_context(nc.sbuf_tensor("lwv", [128, 8, 2048], BF16))
    lwg = glws.enter_context(nc.sbuf_tensor("lwg", [128, 8, 2048], BF16))
    gk1 = glws.enter_context(nc.sbuf_tensor("gk1", [128, 8, 16], BF16))
    gk2 = glws.enter_context(nc.sbuf_tensor("gk2", [16, 1024], BF16))
    gkb = glws.enter_context(nc.sbuf_tensor("gkb", [128, 8], F32))
    rw = glws.enter_context(nc.sbuf_tensor("rw", [128, 4], F32))
    with nc.sbuf_tensor("glstage", [128, 2048], F32) as stg:
        for nm, dst, W, nch, cols in (("lwq", lwq, LWq, 8, 1024), ("lwk", lwk, LWk, 8, 1024),
                                      ("lwv", lwv, LWv, 8, 2048), ("lwg", lwg, LWg, 8, 2048),
                                      ("gk1", gk1, GK1, 8, 16)):
            Wr = W.rearrange("(c p) n -> c p n", p=128)
            cast_load(nc, p, stg, [(dst[:, c, :], Wr[c], cols) for c in range(nch)],
                      nm, "glstg")
        p.dma(stg[0:16, 0:1024], GK2, writes=("glstg",))
        p.copy(gk2[:], stg[0:16, 0:1024], reads=("glstg",), writes=("gk2",))
        GKB_r = GKB.rearrange("(c p) -> c p", p=128)
        for c in range(8):
            p.dma(gkb[:, c:c + 1], GKB_r[c].rearrange("(p o) -> p o", o=1), writes=("gkb",))
        RW_r = RMSW.rearrange("(c p) -> c p", p=128)
        for c in range(4):
            p.dma(rw[:, c:c + 1], RW_r[c].rearrange("(p o) -> p o", o=1), writes=("rw",))
    p.barrier()

    og_d = nc.dram_tensor("og_d", [2048, 2048], BF16,
                          kind="ExternalOutput" if dev else "Internal").ap()
    gla_r = gla_d.rearrange("(t p) n -> t p n", p=128)

    for s in range(2):
        with (nc.sbuf_tensor(f"gxnfm{s}", [128, 8, 2048], BF16) as xnfm,
              nc.sbuf_tensor(f"gS{s}", [128, 2, 4, 512], F32) as Sst,
              nc.sbuf_tensor(f"gS0b{s}", [128, 2, 512], BF16) as S0b):
            for c in range(8):
                p.dma(xnfm[:, c, :], xn1bf_d[s * SEQ:(s + 1) * SEQ, c * 128:(c + 1) * 128],
                      transpose=True, reads=tuple(f"N1d{s * 16 + i}" for i in range(16)),
                      writes=(f"gxn{c}",))
            p.memset(Sst[:], 0.0, writes=("S",))
            for hb in range(NHB):
                t0 = hb * HB
                hbk = f"{s}_{hb}"
                with ExitStack() as _st:
                    glc = _st.enter_context(nc.sbuf_tensor(f"g_gl{hbk}", [128, 2, 256], F32))
                    Gb = _st.enter_context(nc.sbuf_tensor(f"g_G{hbk}", [128, 8, 256], F32))
                    tmp = _st.enter_context(nc.sbuf_tensor(f"g_tmp{hbk}", [128, 4, 256], F32))

                    zb = _st.enter_context(nc.sbuf_tensor(f"g_z{hbk}", [128, 128], F32))
                    qt = _st.enter_context(nc.sbuf_tensor(f"g_qt{hbk}", [128, 8, 256], BF16))
                    ktl = _st.enter_context(nc.sbuf_tensor(f"g_kt{hbk}", [128, 8, 256], BF16))
                    khat = _st.enter_context(nc.sbuf_tensor(f"g_kh{hbk}", [128, 8, 256], BF16))
                    vb = _st.enter_context(nc.sbuf_tensor(f"g_v{hbk}", [128, 2, 2048], BF16))
                    gateb = _st.enter_context(nc.sbuf_tensor(f"g_gate{hbk}", [128, 16, 256], BF16))
                    ogb = _st.enter_context(nc.sbuf_tensor(f"g_og{hbk}", [128, 16, 256], BF16))
                    Asb = _st.enter_context(nc.sbuf_tensor(f"g_A{hbk}", [128, 4, 128], BF16))
                    kTb = _st.enter_context(nc.sbuf_tensor(f"g_kT{hbk}", [128, 2, 128], BF16))
                    sqb = _st.enter_context(nc.sbuf_tensor(f"g_sq{hbk}", [128, 1024], BF16))
                    rrow = _st.enter_context(nc.sbuf_tensor(f"g_rr{hbk}", [1, 512], F32))
                    rsb = _st.enter_context(nc.sbuf_tensor(f"g_rs{hbk}", [128, 4], F32))
                    rstdb = _st.enter_context(nc.sbuf_tensor(f"g_rb{hbk}", [128, 512], BF16))
                    rlow = _st.enter_context(nc.sbuf_tensor(f"g_r{hbk}", [16, 256], BF16))
                    p.memset(zb[:], 0.0, writes=("z",))
                    tsl = slice(t0, t0 + 256)
                    for d in range(8):
                        p.mm(cx.pbC[0:16, 0:256], gk1[:, d, :], xnfm[:, d, tsl],
                             d == 0, d == 7, reads=("gk1", f"gxn{d}"), writes=("pbC",))
                    p.copy(rlow[:], cx.pbC[0:16, 0:256], reads=("pbC",), writes=("rlow",))
                    for c in range(8):
                        r = c % 2
                        pb, pk = (cx.pbC, "pbC") if r == 0 else (cx.pbD, "pbD")
                        p.mm(pb[0:128, 0:256], gk2[:, c * 128:(c + 1) * 128], rlow[:],
                             True, True, reads=("gk2", "rlow"), writes=(pk,))
                        fn = (lambda eng, o=glc[:, r, :], i=pb[0:128, 0:256],
                              b=gkb[:, c:c + 1]: eng.activation(o, i, AF.Identity, bias=b))
                        p.emit("scalar", fn, reads=(pk, "gkb"), writes=(f"glc{r}",))
                        p.act(tmp[:, r, :], glc[:, r, :], AF.Abs, reads=(f"glc{r}",),
                              writes=(f"tA{r}",))
                        p.act(tmp[:, r, :], tmp[:, r, :], AF.Exp, scale=-1.0,
                              reads=(f"tA{r}",), writes=(f"tA{r}",))
                        p.act(tmp[:, r, :], tmp[:, r, :], AF.Ln, bias=1.0,
                              reads=(f"tA{r}",), writes=(f"tA{r}",))
                        p.ts(tmp[:, 2 + r, :], glc[:, r, :], 0.0, None, ALU.min,
                             reads=(f"glc{r}",), writes=(f"tM{r}",))
                        p.tt(glc[:, r, :], tmp[:, 2 + r, :], tmp[:, r, :], ALU.subtract,
                             reads=(f"tM{r}", f"tA{r}"), writes=(f"glc{r}",))
                        for ci in range(2):
                            p.scan(Gb[:, c, ci * 128:(ci + 1) * 128],
                                   glc[:, r, ci * 128:(ci + 1) * 128], zb[:], 0.0,
                                   ALU.add, ALU.add, reads=(f"glc{r}", "z"),
                                   writes=(f"G{c}",))
                    for c in range(8):
                        r = c % 2
                        p.act(glc[:, r, :], Gb[:, c, :], AF.Exp, scale=1.0 / 16,
                              reads=(f"G{c}",), writes=(f"glc{r}",))
                        pb, pk = (cx.pbA, "pbA") if r == 0 else (cx.pbB, "pbB")
                        for d in range(8):
                            p.mm(pb[0:128, 0:256], lwq[:, d, c * 128:(c + 1) * 128],
                                 xnfm[:, d, tsl], d == 0, d == 7,
                                 reads=("lwq", f"gxn{d}"), writes=(pk,))
                        p.stt(qt[:, c, :], pb[0:128, 0:256], 0.0625, glc[:, r, :],
                              ALU.mult, ALU.mult, reads=(pk, f"glc{r}"), writes=(f"qt{c}",))
                        p.act(glc[:, r, :], Gb[:, c, :], AF.Exp, scale=-1.0 / 16,
                              reads=(f"G{c}", f"qt{c}"), writes=(f"glc{r}",))
                        for d in range(8):
                            p.mm(pb[0:128, 256:512], lwk[:, d, c * 128:(c + 1) * 128],
                                 xnfm[:, d, tsl], d == 0, d == 7,
                                 reads=("lwk", f"gxn{d}"), writes=(pk,))
                        p.tt(ktl[:, c, :], pb[0:128, 256:512], glc[:, r, :], ALU.mult,
                             reads=(pk, f"glc{r}"), writes=(f"kt{c}",))
                        for ci in range(2):
                            p.act(rsb[:, 0:1], Gb[:, c, ci * 128 + 127:ci * 128 + 128],
                                  AF.Exp, scale=1.0 / 16, reads=(f"G{c}",), writes=("E4",))
                            p.ts(khat[:, c, ci * 128:(ci + 1) * 128],
                                 ktl[:, c, ci * 128:(ci + 1) * 128], rsb[:, 0:1], None,
                                 ALU.mult, reads=(f"kt{c}", "E4"), writes=(f"kh{c}",))
                    for tt in range(2):
                        for vc in range(4):
                            r = (tt * 4 + vc) % 2
                            pb, pk = (cx.pbA, "pbA") if r == 0 else (cx.pbB, "pbB")
                            for d in range(8):
                                p.mm(pb[0:128, 0:512],
                                     xnfm[:, d, t0 + tt * 128:t0 + (tt + 1) * 128],
                                     lwv[:, d, vc * 512:(vc + 1) * 512], d == 0, d == 7,
                                     reads=("lwv", f"gxn{d}"), writes=(pk,))
                            p.copy(vb[:, tt, vc * 512:(vc + 1) * 512],
                                   pb[0:128, 0:512], reads=(pk,), writes=(f"v{tt}",))
                    for vc in range(16):
                        r = vc % 2
                        pb, pk = (cx.pbA, "pbA") if r == 0 else (cx.pbB, "pbB")
                        for d in range(8):
                            p.mm(pb[0:128, 0:256], lwg[:, d, vc * 128:(vc + 1) * 128],
                                 xnfm[:, d, tsl], d == 0, d == 7,
                                 reads=("lwg", f"gxn{d}"), writes=(pk,))
                        p.act(glc[:, r, :], pb[0:128, 0:256], AF.Exp, scale=-1.0,
                              reads=(pk,), writes=(f"glc{r}",))
                        p.ts(glc[:, r, :], glc[:, r, :], 1.0, None, ALU.add,
                             reads=(f"glc{r}",), writes=(f"glc{r}",))
                        p.recip(glc[:, r, :], glc[:, r, :], reads=(f"glc{r}",),
                                writes=(f"glc{r}",))
                        p.tt(gateb[:, vc, :], pb[0:128, 0:256], glc[:, r, :], ALU.mult,
                             reads=(pk, f"glc{r}"), writes=(f"gate{vc}",))
                    for ci in range(2):
                        csl = slice(ci * 128, (ci + 1) * 128)

                        for h in range(4):
                            cc0, cc1 = 2 * h, 2 * h + 1
                            asl = slice(128 * h, 128 * (h + 1))
                            p.mm(cx.pbC[0:128, asl], ktl[:, cc0, csl], qt[:, cc0, csl],
                                 True, False, reads=(f"kt{cc0}", f"qt{cc0}"),
                                 writes=("pbC",))
                            p.mm(cx.pbC[0:128, asl], ktl[:, cc1, csl], qt[:, cc1, csl],
                                 False, True, reads=(f"kt{cc1}", f"qt{cc1}"),
                                 writes=("pbC",))
                            p.tt(Asb[:, h, :], cx.pbC[0:128, asl], cx.mask128[:], ALU.mult,
                                 reads=("pbC", "c_mask128"), writes=(f"Asb{h}",))
                        for h in range(4):
                            p.copy(S0b[:], Sst[:, :, h, :], reads=("S",), writes=("S0b",))
                            s0k = "S0b"
                            cc0, cc1 = 2 * h, 2 * h + 1
                            pb, pk = (cx.pbA, "pbA") if h < 2 else (cx.pbB, "pbB")
                            off = 512 * (h % 2)
                            for j in range(4):
                                jsl = slice(512 * h + 128 * j, 512 * h + 128 * (j + 1))
                                osl = slice(off + 128 * j, off + 128 * (j + 1))
                                p.mm(pb[0:128, osl], vb[:, ci, jsl], Asb[:, h, :],
                                     True, False, reads=(f"v{ci}", f"Asb{h}"), writes=(pk,))
                                p.mm(pb[0:128, osl],
                                     S0b[:, 0, 128 * j:128 * (j + 1)], qt[:, cc0, csl],
                                     False, False, reads=(s0k, f"qt{cc0}"), writes=(pk,))
                                p.mm(pb[0:128, osl],
                                     S0b[:, 1, 128 * j:128 * (j + 1)], qt[:, cc1, csl],
                                     False, True, reads=(s0k, f"qt{cc1}"), writes=(pk,))
                        for half in range(2):
                            pbh, pkh = (cx.pbA, "pbA") if half == 0 else (cx.pbB, "pbB")
                            p.act(sqb[:], pbh[0:128, 0:1024], AF.Square,
                                  reads=(pkh,), writes=(f"osq{half}",))
                            for hh in range(2):
                                h = 2 * half + hh
                                for j in range(4):
                                    p.mm(cx.pbD[0:1, 128 * h:128 * (h + 1)],
                                         cx.ones_bf_col[:],
                                         sqb[:, 512 * hh + 128 * j:512 * hh + 128 * (j + 1)],
                                         j == 0, j == 3,
                                         reads=(f"osq{half}", "c_ones_bf_col"),
                                         writes=("pbD",))
                        p.copy(rrow[:], cx.pbD[0:1, 0:512], reads=("pbD",), writes=("rrow",))
                        p.ts(rrow[:], rrow[:], 1.0 / 512, 1e-5, ALU.mult, ALU.add,
                             reads=("rrow",), writes=("rrow",))
                        p.act(rrow[:], rrow[:], AF.Ln, reads=("rrow",), writes=("rrow",))
                        p.act(rrow[:], rrow[:], AF.Exp, scale=-0.5, reads=("rrow",),
                              writes=("rrow",))
                        p.mm(cx.pbD[0:128, 0:512], cx.ones_f_row[:], rrow[:],
                             True, True, reads=("rrow", "c_ones_f_row"), writes=("pbD",))
                        p.copy(rstdb[:], cx.pbD[0:128, 0:512], reads=("pbD",),
                               writes=("orb",))
                        for h in range(4):
                            pb, pk = (cx.pbA, "pbA") if h < 2 else (cx.pbB, "pbB")
                            off = 512 * (h % 2)
                            for j in range(4):
                                p.stt(ogb[:, 4 * h + j, csl],
                                      pb[0:128, off + 128 * j:off + 128 * (j + 1)],
                                      rw[:, j:j + 1],
                                      rstdb[:, 128 * h:128 * (h + 1)], ALU.mult, ALU.mult,
                                      reads=(pk, "rw", "orb"), writes=(f"og{4 * h + j}",))
                                p.tt(ogb[:, 4 * h + j, csl], ogb[:, 4 * h + j, csl],
                                     gateb[:, 4 * h + j, csl], ALU.mult,
                                     reads=(f"og{4 * h + j}", f"gate{4 * h + j}"),
                                     writes=(f"og{4 * h + j}",))
                        for h in range(4):
                            cc0, cc1 = 2 * h, 2 * h + 1
                            pT, pTk = (cx.pbT, "pbT") if h % 2 == 0 else (cx.pbU, "pbU")
                            for i, cc in enumerate((cc0, cc1)):
                                p.trans(pT[0:128, 512 * i:512 * i + 128], khat[:, cc, csl],
                                        cx.id128[:], reads=(f"kh{cc}", "c_id128"),
                                        writes=(pTk,))
                            p.copy(kTb[:, 0, :], pT[0:128, 0:128], reads=(pTk,),
                                   writes=("kT0",))
                            p.copy(kTb[:, 1, :], pT[0:128, 512:640], reads=(pTk,),
                                   writes=("kT1",))
                            pb, pk = (cx.pbA, "pbA") if h < 2 else (cx.pbB, "pbB")
                            off = 512 * (h % 2)
                            for i, cc in enumerate((cc0, cc1)):
                                psl = slice(off * 0 + 512 * i, 512 * (i + 1))
                                p.mm(pb[0:128, 512 * i:512 * (i + 1)], kTb[:, i, :],
                                     vb[:, ci, 512 * h:512 * (h + 1)], True, True,
                                     reads=(f"kT{i}", f"v{ci}"), writes=(pk,))
                                p.act(rsb[:, 1:2], Gb[:, cc, ci * 128 + 127:ci * 128 + 128],
                                      AF.Exp, scale=1.0 / 16, reads=(f"G{cc}",),
                                      writes=("E4b",))
                                p.stt(Sst[:, i, h, :], Sst[:, i, h, :], rsb[:, 1:2],
                                      pb[0:128, 512 * i:512 * (i + 1)], ALU.mult,
                                      ALU.add, reads=("S", "E4b", pk), writes=("S",))
                    for vc in range(16):
                        p.dma(og_d[vc * 128:(vc + 1) * 128, t0:t0 + 256], ogb[:, vc, :],
                              reads=(f"og{vc}",), writes=(f"ogd{s}_{hb}_{vc}",))
            p.barrier()
            with (nc.sbuf_tensor(f"lwo{s}", [128, 16, 1024], BF16) as lwo,
                  nc.sbuf_tensor(f"ogs{s}", [128, 2, 16, 128], BF16) as ogs,
                  nc.sbuf_tensor(f"glo{s}", [128, 2, 1024], F32) as glo):
                with nc.sbuf_tensor(f"lwostg{s}", [128, 1024], F32) as stg2:
                    LWo_r = LWo.rearrange("(c p) n -> c p n", p=128)
                    cast_load(nc, p, stg2, [(lwo[:, c, :], LWo_r[c], 1024)
                                            for c in range(16)], "lwo", "lwostg")
                p.barrier()
                ogr = og_d.rearrange("(vc p) t -> p vc t", p=128)
                for ti in range(16):
                    b = ti % 2
                    p.dma(ogs[:, b], ogr[:, :, ti * 128:(ti + 1) * 128],
                          reads=tuple(f"ogd{s}_{ti // 2}_{vc}" for vc in range(16)),
                          writes=(f"ogs{b}",))
                    for vc in range(16):
                        p.mm(cx.pbA[0:128, 0:512], ogs[:, b, vc, :],
                             lwo[:, vc, 0:512], vc == 0, vc == 15,
                             reads=(f"ogs{b}", "lwo"), writes=("pbA",))
                        p.mm(cx.pbA[0:128, 512:1024], ogs[:, b, vc, :],
                             lwo[:, vc, 512:1024], vc == 0, vc == 15,
                             reads=(f"ogs{b}", "lwo"), writes=("pbA",))
                    p.copy(glo[:, b], cx.pbA[0:128, 0:1024], reads=("pbA",),
                           writes=(f"glo{b}",))
                    p.dma(gla_r[s * 16 + ti], glo[:, b], reads=(f"glo{b}",),
                          writes=(f"glad{s * 16 + ti}",))
            p.barrier()
    glws.close()
    p.barrier()
    return og_d


def build_ffn1(nc, p, cx, x1_ap, xn1bf_d, gla_d, w_aps, out_ap, dev=False):
    SEQ = 2048
    Wqkv, Bqkv = w_aps["L1_ffn_W_qkv"], w_aps["L1_ffn_b_qkv"]
    Wg, Bg = w_aps["L1_ffn_W_gate"], w_aps["L1_ffn_b_gate"]
    We, Be = w_aps["L1_ffn_W_experts"], w_aps["L1_ffn_b_experts"]

    f1ws = ExitStack()
    wgate = f1ws.enter_context(nc.sbuf_tensor("wgate", [128, 8, 8], BF16))
    bgr = f1ws.enter_context(nc.sbuf_tensor("bgr", [1, 8], F32))
    bgb = f1ws.enter_context(nc.sbuf_tensor("bgb", [128, 8], F32))
    bqc = f1ws.enter_context(nc.sbuf_tensor("bqc", [128, 16], F32))
    bvr = f1ws.enter_context(nc.sbuf_tensor("bvr", [1, 1024], F32))
    bvb = f1ws.enter_context(nc.sbuf_tensor("bvb", [128, 1024], F32))
    ber = f1ws.enter_context(nc.sbuf_tensor("ber", [1, 8, 1024], BF16))
    with nc.sbuf_tensor("f1stage", [128, 1024], F32) as stg:
        Wg_r = Wg.rearrange("(c p) n -> c p n", p=128)
        cast_load(nc, p, stg, [(wgate[:, c, :], Wg_r[c], 8) for c in range(8)],
                  "wgate", "f1stg")
        p.dma(bgr[:], Bg.rearrange("(o n) -> o n", o=1), writes=("bgr",))
        Bq_r = Bqkv[0:2048].rearrange("(c p) -> c p", p=128)
        for c in range(16):
            p.dma(bqc[:, c:c + 1], Bq_r[c].rearrange("(p o) -> p o", o=1), writes=("bqc",))
        p.dma(bvr[:], Bqkv[2048:3072].rearrange("(o n) -> o n", o=1), writes=("bvr",))
        with nc.sbuf_tensor("berf", [1, 8, 1024], F32) as berf:
            p.dma(berf[:], Be.rearrange("(o e) n -> o e n", o=1), writes=("berf",))
            p.copy(ber[:], berf[:], reads=("berf",), writes=("ber",))
    p.barrier()
    p.mm(cx.pbC[0:128, 0:8], cx.ones_f_row[:], bgr[:], True, True,
         reads=("bgr", "c_ones_f_row"), writes=("pbC",))
    p.copy(bgb[:], cx.pbC[0:128, 0:8], reads=("pbC",), writes=("bgb",))
    p.mm(cx.pbA[0:128, 0:512], cx.ones_f_row[:], bvr[:, 0:512], True, True,
         reads=("bvr", "c_ones_f_row"), writes=("pbA",))
    p.copy(bvb[:, 0:512], cx.pbA[0:128, 0:512], reads=("pbA",), writes=("bvb",))
    p.mm(cx.pbA[0:128, 0:512], cx.ones_f_row[:], bvr[:, 512:1024], True, True,
         reads=("bvr", "c_ones_f_row"), writes=("pbA",))
    p.copy(bvb[:, 512:1024], cx.pbA[0:128, 0:512], reads=("pbA",), writes=("bvb",))

    attn1_d = nc.dram_tensor("attn1_d", [4096, 1024], F32,
                             kind="ExternalOutput" if dev else "Internal").ap()
    if dev:
        moe_dbg = nc.dram_tensor("moe_dbg", [4096, 1024], F32, kind="ExternalOutput").ap()
        cmb_dbg = nc.dram_tensor("cmb_dbg", [4096, 8], F32, kind="ExternalOutput").ap()
    at_r = attn1_d.rearrange("(t p) n -> t p n", p=128)
    x1_r = x1_ap.rearrange("(t p) n -> t p n", p=128)
    gla_r2 = gla_d.rearrange("(t p) n -> t p n", p=128)
    out_r = out_ap.rearrange("(t p) n -> t p n", p=128)

    for s in range(2):
        with ExitStack() as sst:
            xnfm = sst.enter_context(nc.sbuf_tensor(f"fxn{s}", [128, 8, 2048], BF16))
            for c in range(8):
                p.dma(xnfm[:, c, :], xn1bf_d[s * SEQ:(s + 1) * SEQ, c * 128:(c + 1) * 128],
                      transpose=True, reads=tuple(f"N1d{s * 16 + i}" for i in range(16)),
                      writes=(f"fxn{c}",))
            qkv_st = ExitStack()
            qfm = qkv_st.enter_context(nc.sbuf_tensor(f"fq{s}", [128, 8, 2048], BF16))
            kfm = qkv_st.enter_context(nc.sbuf_tensor(f"fk{s}", [128, 8, 2048], BF16))
            vaug = qkv_st.enter_context(nc.sbuf_tensor(f"fv{s}", [128, 16, 16, 68], BF16))
            with nc.sbuf_tensor(f"fwqkv{s}", [128, 8, 3072], BF16) as wqkv:
                with nc.sbuf_tensor(f"fwqs{s}", [128, 1024], F32) as stg3:
                    Wqkv_r = Wqkv.rearrange("(c p) n -> c p n", p=128)
                    cast_load(nc, p, stg3,
                              [(wqkv[:, c, j * 1024:(j + 1) * 1024],
                                Wqkv_r[c][:, j * 1024:(j + 1) * 1024], 1024)
                               for c in range(8) for j in range(3)], "wqkv", "fwqs")
                p.barrier()
                for dstt, base, kk in ((qfm, 0, "fqc"), (kfm, 1024, "fkc")):
                    for c in range(8):
                        for qb in range(4):
                            sl = slice(qb * 512, (qb + 1) * 512)
                            for d in range(8):
                                p.mm(cx.pbA[0:128, 0:512],
                                     wqkv[:, d, base + c * 128:base + (c + 1) * 128],
                                     xnfm[:, d, sl], d == 0, d == 7,
                                     reads=("wqkv", f"fxn{d}"), writes=("pbA",))
                            bc = (base + c * 128) // 128
                            fn = (lambda eng, o=dstt[:, c, sl], i=cx.pbA[0:128, 0:512],
                                  b=bqc[:, bc:bc + 1]: eng.activation(o, i, AF.Identity,
                                                                      bias=b))
                            p.emit("scalar", fn, reads=("pbA", "bqc"), writes=(kk + str(c),))
                p.memset(vaug[:, :, :, 64:65], 1.0, writes=("fva_ones",))
                for i in range(16):
                    for d in range(8):
                        p.mm(cx.pbB[0:128, 0:512], xnfm[:, d, i * 128:(i + 1) * 128],
                             wqkv[:, d, 2048:2560], d == 0, d == 7,
                             reads=("wqkv", f"fxn{d}"), writes=("pbB",))
                        p.mm(cx.pbB[0:128, 512:1024], xnfm[:, d, i * 128:(i + 1) * 128],
                             wqkv[:, d, 2560:3072], d == 0, d == 7,
                             reads=("wqkv", f"fxn{d}"), writes=("pbB",))
                    for h in range(16):
                        p.stt(vaug[:, i, h, 0:64], cx.pbB[0:128, h * 64:(h + 1) * 64], 1.0,
                              bvb[:, h * 64:(h + 1) * 64], ALU.mult, ALU.add,
                              reads=("pbB", "bvb"), writes=(f"fva{i}",))
            p.barrier()
            with (nc.sbuf_tensor(f"feb{s}", [128, 2, 1024], BF16) as ebuf,
                  nc.sbuf_tensor(f"fpv{s}", [65, 2, 512], BF16) as pvs,
                  nc.sbuf_tensor(f"fac{s}", [128, 4, 1024], F32) as acc,
                  nc.sbuf_tensor(f"fep{s}", [128, 8], F32) as eps_t):
                for qb in range(4):
                    for hp in range(8):
                        qsl = slice(qb * 512, (qb + 1) * 512)
                        for kt in range(17):
                            if kt < 16:
                                pb, pk = (cx.pbA, "pbA") if kt % 2 == 0 else (cx.pbB, "pbB")
                                eb, ek = (kt % 2), f"feb{kt % 2}"
                                ksl = slice(kt * 128, (kt + 1) * 128)
                                p.mm(pb[0:128, 0:512], kfm[0:64, hp, ksl],
                                     qfm[0:64, hp, qsl],
                                     True, True, reads=(f"fkc{hp}", f"fqc{hp}"),
                                     writes=(pk,))
                                p.mm(pb[0:128, 512:1024], kfm[64:128, hp, ksl],
                                     qfm[64:128, hp, qsl],
                                     True, True, reads=(f"fkc{hp}", f"fqc{hp}"),
                                     writes=(pk,))
                                p.act(ebuf[:, eb], pb[:], AF.Exp, scale=0.125, reads=(pk,),
                                      writes=(ek,))
                            if kt >= 1:
                                kv = kt - 1
                                eb2, ek2 = (kv % 2), f"feb{kv % 2}"
                                p.mm(cx.pbC[0:65, 0:512], vaug[:, kv, 2 * hp, 0:65],
                                     ebuf[:, eb2, 0:512], kv == 0, kv == 15,
                                     reads=(f"fva{kv}", "fva_ones", ek2), writes=("pbC",))
                                p.mm(cx.pbD[0:65, 0:512], vaug[:, kv, 2 * hp + 1, 0:65],
                                     ebuf[:, eb2, 512:1024], kv == 0, kv == 15,
                                     reads=(f"fva{kv}", "fva_ones", ek2), writes=("pbD",))
                        for hh in range(2):
                            pvp, pvk = (cx.pbC, "pbC") if hh == 0 else (cx.pbD, "pbD")
                            pT, pTk = (cx.pbT, "pbT") if hh == 0 else (cx.pbU, "pbU")
                            n = 2 * hp + hh
                            p.copy(pvs[:, hh], pvp[0:65, 0:512], reads=(pvk,),
                                   writes=(f"fpv{hh}",))
                            for j in range(4):
                                p.trans(pT[0:128, j * 256:j * 256 + 65],
                                        pvs[0:65, hh, j * 128:(j + 1) * 128],
                                        cx.id128[0:65, 0:65],
                                        reads=(f"fpv{hh}", "c_id128"), writes=(pTk,))
                            dcols = pT[0:128, 256 * 0 + 64:256 * 3 + 65:256]
                            p.recip(eps_t[:, 4:8], dcols, reads=(pTk,), writes=("frin",))
                            for j in range(4):
                                p.ts(acc[:, j, n * 64:(n + 1) * 64],
                                     pT[0:128, j * 256:j * 256 + 64], eps_t[:, 4 + j:5 + j],
                                     None,
                                     ALU.mult, reads=(pTk, "frin"), writes=(f"fatt{j}",))
                    for j in range(4):
                        p.dma(at_r[s * 16 + qb * 4 + j], acc[:, j, :], reads=(f"fatt{j}",),
                              writes=(f"at1_d{s * 16 + qb * 4 + j}",))
            p.barrier()
            qkv_st.close()
            comb = sst.enter_context(nc.sbuf_tensor(f"fcb{s}", [128, 16, 8], F32))
            with nc.sbuf_tensor(f"fgt{s}", [128, 6, 8], F32) as gt:
                for i in range(16):
                    for d in range(8):
                        p.mm(cx.pbC[0:128, 0:8], xnfm[:, d, i * 128:(i + 1) * 128],
                             wgate[:, d, :], d == 0, d == 7,
                             reads=(f"fxn{d}", "wgate"), writes=("pbC",))
                    p.tt(gt[:, 0, :], cx.pbC[0:128, 0:8], bgb[:], ALU.add,
                         reads=("pbC", "bgb"), writes=("fg0",))
                    p.red(gt[:, 1, 0:1], gt[:, 0, :], ALU.max, negate=True,
                          reads=("fg0",), writes=("fg1",))
                    fn = (lambda eng, o=gt[:, 2, :], i_=gt[:, 0, :], b=gt[:, 1, 0:1]:
                          eng.activation(o, i_, AF.Exp, bias=b))
                    p.emit("scalar", fn, reads=("fg0", "fg1"), writes=("fg2",))
                    p.red(gt[:, 1, 1:2], gt[:, 2, :], ALU.add, reads=("fg2",),
                          writes=("fg3",))
                    p.recip(gt[:, 1, 1:2], gt[:, 1, 1:2], reads=("fg3",), writes=("fg3",))
                    p.ts(gt[:, 2, :], gt[:, 2, :], gt[:, 1, 1:2], None, ALU.mult,
                         reads=("fg2", "fg3"), writes=("fg2",))
                    p.red(gt[:, 1, 2:3], gt[:, 2, :], ALU.max, reads=("fg2",),
                          writes=("fg4",))
                    p.ts(gt[:, 3, :], gt[:, 2, :], gt[:, 1, 2:3], None, ALU.is_ge,
                         reads=("fg2", "fg4"), writes=("fg5",))
                    p.stt(gt[:, 3, :], gt[:, 3, :], -1e30, gt[:, 2, :], ALU.mult, ALU.add,
                          reads=("fg5", "fg2"), writes=("fg5",))
                    p.red(gt[:, 1, 3:4], gt[:, 3, :], ALU.max, reads=("fg5",),
                          writes=("fg6",))
                    p.ts(gt[:, 4, :], gt[:, 2, :], gt[:, 1, 3:4], None, ALU.is_ge,
                         reads=("fg2", "fg6"), writes=("fg7",))
                    p.tt(comb[:, i, :], gt[:, 2, :], gt[:, 4, :], ALU.mult,
                         reads=("fg2", "fg7"), writes=(f"fcb{i}",))
            p.barrier()
            moe = sst.enter_context(nc.sbuf_tensor(f"fmo{s}", [128, 16, 1024], BF16))
            p.memset(moe[:], 0.0, writes=tuple(f"fmo{i}" for i in range(16)))
            with (nc.sbuf_tensor(f"fwe{s}", [128, 2, 8, 1024], BF16) as wex,
                  nc.sbuf_tensor(f"fwf{s}", [128, 2, 4, 1024], F32) as wf32):
                Wer = We.rearrange("e (c p) n -> e c p n", p=128)
                for e in range(8):
                    wb, wk_ = e % 2, f"fwe{e % 2}"
                    for half in range(2):
                        fk_ = f"fwf{wb}"
                        for c in range(4):
                            p.dma(wf32[:, wb, c, :], Wer[e, half * 4 + c], writes=(fk_,))
                        p.copy(wex[:, wb, half * 4:half * 4 + 4, :], wf32[:, wb],
                               reads=(fk_,), writes=(wk_,))
                    for i in range(16):
                        pb, pk = (cx.pbA, "pbA") if i % 2 == 0 else (cx.pbB, "pbB")
                        for d in range(8):
                            p.mm(pb[0:128, 0:512], xnfm[:, d, i * 128:(i + 1) * 128],
                                 wex[:, wb, d, 0:512], d == 0, False,
                                 reads=(wk_, f"fxn{d}"), writes=(pk,))
                            p.mm(pb[0:128, 512:1024], xnfm[:, d, i * 128:(i + 1) * 128],
                                 wex[:, wb, d, 512:1024], d == 0, False,
                                 reads=(wk_, f"fxn{d}"), writes=(pk,))
                        p.mm(pb[0:128, 0:512], cx.ones_bf_row[:], ber[0:1, e, 0:512],
                             False, True, reads=("ber", "c_ones_bf_row"), writes=(pk,))
                        p.mm(pb[0:128, 512:1024], cx.ones_bf_row[:],
                             ber[0:1, e, 512:1024],
                             False, True, reads=("ber", "c_ones_bf_row"), writes=(pk,))
                        p.stt(moe[:, i, :], pb[0:128, 0:1024], comb[:, i, e:e + 1],
                              moe[:, i, :], ALU.mult, ALU.add,
                              reads=(pk, f"fcb{i}", f"fmo{i}"), writes=(f"fmo{i}",))
            if dev:
                for i in range(16):
                    ti = s * 16 + i
                    p.dma(moe_dbg[ti * 128:(ti + 1) * 128, :], moe[:, i, :], e="gpsimd",
                          reads=(f"fmo{i}",), writes=(f"md{ti}",))
                    p.dma(cmb_dbg[ti * 128:(ti + 1) * 128, :], comb[:, i, :],
                          reads=(f"fcb{i}",), writes=(f"cd{ti}",))
            p.barrier()
            with (nc.sbuf_tensor(f"fo1{s}", [128, 1024], F32) as t1,
                  nc.sbuf_tensor(f"fo2{s}", [128, 1024], F32) as t2,
                  nc.sbuf_tensor(f"fo3{s}", [128, 1024], F32) as t3):
                for i in range(16):
                    ti = s * 16 + i
                    p.dma(t1[:], x1_r[ti], writes=("fo1",))
                    p.dma(t2[:], gla_r2[ti], reads=(f"glad{ti}",), writes=("fo2",))
                    p.dma(t3[:], at_r[ti], reads=(f"at1_d{ti}",), writes=("fo3",))
                    p.tt(t1[:], t1[:], t2[:], ALU.add, reads=("fo1", "fo2"), writes=("fo1",))
                    p.tt(t1[:], t1[:], t3[:], ALU.add, reads=("fo1", "fo3"), writes=("fo1",))
                    p.tt(t1[:], t1[:], moe[:, i, :], ALU.add, reads=("fo1", f"fmo{i}"),
                         writes=("fo1",))
                    p.dma(out_r[ti], t1[:], reads=("fo1",), writes=(f"out_d{ti}",))
            p.barrier()
    f1ws.close()
    p.barrier()


L0_WSHAPES = {
    "L0_attn_Wq": (1024, 1024), "L0_attn_Wk": (1024, 512), "L0_attn_Wv": (1024, 512),
    "L0_attn_memory": (512, 512), "L0_attn_memory_norm": (512,),
    "L0_attn_memory_weight": (1,), "L0_ffn_w_up": (4096, 1024),
    "L0_ffn_w_down": (1024, 4096),
}


def build_nc_l0(dev=False):
    nc = bass.Bass("TRN2", target_bir_lowering=False, debug=False)
    p = Prog(nc)
    cx = Ctx(nc, p)
    x = nc.dram_tensor("x", [4096, 1024], F32, kind="ExternalInput").ap()
    w_aps = {k: nc.dram_tensor(k, list(s), F32, kind="ExternalInput").ap()
             for k, s in L0_WSHAPES.items()}
    x1 = nc.dram_tensor("x1", [4096, 1024], F32, kind="ExternalOutput").ap()
    build_l0(nc, p, cx, x, w_aps, x1, dev=dev)
    p.finalize()
    return nc


def kernel_unfused(**inputs):
    cores = list(range(8))
    x = np.ascontiguousarray(np.asarray(inputs["L0_attn_x"], np.float32))
    shards = [np.ascontiguousarray(x[2 * i:2 * i + 2].reshape(4096, 1024))
              for i in range(8)]

    def wm(shapes):
        return {k: np.ascontiguousarray(np.asarray(inputs[k], np.float32))
                for k in shapes}

    w0, w1a, w1f = wm(L0_WSHAPES), wm(L1A_WSHAPES), wm(L1F_WSHAPES)
    r0 = run_bass_kernel_spmd(build_nc_l0(), [dict(x=s, **w0) for s in shards],
                              core_ids=cores)
    x1s = [r0.results[i]["x1"] for i in range(8)]
    r1 = run_bass_kernel_spmd(build_nc_gla(), [dict(x1=x1s[i], **w1a) for i in range(8)],
                              core_ids=cores)
    r2 = run_bass_kernel_spmd(
        build_nc_ffn1(),
        [dict(x1=x1s[i], gla=r1.results[i]["gla"], **w1f)
         for i in range(8)], core_ids=cores)
    out = np.stack([np.asarray(r2.results[i]["out"]).reshape(2, 2048, 1024)
                    for i in range(8)])
    return out.reshape(16, 2048, 1024).astype(np.float32)


L1A_WSHAPES = {
    "L1_attn_Wq": (1024, 1024), "L1_attn_Wk": (1024, 1024),
    "L1_attn_Wv": (1024, 2048), "L1_attn_Wg": (1024, 2048),
    "L1_attn_gk_w1": (1024, 16), "L1_attn_gk_w2": (16, 1024),
    "L1_attn_gk_b": (1024,), "L1_attn_Wo": (2048, 1024), "L1_attn_rms_w": (512,),
}


def build_nc_gla(dev=False):
    nc = bass.Bass("TRN2", target_bir_lowering=False, debug=False)
    p = Prog(nc)
    cx = Ctx(nc, p)
    x1 = nc.dram_tensor("x1", [4096, 1024], F32, kind="ExternalInput").ap()
    w_aps = {k: nc.dram_tensor(k, list(s), F32, kind="ExternalInput").ap()
             for k, s in L1A_WSHAPES.items()}
    gla = nc.dram_tensor("gla", [4096, 1024], F32, kind="ExternalOutput").ap()
    xn1bf = nc.dram_tensor("xn1bf", [4096, 1024], BF16, kind="Internal").ap()
    build_ln(nc, p, cx, x1, xn1bf, "N1")
    build_gla(nc, p, cx, xn1bf, w_aps, gla, dev=dev)
    p.finalize()
    return nc


L1F_WSHAPES = {
    "L1_ffn_W_qkv": (1024, 3072), "L1_ffn_b_qkv": (3072,),
    "L1_ffn_W_gate": (1024, 8), "L1_ffn_b_gate": (8,),
    "L1_ffn_W_experts": (8, 1024, 1024), "L1_ffn_b_experts": (8, 1024),
}


def build_nc_ffn1(dev=False):
    nc = bass.Bass("TRN2", target_bir_lowering=False, debug=False)
    p = Prog(nc)
    cx = Ctx(nc, p)
    x1 = nc.dram_tensor("x1", [4096, 1024], F32, kind="ExternalInput").ap()
    xn1bf = nc.dram_tensor("xn1bf", [4096, 1024], BF16, kind="Internal").ap()
    gla = nc.dram_tensor("gla", [4096, 1024], F32, kind="ExternalInput").ap()
    w_aps = {k: nc.dram_tensor(k, list(s), F32, kind="ExternalInput").ap()
             for k, s in L1F_WSHAPES.items()}
    out = nc.dram_tensor("out", [4096, 1024], F32, kind="ExternalOutput").ap()
    build_ln(nc, p, cx, x1, xn1bf, "N1")
    build_ffn1(nc, p, cx, x1, xn1bf, gla, w_aps, out, dev=dev)
    p.finalize()
    return nc


ALL_WSHAPES = {}
ALL_WSHAPES.update(L0_WSHAPES)
ALL_WSHAPES.update(L1A_WSHAPES)
ALL_WSHAPES.update(L1F_WSHAPES)


def build_nc_fused(dev=False):
    nc = bass.Bass("TRN2", target_bir_lowering=False, debug=False)
    p = Prog(nc)
    cx = Ctx(nc, p)
    x = nc.dram_tensor("x", [4096, 1024], F32, kind="ExternalInput").ap()
    w_aps = {k: nc.dram_tensor(k, list(s), F32, kind="ExternalInput").ap()
             for k, s in ALL_WSHAPES.items()}
    out = nc.dram_tensor("out", [4096, 1024], F32, kind="ExternalOutput").ap()
    x1 = nc.dram_tensor("x1", [4096, 1024], F32, kind="Internal").ap()
    gla = nc.dram_tensor("gla", [4096, 1024], F32, kind="Internal").ap()
    xn1bf = nc.dram_tensor("xn1bf", [4096, 1024], BF16, kind="Internal").ap()
    build_l0(nc, p, cx, x, w_aps, x1, dev=dev, xn1bf_ap=xn1bf)
    p.barrier()
    build_gla(nc, p, cx, xn1bf, w_aps, gla, dev=dev)
    p.barrier()
    build_ffn1(nc, p, cx, x1, xn1bf, gla, w_aps, out, dev=dev)
    p.finalize()
    return nc


def kernel(**inputs):
    cores = list(range(8))
    x = np.ascontiguousarray(np.asarray(inputs["L0_attn_x"], np.float32))
    shards = [np.ascontiguousarray(x[2 * i:2 * i + 2].reshape(4096, 1024))
              for i in range(8)]
    wmap = {k: np.ascontiguousarray(np.asarray(inputs[k], np.float32))
            for k in ALL_WSHAPES}
    nc = build_nc_fused()
    r = run_bass_kernel_spmd(nc, [dict(x=s, **wmap) for s in shards], core_ids=cores)
    out = np.stack([np.asarray(r.results[i]["out"]).reshape(2, 2048, 1024)
                    for i in range(8)])
    return out.reshape(16, 2048, 1024)
